# version 03
# baseline (speedup 1.0000x reference)
import numpy as np
import ml_dtypes
import concourse.bass as bass
import concourse.mybir as mybir
from concourse.bass_utils import run_bass_kernel_spmd

F32 = mybir.dt.float32
BF16 = mybir.dt.bfloat16
AF = mybir.ActivationFunctionType
ALU = mybir.AluOpType
AXX = mybir.AxisListType.X

B, S, E, H, HD = 4, 4096, 1024, 16, 64
NCH = E // 128
TOK = 512
NT = S // TOK
NSUB = TOK // 128
EPS = 1e-5
OWN = 2048
XQ0, XQW = 3968, 128
QTOT = OWN + XQW
NKC = S // 128
M3 = 3 * E

BF16NP = ml_dtypes.bfloat16


class _Sem:
    def __init__(self, nc, name):
        self.h = nc.alloc_semaphore(name=name)
        self.n = 0


class PB:
    ENGS = ("sync", "gpsimd", "scalar", "vector", "tensor")

    def __init__(self, nc):
        self.nc = nc
        self.streams = {k: [] for k in self.ENGS}
        self._waited = {k: {} for k in self.ENGS}
        self._pools = {}

    def sem(self, name):
        return _Sem(self.nc, name)

    def _pool(self, eng):
        if eng not in self._pools:
            self._pools[eng] = {"sems": [self.sem(f"{eng}_dq{i}") for i in range(12)],
                                "i": 0}
        return self._pools[eng]

    def dmas(self, eng, pairs, slow=False):
        pool = self._pool(eng)
        sem = pool["sems"][pool["i"] % len(pool["sems"])]
        pool["i"] += 1
        if sem.n > 0:
            self.wait(eng, sem)
        for (dst, srcap) in pairs:
            self.op(eng, (lambda e, d=dst, s_=srcap, sl=slow:
                          e.dma_start(d, s_, allow_slow_non_contiguous=sl)),
                    sem=sem, k=16)
        return (sem, sem.n)

    def drain(self, eng):
        self.streams[eng].append(lambda e: e.drain())

    def wait_tok(self, eng, tok):
        if tok is not None:
            self.wait(eng, tok[0], tok[1])

    def wait_all_dma(self, eng_waiter):
        for qeng in ("sync", "gpsimd"):
            if qeng in self._pools:
                for sem in self._pools[qeng]["sems"]:
                    self.wait(eng_waiter, sem)

    def op(self, eng, fn, sem=None, k=1):
        h = None
        if sem is not None:
            sem.n += k
            h = sem.h
        def run(e, fn=fn, h=h, k=k):
            inst = fn(e)
            if h is not None:
                inst.then_inc(h, k)
        self.streams[eng].append(run)

    def wait(self, eng, sem, val=None):
        v = sem.n if val is None else val
        if v <= 0:
            return
        if v <= self._waited[eng].get(id(sem), -1):
            return
        self._waited[eng][id(sem)] = v
        self.streams[eng].append(lambda e, h=sem.h, v=v: e.wait_ge(h, v))

    def emit(self, blk):
        for name in self.ENGS:
            ops = self.streams[name]
            if not ops:
                continue
            def body(e, ops=ops):
                for f in ops:
                    f(e)
            getattr(blk, name)(body)
        self.streams = {k: [] for k in self.ENGS}

    def flush(self):
        if any(self.streams.values()):
            with self.nc.Block() as blk:
                self.emit(blk)


def _ap3(handle, offset, dims):
    return bass.AP(handle, offset, [list(d) for d in dims])


def build_program(nc, dbg=False, phases=("A1", "A2", "B", "C")):
    phases = {p.split(":")[0]: (int(p.split(":")[1]) if ":" in p else None)
              for p in phases}
    spill_kind = "ExternalOutput" if dbg else "Internal"

    d_in = lambda n, shp, dt=F32: nc.dram_tensor(n, shp, dt, kind="ExternalInput")
    x_fm = d_in("x_fm", [E, S])
    x_tm = d_in("x_tm", [S, E])
    xprev = d_in("xprev", [1, E])
    m2048 = d_in("m2048", [128, 1])
    m0x = d_in("m0x", [128, 1])
    mu_r = d_in("mu_r", [128, NCH])
    mu_k = d_in("mu_k", [128, NCH])
    wrT = d_in("wrT", [E, E])
    wkT = d_in("wkT", [E, E])
    wvT = d_in("wvT", [E, E])
    wqkv = d_in("wqkv", [E, M3])
    bqkv = d_in("bqkv", [128, 24])
    wout = d_in("wout", [E, E])
    bout = d_in("bout", [128, NCH])
    out_fm = nc.dram_tensor("out_fm", [E, OWN], F32, kind="ExternalOutput")

    mk_sp = lambda n, shp, dt: nc.dram_tensor(n, shp, dt, kind=spill_kind)
    r1_sp = mk_sp("r1_sp", [E, S], F32)
    x2h_sp = mk_sp("x2h_sp", [E, S], BF16)
    x2l_sp = mk_sp("x2l_sp", [E, S], BF16)
    qh_sp = mk_sp("qh_sp", [E, S], BF16)
    ql_sp = mk_sp("ql_sp", [E, S], BF16)
    kh_sp = mk_sp("kh_sp", [E, S], BF16)
    kl_sp = mk_sp("kl_sp", [E, S], BF16)
    vth_sp = mk_sp("vth_sp", [H, S, HD], BF16)
    vtl_sp = mk_sp("vtl_sp", [H, S, HD], BF16)
    oh_sp = mk_sp("oh_sp", [E, QTOT], BF16)
    ol_sp = mk_sp("ol_sp", [E, QTOT], BF16)
    yprow_sp = mk_sp("yprow_sp", [1, E], F32)
    dbg_bci = mk_sp("dbg_bci", [128, TOK], F32)
    dbg_bcn = mk_sp("dbg_bcn", [128, TOK], F32)
    dbg_st = mk_sp("dbg_st", [128, 16 * NSUB], F32)
    dbg_xtm = mk_sp("dbg_xtm", [128, NSUB * 1024], F32)
    dbg_y = mk_sp("dbg_y", [128, TOK + 1], F32)
    dbg_r2 = mk_sp("dbg_r2", [E, TOK], F32)
    wr_lo = mk_sp("wr_lo", [E, E], BF16)
    wk_lo = mk_sp("wk_lo", [E, E], BF16)
    wv_lo = mk_sp("wv_lo", [E, E], BF16)
    wr_hi_sp = mk_sp("wr_hi_sp", [E, E], BF16)
    wk_hi_sp = mk_sp("wk_hi_sp", [E, E], BF16)
    wv_hi_sp = mk_sp("wv_hi_sp", [E, E], BF16)
    wq_lo = mk_sp("wq_lo", [E, M3], BF16)
    wo_lo = mk_sp("wo_lo", [E, E], BF16)

    id_f32 = nc.inline_tensor(np.eye(128, dtype=np.float32), name="id_f32")
    id_bf = nc.inline_tensor(np.eye(128, dtype=BF16NP), name="id_bf")
    ones_bf = nc.inline_tensor(np.ones((128, 1), dtype=BF16NP), name="ones_bf")
    dmask_np = np.zeros((128, 128), dtype=np.float32)
    dmask_np[0, 0:64] = dmask_np[32, 0:64] = 1.0
    dmask_np[64, 64:128] = dmask_np[96, 64:128] = 1.0
    dmask = nc.inline_tensor(dmask_np, name="dmask")

    P = PB(nc)
    s = {k: P.sem(k) for k in ("ld", "sp", "dve", "act", "pe", "bar", "out")}

    from contextlib import ExitStack

    def sb(ctx, name, shape, dt):
        return ctx.enter_context(nc.sbuf_tensor(name, shape, dt))

    def ps(ctx, name, shape, dt=F32):
        return ctx.enter_context(nc.psum_tensor(name, shape, dt))

    def phase_barrier():
        for eng in PB.ENGS:
            P.wait_all_dma(eng)
            for k in ("dve", "act", "pe"):
                P.wait(eng, s[k])

    id_f32_s = nc.alloc_sbuf_tensor("id_f32_s", [128, 128], F32)
    id_bf_s = nc.alloc_sbuf_tensor("id_bf_s", [128, 128], BF16)
    ones_bf_s = nc.alloc_sbuf_tensor("ones_bf_s", [128, 1], BF16)
    dmask_s = nc.alloc_sbuf_tensor("dmask_s", [128, 128], F32)
    mur_s = nc.alloc_sbuf_tensor("mur_s", [128, NCH], F32)
    muk_s = nc.alloc_sbuf_tensor("muk_s", [128, NCH], F32)
    bqkv_s = nc.alloc_sbuf_tensor("bqkv_s", [128, 24], F32)
    bout_s = nc.alloc_sbuf_tensor("bout_s", [128, NCH], F32)
    m2048_s = nc.alloc_sbuf_tensor("m2048_s", [128, 1], F32)
    m0x_s = nc.alloc_sbuf_tensor("m0x_s", [128, 1], F32)
    ypcol_s = nc.alloc_sbuf_tensor("ypcol_s", [128, NCH], F32)

    const_tok = P.dmas("sync", [
        (dst[:, :], src.ap()[:, :]) for dst, src in
        ((id_f32_s, id_f32), (id_bf_s, id_bf), (ones_bf_s, ones_bf),
         (dmask_s, dmask), (mur_s, mu_r), (muk_s, mu_k), (bqkv_s, bqkv),
         (bout_s, bout), (m2048_s, m2048), (m0x_s, m0x))])
    for eng in ("tensor", "scalar", "vector", "gpsimd"):
        P.wait_tok(eng, const_tok)

    def emit_yprev(xprow_s, yprow_s, pscal_s):
        t = P.dmas("sync", [(xprow_s[:, :], xprev.ap()[:, :])])
        P.wait_tok("scalar", t)
        P.wait_tok("vector", t)
        P.op("scalar", lambda e: e.activation(yprow_s[:, :], xprow_s[:, :],
                                              AF.Square, bias=0.0, scale=1.0), sem=s["act"])
        P.wait("vector", s["act"])
        P.op("vector", lambda e: e.reduce_sum(pscal_s[:, 0:1], xprow_s[:, :], axis=AXX))
        P.op("vector", lambda e: e.reduce_sum(pscal_s[:, 1:2], yprow_s[:, :], axis=AXX))
        P.drain("vector")
        P.op("vector", lambda e: e.tensor_scalar_mul(pscal_s[:, 2:3], pscal_s[:, 0:1], 1.0 / E))
        P.op("vector", lambda e: e.tensor_scalar_mul(pscal_s[:, 3:4], pscal_s[:, 1:2], 1.0 / E))
        P.drain("vector")
        P.op("vector", lambda e: e.tensor_mul(pscal_s[:, 4:5], pscal_s[:, 2:3], pscal_s[:, 2:3]))
        P.drain("vector")
        P.op("vector", lambda e: e.tensor_sub(pscal_s[:, 5:6], pscal_s[:, 3:4], pscal_s[:, 4:5]))
        P.drain("vector")
        P.op("vector", lambda e: e.tensor_scalar_add(pscal_s[:, 5:6], pscal_s[:, 5:6], EPS),
             sem=s["dve"])
        P.wait("scalar", s["dve"])
        P.op("scalar", lambda e: e.activation(pscal_s[:, 6:7], pscal_s[:, 5:6],
                                              AF.Sqrt, bias=0.0, scale=1.0), sem=s["act"])
        P.wait("vector", s["act"])
        P.op("vector", lambda e: e.reciprocal(pscal_s[:, 6:7], pscal_s[:, 6:7]))
        P.op("vector", lambda e: e.tensor_scalar_mul(pscal_s[:, 7:8], pscal_s[:, 2:3], -1.0))
        P.drain("vector")
        P.op("vector", lambda e: e.tensor_scalar(
            yprow_s[:, :], xprow_s[:, :], pscal_s[:, 7:8], pscal_s[:, 6:7],
            ALU.add, ALU.mult), sem=s["dve"])
        P.wait("gpsimd", s["dve"])
        t2 = P.dmas("gpsimd", [(yprow_sp.ap()[:, :], yprow_s[:, :])])
        P.wait_tok("sync", t2)
        return P.dmas("sync", [(ypcol_s[:, :],
                                _ap3(yprow_sp, 0, [[1, 128], [128, NCH]]))],
                      slow=True)



    def emit_weight_prep(w_dram_ap, m, hi_sb, lo_sp_h, hi_sp_h, tmp, lotmp):
        sp_tok = None
        for k in range(NCH):
            P.wait("sync", s["dve"])
            t = P.dmas("sync", [(tmp[:, 0:m], w_dram_ap[k * 128:(k + 1) * 128, :])])
            P.wait_tok("scalar", t)
            hi_slice = hi_sb[:, k * m:(k + 1) * m]
            P.op("scalar", lambda e, o=hi_slice, i=tmp[:, 0:m]:
                 e.activation(o, i, AF.Identity, bias=0.0, scale=1.0), sem=s["act"])
            P.wait("vector", s["act"])
            P.wait_tok("vector", sp_tok)
            P.wait_tok("vector", t)
            P.op("vector", lambda e, o=lotmp[:, 0:m], i=tmp[:, 0:m], h=hi_slice:
                 e.tensor_tensor(o, i, h, ALU.subtract), sem=s["dve"])
            P.wait("gpsimd", s["dve"])
            grp = [(lo_sp_h[k * 128:(k + 1) * 128, :], lotmp[:, 0:m])]
            if hi_sp_h is not None:
                P.wait("gpsimd", s["act"])
                grp.append((hi_sp_h[k * 128:(k + 1) * 128, :], hi_slice))
            sp_tok = P.dmas("gpsimd", grp)

    def mk_stats(ctx, tag, nsub):
        names = ("s1", "s2", "mean", "ex2", "m2", "var", "sd", "inv", "nm")
        return {n: sb(ctx, f"{tag}_{n}", [128, nsub], F32) for n in names}

    def ln_stats_chain(st, nsub):
        P.op("vector", lambda e: e.tensor_scalar_mul(st["mean"][:, 0:nsub], st["s1"][:, 0:nsub], 1.0 / E))
        P.op("vector", lambda e: e.tensor_scalar_mul(st["ex2"][:, 0:nsub], st["s2"][:, 0:nsub], 1.0 / E))
        P.drain("vector")
        P.op("vector", lambda e: e.tensor_mul(st["m2"][:, 0:nsub], st["mean"][:, 0:nsub], st["mean"][:, 0:nsub]))
        P.drain("vector")
        P.op("vector", lambda e: e.tensor_sub(st["var"][:, 0:nsub], st["ex2"][:, 0:nsub], st["m2"][:, 0:nsub]))
        P.drain("vector")
        P.op("vector", lambda e: e.tensor_scalar_add(st["var"][:, 0:nsub], st["var"][:, 0:nsub], EPS),
             sem=s["dve"])
        P.wait("scalar", s["dve"])
        P.op("scalar", lambda e: e.activation(st["sd"][:, 0:nsub], st["var"][:, 0:nsub],
                                              AF.Sqrt, bias=0.0, scale=1.0), sem=s["act"])
        P.wait("vector", s["act"])
        P.op("vector", lambda e: e.reciprocal(st["inv"][:, 0:nsub], st["sd"][:, 0:nsub]))
        P.drain("vector")
        P.op("vector", lambda e: e.tensor_mul(st["nm"][:, 0:nsub], st["mean"][:, 0:nsub], st["inv"][:, 0:nsub]))
        P.drain("vector")
        P.op("vector", lambda e: e.tensor_scalar_mul(st["nm"][:, 0:nsub], st["nm"][:, 0:nsub], -1.0),
             sem=s["dve"])

    def emit_stats_from_tm(st, src_list, xsq, nsub):
        for sub in range(nsub):
            src = src_list[sub]
            P.wait("scalar", s["dve"])
            P.op("scalar", lambda e, o=xsq[:, :], i=src:
                 e.activation(o, i, AF.Square, bias=0.0, scale=1.0), sem=s["act"])
            P.op("vector", lambda e, o=st["s1"][:, sub:sub + 1], i=src:
                 e.reduce_sum(o, i, axis=AXX))
            P.wait("vector", s["act"])
            P.op("vector", lambda e, o=st["s2"][:, sub:sub + 1], i=xsq[:, :]:
                 e.reduce_sum(o, i, axis=AXX), sem=s["dve"])
        ln_stats_chain(st, nsub)

    def bc_tiles(st, nsub, bci_ps, bcn_ps):
        P.wait("tensor", s["dve"])
        for sub in range(nsub):
            iap = st["inv"][:, sub:sub + 1].broadcast_to([128, 128])
            nap = st["nm"][:, sub:sub + 1].broadcast_to([128, 128])
            P.op("tensor", lambda e, o=bci_ps[:, sub * 128:(sub + 1) * 128], w=iap:
                 e.matmul(o, w, id_f32_s[:, :], start=True, stop=True))
            P.op("tensor", lambda e, o=bcn_ps[:, sub * 128:(sub + 1) * 128], w=nap:
                 e.matmul(o, w, id_f32_s[:, :], start=True, stop=True),
                 sem=(s["pe"] if sub == nsub - 1 else None))

    def emit_ffn(bufs, w, tokw, resid_ap_fn, out_fn, out_toks):
        T = tokw
        xr_hi, xr_lo = bufs["xr_hi"], bufs["xr_lo"]
        xk_hi, xk_lo = bufs["xk_hi"], bufs["xk_lo"]
        ka_hi, ka_lo = bufs["ka_hi"], bufs["ka_lo"]
        r_sig, kf, of, pb = bufs["r_sig"], bufs["kf"], bufs["of"], bufs["pb"]

        def sweep(proj, src_hi, src_lo, evac):
            whi = w[proj + "_hi"]
            for o in range(NCH):
                lo_ap = w["lo_load"](proj, o)
                bank = pb[o % 2]
                P.wait("tensor", s["dve"], val=bufs["bank_free_dve"][o % 2])
                P.wait("tensor", s["act"], val=bufs["bank_free_act"][o % 2])
                mms = ([(whi[:, k * E + o * 128: k * E + o * 128 + 128],
                         src_hi[:, k * T:(k + 1) * T]) for k in range(NCH)] +
                       [(whi[:, k * E + o * 128: k * E + o * 128 + 128],
                         src_lo[:, k * T:(k + 1) * T]) for k in range(NCH)] +
                       [(lo_ap[:, k * 128:(k + 1) * 128],
                         src_hi[:, k * T:(k + 1) * T]) for k in range(NCH)])
                n = len(mms)
                for i, (wap, xap) in enumerate(mms):
                    P.op("tensor",
                         lambda e, o_=bank[:, 0:T], w_=wap, x_=xap,
                                st=(i == 0), sp2=(i == n - 1):
                         e.matmul(o_, w_, x_, start=st, stop=sp2),
                         sem=(s["pe"] if i == n - 1 else None))
                evac(o, bank)

        def evac_r(o, bank):
            P.wait("scalar", s["pe"])
            P.op("scalar", lambda e, o_=r_sig[:, o * T:(o + 1) * T], i=bank[:, 0:T]:
                 e.activation(o_, i, AF.Sigmoid, bias=0.0, scale=1.0), sem=s["act"])
            bufs["bank_free_act"][o % 2] = s["act"].n

        def evac_k(o, bank):
            P.wait("vector", s["pe"])
            P.op("vector", lambda e, o_=kf[:, 0:T], i=bank[:, 0:T]:
                 e.tensor_scalar_max(o_, i, 0.0), sem=s["dve"])
            bufs["bank_free_dve"][o % 2] = s["dve"].n
            xfb = bufs["xf"]
            P.op("vector", lambda e, o_=xfb[:, 0:T], i=kf[:, 0:T]:
                 e.tensor_tensor(o_, i, i, ALU.mult), sem=s["dve"])
            P.wait("scalar", s["dve"])
            P.op("scalar", lambda e, o_=ka_hi[:, o * T:(o + 1) * T], i=xfb[:, 0:T]:
                 e.activation(o_, i, AF.Identity, bias=0.0, scale=1.0), sem=s["act"])
            P.wait("vector", s["act"])
            P.op("vector", lambda e, o_=ka_lo[:, o * T:(o + 1) * T], i=xfb[:, 0:T],
                        h=ka_hi[:, o * T:(o + 1) * T]:
                 e.tensor_tensor(o_, i, h, ALU.subtract), sem=s["dve"])

        def evac_v(o, bank):
            P.wait("vector", s["pe"])
            P.wait("vector", s["act"])
            slot = o % 2
            of_ap = of[:, slot * T:(slot + 1) * T]
            if out_toks is not None:
                P.wait_tok("vector", out_toks[slot])
            P.op("vector", lambda e, o_=kf[:, 0:T], i=bank[:, 0:T],
                        r=r_sig[:, o * T:(o + 1) * T]:
                 e.tensor_tensor(o_, i, r, ALU.mult), sem=s["dve"])
            bufs["bank_free_dve"][o % 2] = s["dve"].n
            P.op("vector", lambda e, o_=of_ap, i=kf[:, 0:T], r=resid_ap_fn(o):
                 e.scalar_tensor_tensor(o_, i, 0.5, r, ALU.mult, ALU.add),
                 sem=s["dve"])
            out_fn(o, of_ap)

        sweep("wr", xr_hi, xr_lo, evac_r)
        sweep("wk", xk_hi, xk_lo, evac_k)
        P.wait("tensor", s["dve"])
        P.wait("tensor", s["act"])
        sweep("wv", ka_hi, ka_lo, evac_v)

    def emit_mix(bufs, x_src_fn, tokw, carry_init_fn, pre_normalized):
        T = tokw
        y_ext, ytmp, dbuf, xf = bufs["y_ext"], bufs["ytmp"], bufs["dbuf"], bufs["xf"]
        for c in range(NCH):
            carry_init_fn(c)
            x_c = x_src_fn(c)
            if pre_normalized:
                P.op("vector", lambda e, x=x_c: e.tensor_copy(y_ext[:, 1:T + 1], x))
            else:
                P.op("vector", lambda e, x=x_c, b=bufs["bci"][:, 0:T]:
                     e.tensor_tensor(ytmp[:, 0:T], x, b, ALU.mult))
                P.op("vector", lambda e, b=bufs["bcn"][:, 0:T]:
                     e.tensor_tensor(y_ext[:, 1:T + 1], ytmp[:, 0:T], b, ALU.add))
            P.op("vector", lambda e: e.tensor_tensor(
                dbuf[:, 0:T], y_ext[:, 1:T + 1], y_ext[:, 0:T], ALU.subtract))
            for (mu, hi, lo) in ((mur_s, bufs["xr_hi"], bufs["xr_lo"]),
                                 (muk_s, bufs["xk_hi"], bufs["xk_lo"])):
                P.op("vector", lambda e, m=mu[:, c:c + 1]:
                     e.scalar_tensor_tensor(xf[:, 0:T], dbuf[:, 0:T], m,
                                            y_ext[:, 0:T], ALU.mult, ALU.add),
                     sem=s["dve"])
                P.wait("scalar", s["dve"])
                P.op("scalar", lambda e, o=hi[:, c * T:(c + 1) * T]:
                     e.activation(o, xf[:, 0:T], AF.Identity, bias=0.0, scale=1.0),
                     sem=s["act"])
                P.wait("vector", s["act"])
                P.op("vector", lambda e, o=lo[:, c * T:(c + 1) * T],
                            h=hi[:, c * T:(c + 1) * T]:
                     e.tensor_tensor(o, xf[:, 0:T], h, ALU.subtract), sem=s["dve"])
            P.op("vector", lambda e, c=c: e.tensor_copy(
                bufs["ycar"][:, c:c + 1], y_ext[:, T:T + 1]),
                sem=(s["dve"] if c == NCH - 1 else None))

    with ExitStack() as ctx:
        with ExitStack() as yctx:
            xprow_s = sb(yctx, "xprow_s", [1, E], F32)
            yprow_s = sb(yctx, "yprow_s", [1, E], F32)
            pscal_s = sb(yctx, "pscal_s", [1, 8], F32)
            ypcol_tok = emit_yprev(xprow_s, yprow_s, pscal_s)
        wtmp = sb(ctx, "p0_wtmp", [128, E], F32)
        lotmp = sb(ctx, "p0_lotmp", [128, E], BF16)
        wr_hi_s = sb(ctx, "wr_hi_s", [128, NCH * E], BF16)
        wk_hi_s = sb(ctx, "wk_hi_s", [128, NCH * E], BF16)
        wv_hi_s = sb(ctx, "wv_hi_s", [128, NCH * E], BF16)
        for (wd, hs, lsp, hsp) in ((wrT, wr_hi_s, wr_lo, wr_hi_sp),
                                   (wkT, wk_hi_s, wk_lo, wk_hi_sp),
                                   (wvT, wv_hi_s, wv_lo, wv_hi_sp)):
            emit_weight_prep(wd.ap(), E, hs, lsp.ap(), hsp.ap(), wtmp, lotmp)
        phase_barrier()

        xs = sb(ctx, "a1_xs", [128, NCH * TOK], F32)
        xtm = sb(ctx, "a1_xtm", [128, NSUB * 1024], F32)
        xsq = sb(ctx, "a1_xsq", [128, 1024], F32)
        st1 = mk_stats(ctx, "a1_st1", NSUB)
        st2 = mk_stats(ctx, "a1_st2", NSUB)
        bci = sb(ctx, "a1_bci", [128, TOK], F32)
        bcn = sb(ctx, "a1_bcn", [128, TOK], F32)
        y_ext = sb(ctx, "a1_yext", [128, TOK + 1], F32)
        ytmp = sb(ctx, "a1_ytmp", [128, TOK], F32)
        dbuf = sb(ctx, "a1_dbuf", [128, TOK], F32)
        xf = sb(ctx, "a1_xf", [128, TOK], F32)
        ycar = sb(ctx, "a1_ycar", [128, NCH], F32)
        xr_hi = sb(ctx, "a1_xr_hi", [128, NCH * TOK], BF16)
        xr_lo = sb(ctx, "a1_xr_lo", [128, NCH * TOK], BF16)
        xk_hi = sb(ctx, "a1_xk_hi", [128, NCH * TOK], BF16)
        xk_lo = sb(ctx, "a1_xk_lo", [128, NCH * TOK], BF16)
        ka_hi = sb(ctx, "a1_ka_hi", [128, NCH * TOK], BF16)
        ka_lo = sb(ctx, "a1_ka_lo", [128, NCH * TOK], BF16)
        r_sig = sb(ctx, "a1_rsig", [128, NCH * TOK], F32)
        kf = sb(ctx, "a1_kf", [128, TOK], F32)
        of = sb(ctx, "a1_of", [128, 2 * TOK], F32)
        r1t = sb(ctx, "a1_r1", [128, NCH * TOK], F32)
        wlo_ring = sb(ctx, "a1_wlo", [128, 2 * NCH * 128], BF16)
        x2h = sb(ctx, "a1_x2h", [128, 2 * TOK], BF16)
        x2l = sb(ctx, "a1_x2l", [128, 2 * TOK], BF16)
        pb = [ps(ctx, "a1_pb0", [128, TOK], F32), ps(ctx, "a1_pb1", [128, TOK], F32)]
        bci_ps = ps(ctx, "a1_bci_ps", [128, TOK], F32)
        bcn_ps = ps(ctx, "a1_bcn_ps", [128, TOK], F32)
        pt_ps = ps(ctx, "a1_pt_ps", [128, 1024], F32)

        wlo_sps = {"wr": wr_lo, "wk": wk_lo, "wv": wv_lo}
        ring_state = {"i": 0, "loaded": {}}
        x2_free = [None, None]
        r1_sp_tok = [None]

        def lo_load(proj, o):
            slot = ring_state["i"] % 2
            ring_state["i"] += 1
            base = slot * NCH * 128
            src = _ap3(wlo_sps[proj], o * 128, [[E, 128], [128 * E, NCH], [1, 128]])
            P.wait("sync", s["pe"], val=ring_state["loaded"].get(slot, 0))
            t = P.dmas("sync", [(wlo_ring[:, base:base + NCH * 128], src)])
            P.wait_tok("tensor", t)
            ring_state["loaded"][slot] = s["pe"].n + 1
            return wlo_ring[:, base:base + NCH * 128]

        wdict = {"wr_hi": wr_hi_s, "wk_hi": wk_hi_s, "wv_hi": wv_hi_s,
                 "lo_load": lo_load}
        bufs = {"xr_hi": xr_hi, "xr_lo": xr_lo, "xk_hi": xk_hi, "xk_lo": xk_lo,
                "ka_hi": ka_hi, "ka_lo": ka_lo, "r_sig": r_sig, "kf": kf,
                "of": of, "pb": pb, "y_ext": y_ext, "ytmp": ytmp, "dbuf": dbuf,
                "xf": xf, "ycar": ycar, "bci": bci, "bcn": bcn,
                "bank_free_dve": [0, 0], "bank_free_act": [0, 0],
                "of_free": [0, 0]}

        for j in range(0 if "A1" not in phases else (phases["A1"] or NT)):
            T0 = j * TOK
            P.wait("sync", s["dve"])
            P.wait("sync", s["act"])
            t_x = P.dmas("sync", [
                (xs[:, :].rearrange("p (c t) -> p c t", c=NCH),
                 _ap3(x_fm, T0, [[S, 128], [128 * S, NCH], [1, TOK]])),
                (xtm[:, :].rearrange("p (s f) -> p s f", s=NSUB),
                 _ap3(x_tm, T0 * E, [[E, 128], [128 * E, NSUB], [1, E]]))])
            P.wait_tok("scalar", t_x)
            P.wait_tok("vector", t_x)
            if j == 0:
                P.wait_tok("vector", ypcol_tok)
            if "NOLN1" not in phases:
                emit_stats_from_tm(
                    st1, [xtm[:, sub * 1024:(sub + 1) * 1024] for sub in range(NSUB)],
                    xsq, NSUB)
                if "NOBC" not in phases:
                    bc_tiles(st1, NSUB, bci_ps, bcn_ps)
                    P.wait("vector", s["pe"])
                    P.op("vector", lambda e: e.tensor_copy(bci[:, :], bci_ps[:, 0:TOK]))
                    P.op("vector", lambda e: e.tensor_copy(bcn[:, :], bcn_ps[:, 0:TOK]),
                         sem=s["dve"])

            if dbg and j == 0:
                P.wait("gpsimd", s["dve"])
                P.dmas("gpsimd", [(dbg_bci.ap()[:, :], bci[:, :]),
                                  (dbg_bcn.ap()[:, :], bcn[:, :]),
                                  (dbg_xtm.ap()[:, :], xtm[:, :])] + [
                    (dbg_st.ap()[:, NSUB * i:NSUB * (i + 1)], st1[nm_][:, :])
                    for i, nm_ in enumerate(("s1", "s2", "mean", "ex2", "m2",
                                             "var", "sd", "inv", "nm"))])

            def carry_init(c, j=j):
                if j == 0:
                    P.op("vector", lambda e, c=c: e.tensor_copy(
                        y_ext[:, 0:1], ypcol_s[:, c:c + 1]))
                elif j == (OWN // TOK):
                    P.op("vector", lambda e, c=c: e.tensor_tensor(
                        y_ext[:, 0:1], ycar[:, c:c + 1], m2048_s[:, 0:1], ALU.mult))
                else:
                    P.op("vector", lambda e, c=c: e.tensor_copy(
                        y_ext[:, 0:1], ycar[:, c:c + 1]))

            if "NOMIX" not in phases:
                emit_mix(bufs, lambda c: xs[:, c * TOK:(c + 1) * TOK], TOK,
                         carry_init, pre_normalized=False)
            P.wait("tensor", s["dve"])

            def out_r1(o, src_ap):
                P.wait_tok("vector", r1_sp_tok[0])
                P.op("vector", lambda e, o_=r1t[:, o * TOK:(o + 1) * TOK], i=src_ap:
                     e.tensor_copy(o_, i), sem=s["dve"])

            if "NOFFN" not in phases:
                emit_ffn(bufs, wdict, TOK,
                         lambda o: xs[:, o * TOK:(o + 1) * TOK], out_r1, None)

            P.wait("gpsimd", s["dve"])
            r1_sp_tok[0] = P.dmas("gpsimd", [
                (_ap3(r1_sp, T0, [[S, 128], [128 * S, NCH], [1, TOK]]),
                 r1t[:, :].rearrange("p (c t) -> p c t", c=NCH))])

            for sub in range(0 if "NOLN2" in phases else NSUB):
                P.wait("tensor", s["dve"])
                P.wait("tensor", s["act"])
                for c in range(NCH):
                    P.op("tensor", lambda e, o=pt_ps[:, c * 128:(c + 1) * 128],
                                i=r1t[:, c * TOK + sub * 128: c * TOK + sub * 128 + 128]:
                         e.transpose(o, i, id_f32_s[:, :]),
                         sem=(s["pe"] if c == NCH - 1 else None))
                P.wait("scalar", s["pe"])
                P.wait("scalar", s["dve"])
                P.op("scalar", lambda e: e.activation(
                    xsq[:, :], pt_ps[:, 0:1024], AF.Identity, bias=0.0, scale=1.0),
                    sem=s["act"])
                P.wait("vector", s["act"])
                P.op("vector", lambda e, o=st2["s1"][:, sub:sub + 1]:
                     e.reduce_sum(o, xsq[:, :], axis=AXX), sem=s["dve"])
                P.wait("scalar", s["dve"])
                P.op("scalar", lambda e: e.activation(
                    xsq[:, :], pt_ps[:, 0:1024], AF.Square, bias=0.0, scale=1.0),
                    sem=s["act"])
                P.wait("vector", s["act"])
                P.op("vector", lambda e, o=st2["s2"][:, sub:sub + 1]:
                     e.reduce_sum(o, xsq[:, :], axis=AXX), sem=s["dve"])
            if "NOLN2" not in phases:
                ln_stats_chain(st2, NSUB)
                bc_tiles(st2, NSUB, bci_ps, bcn_ps)
                P.wait("vector", s["pe"])
                P.op("vector", lambda e: e.tensor_copy(bci[:, :], bci_ps[:, 0:TOK]))
                P.op("vector", lambda e: e.tensor_copy(bcn[:, :], bcn_ps[:, 0:TOK]),
                     sem=s["dve"])
            for c in range(0 if "NOLN2" in phases else NCH):
                slot = c % 2
                sl = slice(slot * TOK, (slot + 1) * TOK)
                P.op("vector", lambda e, c=c: e.tensor_tensor(
                    ytmp[:, 0:TOK], r1t[:, c * TOK:(c + 1) * TOK], bci[:, :], ALU.mult))
                P.op("vector", lambda e: e.tensor_tensor(
                    xf[:, 0:TOK], ytmp[:, 0:TOK], bcn[:, :], ALU.add), sem=s["dve"])
                P.wait("scalar", s["dve"])
                P.wait_tok("scalar", x2_free[slot])
                P.op("scalar", lambda e, sl=sl: e.activation(
                    x2h[:, sl], xf[:, 0:TOK],
                    AF.Identity, bias=0.0, scale=1.0), sem=s["act"])
                P.wait("vector", s["act"])
                P.wait_tok("vector", x2_free[slot])
                P.op("vector", lambda e, sl=sl: e.tensor_tensor(
                    x2l[:, sl], xf[:, 0:TOK],
                    x2h[:, sl], ALU.subtract), sem=s["dve"])
                P.wait("gpsimd", s["dve"])
                x2_free[slot] = P.dmas("gpsimd", [
                    (x2h_sp.ap()[c * 128:(c + 1) * 128, T0:T0 + TOK], x2h[:, sl]),
                    (x2l_sp.ap()[c * 128:(c + 1) * 128, T0:T0 + TOK], x2l[:, sl])])
        phase_barrier()
        P.flush()


    if "A2" in phases:
      with ExitStack() as ctx:
        wq_hi_s = sb(ctx, "wq_hi_s", [128, NCH * M3], BF16)
        with ExitStack() as wctx:
            wtmp = sb(wctx, "p1_wtmp", [128, M3], F32)
            lotmp = sb(wctx, "p1_lotmp", [128, M3], BF16)
            emit_weight_prep(wqkv.ap(), M3, wq_hi_s, wq_lo.ap(), None, wtmp, lotmp)
        phase_barrier()

        x2ht_b = [sb(ctx, f"a2_x2h{i}", [128, NCH * TOK], BF16) for i in range(2)]
        x2lt_b = [sb(ctx, f"a2_x2l{i}", [128, NCH * TOK], BF16) for i in range(2)]
        qf = sb(ctx, "a2_qf", [128, TOK], F32)
        sth = sb(ctx, "a2_sth", [128, TOK], BF16)
        stl = sb(ctx, "a2_stl", [128, TOK], BF16)
        vst = sb(ctx, "a2_vst", [128, 2 * 128], BF16)
        wlo_ring = sb(ctx, "a2_wlo", [128, 2 * NCH * 128], BF16)
        pb = [ps(ctx, "a2_pb0", [128, TOK], F32), ps(ctx, "a2_pb1", [128, TOK], F32)]
        ptb = [ps(ctx, "a2_pt0", [128, 128], BF16), ps(ctx, "a2_pt1", [128, 128], BF16)]

        ring_state = {"i": 0, "loaded": {}}

        def qkv_lo_load(o):
            slot = ring_state["i"] % 2
            ring_state["i"] += 1
            base = slot * NCH * 128
            src = _ap3(wq_lo, o * 128, [[M3, 128], [128 * M3, NCH], [1, 128]])
            P.wait("sync", s["pe"], val=ring_state["loaded"].get(slot, 0))
            t = P.dmas("sync", [(wlo_ring[:, base:base + NCH * 128], src)])
            P.wait_tok("tensor", t)
            ring_state["loaded"][slot] = s["pe"].n + 1
            return wlo_ring[:, base:base + NCH * 128]

        bank_free = {"dve": [0, 0], "act": [0, 0]}
        ptk_free = [0, 0]
        vst_free = [None, None]
        qk_sp_tok = [None]

        tile_pe_free = [0, 0]
        tile_tok = [None, None]

        def a2_load(j):
            u = j % 2
            T0j = j * TOK
            P.wait("sync", s["pe"], val=tile_pe_free[u])
            return P.dmas("sync", [
                (x2ht_b[u][:, :].rearrange("p (c t) -> p c t", c=NCH),
                 _ap3(x2h_sp, T0j, [[S, 128], [128 * S, NCH], [1, TOK]])),
                (x2lt_b[u][:, :].rearrange("p (c t) -> p c t", c=NCH),
                 _ap3(x2l_sp, T0j, [[S, 128], [128 * S, NCH], [1, TOK]]))])

        tile_tok[0] = a2_load(0)
        for j in range(NT):
            T0 = j * TOK
            u = j % 2
            if j + 1 < NT:
                tile_tok[(j + 1) % 2] = a2_load(j + 1)
            x2ht, x2lt = x2ht_b[u], x2lt_b[u]
            P.wait_tok("tensor", tile_tok[u])
            for o in range(24):
                is_q = o < 8
                is_v = o >= 16
                if is_q:
                    if T0 < OWN:
                        cs, cw = 0, TOK
                    elif T0 + TOK == S:
                        cs, cw = XQ0 - T0, S - XQ0
                    else:
                        continue
                else:
                    cs, cw = 0, TOK
                lo_ap = qkv_lo_load(o)
                bank = pb[o % 2]
                P.wait("tensor", s["dve"], val=bank_free["dve"][o % 2])
                P.wait("tensor", s["act"], val=bank_free["act"][o % 2])
                mms = ([(wq_hi_s[:, k * M3 + o * 128: k * M3 + o * 128 + 128],
                         x2ht[:, k * TOK + cs: k * TOK + cs + cw]) for k in range(NCH)] +
                       [(wq_hi_s[:, k * M3 + o * 128: k * M3 + o * 128 + 128],
                         x2lt[:, k * TOK + cs: k * TOK + cs + cw]) for k in range(NCH)] +
                       [(lo_ap[:, k * 128:(k + 1) * 128],
                         x2ht[:, k * TOK + cs: k * TOK + cs + cw]) for k in range(NCH)])
                n = len(mms)
                for i, (wap, xap) in enumerate(mms):
                    P.op("tensor",
                         lambda e, o_=bank[:, 0:cw], w_=wap, x_=xap,
                                st=(i == 0), sp2=(i == n - 1):
                         e.matmul(o_, w_, x_, start=st, stop=sp2),
                         sem=(s["pe"] if i == n - 1 else None))
                P.wait("vector", s["pe"])
                P.wait_tok("vector", qk_sp_tok[0])
                if is_q:
                    P.op("vector", lambda e, o_=qf[:, 0:cw], i=bank[:, 0:cw],
                                bb=bqkv_s[:, o:o + 1]:
                         e.tensor_scalar(o_, i, bb, 0.125, ALU.add, ALU.mult),
                         sem=s["dve"])
                else:
                    P.op("vector", lambda e, o_=qf[:, 0:cw], i=bank[:, 0:cw],
                                bb=bqkv_s[:, o:o + 1]:
                         e.tensor_scalar(o_, i, bb, None, ALU.add), sem=s["dve"])
                bank_free["dve"][o % 2] = s["dve"].n
                P.wait("scalar", s["dve"])
                P.wait("scalar", s["pe"])
                P.wait_tok("scalar", qk_sp_tok[0])
                P.op("scalar", lambda e, o_=sth[:, 0:cw], i=qf[:, 0:cw]:
                     e.activation(o_, i, AF.Identity, bias=0.0, scale=1.0), sem=s["act"])
                bank_free["act"][o % 2] = s["act"].n
                P.wait("vector", s["act"])
                P.wait("vector", s["pe"])
                P.op("vector", lambda e, o_=stl[:, 0:cw], i=qf[:, 0:cw], h=sth[:, 0:cw]:
                     e.tensor_tensor(o_, i, h, ALU.subtract), sem=s["dve"])
                if not is_v:
                    sp_h = qh_sp if is_q else kh_sp
                    sp_l = ql_sp if is_q else kl_sp
                    row = (o if is_q else o - 8) * 128
                    P.wait("gpsimd", s["act"])
                    P.wait("gpsimd", s["dve"])
                    qk_sp_tok[0] = P.dmas("gpsimd", [
                        (sp_h.ap()[row:row + 128, T0 + cs:T0 + cs + cw], sth[:, 0:cw]),
                        (sp_l.ap()[row:row + 128, T0 + cs:T0 + cs + cw], stl[:, 0:cw])])
                else:
                    hA = (o - 16) * 2
                    P.wait("tensor", s["act"])
                    P.wait("tensor", s["dve"])
                    for sub in range(NSUB):
                        for hl, stt in ((0, sth), (1, stl)):
                            pslot = (sub * 2 + hl) % 2
                            ptk = ptb[pslot]
                            P.wait("tensor", s["dve"], val=ptk_free[pslot])
                            P.op("tensor", lambda e, o_=ptk[:, :],
                                        i=stt[:, sub * 128:(sub + 1) * 128]:
                                 e.transpose(o_, i, id_bf_s[:, :]), sem=s["pe"])
                            P.wait("vector", s["pe"])
                            P.wait_tok("vector", vst_free[pslot])
                            vsl = vst[:, pslot * 128:(pslot + 1) * 128]
                            P.op("vector", lambda e, o_=vsl, i=ptk[:, :]:
                                 e.tensor_copy(o_, i), sem=s["dve"])
                            ptk_free[pslot] = s["dve"].n
                            P.wait("gpsimd", s["dve"])
                            spv = vth_sp if hl == 0 else vtl_sp
                            dst = _ap3(spv, hA * S * HD + (T0 + sub * 128) * HD,
                                       [[HD, 128], [S * HD, 2], [1, HD]])
                            vst_free[pslot] = P.dmas("gpsimd", [
                                (dst, vsl.rearrange("p (h d) -> p h d", h=2))])
            tile_pe_free[u] = s["pe"].n
        phase_barrier()
        P.flush()

    QGS = [(0, 512), (512, 512), (1024, 512), (1536, 512), (2048, XQW)]
    if "B" in phases:
      with ExitStack() as ctx:
        kp_hi = [sb(ctx, f"b_kph{i}", [128, S], BF16) for i in range(2)]
        kp_lo = [sb(ctx, f"b_kpl{i}", [128, S], BF16) for i in range(2)]
        qp_hi = [sb(ctx, f"b_qph{i}", [128, QTOT], BF16) for i in range(2)]
        qp_lo = [sb(ctx, f"b_qpl{i}", [128, QTOT], BF16) for i in range(2)]
        vA_hi = [sb(ctx, f"b_vah{i}", [128, NKC * HD], BF16) for i in range(2)]
        vA_lo = [sb(ctx, f"b_val{i}", [128, NKC * HD], BF16) for i in range(2)]
        vB_hi = [sb(ctx, f"b_vbh{i}", [128, NKC * HD], BF16) for i in range(2)]
        vB_lo = [sb(ctx, f"b_vbl{i}", [128, NKC * HD], BF16) for i in range(2)]
        paf = [sb(ctx, f"b_paf{i}", [128, 512], F32) for i in range(2)]
        pbf = [sb(ctx, f"b_pbf{i}", [128, 512], F32) for i in range(2)]
        pah = [sb(ctx, f"b_pah{i}", [128, 512], BF16) for i in range(2)]
        pal = [sb(ctx, f"b_pal{i}", [128, 512], BF16) for i in range(2)]
        pbh = [sb(ctx, f"b_pbh{i}", [128, 512], BF16) for i in range(2)]
        pbl = [sb(ctx, f"b_pbl{i}", [128, 512], BF16) for i in range(2)]
        dsb = sb(ctx, "b_dsb", [128, 512], F32)
        binv = sb(ctx, "b_binv", [128, 512], F32)
        o_f = sb(ctx, "b_of", [128, 512], F32)
        o_hi = sb(ctx, "b_ohi", [128, 512], BF16)
        o_lo = sb(ctx, "b_olo", [128, 512], BF16)
        st_ps = [(ps(ctx, "b_stA0", [128, 512], F32), ps(ctx, "b_stB0", [128, 512], F32)),
                 (ps(ctx, "b_stA1", [128, 512], F32), ps(ctx, "b_stB1", [128, 512], F32))]
        pv = ps(ctx, "b_pv", [128, 512], F32)
        dps = ps(ctx, "b_dps", [128, 512], F32)
        ddp = ps(ctx, "b_ddp", [128, 512], F32)
        o_sp_tok = [None]
        P.op("vector", lambda e: e.memset(dsb[:, :], 0.0))
        for t in (st_ps[0][0], st_ps[0][1], st_ps[1][0], st_ps[1][1], pv, dps, ddp):
            P.op("vector", lambda e, t=t: e.memset(t[:, :], 0.0))
        P.op("vector", lambda e: e.engine_nop(), sem=s["dve"])

        pair_free_pe = [0, 0]
        pair_tok = [None, None]
        g_exp = [0, 0]
        g_lo = [0, 0]
        g_castA = [0, 0]
        g_pv = [0, 0]

        def emit_pair_loads(pc):
            u = pc % 2
            grp = []
            for dst, spv in ((kp_hi[u], kh_sp), (kp_lo[u], kl_sp)):
                grp.append((dst[:, :], spv.ap()[pc * 128:(pc + 1) * 128, :]))
            for dst, spv in ((qp_hi[u], qh_sp), (qp_lo[u], ql_sp)):
                grp.append((dst[:, 0:OWN], spv.ap()[pc * 128:(pc + 1) * 128, 0:OWN]))
                grp.append((dst[:, OWN:QTOT], spv.ap()[pc * 128:(pc + 1) * 128, XQ0:S]))
            for dst, spv, h in ((vA_hi[u], vth_sp, 2 * pc), (vA_lo[u], vtl_sp, 2 * pc),
                                (vB_hi[u], vth_sp, 2 * pc + 1), (vB_lo[u], vtl_sp, 2 * pc + 1)):
                grp.append((dst[:, :].rearrange("p (c d) -> p c d", c=NKC),
                            _ap3(spv, h * S * HD, [[HD, 128], [128 * HD, NKC], [1, HD]])))
            P.wait("sync", s["pe"], val=pair_free_pe[u])
            return P.dmas("sync", grp)

        pair_tok[0] = emit_pair_loads(0)
        for pc in range(8):
            u = pc % 2
            if pc + 1 < 8:
                pair_tok[(pc + 1) % 2] = emit_pair_loads(pc + 1)
            P.wait_tok("tensor", pair_tok[u])
            kph, kpl = kp_hi[u], kp_lo[u]
            qph, qpl = qp_hi[u], qp_lo[u]
            vah, val_, vbh, vbl = vA_hi[u], vA_lo[u], vB_hi[u], vB_lo[u]

            for (g0, gw) in QGS:
                lo_done = {}
                castA_done = {}

                def emit_pv(cc, g0=g0, gw=gw, vah=vah, val_=val_, vbh=vbh, vbl=vbl):
                    v = cc % 2
                    P.wait("tensor", s["dve"], val=lo_done[cc])
                    P.wait("tensor", s["act"], val=castA_done[cc])
                    pvs = [(vah, pah[v]), (val_, pah[v]), (vah, pal[v])]
                    pvsB = [(vbh, pbh[v]), (vbl, pbh[v]), (vbh, pbl[v])]
                    st_c, sp_c = (cc == 0), (cc == NKC - 1)
                    for i in range(3):
                        va_, pa_ = pvs[i]
                        vb_, pb_ = pvsB[i]
                        st_i = st_c and (i == 0)
                        sp_i = sp_c and (i == 2)
                        P.op("tensor", lambda e, w_=va_[:, cc * HD:(cc + 1) * HD],
                                    x_=pa_[:, 0:gw], a=st_i, b_=sp_i:
                             e.matmul(pv[0:64, 0:gw], w_, x_, start=a, stop=b_,
                                      tile_position=(0, 0), skip_group_check=True))
                        P.op("tensor", lambda e, w_=vb_[:, cc * HD:(cc + 1) * HD],
                                    x_=pb_[:, 0:gw], a=st_i, b_=sp_i:
                             e.matmul(pv[64:128, 0:gw], w_, x_, start=a, stop=b_,
                                      tile_position=(0, 64), skip_group_check=True))
                    dmov = [(pah[v], 0), (pal[v], 32), (pbh[v], 64), (pbl[v], 96)]
                    for i, (pm, col) in enumerate(dmov):
                        P.op("tensor", lambda e, o_=dps[col:col + 1, 0:gw],
                                    x_=pm[:, 0:gw], a=st_c, b_=sp_c, col=col:
                             e.matmul(o_, ones_bf_s[:, :], x_, start=a, stop=b_,
                                      tile_position=(0, col), skip_group_check=True),
                             sem=(s["pe"] if i == len(dmov) - 1 else None))
                    g_pv[v] = s["pe"].n

                for c in range(NKC):
                    v = c % 2
                    stA, stB = st_ps[v]
                    P.wait("tensor", s["act"], val=g_exp[v])
                    qk = [(kph, qph), (kph, qpl), (kpl, qph)]
                    for i, (kk, qq) in enumerate(qk):
                        st_, sp2_ = (i == 0), (i == len(qk) - 1)
                        P.op("tensor", lambda e, o_=stA[:, 0:gw],
                                    w_=kk[0:64, c * 128:(c + 1) * 128],
                                    x_=qq[0:64, g0:g0 + gw], a=st_, b_=sp2_:
                             e.matmul(o_, w_, x_, start=a, stop=b_,
                                      tile_position=(0, 0)))
                        P.op("tensor", lambda e, o_=stB[:, 0:gw],
                                    w_=kk[64:128, c * 128:(c + 1) * 128],
                                    x_=qq[64:128, g0:g0 + gw], a=st_, b_=sp2_:
                             e.matmul(o_, w_, x_, start=a, stop=b_,
                                      tile_position=(64, 0)),
                             sem=(s["pe"] if sp2_ else None))
                    qk_done_c = s["pe"].n
                    if c >= 1:
                        emit_pv(c - 1)
                    P.wait("scalar", s["pe"], val=qk_done_c)
                    P.wait("scalar", s["dve"], val=g_lo[v])
                    P.op("scalar", lambda e, v=v, gw=gw, sA=stA: e.activation(
                        paf[v][:, 0:gw], sA[:, 0:gw], AF.Exp, bias=0.0, scale=1.0))
                    P.op("scalar", lambda e, v=v, gw=gw, sB=stB: e.activation(
                        pbf[v][:, 0:gw], sB[:, 0:gw], AF.Exp, bias=0.0, scale=1.0),
                        sem=s["act"])
                    g_exp[v] = s["act"].n
                    exp_done_c = s["act"].n
                    P.wait("scalar", s["pe"], val=g_pv[v])
                    P.op("scalar", lambda e, v=v, gw=gw: e.activation(
                        pah[v][:, 0:gw], paf[v][:, 0:gw], AF.Identity,
                        bias=0.0, scale=1.0), sem=s["act"])
                    castA_done[c] = s["act"].n
                    P.wait("vector", s["act"], val=exp_done_c)
                    P.wait("vector", s["pe"], val=g_pv[v])
                    P.op("vector", lambda e, v=v, gw=gw: e.tensor_copy(
                        pbh[v][:, 0:gw], pbf[v][:, 0:gw]))
                    P.wait("vector", s["act"], val=castA_done[c])
                    P.op("vector", lambda e, v=v, gw=gw: e.tensor_tensor(
                        pal[v][:, 0:gw], paf[v][:, 0:gw], pah[v][:, 0:gw], ALU.subtract))
                    P.op("vector", lambda e, v=v, gw=gw: e.tensor_tensor(
                        pbl[v][:, 0:gw], pbf[v][:, 0:gw], pbh[v][:, 0:gw], ALU.subtract),
                        sem=s["dve"])
                    lo_done[c] = s["dve"].n
                    g_lo[v] = s["dve"].n
                emit_pv(NKC - 1)

                P.wait("vector", s["pe"])
                for drow in (0, 32, 64, 96):
                    P.op("vector", lambda e, gw=gw, dr=drow: e.tensor_copy(
                        dsb[dr:dr + 1, 0:gw], dps[dr:dr + 1, 0:gw]),
                        sem=(s["dve"] if drow == 96 else None))
                P.wait("tensor", s["dve"])
                P.op("tensor", lambda e, gw=gw: e.matmul(
                    ddp[:, 0:gw], dmask_s[:, :], dsb[:, 0:gw], start=True, stop=True),
                    sem=s["pe"])
                P.wait("vector", s["pe"])
                P.op("vector", lambda e, gw=gw: e.reciprocal(binv[:, 0:gw], ddp[:, 0:gw]),
                     sem=s["dve"])
                P.op("vector", lambda e, gw=gw: e.tensor_tensor(
                    o_f[:, 0:gw], pv[:, 0:gw], binv[:, 0:gw], ALU.mult), sem=s["dve"])
                P.wait("scalar", s["dve"])
                P.wait_tok("scalar", o_sp_tok[0])
                P.op("scalar", lambda e, gw=gw: e.activation(
                    o_hi[:, 0:gw], o_f[:, 0:gw], AF.Identity, bias=0.0, scale=1.0),
                    sem=s["act"])
                P.wait("vector", s["act"])
                P.wait_tok("vector", o_sp_tok[0])
                P.op("vector", lambda e, gw=gw: e.tensor_tensor(
                    o_lo[:, 0:gw], o_f[:, 0:gw], o_hi[:, 0:gw], ALU.subtract),
                    sem=s["dve"])
                P.wait("gpsimd", s["dve"])
                P.wait("gpsimd", s["act"])
                o_sp_tok[0] = P.dmas("gpsimd", [
                    (oh_sp.ap()[pc * 128:(pc + 1) * 128, g0:g0 + gw], o_hi[:, 0:gw]),
                    (ol_sp.ap()[pc * 128:(pc + 1) * 128, g0:g0 + gw], o_lo[:, 0:gw])])
            pair_free_pe[u] = s["pe"].n
        phase_barrier()
        P.flush()

    if "C" in phases:
      with ExitStack() as ctx:
        wo_hi_s = sb(ctx, "wo_hi_s", [128, NCH * E], BF16)
        wr_hi_s2 = sb(ctx, "wr_hi_s2", [128, NCH * E], BF16)
        wk_hi_s2 = sb(ctx, "wk_hi_s2", [128, NCH * E], BF16)
        wv_hi_s2 = sb(ctx, "wv_hi_s2", [128, NCH * E], BF16)
        with ExitStack() as wctx:
            wtmp = sb(wctx, "p2_wtmp", [128, E], F32)
            lotmp = sb(wctx, "p2_lotmp", [128, E], BF16)
            emit_weight_prep(wout.ap(), E, wo_hi_s, wo_lo.ap(), None, wtmp, lotmp)
        P.dmas("sync", [
            (hs[:, :].rearrange("p (k m) -> p k m", k=NCH),
             _ap3(hsp, 0, [[E, 128], [128 * E, NCH], [1, E]]))
            for hs, hsp in ((wr_hi_s2, wr_hi_sp), (wk_hi_s2, wk_hi_sp),
                            (wv_hi_s2, wv_hi_sp))])
        phase_barrier()

        oht = sb(ctx, "c_oh", [128, NCH * TOK], BF16)
        olt = sb(ctx, "c_ol", [128, NCH * TOK], BF16)
        r1ct = sb(ctx, "c_r1", [128, NCH * TOK], F32)
        r2t = sb(ctx, "c_r2", [128, NCH * TOK], F32)
        xsq = sb(ctx, "c_xsq", [128, 1024], F32)
        st3 = mk_stats(ctx, "c_st3", NSUB)
        bci = sb(ctx, "c_bci", [128, TOK], F32)
        bcn = sb(ctx, "c_bcn", [128, TOK], F32)
        y_ext = sb(ctx, "c_yext", [128, TOK + 1], F32)
        ytmp = sb(ctx, "c_ytmp", [128, TOK], F32)
        dbuf = sb(ctx, "c_dbuf", [128, TOK], F32)
        xf = sb(ctx, "c_xf", [128, TOK], F32)
        ycar = sb(ctx, "c_ycar", [128, NCH], F32)
        x3car = sb(ctx, "c_x3car", [128, NCH], F32)
        xr_hi = sb(ctx, "c_xr_hi", [128, NCH * TOK], BF16)
        xr_lo = sb(ctx, "c_xr_lo", [128, NCH * TOK], BF16)
        xk_hi = sb(ctx, "c_xk_hi", [128, NCH * TOK], BF16)
        xk_lo = sb(ctx, "c_xk_lo", [128, NCH * TOK], BF16)
        ka_hi = sb(ctx, "c_ka_hi", [128, NCH * TOK], BF16)
        ka_lo = sb(ctx, "c_ka_lo", [128, NCH * TOK], BF16)
        r_sig = sb(ctx, "c_rsig", [128, NCH * TOK], F32)
        kf = sb(ctx, "c_kf", [128, TOK], F32)
        of = sb(ctx, "c_of", [128, 2 * TOK], F32)
        wlo_ring = sb(ctx, "c_wlo", [128, 2 * NCH * 128], BF16)
        wolo_ring = wlo_ring
        pb = [ps(ctx, "c_pb0", [128, TOK], F32), ps(ctx, "c_pb1", [128, TOK], F32)]
        bci_ps = ps(ctx, "c_bci_ps", [128, TOK], F32)
        bcn_ps = ps(ctx, "c_bcn_ps", [128, TOK], F32)
        pt_ps = ps(ctx, "c_pt_ps", [128, 1024], F32)
        for t in (pb[0], pb[1], bci_ps, bcn_ps, pt_ps):
            P.op("vector", lambda e, t=t: e.memset(t[:, :], 0.0))
        P.op("vector", lambda e: e.memset(ycar[:, :], 0.0))
        P.op("vector", lambda e: e.memset(x3car[:, :], 0.0))
        P.op("vector", lambda e: e.engine_nop(), sem=s["dve"])

        wlo_sps = {"wr": wr_lo, "wk": wk_lo, "wv": wv_lo}
        ring_state = {"i": 0, "loaded": {}}
        wo_ring = {"i": 0, "loaded": {}}
        out_toks = [None, None]

        def lo_load(proj, o):
            slot = ring_state["i"] % 2
            ring_state["i"] += 1
            base = slot * NCH * 128
            src = _ap3(wlo_sps[proj], o * 128, [[E, 128], [128 * E, NCH], [1, 128]])
            P.wait("sync", s["pe"], val=ring_state["loaded"].get(slot, 0))
            t = P.dmas("sync", [(wlo_ring[:, base:base + NCH * 128], src)])
            P.wait_tok("tensor", t)
            ring_state["loaded"][slot] = s["pe"].n + 1
            return wlo_ring[:, base:base + NCH * 128]

        def wo_lo_load(o):
            slot = ring_state["i"] % 2
            ring_state["i"] += 1
            base = slot * NCH * 128
            src = _ap3(wo_lo, o * 128, [[E, 128], [128 * E, NCH], [1, 128]])
            P.wait("sync", s["pe"], val=ring_state["loaded"].get(slot, 0))
            t = P.dmas("sync", [(wolo_ring[:, base:base + NCH * 128], src)])
            P.wait_tok("tensor", t)
            ring_state["loaded"][slot] = s["pe"].n + 1
            return wolo_ring[:, base:base + NCH * 128]

        wdict = {"wr_hi": wr_hi_s2, "wk_hi": wk_hi_s2, "wv_hi": wv_hi_s2,
                 "lo_load": lo_load}
        bufs = {"xr_hi": xr_hi, "xr_lo": xr_lo, "xk_hi": xk_hi, "xk_lo": xk_lo,
                "ka_hi": ka_hi, "ka_lo": ka_lo, "r_sig": r_sig, "kf": kf,
                "of": of, "pb": pb, "y_ext": y_ext, "ytmp": ytmp, "dbuf": dbuf,
                "xf": xf, "ycar": ycar, "bci": bci, "bcn": bcn,
                "bank_free_dve": [0, 0], "bank_free_act": [0, 0],
                "of_free": [0, 0]}

        def tile_wout_ln3(oc0, tokw, r1_off):
            T = tokw
            nsub = T // 128
            P.wait("sync", s["dve"])
            P.wait("sync", s["pe"])
            t_c = P.dmas("sync", [
                (oht[:, 0:NCH * T].rearrange("p (c t) -> p c t", c=NCH),
                 _ap3(oh_sp, oc0, [[QTOT, 128], [128 * QTOT, NCH], [1, T]])),
                (olt[:, 0:NCH * T].rearrange("p (c t) -> p c t", c=NCH),
                 _ap3(ol_sp, oc0, [[QTOT, 128], [128 * QTOT, NCH], [1, T]])),
                (r1ct[:, 0:NCH * T].rearrange("p (c t) -> p c t", c=NCH),
                 _ap3(r1_sp, r1_off, [[S, 128], [128 * S, NCH], [1, T]]))])
            P.wait_tok("tensor", t_c)
            P.wait_tok("vector", t_c)
            for o in range(NCH):
                lo_ap = wo_lo_load(o)
                bank = pb[o % 2]
                P.wait("tensor", s["dve"], val=bufs["bank_free_dve"][o % 2])
                mms = ([(wo_hi_s[:, k * E + o * 128: k * E + o * 128 + 128],
                         oht[:, k * T:(k + 1) * T]) for k in range(NCH)] +
                       [(wo_hi_s[:, k * E + o * 128: k * E + o * 128 + 128],
                         olt[:, k * T:(k + 1) * T]) for k in range(NCH)] +
                       [(lo_ap[:, k * 128:(k + 1) * 128],
                         oht[:, k * T:(k + 1) * T]) for k in range(NCH)])
                n = len(mms)
                for i, (wap, xap) in enumerate(mms):
                    P.op("tensor",
                         lambda e, o_=bank[:, 0:T], w_=wap, x_=xap,
                                st=(i == 0), sp2=(i == n - 1):
                         e.matmul(o_, w_, x_, start=st, stop=sp2),
                         sem=(s["pe"] if i == n - 1 else None))
                P.wait("vector", s["pe"])
                P.op("vector", lambda e, o=o, T=T, b_=bank: e.scalar_tensor_tensor(
                    r2t[:, o * T:(o + 1) * T], b_[:, 0:T], bout_s[:, o:o + 1],
                    r1ct[:, o * T:(o + 1) * T], ALU.add, ALU.add), sem=s["dve"])
                bufs["bank_free_dve"][o % 2] = s["dve"].n
            for sub in range(nsub):
                P.wait("tensor", s["dve"])
                P.wait("tensor", s["act"])
                for c in range(NCH):
                    P.op("tensor", lambda e, o=pt_ps[:, c * 128:(c + 1) * 128],
                                i=r2t[:, c * T + sub * 128: c * T + sub * 128 + 128]:
                         e.transpose(o, i, id_f32_s[:, :]),
                         sem=(s["pe"] if c == NCH - 1 else None))
                P.wait("scalar", s["pe"])
                P.wait("scalar", s["dve"])
                P.op("scalar", lambda e: e.activation(
                    xsq[:, :], pt_ps[:, 0:1024], AF.Identity, bias=0.0, scale=1.0),
                    sem=s["act"])
                P.wait("vector", s["act"])
                P.op("vector", lambda e, o=st3["s1"][:, sub:sub + 1]:
                     e.reduce_sum(o, xsq[:, :], axis=AXX), sem=s["dve"])
                P.wait("scalar", s["dve"])
                P.op("scalar", lambda e: e.activation(
                    xsq[:, :], pt_ps[:, 0:1024], AF.Square, bias=0.0, scale=1.0),
                    sem=s["act"])
                P.wait("vector", s["act"])
                P.op("vector", lambda e, o=st3["s2"][:, sub:sub + 1]:
                     e.reduce_sum(o, xsq[:, :], axis=AXX), sem=s["dve"])
            ln_stats_chain(st3, nsub)
            bc_tiles(st3, nsub, bci_ps, bcn_ps)
            P.wait("vector", s["pe"])
            P.op("vector", lambda e, T=T: e.tensor_copy(bci[:, 0:T], bci_ps[:, 0:T]))
            P.op("vector", lambda e, T=T: e.tensor_copy(bcn[:, 0:T], bcn_ps[:, 0:T]),
                 sem=s["dve"])

        c_tiles = phases.get("C")
        if c_tiles != 0 and "NOXT" not in phases:
            tile_wout_ln3(OWN, XQW, XQ0)
        lc = XQW - 1
        for c in range(NCH if (c_tiles != 0 and "NOXT" not in phases) else 0):
            P.op("vector", lambda e, c=c: e.tensor_tensor(
                ytmp[:, 0:1], r2t[:, c * XQW + lc: c * XQW + lc + 1],
                bci[:, lc:lc + 1], ALU.mult))
            P.drain("vector")
            P.op("vector", lambda e, c=c: e.tensor_tensor(
                ytmp[:, 1:2], ytmp[:, 0:1], bcn[:, lc:lc + 1], ALU.add))
            P.drain("vector")
            P.op("vector", lambda e, c=c: e.tensor_tensor(
                x3car[:, c:c + 1], ytmp[:, 1:2], m0x_s[:, 0:1], ALU.mult),
                sem=s["dve"])

        NTC = OWN // TOK if c_tiles is None else min(c_tiles, OWN // TOK)
        for j in range(NTC):
            T0 = j * TOK
            tile_wout_ln3(T0, TOK, T0)
            if dbg and j == 0:
                P.wait("gpsimd", s["dve"])
                P.dmas("gpsimd", [(
                    _ap3(dbg_r2, 0, [[TOK, 128], [128 * TOK, NCH], [1, TOK]]),
                    r2t[:, :].rearrange("p (c t) -> p c t", c=NCH))])

            def carry_init(c, j=j):
                src = x3car if (j == 0 and "NOXT" not in phases) else ycar
                P.op("vector", lambda e, c=c, sr=src: e.tensor_copy(
                    y_ext[:, 0:1], sr[:, c:c + 1]))

            emit_mix(bufs, lambda c: r2t[:, c * TOK:(c + 1) * TOK], TOK,
                     carry_init, pre_normalized=False)
            P.wait("tensor", s["dve"])

            def out_store(o, src_ap, T0=T0):
                P.wait("gpsimd", s["dve"])
                out_toks[o % 2] = P.dmas("gpsimd", [
                    (out_fm.ap()[o * 128:(o + 1) * 128, T0:T0 + TOK], src_ap)])

            emit_ffn(bufs, wdict, TOK,
                     lambda o: r2t[:, o * TOK:(o + 1) * TOK], out_store, out_toks)

        P.wait_all_dma("gpsimd")
        P.wait_all_dma("sync")
        P.flush()

    P.flush()
    return P



_QKV_PERM = np.concatenate([
    np.concatenate([np.arange(h * 192 + j * 64, h * 192 + j * 64 + 64)
                    for h in range(H)])
    for j in range(3)])


def _col8(v):
    return np.ascontiguousarray(np.asarray(v, np.float32).reshape(NCH, 128).T)


def _core_inputs(shared, x, b, p):
    q0g = p * OWN
    xb = np.asarray(x[b], np.float32)
    xr = np.roll(xb, -q0g, axis=0)
    xprev = xb[q0g - 1:q0g] if p == 1 else np.zeros((1, E), np.float32)
    d = {
        "x_fm": np.ascontiguousarray(xr.T),
        "x_tm": np.ascontiguousarray(xr),
        "xprev": np.ascontiguousarray(xprev),
        "m2048": np.full((128, 1), 1.0 - p, np.float32),
        "m0x": np.full((128, 1), float(p), np.float32),
    }
    d.update(shared)
    return d


def kernel(L0_attn_x, L0_attn_Wqkv, L0_attn_bqkv, L0_attn_Wout, L0_attn_bout,
           L0_ffn_W_r, L0_ffn_W_k, L0_ffn_W_v, L0_ffn_mu_r, L0_ffn_mu_k):
    x = np.asarray(L0_attn_x, np.float32)
    shared = {
        "wqkv": np.ascontiguousarray(np.asarray(L0_attn_Wqkv, np.float32)[:, _QKV_PERM]),
        "bqkv": np.ascontiguousarray(
            np.asarray(L0_attn_bqkv, np.float32)[_QKV_PERM].reshape(24, 128).T),
        "wout": np.ascontiguousarray(np.asarray(L0_attn_Wout, np.float32)),
        "bout": _col8(L0_attn_bout),
        "wrT": np.ascontiguousarray(np.asarray(L0_ffn_W_r, np.float32).T),
        "wkT": np.ascontiguousarray(np.asarray(L0_ffn_W_k, np.float32).T),
        "wvT": np.ascontiguousarray(np.asarray(L0_ffn_W_v, np.float32).T),
        "mu_r": _col8(L0_ffn_mu_r),
        "mu_k": _col8(L0_ffn_mu_k),
    }

    nc = bass.Bass("TRN2", target_bir_lowering=False, detect_race_conditions=False)
    build_program(nc)

    in_maps = [_core_inputs(shared, x, c // 2, c % 2) for c in range(8)]
    res = run_bass_kernel_spmd(nc, in_maps, core_ids=list(range(8)))
    out = np.empty((B, S, E), np.float32)
    for c in range(8):
        b, p = c // 2, c % 2
        out[b, p * OWN:(p + 1) * OWN, :] = np.asarray(res.results[c]["out_fm"]).T
    return out


# version 04
# speedup vs baseline: 1.0662x; 1.0662x over previous
import numpy as np
import ml_dtypes
import concourse.bass as bass
import concourse.mybir as mybir
from concourse.bass_utils import run_bass_kernel_spmd

F32 = mybir.dt.float32
BF16 = mybir.dt.bfloat16
AF = mybir.ActivationFunctionType
ALU = mybir.AluOpType
AXX = mybir.AxisListType.X

B, S, E, H, HD = 4, 4096, 1024, 16, 64
NCH = E // 128
TOK = 512
TOKA = 256
NTA = S // TOKA
NSUBA = TOKA // 128
NSLOTA = 4
NT = S // TOK
NSUB = TOK // 128
EPS = 1e-5
OWN = 2048
XQ0, XQW = 3968, 128
QTOT = OWN + XQW
NKC = S // 128
M3 = 3 * E

BF16NP = ml_dtypes.bfloat16


class _Sem:
    def __init__(self, nc, name):
        self.h = nc.alloc_semaphore(name=name)
        self.n = 0


class PB:
    ENGS = ("sync", "gpsimd", "scalar", "vector", "tensor")

    def __init__(self, nc):
        self.nc = nc
        self.streams = {k: [] for k in self.ENGS}
        self._waited = {k: {} for k in self.ENGS}
        self._pools = {}

    def sem(self, name):
        return _Sem(self.nc, name)

    def _pool(self, eng):
        if eng not in self._pools:
            self._pools[eng] = {"sems": [self.sem(f"{eng}_dq{i}") for i in range(12)],
                                "i": 0}
        return self._pools[eng]

    def dmas(self, eng, pairs, slow=False):
        pool = self._pool(eng)
        sem = pool["sems"][pool["i"] % len(pool["sems"])]
        pool["i"] += 1
        if sem.n > 0:
            self.wait(eng, sem)
        for (dst, srcap) in pairs:
            self.op(eng, (lambda e, d=dst, s_=srcap, sl=slow:
                          e.dma_start(d, s_, allow_slow_non_contiguous=sl)),
                    sem=sem, k=16)
        return (sem, sem.n)

    def drain(self, eng):
        self.streams[eng].append(lambda e: e.drain())

    def wait_tok(self, eng, tok):
        if tok is not None:
            self.wait(eng, tok[0], tok[1])

    def wait_all_dma(self, eng_waiter):
        for qeng in list(self._pools):
            for sem in self._pools[qeng]["sems"]:
                self.wait(eng_waiter, sem)

    def op(self, eng, fn, sem=None, k=1):
        h = None
        if sem is not None:
            sem.n += k
            h = sem.h
        def run(e, fn=fn, h=h, k=k):
            inst = fn(e)
            if h is not None:
                inst.then_inc(h, k)
        self.streams[eng].append(run)

    def wait(self, eng, sem, val=None):
        v = sem.n if val is None else val
        if v <= 0:
            return
        if v <= self._waited[eng].get(id(sem), -1):
            return
        self._waited[eng][id(sem)] = v
        self.streams[eng].append(lambda e, h=sem.h, v=v: e.wait_ge(h, v))

    def emit(self, blk):
        for name in self.ENGS:
            ops = self.streams[name]
            if not ops:
                continue
            def body(e, ops=ops):
                for f in ops:
                    f(e)
            getattr(blk, name)(body)
        self.streams = {k: [] for k in self.ENGS}

    def flush(self):
        if any(self.streams.values()):
            with self.nc.Block() as blk:
                self.emit(blk)


def _ap3(handle, offset, dims):
    return bass.AP(handle, offset, [list(d) for d in dims])


def build_program(nc, dbg=False, phases=("A1", "A2", "B", "C")):
    phases = {p.split(":")[0]: (int(p.split(":")[1]) if ":" in p else None)
              for p in phases}
    spill_kind = "ExternalOutput" if dbg else "Internal"

    d_in = lambda n, shp, dt=F32: nc.dram_tensor(n, shp, dt, kind="ExternalInput")
    x_fm = d_in("x_fm", [E, S])
    x_tm = d_in("x_tm", [S, E])
    xprev = d_in("xprev", [1, E])
    m2048 = d_in("m2048", [128, 1])
    m0x = d_in("m0x", [128, 1])
    mu_r = d_in("mu_r", [128, NCH])
    mu_k = d_in("mu_k", [128, NCH])
    wrT = d_in("wrT", [E, E])
    wkT = d_in("wkT", [E, E])
    wvT = d_in("wvT", [E, E])
    wqkv = d_in("wqkv", [E, M3])
    bqkv = d_in("bqkv", [128, 24])
    wout = d_in("wout", [E, E])
    bout = d_in("bout", [128, NCH])
    out_fm = nc.dram_tensor("out_fm", [E, OWN], F32, kind="ExternalOutput")

    mk_sp = lambda n, shp, dt: nc.dram_tensor(n, shp, dt, kind=spill_kind)
    r1_sp = mk_sp("r1_sp", [E, S], F32)
    x2h_sp = mk_sp("x2h_sp", [E, S], BF16)
    x2l_sp = mk_sp("x2l_sp", [E, S], BF16)
    qh_sp = mk_sp("qh_sp", [E, S], BF16)
    ql_sp = mk_sp("ql_sp", [E, S], BF16)
    kh_sp = mk_sp("kh_sp", [E, S], BF16)
    kl_sp = mk_sp("kl_sp", [E, S], BF16)
    vth_sp = mk_sp("vth_sp", [H, S, HD], BF16)
    vtl_sp = mk_sp("vtl_sp", [H, S, HD], BF16)
    oh_sp = mk_sp("oh_sp", [E, QTOT], BF16)
    ol_sp = mk_sp("ol_sp", [E, QTOT], BF16)
    yprow_sp = mk_sp("yprow_sp", [1, E], F32)
    dbg_bci = mk_sp("dbg_bci", [128, TOK], F32)
    dbg_bcn = mk_sp("dbg_bcn", [128, TOK], F32)
    dbg_st = mk_sp("dbg_st", [128, 16 * NSUB], F32)
    dbg_xtm = mk_sp("dbg_xtm", [128, NSUB * 1024], F32)
    dbg_y = mk_sp("dbg_y", [128, TOK + 1], F32)
    dbg_r2 = mk_sp("dbg_r2", [E, TOK], F32)
    wr_lo = mk_sp("wr_lo", [E, E], BF16)
    wk_lo = mk_sp("wk_lo", [E, E], BF16)
    wv_lo = mk_sp("wv_lo", [E, E], BF16)
    wr_hi_sp = mk_sp("wr_hi_sp", [E, E], BF16)
    wk_hi_sp = mk_sp("wk_hi_sp", [E, E], BF16)
    wv_hi_sp = mk_sp("wv_hi_sp", [E, E], BF16)
    wq_lo = mk_sp("wq_lo", [E, M3], BF16)
    wo_lo = mk_sp("wo_lo", [E, E], BF16)

    id_f32 = nc.inline_tensor(np.eye(128, dtype=np.float32), name="id_f32")
    id_bf = nc.inline_tensor(np.eye(128, dtype=BF16NP), name="id_bf")
    ones_bf = nc.inline_tensor(np.ones((128, 1), dtype=BF16NP), name="ones_bf")
    dmask_np = np.zeros((128, 128), dtype=np.float32)
    dmask_np[0, 0:64] = dmask_np[32, 0:64] = 1.0
    dmask_np[64, 64:128] = dmask_np[96, 64:128] = 1.0
    dmask = nc.inline_tensor(dmask_np, name="dmask")

    P = PB(nc)
    s = {k: P.sem(k) for k in ("ld", "sp", "dve", "act", "pe", "bar", "out")}

    from contextlib import ExitStack

    def sb(ctx, name, shape, dt):
        return ctx.enter_context(nc.sbuf_tensor(name, shape, dt))

    def ps(ctx, name, shape, dt=F32):
        return ctx.enter_context(nc.psum_tensor(name, shape, dt))

    def phase_barrier():
        for eng in PB.ENGS:
            P.wait_all_dma(eng)
            for k in ("dve", "act", "pe"):
                P.wait(eng, s[k])

    id_f32_s = nc.alloc_sbuf_tensor("id_f32_s", [128, 128], F32)
    id_bf_s = nc.alloc_sbuf_tensor("id_bf_s", [128, 128], BF16)
    ones_bf_s = nc.alloc_sbuf_tensor("ones_bf_s", [128, 1], BF16)
    dmask_s = nc.alloc_sbuf_tensor("dmask_s", [128, 128], F32)
    mur_s = nc.alloc_sbuf_tensor("mur_s", [128, NCH], F32)
    muk_s = nc.alloc_sbuf_tensor("muk_s", [128, NCH], F32)
    bqkv_s = nc.alloc_sbuf_tensor("bqkv_s", [128, 24], F32)
    bout_s = nc.alloc_sbuf_tensor("bout_s", [128, NCH], F32)
    m2048_s = nc.alloc_sbuf_tensor("m2048_s", [128, 1], F32)
    m0x_s = nc.alloc_sbuf_tensor("m0x_s", [128, 1], F32)
    ypcol_s = nc.alloc_sbuf_tensor("ypcol_s", [128, NCH], F32)

    const_tok = P.dmas("sync", [
        (dst[:, :], src.ap()[:, :]) for dst, src in
        ((id_f32_s, id_f32), (id_bf_s, id_bf), (ones_bf_s, ones_bf),
         (dmask_s, dmask), (mur_s, mu_r), (muk_s, mu_k), (bqkv_s, bqkv),
         (bout_s, bout), (m2048_s, m2048), (m0x_s, m0x))])
    for eng in ("tensor", "scalar", "vector", "gpsimd"):
        P.wait_tok(eng, const_tok)

    def emit_yprev(xprow_s, yprow_s, pscal_s):
        t = P.dmas("sync", [(xprow_s[:, :], xprev.ap()[:, :])])
        P.wait_tok("scalar", t)
        P.wait_tok("vector", t)
        P.op("scalar", lambda e: e.activation(yprow_s[:, :], xprow_s[:, :],
                                              AF.Square, bias=0.0, scale=1.0), sem=s["act"])
        P.wait("vector", s["act"])
        P.op("vector", lambda e: e.reduce_sum(pscal_s[:, 0:1], xprow_s[:, :], axis=AXX))
        P.op("vector", lambda e: e.reduce_sum(pscal_s[:, 1:2], yprow_s[:, :], axis=AXX))
        P.drain("vector")
        P.op("vector", lambda e: e.tensor_scalar_mul(pscal_s[:, 2:3], pscal_s[:, 0:1], 1.0 / E))
        P.op("vector", lambda e: e.tensor_scalar_mul(pscal_s[:, 3:4], pscal_s[:, 1:2], 1.0 / E))
        P.drain("vector")
        P.op("vector", lambda e: e.tensor_mul(pscal_s[:, 4:5], pscal_s[:, 2:3], pscal_s[:, 2:3]))
        P.drain("vector")
        P.op("vector", lambda e: e.tensor_sub(pscal_s[:, 5:6], pscal_s[:, 3:4], pscal_s[:, 4:5]))
        P.drain("vector")
        P.op("vector", lambda e: e.tensor_scalar_add(pscal_s[:, 5:6], pscal_s[:, 5:6], EPS),
             sem=s["dve"])
        P.wait("scalar", s["dve"])
        P.op("scalar", lambda e: e.activation(pscal_s[:, 6:7], pscal_s[:, 5:6],
                                              AF.Sqrt, bias=0.0, scale=1.0), sem=s["act"])
        P.wait("vector", s["act"])
        P.op("vector", lambda e: e.reciprocal(pscal_s[:, 6:7], pscal_s[:, 6:7]))
        P.op("vector", lambda e: e.tensor_scalar_mul(pscal_s[:, 7:8], pscal_s[:, 2:3], -1.0))
        P.drain("vector")
        P.op("vector", lambda e: e.tensor_scalar(
            yprow_s[:, :], xprow_s[:, :], pscal_s[:, 7:8], pscal_s[:, 6:7],
            ALU.add, ALU.mult), sem=s["dve"])
        P.wait("gpsimd", s["dve"])
        t2 = P.dmas("gpsimd", [(yprow_sp.ap()[:, :], yprow_s[:, :])])
        P.wait_tok("sync", t2)
        return P.dmas("sync", [(ypcol_s[:, :],
                                _ap3(yprow_sp, 0, [[1, 128], [128, NCH]]))],
                      slow=True)



    def emit_weight_prep(w_dram_ap, m, hi_sb, lo_sp_h, hi_sp_h, tmp, lotmp,
                         prev_tok=None):
        ncho = m // 128
        sp_tok = prev_tok
        for k in range(NCH):
            P.wait("sync", s["dve"])
            t = P.dmas("sync", [(tmp[:, 0:m], w_dram_ap[k * 128:(k + 1) * 128, :])])
            P.wait_tok("scalar", t)
            hi_slice = hi_sb[:, k * m:(k + 1) * m]
            P.op("scalar", lambda e, o=hi_slice, i=tmp[:, 0:m]:
                 e.activation(o, i, AF.Identity, bias=0.0, scale=1.0), sem=s["act"])
            P.wait("vector", s["act"])
            P.wait_tok("vector", sp_tok)
            P.wait_tok("vector", t)
            P.op("vector", lambda e, o=lotmp[:, 0:m], i=tmp[:, 0:m], h=hi_slice:
                 e.tensor_tensor(o, i, h, ALU.subtract), sem=s["dve"])
            P.wait("gpsimd", s["dve"])
            grp = [(_ap3(lo_sp_h, k * 128,
                         [[NCH * 128, 128], [128 * NCH * 128, ncho], [1, 128]]),
                    lotmp[:, 0:m].rearrange("p (o c) -> p o c", o=ncho))]
            if hi_sp_h is not None:
                P.wait("gpsimd", s["act"])
                grp.append((hi_sp_h[k * 128:(k + 1) * 128, :], hi_slice))
            sp_tok = P.dmas("gpsimd", grp)
        return sp_tok

    def mk_stats(ctx, tag, nsub):
        names = ("s1", "s2", "mean", "ex2", "m2", "var", "sd", "inv", "nm")
        return {n: sb(ctx, f"{tag}_{n}", [128, nsub], F32) for n in names}

    def ln_stats_chain(st, nsub):
        P.drain("vector")
        P.op("vector", lambda e: e.tensor_scalar_mul(st["mean"][:, 0:nsub], st["s1"][:, 0:nsub], 1.0 / E))
        P.op("vector", lambda e: e.tensor_scalar_mul(st["ex2"][:, 0:nsub], st["s2"][:, 0:nsub], 1.0 / E))
        P.drain("vector")
        P.op("vector", lambda e: e.tensor_mul(st["m2"][:, 0:nsub], st["mean"][:, 0:nsub], st["mean"][:, 0:nsub]))
        P.drain("vector")
        P.op("vector", lambda e: e.tensor_sub(st["var"][:, 0:nsub], st["ex2"][:, 0:nsub], st["m2"][:, 0:nsub]))
        P.drain("vector")
        P.op("vector", lambda e: e.tensor_scalar_add(st["var"][:, 0:nsub], st["var"][:, 0:nsub], EPS),
             sem=s["dve"])
        P.wait("scalar", s["dve"])
        P.op("scalar", lambda e: e.activation(st["sd"][:, 0:nsub], st["var"][:, 0:nsub],
                                              AF.Sqrt, bias=0.0, scale=1.0), sem=s["act"])
        P.wait("vector", s["act"])
        P.op("vector", lambda e: e.reciprocal(st["inv"][:, 0:nsub], st["sd"][:, 0:nsub]))
        P.drain("vector")
        P.op("vector", lambda e: e.tensor_mul(st["nm"][:, 0:nsub], st["mean"][:, 0:nsub], st["inv"][:, 0:nsub]))
        P.drain("vector")
        P.op("vector", lambda e: e.tensor_scalar_mul(st["nm"][:, 0:nsub], st["nm"][:, 0:nsub], -1.0),
             sem=s["dve"])

    def stat_sub(st, src, xsq, sub):
        P.wait("scalar", s["dve"])
        P.op("scalar", lambda e, o=xsq[:, :], i=src:
             e.activation(o, i, AF.Square, bias=0.0, scale=1.0), sem=s["act"])
        P.op("vector", lambda e, o=st["s1"][:, sub:sub + 1], i=src:
             e.reduce_sum(o, i, axis=AXX))
        P.wait("vector", s["act"])
        P.op("vector", lambda e, o=st["s2"][:, sub:sub + 1], i=xsq[:, :]:
             e.reduce_sum(o, i, axis=AXX), sem=s["dve"])

    def emit_stats_from_tm(st, src_list, xsq, nsub):
        for sub in range(nsub):
            stat_sub(st, src_list[sub], xsq, sub)
        ln_stats_chain(st, nsub)

    def bc_tiles(st, nsub, bci_ps, bcn_ps, dve_val=None):
        P.wait("tensor", s["dve"], val=dve_val)
        for sub in range(nsub):
            iap = st["inv"][:, sub:sub + 1].broadcast_to([128, 128])
            nap = st["nm"][:, sub:sub + 1].broadcast_to([128, 128])
            P.op("tensor", lambda e, o=bci_ps[:, sub * 128:(sub + 1) * 128], w=iap:
                 e.matmul(o, w, id_f32_s[:, :], start=True, stop=True))
            P.op("tensor", lambda e, o=bcn_ps[:, sub * 128:(sub + 1) * 128], w=nap:
                 e.matmul(o, w, id_f32_s[:, :], start=True, stop=True),
                 sem=(s["pe"] if sub == nsub - 1 else None))

    def emit_ffn(bufs, w, tokw, resid_ap_fn, out_fn, out_toks,
                 side=None, out_ap_fn=None):
        T = tokw
        xr_hi, xr_lo = bufs["xr_hi"], bufs["xr_lo"]
        xk_hi, xk_lo = bufs["xk_hi"], bufs["xk_lo"]
        ka_hi, ka_lo = bufs["ka_hi"], bufs["ka_lo"]
        r_sig, kf, of, pb = bufs["r_sig"], bufs["kf"], bufs["of"], bufs["pb"]
        side = list(side) if side else []
        kmark = {"dve": 0, "act": 0}

        def sweep(proj, src_hi, src_lo, evac):
            whi = w[proj + "_hi"]
            for o in range(NCH):
                lo_ap = w["lo_load"](proj, o)
                bank = pb[o % 2]
                P.wait("tensor", s["dve"], val=bufs["bank_free_dve"][o % 2])
                P.wait("tensor", s["act"], val=bufs["bank_free_act"][o % 2])
                mms = ([(whi[:, k * E + o * 128: k * E + o * 128 + 128],
                         src_hi[:, k * T:(k + 1) * T]) for k in range(NCH)] +
                       [(whi[:, k * E + o * 128: k * E + o * 128 + 128],
                         src_lo[:, k * T:(k + 1) * T]) for k in range(NCH)] +
                       [(lo_ap[:, k * 128:(k + 1) * 128],
                         src_hi[:, k * T:(k + 1) * T]) for k in range(NCH)])
                n = len(mms)
                for i, (wap, xap) in enumerate(mms):
                    P.op("tensor",
                         lambda e, o_=bank[:, 0:T], w_=wap, x_=xap,
                                st=(i == 0), sp2=(i == n - 1):
                         e.matmul(o_, w_, x_, start=st, stop=sp2),
                         sem=(s["pe"] if i == n - 1 else None))
                evac(o, bank)
                if side:
                    side.pop(0)()

        def evac_r(o, bank):
            P.wait("scalar", s["pe"])
            P.op("scalar", lambda e, o_=r_sig[:, o * T:(o + 1) * T], i=bank[:, 0:T]:
                 e.activation(o_, i, AF.Sigmoid, bias=0.0, scale=1.0), sem=s["act"])
            bufs["bank_free_act"][o % 2] = s["act"].n

        def evac_k(o, bank):
            P.wait("vector", s["pe"])
            P.op("vector", lambda e, o_=kf[:, 0:T], i=bank[:, 0:T]:
                 e.tensor_scalar_max(o_, i, 0.0), sem=s["dve"])
            bufs["bank_free_dve"][o % 2] = s["dve"].n
            xfb = bufs.get("kxf", bufs["xf"])
            P.op("vector", lambda e, o_=xfb[:, 0:T], i=kf[:, 0:T]:
                 e.tensor_tensor(o_, i, i, ALU.mult), sem=s["dve"])
            P.wait("scalar", s["dve"])
            P.op("scalar", lambda e, o_=ka_hi[:, o * T:(o + 1) * T], i=xfb[:, 0:T]:
                 e.activation(o_, i, AF.Identity, bias=0.0, scale=1.0), sem=s["act"])
            P.wait("vector", s["act"])
            P.op("vector", lambda e, o_=ka_lo[:, o * T:(o + 1) * T], i=xfb[:, 0:T],
                        h=ka_hi[:, o * T:(o + 1) * T]:
                 e.tensor_tensor(o_, i, h, ALU.subtract), sem=s["dve"])
            kmark["dve"] = s["dve"].n
            kmark["act"] = s["act"].n

        def evac_v(o, bank):
            P.wait("vector", s["pe"])
            P.wait("vector", s["act"])
            slot = o % 2
            if out_ap_fn is None:
                of_ap = of[:, slot * T:(slot + 1) * T]
                if out_toks is not None:
                    P.wait_tok("vector", out_toks[slot])
            else:
                of_ap = out_ap_fn(o)
            P.op("vector", lambda e, o_=kf[:, 0:T], i=bank[:, 0:T],
                        r=r_sig[:, o * T:(o + 1) * T]:
                 e.tensor_tensor(o_, i, r, ALU.mult), sem=s["dve"])
            bufs["bank_free_dve"][o % 2] = s["dve"].n
            P.op("vector", lambda e, o_=of_ap, i=kf[:, 0:T], r=resid_ap_fn(o):
                 e.scalar_tensor_tensor(o_, i, 0.5, r, ALU.mult, ALU.add),
                 sem=s["dve"])
            if out_fn is not None:
                out_fn(o, of_ap)

        sweep("wr", xr_hi, xr_lo, evac_r)
        sweep("wk", xk_hi, xk_lo, evac_k)
        P.wait("tensor", s["dve"], val=kmark["dve"])
        P.wait("tensor", s["act"], val=kmark["act"])
        sweep("wv", ka_hi, ka_lo, evac_v)
        while side:
            side.pop(0)()

    def emit_mix_chunk(bufs, c, x_c, T, carry_init_fn, pre_normalized):
        y_ext, ytmp, dbuf, xf = bufs["y_ext"], bufs["ytmp"], bufs["dbuf"], bufs["xf"]
        carry_init_fn(c)
        if pre_normalized:
            P.op("vector", lambda e, x=x_c, T=T: e.tensor_copy(y_ext[:, 1:T + 1], x))
        else:
            P.op("vector", lambda e, x=x_c, T=T, b=bufs["bci"][:, 0:T]:
                 e.tensor_tensor(ytmp[:, 0:T], x, b, ALU.mult))
            P.op("vector", lambda e, T=T, b=bufs["bcn"][:, 0:T]:
                 e.tensor_tensor(y_ext[:, 1:T + 1], ytmp[:, 0:T], b, ALU.add))
        P.op("vector", lambda e, T=T: e.tensor_tensor(
            dbuf[:, 0:T], y_ext[:, 1:T + 1], y_ext[:, 0:T], ALU.subtract))
        for (mu, hi, lo) in ((mur_s, bufs["xr_hi"], bufs["xr_lo"]),
                             (muk_s, bufs["xk_hi"], bufs["xk_lo"])):
            P.op("vector", lambda e, T=T, m=mu[:, c:c + 1]:
                 e.scalar_tensor_tensor(xf[:, 0:T], dbuf[:, 0:T], m,
                                        y_ext[:, 0:T], ALU.mult, ALU.add),
                 sem=s["dve"])
            P.wait("scalar", s["dve"])
            P.op("scalar", lambda e, T=T, o=hi[:, c * T:(c + 1) * T]:
                 e.activation(o, xf[:, 0:T], AF.Identity, bias=0.0, scale=1.0),
                 sem=s["act"])
            P.wait("vector", s["act"])
            P.op("vector", lambda e, T=T, o=lo[:, c * T:(c + 1) * T],
                        h=hi[:, c * T:(c + 1) * T]:
                 e.tensor_tensor(o, xf[:, 0:T], h, ALU.subtract), sem=s["dve"])
        P.op("vector", lambda e, c=c, T=T: e.tensor_copy(
            bufs["ycar"][:, c:c + 1], y_ext[:, T:T + 1]), sem=s["dve"])

    def emit_mix(bufs, x_src_fn, tokw, carry_init_fn, pre_normalized):
        for c in range(NCH):
            emit_mix_chunk(bufs, c, x_src_fn(c), tokw, carry_init_fn,
                           pre_normalized)

    with ExitStack() as ctx:
        with ExitStack() as yctx:
            xprow_s = sb(yctx, "xprow_s", [1, E], F32)
            yprow_s = sb(yctx, "yprow_s", [1, E], F32)
            pscal_s = sb(yctx, "pscal_s", [1, 8], F32)
            ypcol_tok = emit_yprev(xprow_s, yprow_s, pscal_s)
        wtmp = sb(ctx, "p0_wtmp", [128, E], F32)
        lotmp = sb(ctx, "p0_lotmp", [128, E], BF16)
        wr_hi_s = sb(ctx, "wr_hi_s", [128, NCH * E], BF16)
        wk_hi_s = sb(ctx, "wk_hi_s", [128, NCH * E], BF16)
        wv_hi_s = sb(ctx, "wv_hi_s", [128, NCH * E], BF16)
        wp_tok = None
        for (wd, hs, lsp, hsp) in ((wrT, wr_hi_s, wr_lo, wr_hi_sp),
                                   (wkT, wk_hi_s, wk_lo, wk_hi_sp),
                                   (wvT, wv_hi_s, wv_lo, wv_hi_sp)):
            wp_tok = emit_weight_prep(wd.ap(), E, hs, lsp, hsp.ap(), wtmp,
                                      lotmp, prev_tok=wp_tok)
        phase_barrier()

        xs_b = [sb(ctx, f"a1_xs{i}", [128, NCH * TOKA], F32) for i in range(2)]
        xtm_b = [sb(ctx, f"a1_xtm{i}", [128, NSUBA * 1024], F32) for i in range(2)]
        xsq = sb(ctx, "a1_xsq", [128, 1024], F32)
        pt_sb = sb(ctx, "a1_ptsb", [128, NSUBA * 1024], F32)
        st1 = mk_stats(ctx, "a1_st1", NSUBA)
        st2 = mk_stats(ctx, "a1_st2", NSUBA)
        bci1_b = [sb(ctx, f"a1_bci{i}", [128, TOKA], F32) for i in range(2)]
        bcn1_b = [sb(ctx, f"a1_bcn{i}", [128, TOKA], F32) for i in range(2)]
        bci2 = sb(ctx, "a1_bci2", [128, TOKA], F32)
        bcn2 = sb(ctx, "a1_bcn2", [128, TOKA], F32)
        y_ext = sb(ctx, "a1_yext", [128, TOKA + 1], F32)
        ytmp = sb(ctx, "a1_ytmp", [128, TOKA], F32)
        dbuf = sb(ctx, "a1_dbuf", [128, TOKA], F32)
        xf = sb(ctx, "a1_xf", [128, TOKA], F32)
        kxf = sb(ctx, "a1_kxf", [128, TOKA], F32)
        ycar = sb(ctx, "a1_ycar", [128, NCH], F32)
        xr_hi_b = [sb(ctx, f"a1_xrh{i}", [128, NCH * TOKA], BF16) for i in range(2)]
        xr_lo_b = [sb(ctx, f"a1_xrl{i}", [128, NCH * TOKA], BF16) for i in range(2)]
        xk_hi_b = [sb(ctx, f"a1_xkh{i}", [128, NCH * TOKA], BF16) for i in range(2)]
        xk_lo_b = [sb(ctx, f"a1_xkl{i}", [128, NCH * TOKA], BF16) for i in range(2)]
        ka_hi = sb(ctx, "a1_ka_hi", [128, NCH * TOKA], BF16)
        ka_lo = sb(ctx, "a1_ka_lo", [128, NCH * TOKA], BF16)
        r_sig = sb(ctx, "a1_rsig", [128, NCH * TOKA], F32)
        kf = sb(ctx, "a1_kf", [128, TOKA], F32)
        of = sb(ctx, "a1_of", [128, 2 * TOKA], F32)
        r1t_b = [sb(ctx, f"a1_r1{i}", [128, NCH * TOKA], F32) for i in range(2)]
        wlo_ring = sb(ctx, "a1_wlo", [128, NSLOTA * NCH * 128], BF16)
        x2h = sb(ctx, "a1_x2h", [128, 2 * TOKA], BF16)
        x2l = sb(ctx, "a1_x2l", [128, 2 * TOKA], BF16)
        pb = [ps(ctx, "a1_pb0", [128, TOKA], F32), ps(ctx, "a1_pb1", [128, TOKA], F32)]
        bci_ps = ps(ctx, "a1_bci_ps", [128, TOKA], F32)
        bcn_ps = ps(ctx, "a1_bcn_ps", [128, TOKA], F32)
        pt_ps = ps(ctx, "a1_pt_ps", [128, 1024], F32)

        wlo_sps = {"wr": wr_lo, "wk": wk_lo, "wv": wv_lo}
        ring_state = {"i": 0, "loaded": {}}
        x2_free = [None, None]
        r1_sp_tok = [None, None]
        t_xs = [None, None]
        t_xtm = [None, None]
        ffn_pe_mark = [0, 0]
        mix_mark = [{"dve": 0, "act": 0}, {"dve": 0, "act": 0}]

        def lo_load(proj, o):
            slot = ring_state["i"] % NSLOTA
            ring_state["i"] += 1
            base = slot * NCH * 128
            src = _ap3(wlo_sps[proj], o * NCH * 128 * 128,
                       [[NCH * 128, 128], [128, NCH], [1, 128]])
            P.wait("sync", s["pe"], val=ring_state["loaded"].get(slot, 0))
            t = P.dmas("sync", [(wlo_ring[:, base:base + NCH * 128], src)])
            P.wait_tok("tensor", t)
            ring_state["loaded"][slot] = s["pe"].n + 1
            return wlo_ring[:, base:base + NCH * 128]

        wdict = {"wr_hi": wr_hi_s, "wk_hi": wk_hi_s, "wv_hi": wv_hi_s,
                 "lo_load": lo_load}
        bank_free_dve = [0, 0]
        bank_free_act = [0, 0]
        bufs_u = []
        for i in range(2):
            bufs_u.append({
                "xr_hi": xr_hi_b[i], "xr_lo": xr_lo_b[i],
                "xk_hi": xk_hi_b[i], "xk_lo": xk_lo_b[i],
                "ka_hi": ka_hi, "ka_lo": ka_lo, "r_sig": r_sig, "kf": kf,
                "kxf": kxf, "of": of, "pb": pb, "y_ext": y_ext, "ytmp": ytmp,
                "dbuf": dbuf, "xf": xf, "ycar": ycar,
                "bci": bci1_b[i], "bcn": bcn1_b[i],
                "bank_free_dve": bank_free_dve, "bank_free_act": bank_free_act})

        def emit_tile_loads(jj):
            u2 = jj % 2
            T0j = jj * TOKA
            P.wait("scalar", s["dve"])
            t_xtm[u2] = P.dmas("scalar", [
                (xtm_b[u2][:, :].rearrange("p (s f) -> p s f", s=NSUBA),
                 _ap3(x_tm, T0j * E, [[E, 128], [128 * E, NSUBA], [1, E]]))])
            t_xs[u2] = P.dmas("scalar", [
                (xs_b[u2][:, :].rearrange("p (c t) -> p c t", c=NCH),
                 _ap3(x_fm, T0j, [[S, 128], [128 * S, NCH], [1, TOKA]]))])

        def mk_carry(jj):
            def carry_init(c, jj=jj):
                if jj == 0:
                    P.op("vector", lambda e, c=c: e.tensor_copy(
                        y_ext[:, 0:1], ypcol_s[:, c:c + 1]))
                elif jj == OWN // TOKA:
                    P.op("vector", lambda e, c=c: e.tensor_tensor(
                        y_ext[:, 0:1], ycar[:, c:c + 1], m2048_s[:, 0:1],
                        ALU.mult))
                else:
                    P.op("vector", lambda e, c=c: e.tensor_copy(
                        y_ext[:, 0:1], ycar[:, c:c + 1]))
            return carry_init

        def prod_thunks(jj):
            u2 = jj % 2
            th = []
            for sub in range(NSUBA):
                def t_stat(sub=sub, u2=u2):
                    P.wait_tok("scalar", t_xtm[u2])
                    P.wait_tok("vector", t_xtm[u2])
                    stat_sub(st1, xtm_b[u2][:, sub * 1024:(sub + 1) * 1024],
                             xsq, sub)
                th.append(t_stat)
            th.append(lambda: ln_stats_chain(st1, NSUBA))

            def t_bc1(u2=u2):
                bc_tiles(st1, NSUBA, bci_ps, bcn_ps)
                P.wait("vector", s["pe"])
                P.op("vector", lambda e: e.tensor_copy(bci1_b[u2][:, :],
                                                       bci_ps[:, 0:TOKA]))
                P.op("vector", lambda e: e.tensor_copy(bcn1_b[u2][:, :],
                                                       bcn_ps[:, 0:TOKA]),
                     sem=s["dve"])
                P.drain("vector")
            th.append(t_bc1)
            ci = mk_carry(jj)
            for c in range(NCH):
                def t_mix(c=c, u2=u2, ci=ci):
                    P.wait("vector", s["pe"], val=ffn_pe_mark[u2])
                    P.wait_tok("vector", t_xs[u2])
                    emit_mix_chunk(bufs_u[u2], c,
                                   xs_b[u2][:, c * TOKA:(c + 1) * TOKA],
                                   TOKA, ci, False)
                    if c == NCH - 1:
                        mix_mark[u2]["dve"] = s["dve"].n
                        mix_mark[u2]["act"] = s["act"].n
                th.append(t_mix)
            return th

        def ln2_stat_thunks(jj):
            th = []
            for sub in range(NSUBA):
                def t_l(sub=sub):
                    P.wait("vector", s["act"])
                    stat_sub(st2, pt_sb[:, sub * 1024:(sub + 1) * 1024],
                             xsq, sub)
                th.append(t_l)
            th.append(lambda: ln_stats_chain(st2, NSUBA))

            def t_bc2():
                bc_tiles(st2, NSUBA, bci_ps, bcn_ps)
                P.wait("vector", s["pe"])
                P.op("vector", lambda e: e.tensor_copy(bci2[:, :],
                                                       bci_ps[:, 0:TOKA]))
                P.op("vector", lambda e: e.tensor_copy(bcn2[:, :],
                                                       bcn_ps[:, 0:TOKA]),
                     sem=s["dve"])
                P.drain("vector")
            th.append(t_bc2)
            return th

        def ln2_split_thunks(jj):
            u = jj % 2
            T0j = jj * TOKA
            th = []
            for c in range(NCH):
                def t_sp(c=c, u=u, T0j=T0j):
                    slot = c % 2
                    sl = slice(slot * TOKA, (slot + 1) * TOKA)
                    P.op("vector", lambda e, c=c, u=u: e.tensor_tensor(
                        ytmp[:, 0:TOKA], r1t_b[u][:, c * TOKA:(c + 1) * TOKA],
                        bci2[:, :], ALU.mult))
                    P.op("vector", lambda e: e.tensor_tensor(
                        xf[:, 0:TOKA], ytmp[:, 0:TOKA], bcn2[:, :], ALU.add),
                        sem=s["dve"])
                    P.wait("scalar", s["dve"])
                    P.wait_tok("scalar", x2_free[slot])
                    P.op("scalar", lambda e, sl=sl: e.activation(
                        x2h[:, sl], xf[:, 0:TOKA], AF.Identity, bias=0.0,
                        scale=1.0), sem=s["act"])
                    P.wait("vector", s["act"])
                    P.wait_tok("vector", x2_free[slot])
                    P.op("vector", lambda e, sl=sl: e.tensor_tensor(
                        x2l[:, sl], xf[:, 0:TOKA], x2h[:, sl], ALU.subtract),
                        sem=s["dve"])
                    P.wait("gpsimd", s["dve"])
                    x2_free[slot] = P.dmas("gpsimd", [
                        (x2h_sp.ap()[c * 128:(c + 1) * 128, T0j:T0j + TOKA],
                         x2h[:, sl]),
                        (x2l_sp.ap()[c * 128:(c + 1) * 128, T0j:T0j + TOKA],
                         x2l[:, sl])])
                th.append(t_sp)
            return th

        def emit_ln2_transposes(jj):
            u = jj % 2
            for sub in range(NSUBA):
                P.wait("tensor", s["dve"])
                P.wait("tensor", s["act"])
                for c in range(NCH):
                    P.op("tensor", lambda e, u=u,
                                o=pt_ps[:, c * 128:(c + 1) * 128],
                                i=r1t_b[u][:, c * TOKA + sub * 128:
                                            c * TOKA + sub * 128 + 128]:
                         e.transpose(o, i, id_f32_s[:, :]),
                         sem=(s["pe"] if c == NCH - 1 else None))
                P.wait("scalar", s["pe"])
                P.wait("scalar", s["dve"])
                P.op("scalar", lambda e, sub=sub: e.activation(
                    pt_sb[:, sub * 1024:(sub + 1) * 1024], pt_ps[:, 0:1024],
                    AF.Identity, bias=0.0, scale=1.0), sem=s["act"])

        NTA_run = NTA if phases.get("A1") in (None, 0) else min(phases["A1"], NTA)
        if "A1" in phases and NTA_run:
            emit_tile_loads(0)
            P.wait_tok("vector", ypcol_tok)
            for t in prod_thunks(0):
                t()
            for j in range(NTA_run):
                u = j % 2
                if j + 1 < NTA_run:
                    emit_tile_loads(j + 1)
                pt = prod_thunks(j + 1) if j + 1 < NTA_run else None
                lt = ln2_stat_thunks(j - 1) if j >= 1 else None
                sp = ln2_split_thunks(j - 1) if j >= 1 else None
                side = []
                if lt:
                    side += lt[:2]
                if pt:
                    side += pt[:2]
                if lt:
                    side.append(lt[2])
                if pt:
                    side.append(pt[2])
                if lt:
                    side.append(lt[3])
                if pt:
                    side.append(pt[3])
                for c in range(NCH):
                    if sp:
                        side.append(sp[c])
                    if pt:
                        side.append(pt[4 + c])
                P.wait("tensor", s["dve"], val=mix_mark[u]["dve"])
                P.wait("tensor", s["act"], val=mix_mark[u]["act"])
                P.wait_tok("vector", r1_sp_tok[u])
                emit_ffn(bufs_u[u], wdict, TOKA,
                         (lambda o, u=u: xs_b[u][:, o * TOKA:(o + 1) * TOKA]),
                         None, None, side=side,
                         out_ap_fn=(lambda o, u=u:
                                    r1t_b[u][:, o * TOKA:(o + 1) * TOKA]))
                ffn_pe_mark[u] = s["pe"].n
                emit_ln2_transposes(j)
                P.wait("gpsimd", s["dve"])
                r1_sp_tok[u] = P.dmas("gpsimd", [
                    (_ap3(r1_sp, j * TOKA, [[S, 128], [128 * S, NCH], [1, TOKA]]),
                     r1t_b[u][:, :].rearrange("p (c t) -> p c t", c=NCH))])
            for t in ln2_stat_thunks(NTA_run - 1) + ln2_split_thunks(NTA_run - 1):
                t()
        phase_barrier()
        P.flush()


    if "A2" in phases:
      with ExitStack() as ctx:
        wq_hi_s = sb(ctx, "wq_hi_s", [128, NCH * M3], BF16)
        with ExitStack() as wctx:
            wtmp = sb(wctx, "p1_wtmp", [128, M3], F32)
            lotmp = sb(wctx, "p1_lotmp", [128, M3], BF16)
            emit_weight_prep(wqkv.ap(), M3, wq_hi_s, wq_lo, None, wtmp, lotmp)
        phase_barrier()

        x2ht_b = [sb(ctx, f"a2_x2h{i}", [128, NCH * TOK], BF16) for i in range(2)]
        x2lt_b = [sb(ctx, f"a2_x2l{i}", [128, NCH * TOK], BF16) for i in range(2)]
        qf = sb(ctx, "a2_qf", [128, TOK], F32)
        sth = sb(ctx, "a2_sth", [128, TOK], BF16)
        stl = sb(ctx, "a2_stl", [128, TOK], BF16)
        vst = sb(ctx, "a2_vst", [128, 2 * 128], BF16)
        wlo_ring = sb(ctx, "a2_wlo", [128, 2 * NCH * 128], BF16)
        pb = [ps(ctx, "a2_pb0", [128, TOK], F32), ps(ctx, "a2_pb1", [128, TOK], F32)]
        ptb = [ps(ctx, "a2_pt0", [128, 128], BF16), ps(ctx, "a2_pt1", [128, 128], BF16)]

        ring_state = {"i": 0, "loaded": {}}

        def qkv_lo_load(o):
            slot = ring_state["i"] % 2
            ring_state["i"] += 1
            base = slot * NCH * 128
            src = _ap3(wq_lo, o * NCH * 128 * 128,
                       [[NCH * 128, 128], [128, NCH], [1, 128]])
            P.wait("sync", s["pe"], val=ring_state["loaded"].get(slot, 0))
            t = P.dmas("sync", [(wlo_ring[:, base:base + NCH * 128], src)])
            P.wait_tok("tensor", t)
            ring_state["loaded"][slot] = s["pe"].n + 1
            return wlo_ring[:, base:base + NCH * 128]

        bank_free = {"dve": [0, 0], "act": [0, 0]}
        ptk_free = [0, 0]
        vst_free = [None, None]
        qk_sp_tok = [None]

        tile_pe_free = [0, 0]
        tile_tok = [None, None]

        def a2_load(j):
            u = j % 2
            T0j = j * TOK
            P.wait("sync", s["pe"], val=tile_pe_free[u])
            return P.dmas("sync", [
                (x2ht_b[u][:, :].rearrange("p (c t) -> p c t", c=NCH),
                 _ap3(x2h_sp, T0j, [[S, 128], [128 * S, NCH], [1, TOK]])),
                (x2lt_b[u][:, :].rearrange("p (c t) -> p c t", c=NCH),
                 _ap3(x2l_sp, T0j, [[S, 128], [128 * S, NCH], [1, TOK]]))])

        tile_tok[0] = a2_load(0)
        for j in range(NT):
            T0 = j * TOK
            u = j % 2
            if j + 1 < NT:
                tile_tok[(j + 1) % 2] = a2_load(j + 1)
            x2ht, x2lt = x2ht_b[u], x2lt_b[u]
            P.wait_tok("tensor", tile_tok[u])
            for o in range(24):
                is_q = o < 8
                is_v = o >= 16
                if is_q:
                    if T0 < OWN:
                        cs, cw = 0, TOK
                    elif T0 + TOK == S:
                        cs, cw = XQ0 - T0, S - XQ0
                    else:
                        continue
                else:
                    cs, cw = 0, TOK
                lo_ap = qkv_lo_load(o)
                bank = pb[o % 2]
                P.wait("tensor", s["dve"], val=bank_free["dve"][o % 2])
                P.wait("tensor", s["act"], val=bank_free["act"][o % 2])
                mms = ([(wq_hi_s[:, k * M3 + o * 128: k * M3 + o * 128 + 128],
                         x2ht[:, k * TOK + cs: k * TOK + cs + cw]) for k in range(NCH)] +
                       [(wq_hi_s[:, k * M3 + o * 128: k * M3 + o * 128 + 128],
                         x2lt[:, k * TOK + cs: k * TOK + cs + cw]) for k in range(NCH)] +
                       [(lo_ap[:, k * 128:(k + 1) * 128],
                         x2ht[:, k * TOK + cs: k * TOK + cs + cw]) for k in range(NCH)])
                n = len(mms)
                for i, (wap, xap) in enumerate(mms):
                    P.op("tensor",
                         lambda e, o_=bank[:, 0:cw], w_=wap, x_=xap,
                                st=(i == 0), sp2=(i == n - 1):
                         e.matmul(o_, w_, x_, start=st, stop=sp2),
                         sem=(s["pe"] if i == n - 1 else None))
                P.wait("vector", s["pe"])
                P.wait_tok("vector", qk_sp_tok[0])
                if is_q:
                    P.op("vector", lambda e, o_=qf[:, 0:cw], i=bank[:, 0:cw],
                                bb=bqkv_s[:, o:o + 1]:
                         e.tensor_scalar(o_, i, bb, 0.125, ALU.add, ALU.mult),
                         sem=s["dve"])
                else:
                    P.op("vector", lambda e, o_=qf[:, 0:cw], i=bank[:, 0:cw],
                                bb=bqkv_s[:, o:o + 1]:
                         e.tensor_scalar(o_, i, bb, None, ALU.add), sem=s["dve"])
                bank_free["dve"][o % 2] = s["dve"].n
                P.wait("scalar", s["dve"])
                P.wait("scalar", s["pe"])
                P.wait_tok("scalar", qk_sp_tok[0])
                P.op("scalar", lambda e, o_=sth[:, 0:cw], i=qf[:, 0:cw]:
                     e.activation(o_, i, AF.Identity, bias=0.0, scale=1.0), sem=s["act"])
                bank_free["act"][o % 2] = s["act"].n
                P.wait("vector", s["act"])
                P.wait("vector", s["pe"])
                P.op("vector", lambda e, o_=stl[:, 0:cw], i=qf[:, 0:cw], h=sth[:, 0:cw]:
                     e.tensor_tensor(o_, i, h, ALU.subtract), sem=s["dve"])
                if not is_v:
                    sp_h = qh_sp if is_q else kh_sp
                    sp_l = ql_sp if is_q else kl_sp
                    row = (o if is_q else o - 8) * 128
                    P.wait("gpsimd", s["act"])
                    P.wait("gpsimd", s["dve"])
                    qk_sp_tok[0] = P.dmas("gpsimd", [
                        (sp_h.ap()[row:row + 128, T0 + cs:T0 + cs + cw], sth[:, 0:cw]),
                        (sp_l.ap()[row:row + 128, T0 + cs:T0 + cs + cw], stl[:, 0:cw])])
                else:
                    hA = (o - 16) * 2
                    P.wait("tensor", s["act"])
                    P.wait("tensor", s["dve"])
                    for sub in range(NSUB):
                        for hl, stt in ((0, sth), (1, stl)):
                            pslot = (sub * 2 + hl) % 2
                            ptk = ptb[pslot]
                            P.wait("tensor", s["dve"], val=ptk_free[pslot])
                            P.op("tensor", lambda e, o_=ptk[:, :],
                                        i=stt[:, sub * 128:(sub + 1) * 128]:
                                 e.transpose(o_, i, id_bf_s[:, :]), sem=s["pe"])
                            P.wait("vector", s["pe"])
                            P.wait_tok("vector", vst_free[pslot])
                            vsl = vst[:, pslot * 128:(pslot + 1) * 128]
                            P.op("vector", lambda e, o_=vsl, i=ptk[:, :]:
                                 e.tensor_copy(o_, i), sem=s["dve"])
                            ptk_free[pslot] = s["dve"].n
                            P.wait("gpsimd", s["dve"])
                            spv = vth_sp if hl == 0 else vtl_sp
                            dst = _ap3(spv, hA * S * HD + (T0 + sub * 128) * HD,
                                       [[HD, 128], [S * HD, 2], [1, HD]])
                            vst_free[pslot] = P.dmas("gpsimd", [
                                (dst, vsl.rearrange("p (h d) -> p h d", h=2))])
            tile_pe_free[u] = s["pe"].n
        phase_barrier()
        P.flush()

    QGS = [(0, 512), (512, 512), (1024, 512), (1536, 512), (2048, XQW)]
    if "B" in phases:
      with ExitStack() as ctx:
        kp_hi = [sb(ctx, f"b_kph{i}", [128, S], BF16) for i in range(2)]
        kp_lo = [sb(ctx, f"b_kpl{i}", [128, S], BF16) for i in range(2)]
        qp_hi = [sb(ctx, f"b_qph{i}", [128, QTOT], BF16) for i in range(2)]
        qp_lo = [sb(ctx, f"b_qpl{i}", [128, QTOT], BF16) for i in range(2)]
        vA_hi = [sb(ctx, f"b_vah{i}", [128, NKC * HD], BF16) for i in range(2)]
        vA_lo = [sb(ctx, f"b_val{i}", [128, NKC * HD], BF16) for i in range(2)]
        vB_hi = [sb(ctx, f"b_vbh{i}", [128, NKC * HD], BF16) for i in range(2)]
        vB_lo = [sb(ctx, f"b_vbl{i}", [128, NKC * HD], BF16) for i in range(2)]
        paf = [sb(ctx, f"b_paf{i}", [128, 512], F32) for i in range(2)]
        pbf = [sb(ctx, f"b_pbf{i}", [128, 512], F32) for i in range(2)]
        pah = [sb(ctx, f"b_pah{i}", [128, 512], BF16) for i in range(2)]
        pal = [sb(ctx, f"b_pal{i}", [128, 512], BF16) for i in range(2)]
        pbh = [sb(ctx, f"b_pbh{i}", [128, 512], BF16) for i in range(2)]
        pbl = [sb(ctx, f"b_pbl{i}", [128, 512], BF16) for i in range(2)]
        dsb = sb(ctx, "b_dsb", [128, 512], F32)
        binv = sb(ctx, "b_binv", [128, 512], F32)
        o_f = sb(ctx, "b_of", [128, 512], F32)
        o_hi = sb(ctx, "b_ohi", [128, 512], BF16)
        o_lo = sb(ctx, "b_olo", [128, 512], BF16)
        st_ps = [(ps(ctx, "b_stA0", [128, 512], F32), ps(ctx, "b_stB0", [128, 512], F32)),
                 (ps(ctx, "b_stA1", [128, 512], F32), ps(ctx, "b_stB1", [128, 512], F32))]
        pv = ps(ctx, "b_pv", [128, 512], F32)
        dps = ps(ctx, "b_dps", [128, 512], F32)
        ddp = ps(ctx, "b_ddp", [128, 512], F32)
        o_sp_tok = [None]
        P.op("vector", lambda e: e.memset(dsb[:, :], 0.0))
        for t in (st_ps[0][0], st_ps[0][1], st_ps[1][0], st_ps[1][1], pv, dps, ddp):
            P.op("vector", lambda e, t=t: e.memset(t[:, :], 0.0))
        P.op("vector", lambda e: e.engine_nop(), sem=s["dve"])

        pair_free_pe = [0, 0]
        pair_tok = [None, None]
        g_exp = [0, 0]
        g_lo = [0, 0]
        g_castA = [0, 0]
        g_pv = [0, 0]

        def emit_pair_loads(pc):
            u = pc % 2
            grp = []
            for dst, spv in ((kp_hi[u], kh_sp), (kp_lo[u], kl_sp)):
                grp.append((dst[:, :], spv.ap()[pc * 128:(pc + 1) * 128, :]))
            for dst, spv in ((qp_hi[u], qh_sp), (qp_lo[u], ql_sp)):
                grp.append((dst[:, 0:OWN], spv.ap()[pc * 128:(pc + 1) * 128, 0:OWN]))
                grp.append((dst[:, OWN:QTOT], spv.ap()[pc * 128:(pc + 1) * 128, XQ0:S]))
            for dst, spv, h in ((vA_hi[u], vth_sp, 2 * pc), (vA_lo[u], vtl_sp, 2 * pc),
                                (vB_hi[u], vth_sp, 2 * pc + 1), (vB_lo[u], vtl_sp, 2 * pc + 1)):
                grp.append((dst[:, :].rearrange("p (c d) -> p c d", c=NKC),
                            _ap3(spv, h * S * HD, [[HD, 128], [128 * HD, NKC], [1, HD]])))
            P.wait("sync", s["pe"], val=pair_free_pe[u])
            return P.dmas("sync", grp)

        pair_tok[0] = emit_pair_loads(0)
        for pc in range(8):
            u = pc % 2
            if pc + 1 < 8:
                pair_tok[(pc + 1) % 2] = emit_pair_loads(pc + 1)
            P.wait_tok("tensor", pair_tok[u])
            kph, kpl = kp_hi[u], kp_lo[u]
            qph, qpl = qp_hi[u], qp_lo[u]
            vah, val_, vbh, vbl = vA_hi[u], vA_lo[u], vB_hi[u], vB_lo[u]

            for (g0, gw) in QGS:
                lo_done = {}
                castA_done = {}

                def emit_pv(cc, g0=g0, gw=gw, vah=vah, val_=val_, vbh=vbh, vbl=vbl):
                    v = cc % 2
                    P.wait("tensor", s["dve"], val=lo_done[cc])
                    P.wait("tensor", s["act"], val=castA_done[cc])
                    pvs = [(vah, pah[v]), (val_, pah[v]), (vah, pal[v])]
                    pvsB = [(vbh, pbh[v]), (vbl, pbh[v]), (vbh, pbl[v])]
                    st_c, sp_c = (cc == 0), (cc == NKC - 1)
                    for i in range(3):
                        va_, pa_ = pvs[i]
                        vb_, pb_ = pvsB[i]
                        st_i = st_c and (i == 0)
                        sp_i = sp_c and (i == 2)
                        P.op("tensor", lambda e, w_=va_[:, cc * HD:(cc + 1) * HD],
                                    x_=pa_[:, 0:gw], a=st_i, b_=sp_i:
                             e.matmul(pv[0:64, 0:gw], w_, x_, start=a, stop=b_,
                                      tile_position=(0, 0), skip_group_check=True))
                        P.op("tensor", lambda e, w_=vb_[:, cc * HD:(cc + 1) * HD],
                                    x_=pb_[:, 0:gw], a=st_i, b_=sp_i:
                             e.matmul(pv[64:128, 0:gw], w_, x_, start=a, stop=b_,
                                      tile_position=(0, 64), skip_group_check=True))
                    dmov = [(pah[v], 0), (pal[v], 32), (pbh[v], 64), (pbl[v], 96)]
                    for i, (pm, col) in enumerate(dmov):
                        P.op("tensor", lambda e, o_=dps[col:col + 1, 0:gw],
                                    x_=pm[:, 0:gw], a=st_c, b_=sp_c, col=col:
                             e.matmul(o_, ones_bf_s[:, :], x_, start=a, stop=b_,
                                      tile_position=(0, col), skip_group_check=True),
                             sem=(s["pe"] if i == len(dmov) - 1 else None))
                    g_pv[v] = s["pe"].n

                for c in range(NKC):
                    v = c % 2
                    stA, stB = st_ps[v]
                    P.wait("tensor", s["act"], val=g_exp[v])
                    qk = [(kph, qph), (kph, qpl), (kpl, qph)]
                    for i, (kk, qq) in enumerate(qk):
                        st_, sp2_ = (i == 0), (i == len(qk) - 1)
                        P.op("tensor", lambda e, o_=stA[:, 0:gw],
                                    w_=kk[0:64, c * 128:(c + 1) * 128],
                                    x_=qq[0:64, g0:g0 + gw], a=st_, b_=sp2_:
                             e.matmul(o_, w_, x_, start=a, stop=b_,
                                      tile_position=(0, 0)))
                        P.op("tensor", lambda e, o_=stB[:, 0:gw],
                                    w_=kk[64:128, c * 128:(c + 1) * 128],
                                    x_=qq[64:128, g0:g0 + gw], a=st_, b_=sp2_:
                             e.matmul(o_, w_, x_, start=a, stop=b_,
                                      tile_position=(64, 0)),
                             sem=(s["pe"] if sp2_ else None))
                    qk_done_c = s["pe"].n
                    if c >= 1:
                        emit_pv(c - 1)
                    P.wait("scalar", s["pe"], val=qk_done_c)
                    P.wait("scalar", s["dve"], val=g_lo[v])
                    P.op("scalar", lambda e, v=v, gw=gw, sA=stA: e.activation(
                        paf[v][:, 0:gw], sA[:, 0:gw], AF.Exp, bias=0.0, scale=1.0))
                    P.op("scalar", lambda e, v=v, gw=gw, sB=stB: e.activation(
                        pbf[v][:, 0:gw], sB[:, 0:gw], AF.Exp, bias=0.0, scale=1.0),
                        sem=s["act"])
                    g_exp[v] = s["act"].n
                    exp_done_c = s["act"].n
                    P.wait("scalar", s["pe"], val=g_pv[v])
                    P.op("scalar", lambda e, v=v, gw=gw: e.activation(
                        pah[v][:, 0:gw], paf[v][:, 0:gw], AF.Identity,
                        bias=0.0, scale=1.0), sem=s["act"])
                    castA_done[c] = s["act"].n
                    P.wait("vector", s["act"], val=exp_done_c)
                    P.wait("vector", s["pe"], val=g_pv[v])
                    P.op("vector", lambda e, v=v, gw=gw: e.tensor_copy(
                        pbh[v][:, 0:gw], pbf[v][:, 0:gw]))
                    P.wait("vector", s["act"], val=castA_done[c])
                    P.op("vector", lambda e, v=v, gw=gw: e.tensor_tensor(
                        pal[v][:, 0:gw], paf[v][:, 0:gw], pah[v][:, 0:gw], ALU.subtract))
                    P.op("vector", lambda e, v=v, gw=gw: e.tensor_tensor(
                        pbl[v][:, 0:gw], pbf[v][:, 0:gw], pbh[v][:, 0:gw], ALU.subtract),
                        sem=s["dve"])
                    lo_done[c] = s["dve"].n
                    g_lo[v] = s["dve"].n
                emit_pv(NKC - 1)

                P.wait("vector", s["pe"])
                for drow in (0, 32, 64, 96):
                    P.op("vector", lambda e, gw=gw, dr=drow: e.tensor_copy(
                        dsb[dr:dr + 1, 0:gw], dps[dr:dr + 1, 0:gw]),
                        sem=(s["dve"] if drow == 96 else None))
                P.wait("tensor", s["dve"])
                P.op("tensor", lambda e, gw=gw: e.matmul(
                    ddp[:, 0:gw], dmask_s[:, :], dsb[:, 0:gw], start=True, stop=True),
                    sem=s["pe"])
                P.wait("vector", s["pe"])
                P.op("vector", lambda e, gw=gw: e.reciprocal(binv[:, 0:gw], ddp[:, 0:gw]),
                     sem=s["dve"])
                P.op("vector", lambda e, gw=gw: e.tensor_tensor(
                    o_f[:, 0:gw], pv[:, 0:gw], binv[:, 0:gw], ALU.mult), sem=s["dve"])
                P.wait("scalar", s["dve"])
                P.wait_tok("scalar", o_sp_tok[0])
                P.op("scalar", lambda e, gw=gw: e.activation(
                    o_hi[:, 0:gw], o_f[:, 0:gw], AF.Identity, bias=0.0, scale=1.0),
                    sem=s["act"])
                P.wait("vector", s["act"])
                P.wait_tok("vector", o_sp_tok[0])
                P.op("vector", lambda e, gw=gw: e.tensor_tensor(
                    o_lo[:, 0:gw], o_f[:, 0:gw], o_hi[:, 0:gw], ALU.subtract),
                    sem=s["dve"])
                P.wait("gpsimd", s["dve"])
                P.wait("gpsimd", s["act"])
                o_sp_tok[0] = P.dmas("gpsimd", [
                    (oh_sp.ap()[pc * 128:(pc + 1) * 128, g0:g0 + gw], o_hi[:, 0:gw]),
                    (ol_sp.ap()[pc * 128:(pc + 1) * 128, g0:g0 + gw], o_lo[:, 0:gw])])
            pair_free_pe[u] = s["pe"].n
        phase_barrier()
        P.flush()

    if "C" in phases:
      with ExitStack() as ctx:
        wo_hi_s = sb(ctx, "wo_hi_s", [128, NCH * E], BF16)
        wr_hi_s2 = sb(ctx, "wr_hi_s2", [128, NCH * E], BF16)
        wk_hi_s2 = sb(ctx, "wk_hi_s2", [128, NCH * E], BF16)
        wv_hi_s2 = sb(ctx, "wv_hi_s2", [128, NCH * E], BF16)
        with ExitStack() as wctx:
            wtmp = sb(wctx, "p2_wtmp", [128, E], F32)
            lotmp = sb(wctx, "p2_lotmp", [128, E], BF16)
            emit_weight_prep(wout.ap(), E, wo_hi_s, wo_lo, None, wtmp, lotmp)
        P.dmas("sync", [
            (hs[:, :].rearrange("p (k m) -> p k m", k=NCH),
             _ap3(hsp, 0, [[E, 128], [128 * E, NCH], [1, E]]))
            for hs, hsp in ((wr_hi_s2, wr_hi_sp), (wk_hi_s2, wk_hi_sp),
                            (wv_hi_s2, wv_hi_sp))])
        phase_barrier()

        oht = sb(ctx, "c_oh", [128, NCH * TOK], BF16)
        olt = sb(ctx, "c_ol", [128, NCH * TOK], BF16)
        r1ct = sb(ctx, "c_r1", [128, NCH * TOK], F32)
        r2t = sb(ctx, "c_r2", [128, NCH * TOK], F32)
        xsq = sb(ctx, "c_xsq", [128, 1024], F32)
        st3 = mk_stats(ctx, "c_st3", NSUB)
        bci = sb(ctx, "c_bci", [128, TOK], F32)
        bcn = sb(ctx, "c_bcn", [128, TOK], F32)
        y_ext = sb(ctx, "c_yext", [128, TOK + 1], F32)
        ytmp = sb(ctx, "c_ytmp", [128, TOK], F32)
        dbuf = sb(ctx, "c_dbuf", [128, TOK], F32)
        xf = sb(ctx, "c_xf", [128, TOK], F32)
        ycar = sb(ctx, "c_ycar", [128, NCH], F32)
        x3car = sb(ctx, "c_x3car", [128, NCH], F32)
        xr_hi = sb(ctx, "c_xr_hi", [128, NCH * TOK], BF16)
        xr_lo = sb(ctx, "c_xr_lo", [128, NCH * TOK], BF16)
        xk_hi = sb(ctx, "c_xk_hi", [128, NCH * TOK], BF16)
        xk_lo = sb(ctx, "c_xk_lo", [128, NCH * TOK], BF16)
        ka_hi = sb(ctx, "c_ka_hi", [128, NCH * TOK], BF16)
        ka_lo = sb(ctx, "c_ka_lo", [128, NCH * TOK], BF16)
        r_sig = sb(ctx, "c_rsig", [128, NCH * TOK], F32)
        kf = sb(ctx, "c_kf", [128, TOK], F32)
        of = sb(ctx, "c_of", [128, 2 * TOK], F32)
        wlo_ring = sb(ctx, "c_wlo", [128, 2 * NCH * 128], BF16)
        wolo_ring = wlo_ring
        pb = [ps(ctx, "c_pb0", [128, TOK], F32), ps(ctx, "c_pb1", [128, TOK], F32)]
        bci_ps = ps(ctx, "c_bci_ps", [128, TOK], F32)
        bcn_ps = ps(ctx, "c_bcn_ps", [128, TOK], F32)
        pt_ps = ps(ctx, "c_pt_ps", [128, 1024], F32)
        for t in (pb[0], pb[1], bci_ps, bcn_ps, pt_ps):
            P.op("vector", lambda e, t=t: e.memset(t[:, :], 0.0))
        P.op("vector", lambda e: e.memset(ycar[:, :], 0.0))
        P.op("vector", lambda e: e.memset(x3car[:, :], 0.0))
        P.op("vector", lambda e: e.engine_nop(), sem=s["dve"])

        wlo_sps = {"wr": wr_lo, "wk": wk_lo, "wv": wv_lo}
        ring_state = {"i": 0, "loaded": {}}
        wo_ring = {"i": 0, "loaded": {}}
        out_toks = [None, None]

        def lo_load(proj, o):
            slot = ring_state["i"] % 2
            ring_state["i"] += 1
            base = slot * NCH * 128
            src = _ap3(wlo_sps[proj], o * NCH * 128 * 128,
                       [[NCH * 128, 128], [128, NCH], [1, 128]])
            P.wait("sync", s["pe"], val=ring_state["loaded"].get(slot, 0))
            t = P.dmas("sync", [(wlo_ring[:, base:base + NCH * 128], src)])
            P.wait_tok("tensor", t)
            ring_state["loaded"][slot] = s["pe"].n + 1
            return wlo_ring[:, base:base + NCH * 128]

        def wo_lo_load(o):
            slot = ring_state["i"] % 2
            ring_state["i"] += 1
            base = slot * NCH * 128
            src = _ap3(wo_lo, o * NCH * 128 * 128,
                       [[NCH * 128, 128], [128, NCH], [1, 128]])
            P.wait("sync", s["pe"], val=ring_state["loaded"].get(slot, 0))
            t = P.dmas("sync", [(wolo_ring[:, base:base + NCH * 128], src)])
            P.wait_tok("tensor", t)
            ring_state["loaded"][slot] = s["pe"].n + 1
            return wolo_ring[:, base:base + NCH * 128]

        wdict = {"wr_hi": wr_hi_s2, "wk_hi": wk_hi_s2, "wv_hi": wv_hi_s2,
                 "lo_load": lo_load}
        bufs = {"xr_hi": xr_hi, "xr_lo": xr_lo, "xk_hi": xk_hi, "xk_lo": xk_lo,
                "ka_hi": ka_hi, "ka_lo": ka_lo, "r_sig": r_sig, "kf": kf,
                "of": of, "pb": pb, "y_ext": y_ext, "ytmp": ytmp, "dbuf": dbuf,
                "xf": xf, "ycar": ycar, "bci": bci, "bcn": bcn,
                "bank_free_dve": [0, 0], "bank_free_act": [0, 0],
                "of_free": [0, 0]}

        def tile_wout_ln3(oc0, tokw, r1_off):
            T = tokw
            nsub = T // 128
            P.wait("sync", s["dve"])
            P.wait("sync", s["pe"])
            t_c = P.dmas("sync", [
                (oht[:, 0:NCH * T].rearrange("p (c t) -> p c t", c=NCH),
                 _ap3(oh_sp, oc0, [[QTOT, 128], [128 * QTOT, NCH], [1, T]])),
                (olt[:, 0:NCH * T].rearrange("p (c t) -> p c t", c=NCH),
                 _ap3(ol_sp, oc0, [[QTOT, 128], [128 * QTOT, NCH], [1, T]])),
                (r1ct[:, 0:NCH * T].rearrange("p (c t) -> p c t", c=NCH),
                 _ap3(r1_sp, r1_off, [[S, 128], [128 * S, NCH], [1, T]]))])
            P.wait_tok("tensor", t_c)
            P.wait_tok("vector", t_c)
            for o in range(NCH):
                lo_ap = wo_lo_load(o)
                bank = pb[o % 2]
                P.wait("tensor", s["dve"], val=bufs["bank_free_dve"][o % 2])
                mms = ([(wo_hi_s[:, k * E + o * 128: k * E + o * 128 + 128],
                         oht[:, k * T:(k + 1) * T]) for k in range(NCH)] +
                       [(wo_hi_s[:, k * E + o * 128: k * E + o * 128 + 128],
                         olt[:, k * T:(k + 1) * T]) for k in range(NCH)] +
                       [(lo_ap[:, k * 128:(k + 1) * 128],
                         oht[:, k * T:(k + 1) * T]) for k in range(NCH)])
                n = len(mms)
                for i, (wap, xap) in enumerate(mms):
                    P.op("tensor",
                         lambda e, o_=bank[:, 0:T], w_=wap, x_=xap,
                                st=(i == 0), sp2=(i == n - 1):
                         e.matmul(o_, w_, x_, start=st, stop=sp2),
                         sem=(s["pe"] if i == n - 1 else None))
                P.wait("vector", s["pe"])
                P.op("vector", lambda e, o=o, T=T, b_=bank: e.scalar_tensor_tensor(
                    r2t[:, o * T:(o + 1) * T], b_[:, 0:T], bout_s[:, o:o + 1],
                    r1ct[:, o * T:(o + 1) * T], ALU.add, ALU.add), sem=s["dve"])
                bufs["bank_free_dve"][o % 2] = s["dve"].n
            for sub in range(nsub):
                P.wait("tensor", s["dve"])
                P.wait("tensor", s["act"])
                for c in range(NCH):
                    P.op("tensor", lambda e, o=pt_ps[:, c * 128:(c + 1) * 128],
                                i=r2t[:, c * T + sub * 128: c * T + sub * 128 + 128]:
                         e.transpose(o, i, id_f32_s[:, :]),
                         sem=(s["pe"] if c == NCH - 1 else None))
                P.wait("scalar", s["pe"])
                P.wait("scalar", s["dve"])
                P.op("scalar", lambda e: e.activation(
                    xsq[:, :], pt_ps[:, 0:1024], AF.Identity, bias=0.0, scale=1.0),
                    sem=s["act"])
                P.wait("vector", s["act"])
                P.op("vector", lambda e, o=st3["s1"][:, sub:sub + 1]:
                     e.reduce_sum(o, xsq[:, :], axis=AXX), sem=s["dve"])
                P.wait("scalar", s["dve"])
                P.op("scalar", lambda e: e.activation(
                    xsq[:, :], pt_ps[:, 0:1024], AF.Square, bias=0.0, scale=1.0),
                    sem=s["act"])
                P.wait("vector", s["act"])
                P.op("vector", lambda e, o=st3["s2"][:, sub:sub + 1]:
                     e.reduce_sum(o, xsq[:, :], axis=AXX), sem=s["dve"])
            ln_stats_chain(st3, nsub)
            bc_tiles(st3, nsub, bci_ps, bcn_ps)
            P.wait("vector", s["pe"])
            P.op("vector", lambda e, T=T: e.tensor_copy(bci[:, 0:T], bci_ps[:, 0:T]))
            P.op("vector", lambda e, T=T: e.tensor_copy(bcn[:, 0:T], bcn_ps[:, 0:T]),
                 sem=s["dve"])

        c_tiles = phases.get("C")
        if c_tiles != 0 and "NOXT" not in phases:
            tile_wout_ln3(OWN, XQW, XQ0)
        lc = XQW - 1
        for c in range(NCH if (c_tiles != 0 and "NOXT" not in phases) else 0):
            P.op("vector", lambda e, c=c: e.tensor_tensor(
                ytmp[:, 0:1], r2t[:, c * XQW + lc: c * XQW + lc + 1],
                bci[:, lc:lc + 1], ALU.mult))
            P.drain("vector")
            P.op("vector", lambda e, c=c: e.tensor_tensor(
                ytmp[:, 1:2], ytmp[:, 0:1], bcn[:, lc:lc + 1], ALU.add))
            P.drain("vector")
            P.op("vector", lambda e, c=c: e.tensor_tensor(
                x3car[:, c:c + 1], ytmp[:, 1:2], m0x_s[:, 0:1], ALU.mult),
                sem=s["dve"])

        NTC = OWN // TOK if c_tiles is None else min(c_tiles, OWN // TOK)
        for j in range(NTC):
            T0 = j * TOK
            tile_wout_ln3(T0, TOK, T0)
            if dbg and j == 0:
                P.wait("gpsimd", s["dve"])
                P.dmas("gpsimd", [(
                    _ap3(dbg_r2, 0, [[TOK, 128], [128 * TOK, NCH], [1, TOK]]),
                    r2t[:, :].rearrange("p (c t) -> p c t", c=NCH))])

            def carry_init(c, j=j):
                src = x3car if (j == 0 and "NOXT" not in phases) else ycar
                P.op("vector", lambda e, c=c, sr=src: e.tensor_copy(
                    y_ext[:, 0:1], sr[:, c:c + 1]))

            emit_mix(bufs, lambda c: r2t[:, c * TOK:(c + 1) * TOK], TOK,
                     carry_init, pre_normalized=False)
            P.wait("tensor", s["dve"])

            def out_store(o, src_ap, T0=T0):
                P.wait("gpsimd", s["dve"])
                out_toks[o % 2] = P.dmas("gpsimd", [
                    (out_fm.ap()[o * 128:(o + 1) * 128, T0:T0 + TOK], src_ap)])

            emit_ffn(bufs, wdict, TOK,
                     lambda o: r2t[:, o * TOK:(o + 1) * TOK], out_store, out_toks)

        P.wait_all_dma("gpsimd")
        P.wait_all_dma("sync")
        P.flush()

    P.flush()
    return P



_QKV_PERM = np.concatenate([
    np.concatenate([np.arange(h * 192 + j * 64, h * 192 + j * 64 + 64)
                    for h in range(H)])
    for j in range(3)])


def _col8(v):
    return np.ascontiguousarray(np.asarray(v, np.float32).reshape(NCH, 128).T)


def _core_inputs(shared, x, b, p):
    q0g = p * OWN
    xb = np.asarray(x[b], np.float32)
    xr = np.roll(xb, -q0g, axis=0)
    xprev = xb[q0g - 1:q0g] if p == 1 else np.zeros((1, E), np.float32)
    d = {
        "x_fm": np.ascontiguousarray(xr.T),
        "x_tm": np.ascontiguousarray(xr),
        "xprev": np.ascontiguousarray(xprev),
        "m2048": np.full((128, 1), 1.0 - p, np.float32),
        "m0x": np.full((128, 1), float(p), np.float32),
    }
    d.update(shared)
    return d


def kernel(L0_attn_x, L0_attn_Wqkv, L0_attn_bqkv, L0_attn_Wout, L0_attn_bout,
           L0_ffn_W_r, L0_ffn_W_k, L0_ffn_W_v, L0_ffn_mu_r, L0_ffn_mu_k):
    x = np.asarray(L0_attn_x, np.float32)
    shared = {
        "wqkv": np.ascontiguousarray(np.asarray(L0_attn_Wqkv, np.float32)[:, _QKV_PERM]),
        "bqkv": np.ascontiguousarray(
            np.asarray(L0_attn_bqkv, np.float32)[_QKV_PERM].reshape(24, 128).T),
        "wout": np.ascontiguousarray(np.asarray(L0_attn_Wout, np.float32)),
        "bout": _col8(L0_attn_bout),
        "wrT": np.ascontiguousarray(np.asarray(L0_ffn_W_r, np.float32).T),
        "wkT": np.ascontiguousarray(np.asarray(L0_ffn_W_k, np.float32).T),
        "wvT": np.ascontiguousarray(np.asarray(L0_ffn_W_v, np.float32).T),
        "mu_r": _col8(L0_ffn_mu_r),
        "mu_k": _col8(L0_ffn_mu_k),
    }

    nc = bass.Bass("TRN2", target_bir_lowering=False, detect_race_conditions=False)
    build_program(nc)

    in_maps = [_core_inputs(shared, x, c // 2, c % 2) for c in range(8)]
    res = run_bass_kernel_spmd(nc, in_maps, core_ids=list(range(8)))
    out = np.empty((B, S, E), np.float32)
    for c in range(8):
        b, p = c // 2, c % 2
        out[b, p * OWN:(p + 1) * OWN, :] = np.asarray(res.results[c]["out_fm"]).T
    return out


# version 05
# speedup vs baseline: 1.1004x; 1.0321x over previous
import numpy as np
import ml_dtypes
import concourse.bass as bass
import concourse.mybir as mybir
from concourse.bass_utils import run_bass_kernel_spmd

F32 = mybir.dt.float32
BF16 = mybir.dt.bfloat16
AF = mybir.ActivationFunctionType
ALU = mybir.AluOpType
AXX = mybir.AxisListType.X

B, S, E, H, HD = 4, 4096, 1024, 16, 64
NCH = E // 128
TOK = 512
TOKA = 256
NTA = S // TOKA
NSUBA = TOKA // 128
NSLOTA = 4
NT = S // TOK
NSUB = TOK // 128
EPS = 1e-5
OWN = 2048
XQ0, XQW = 3968, 128
QTOT = OWN + XQW
NKC = S // 128
M3 = 3 * E

BF16NP = ml_dtypes.bfloat16


class _Sem:
    def __init__(self, nc, name):
        self.h = nc.alloc_semaphore(name=name)
        self.n = 0


class PB:
    ENGS = ("sync", "gpsimd", "scalar", "vector", "tensor")

    def __init__(self, nc):
        self.nc = nc
        self.streams = {k: [] for k in self.ENGS}
        self._waited = {k: {} for k in self.ENGS}
        self._pools = {}

    def sem(self, name):
        return _Sem(self.nc, name)

    def _pool(self, eng):
        if eng not in self._pools:
            self._pools[eng] = {"sems": [self.sem(f"{eng}_dq{i}") for i in range(12)],
                                "i": 0}
        return self._pools[eng]

    def dmas(self, eng, pairs, slow=False):
        pool = self._pool(eng)
        sem = pool["sems"][pool["i"] % len(pool["sems"])]
        pool["i"] += 1
        if sem.n > 0:
            self.wait(eng, sem)
        for (dst, srcap) in pairs:
            self.op(eng, (lambda e, d=dst, s_=srcap, sl=slow:
                          e.dma_start(d, s_, allow_slow_non_contiguous=sl)),
                    sem=sem, k=16)
        return (sem, sem.n)

    def drain(self, eng):
        self.streams[eng].append(lambda e: e.drain())

    def wait_tok(self, eng, tok):
        if tok is not None:
            self.wait(eng, tok[0], tok[1])

    def wait_all_dma(self, eng_waiter):
        for qeng in list(self._pools):
            for sem in self._pools[qeng]["sems"]:
                self.wait(eng_waiter, sem)

    def op(self, eng, fn, sem=None, k=1):
        h = None
        if sem is not None:
            sem.n += k
            h = sem.h
        def run(e, fn=fn, h=h, k=k):
            inst = fn(e)
            if h is not None:
                inst.then_inc(h, k)
        self.streams[eng].append(run)

    def wait(self, eng, sem, val=None):
        v = sem.n if val is None else val
        if v <= 0:
            return
        if v <= self._waited[eng].get(id(sem), -1):
            return
        self._waited[eng][id(sem)] = v
        self.streams[eng].append(lambda e, h=sem.h, v=v: e.wait_ge(h, v))

    def emit(self, blk):
        for name in self.ENGS:
            ops = self.streams[name]
            if not ops:
                continue
            def body(e, ops=ops):
                for f in ops:
                    f(e)
            getattr(blk, name)(body)
        self.streams = {k: [] for k in self.ENGS}

    def flush(self):
        if any(self.streams.values()):
            with self.nc.Block() as blk:
                self.emit(blk)


def _ap3(handle, offset, dims):
    return bass.AP(handle, offset, [list(d) for d in dims])


def build_program(nc, dbg=False, phases=("A1", "A2", "B", "C")):
    phases = {p.split(":")[0]: (int(p.split(":")[1]) if ":" in p else None)
              for p in phases}
    spill_kind = "ExternalOutput" if dbg else "Internal"

    d_in = lambda n, shp, dt=F32: nc.dram_tensor(n, shp, dt, kind="ExternalInput")
    x_fm = d_in("x_fm", [E, S])
    x_tm = d_in("x_tm", [S, E])
    xprev = d_in("xprev", [1, E])
    m2048 = d_in("m2048", [128, 1])
    m0x = d_in("m0x", [128, 1])
    mu_r = d_in("mu_r", [128, NCH])
    mu_k = d_in("mu_k", [128, NCH])
    wrT = d_in("wrT", [E, E])
    wkT = d_in("wkT", [E, E])
    wvT = d_in("wvT", [E, E])
    wqkv = d_in("wqkv", [E, M3])
    bqkv = d_in("bqkv", [128, 24])
    wout = d_in("wout", [E, E])
    bout = d_in("bout", [128, NCH])
    out_fm = nc.dram_tensor("out_fm", [E, OWN], F32, kind="ExternalOutput")

    mk_sp = lambda n, shp, dt: nc.dram_tensor(n, shp, dt, kind=spill_kind)
    r1_sp = mk_sp("r1_sp", [E, S], F32)
    x2h_sp = mk_sp("x2h_sp", [E, S], BF16)
    x2l_sp = mk_sp("x2l_sp", [E, S], BF16)
    qh_sp = mk_sp("qh_sp", [E, S], BF16)
    ql_sp = mk_sp("ql_sp", [E, S], BF16)
    kh_sp = mk_sp("kh_sp", [E, S], BF16)
    kl_sp = mk_sp("kl_sp", [E, S], BF16)
    vth_sp = mk_sp("vth_sp", [H, S, HD], BF16)
    vtl_sp = mk_sp("vtl_sp", [H, S, HD], BF16)
    oh_sp = mk_sp("oh_sp", [E, QTOT], BF16)
    ol_sp = mk_sp("ol_sp", [E, QTOT], BF16)
    yprow_sp = mk_sp("yprow_sp", [1, E], F32)
    dbg_bci = mk_sp("dbg_bci", [128, TOK], F32)
    dbg_bcn = mk_sp("dbg_bcn", [128, TOK], F32)
    dbg_st = mk_sp("dbg_st", [128, 16 * NSUB], F32)
    dbg_xtm = mk_sp("dbg_xtm", [128, NSUB * 1024], F32)
    dbg_y = mk_sp("dbg_y", [128, TOK + 1], F32)
    dbg_r2 = mk_sp("dbg_r2", [E, TOK], F32)
    wr_lo = mk_sp("wr_lo", [E, E], BF16)
    wk_lo = mk_sp("wk_lo", [E, E], BF16)
    wv_lo = mk_sp("wv_lo", [E, E], BF16)
    wr_hi_sp = mk_sp("wr_hi_sp", [E, E], BF16)
    wk_hi_sp = mk_sp("wk_hi_sp", [E, E], BF16)
    wv_hi_sp = mk_sp("wv_hi_sp", [E, E], BF16)
    wq_lo = mk_sp("wq_lo", [E, M3], BF16)
    wo_lo = mk_sp("wo_lo", [E, E], BF16)

    id_f32 = nc.inline_tensor(np.eye(128, dtype=np.float32), name="id_f32")
    id_bf = nc.inline_tensor(np.eye(128, dtype=BF16NP), name="id_bf")
    ones_bf = nc.inline_tensor(np.ones((128, 1), dtype=BF16NP), name="ones_bf")
    dmask_np = np.zeros((128, 128), dtype=np.float32)
    dmask_np[0, 0:64] = dmask_np[32, 0:64] = 1.0
    dmask_np[64, 64:128] = dmask_np[96, 64:128] = 1.0
    dmask = nc.inline_tensor(dmask_np, name="dmask")

    P = PB(nc)
    s = {k: P.sem(k) for k in ("ld", "sp", "dve", "act", "pe", "bar", "out")}

    from contextlib import ExitStack

    def sb(ctx, name, shape, dt):
        return ctx.enter_context(nc.sbuf_tensor(name, shape, dt))

    def ps(ctx, name, shape, dt=F32):
        return ctx.enter_context(nc.psum_tensor(name, shape, dt))

    def phase_barrier():
        for eng in PB.ENGS:
            P.wait_all_dma(eng)
            for k in ("dve", "act", "pe"):
                P.wait(eng, s[k])

    id_f32_s = nc.alloc_sbuf_tensor("id_f32_s", [128, 128], F32)
    id_bf_s = nc.alloc_sbuf_tensor("id_bf_s", [128, 128], BF16)
    ones_bf_s = nc.alloc_sbuf_tensor("ones_bf_s", [128, 1], BF16)
    dmask_s = nc.alloc_sbuf_tensor("dmask_s", [128, 128], F32)
    mur_s = nc.alloc_sbuf_tensor("mur_s", [128, NCH], F32)
    muk_s = nc.alloc_sbuf_tensor("muk_s", [128, NCH], F32)
    bqkv_s = nc.alloc_sbuf_tensor("bqkv_s", [128, 24], F32)
    bout_s = nc.alloc_sbuf_tensor("bout_s", [128, NCH], F32)
    m2048_s = nc.alloc_sbuf_tensor("m2048_s", [128, 1], F32)
    m0x_s = nc.alloc_sbuf_tensor("m0x_s", [128, 1], F32)
    ypcol_s = nc.alloc_sbuf_tensor("ypcol_s", [128, NCH], F32)

    const_tok = P.dmas("sync", [
        (dst[:, :], src.ap()[:, :]) for dst, src in
        ((id_f32_s, id_f32), (id_bf_s, id_bf), (ones_bf_s, ones_bf),
         (dmask_s, dmask), (mur_s, mu_r), (muk_s, mu_k), (bqkv_s, bqkv),
         (bout_s, bout), (m2048_s, m2048), (m0x_s, m0x))])
    for eng in ("tensor", "scalar", "vector", "gpsimd"):
        P.wait_tok(eng, const_tok)

    def emit_yprev(xprow_s, yprow_s, pscal_s):
        t = P.dmas("sync", [(xprow_s[:, :], xprev.ap()[:, :])])
        P.wait_tok("scalar", t)
        P.wait_tok("vector", t)
        P.op("scalar", lambda e: e.activation(yprow_s[:, :], xprow_s[:, :],
                                              AF.Square, bias=0.0, scale=1.0), sem=s["act"])
        P.wait("vector", s["act"])
        P.op("vector", lambda e: e.reduce_sum(pscal_s[:, 0:1], xprow_s[:, :], axis=AXX))
        P.op("vector", lambda e: e.reduce_sum(pscal_s[:, 1:2], yprow_s[:, :], axis=AXX))
        P.drain("vector")
        P.op("vector", lambda e: e.tensor_scalar_mul(pscal_s[:, 2:3], pscal_s[:, 0:1], 1.0 / E))
        P.op("vector", lambda e: e.tensor_scalar_mul(pscal_s[:, 3:4], pscal_s[:, 1:2], 1.0 / E))
        P.drain("vector")
        P.op("vector", lambda e: e.tensor_mul(pscal_s[:, 4:5], pscal_s[:, 2:3], pscal_s[:, 2:3]))
        P.drain("vector")
        P.op("vector", lambda e: e.tensor_sub(pscal_s[:, 5:6], pscal_s[:, 3:4], pscal_s[:, 4:5]))
        P.drain("vector")
        P.op("vector", lambda e: e.tensor_scalar_add(pscal_s[:, 5:6], pscal_s[:, 5:6], EPS),
             sem=s["dve"])
        P.wait("scalar", s["dve"])
        P.op("scalar", lambda e: e.activation(pscal_s[:, 6:7], pscal_s[:, 5:6],
                                              AF.Sqrt, bias=0.0, scale=1.0), sem=s["act"])
        P.wait("vector", s["act"])
        P.op("vector", lambda e: e.reciprocal(pscal_s[:, 6:7], pscal_s[:, 6:7]))
        P.op("vector", lambda e: e.tensor_scalar_mul(pscal_s[:, 7:8], pscal_s[:, 2:3], -1.0))
        P.drain("vector")
        P.op("vector", lambda e: e.tensor_scalar(
            yprow_s[:, :], xprow_s[:, :], pscal_s[:, 7:8], pscal_s[:, 6:7],
            ALU.add, ALU.mult), sem=s["dve"])
        P.wait("gpsimd", s["dve"])
        t2 = P.dmas("gpsimd", [(yprow_sp.ap()[:, :], yprow_s[:, :])])
        P.wait_tok("sync", t2)
        return P.dmas("sync", [(ypcol_s[:, :],
                                _ap3(yprow_sp, 0, [[1, 128], [128, NCH]]))],
                      slow=True)



    def emit_weight_prep(w_dram_ap, m, hi_sb, lo_sp_h, hi_sp_h, tmp, lotmp,
                         prev_tok=None):
        ncho = m // 128
        sp_tok = prev_tok
        for k in range(NCH):
            P.wait("sync", s["dve"])
            t = P.dmas("sync", [(tmp[:, 0:m], w_dram_ap[k * 128:(k + 1) * 128, :])])
            P.wait_tok("scalar", t)
            hi_slice = hi_sb[:, k * m:(k + 1) * m]
            P.op("scalar", lambda e, o=hi_slice, i=tmp[:, 0:m]:
                 e.activation(o, i, AF.Identity, bias=0.0, scale=1.0), sem=s["act"])
            P.wait("vector", s["act"])
            P.wait_tok("vector", sp_tok)
            P.wait_tok("vector", t)
            P.op("vector", lambda e, o=lotmp[:, 0:m], i=tmp[:, 0:m], h=hi_slice:
                 e.tensor_tensor(o, i, h, ALU.subtract), sem=s["dve"])
            P.wait("gpsimd", s["dve"])
            grp = [(_ap3(lo_sp_h, k * 128,
                         [[NCH * 128, 128], [128 * NCH * 128, ncho], [1, 128]]),
                    lotmp[:, 0:m].rearrange("p (o c) -> p o c", o=ncho))]
            if hi_sp_h is not None:
                P.wait("gpsimd", s["act"])
                grp.append((hi_sp_h[k * 128:(k + 1) * 128, :], hi_slice))
            sp_tok = P.dmas("gpsimd", grp)
        return sp_tok

    def mk_stats(ctx, tag, nsub):
        names = ("s1", "s2", "mean", "ex2", "m2", "var", "sd", "inv", "nm")
        return {n: sb(ctx, f"{tag}_{n}", [128, nsub], F32) for n in names}

    def ln_stats_chain(st, nsub):
        P.drain("vector")
        P.op("vector", lambda e: e.tensor_scalar_mul(st["mean"][:, 0:nsub], st["s1"][:, 0:nsub], 1.0 / E))
        P.op("vector", lambda e: e.tensor_scalar_mul(st["ex2"][:, 0:nsub], st["s2"][:, 0:nsub], 1.0 / E))
        P.drain("vector")
        P.op("vector", lambda e: e.tensor_mul(st["m2"][:, 0:nsub], st["mean"][:, 0:nsub], st["mean"][:, 0:nsub]))
        P.drain("vector")
        P.op("vector", lambda e: e.tensor_sub(st["var"][:, 0:nsub], st["ex2"][:, 0:nsub], st["m2"][:, 0:nsub]))
        P.drain("vector")
        P.op("vector", lambda e: e.tensor_scalar_add(st["var"][:, 0:nsub], st["var"][:, 0:nsub], EPS),
             sem=s["dve"])
        P.wait("scalar", s["dve"])
        P.op("scalar", lambda e: e.activation(st["sd"][:, 0:nsub], st["var"][:, 0:nsub],
                                              AF.Sqrt, bias=0.0, scale=1.0), sem=s["act"])
        P.wait("vector", s["act"])
        P.op("vector", lambda e: e.reciprocal(st["inv"][:, 0:nsub], st["sd"][:, 0:nsub]))
        P.drain("vector")
        P.op("vector", lambda e: e.tensor_mul(st["nm"][:, 0:nsub], st["mean"][:, 0:nsub], st["inv"][:, 0:nsub]))
        P.drain("vector")
        P.op("vector", lambda e: e.tensor_scalar_mul(st["nm"][:, 0:nsub], st["nm"][:, 0:nsub], -1.0),
             sem=s["dve"])

    def stat_sub(st, src, xsq, sub):
        P.op("vector", lambda e, o=xsq[:, :], i=src:
             e.tensor_tensor(o, i, i, ALU.mult))
        P.op("vector", lambda e, o=st["s1"][:, sub:sub + 1], i=src:
             e.reduce_sum(o, i, axis=AXX))
        P.op("vector", lambda e, o=st["s2"][:, sub:sub + 1], i=xsq[:, :]:
             e.reduce_sum(o, i, axis=AXX), sem=s["dve"])

    def emit_stats_from_tm(st, src_list, xsq, nsub):
        for sub in range(nsub):
            stat_sub(st, src_list[sub], xsq, sub)
        ln_stats_chain(st, nsub)

    def bc_tiles(st, nsub, bci_ps, bcn_ps, dve_val=None):
        P.wait("tensor", s["dve"], val=dve_val)
        for sub in range(nsub):
            iap = st["inv"][:, sub:sub + 1].broadcast_to([128, 128])
            nap = st["nm"][:, sub:sub + 1].broadcast_to([128, 128])
            P.op("tensor", lambda e, o=bci_ps[:, sub * 128:(sub + 1) * 128], w=iap:
                 e.matmul(o, w, id_f32_s[:, :], start=True, stop=True))
            P.op("tensor", lambda e, o=bcn_ps[:, sub * 128:(sub + 1) * 128], w=nap:
                 e.matmul(o, w, id_f32_s[:, :], start=True, stop=True),
                 sem=(s["pe"] if sub == nsub - 1 else None))

    def emit_ffn(bufs, w, tokw, resid_ap_fn, out_fn, out_toks,
                 side=None, out_ap_fn=None):
        T = tokw
        xr_hi, xr_lo = bufs["xr_hi"], bufs["xr_lo"]
        xk_hi, xk_lo = bufs["xk_hi"], bufs["xk_lo"]
        ka_hi, ka_lo = bufs["ka_hi"], bufs["ka_lo"]
        r_sig, kf, of, pb = bufs["r_sig"], bufs["kf"], bufs["of"], bufs["pb"]
        side = list(side) if side else []
        kmark = {"dve": 0, "act": 0}

        def sweep(proj, src_hi, src_lo, evac):
            whi = w[proj + "_hi"]
            for o in range(NCH):
                lo_ap = w["lo_load"](proj, o)
                bank = pb[o % 2]
                P.wait("tensor", s["dve"], val=bufs["bank_free_dve"][o % 2])
                P.wait("tensor", s["act"], val=bufs["bank_free_act"][o % 2])
                mms = ([(whi[:, k * E + o * 128: k * E + o * 128 + 128],
                         src_hi[:, k * T:(k + 1) * T]) for k in range(NCH)] +
                       [(whi[:, k * E + o * 128: k * E + o * 128 + 128],
                         src_lo[:, k * T:(k + 1) * T]) for k in range(NCH)] +
                       [(lo_ap[:, k * 128:(k + 1) * 128],
                         src_hi[:, k * T:(k + 1) * T]) for k in range(NCH)])
                n = len(mms)
                for i, (wap, xap) in enumerate(mms):
                    P.op("tensor",
                         lambda e, o_=bank[:, 0:T], w_=wap, x_=xap,
                                st=(i == 0), sp2=(i == n - 1):
                         e.matmul(o_, w_, x_, start=st, stop=sp2),
                         sem=(s["pe"] if i == n - 1 else None))
                evac(o, bank)
                if side:
                    side.pop(0)()

        def evac_r(o, bank):
            P.wait("scalar", s["pe"])
            P.op("scalar", lambda e, o_=r_sig[:, o * T:(o + 1) * T], i=bank[:, 0:T]:
                 e.activation(o_, i, AF.Sigmoid, bias=0.0, scale=1.0), sem=s["act"])
            bufs["bank_free_act"][o % 2] = s["act"].n

        def evac_k(o, bank):
            P.wait("vector", s["pe"])
            P.op("vector", lambda e, o_=kf[:, 0:T], i=bank[:, 0:T]:
                 e.tensor_scalar_max(o_, i, 0.0), sem=s["dve"])
            bufs["bank_free_dve"][o % 2] = s["dve"].n
            xfb = bufs.get("kxf", bufs["xf"])
            P.op("vector", lambda e, o_=xfb[:, 0:T], i=kf[:, 0:T]:
                 e.tensor_tensor(o_, i, i, ALU.mult))
            P.op("vector", lambda e, o_=ka_hi[:, o * T:(o + 1) * T], i=xfb[:, 0:T]:
                 e.tensor_copy(o_, i))
            P.op("vector", lambda e, o_=ka_lo[:, o * T:(o + 1) * T], i=xfb[:, 0:T],
                        h=ka_hi[:, o * T:(o + 1) * T]:
                 e.tensor_tensor(o_, i, h, ALU.subtract), sem=s["dve"])
            kmark["dve"] = s["dve"].n
            kmark["act"] = s["act"].n

        def evac_v(o, bank):
            P.wait("vector", s["pe"])
            P.wait("vector", s["act"])
            slot = o % 2
            if out_ap_fn is None:
                of_ap = of[:, slot * T:(slot + 1) * T]
                if out_toks is not None:
                    P.wait_tok("vector", out_toks[slot])
            else:
                of_ap = out_ap_fn(o)
            P.op("vector", lambda e, o_=kf[:, 0:T], i=bank[:, 0:T],
                        r=r_sig[:, o * T:(o + 1) * T]:
                 e.tensor_tensor(o_, i, r, ALU.mult), sem=s["dve"])
            bufs["bank_free_dve"][o % 2] = s["dve"].n
            P.op("vector", lambda e, o_=of_ap, i=kf[:, 0:T], r=resid_ap_fn(o):
                 e.scalar_tensor_tensor(o_, i, 0.5, r, ALU.mult, ALU.add),
                 sem=s["dve"])
            if out_fn is not None:
                out_fn(o, of_ap)

        sweep("wr", xr_hi, xr_lo, evac_r)
        sweep("wk", xk_hi, xk_lo, evac_k)
        P.wait("tensor", s["dve"], val=kmark["dve"])
        P.wait("tensor", s["act"], val=kmark["act"])
        sweep("wv", ka_hi, ka_lo, evac_v)
        while side:
            side.pop(0)()

    def emit_mix_chunk(bufs, c, x_c, T, carry_init_fn, pre_normalized):
        y_ext, ytmp, dbuf, xf = bufs["y_ext"], bufs["ytmp"], bufs["dbuf"], bufs["xf"]
        carry_init_fn(c)
        if pre_normalized:
            P.op("vector", lambda e, x=x_c, T=T: e.tensor_copy(y_ext[:, 1:T + 1], x))
        else:
            P.op("vector", lambda e, x=x_c, T=T, b=bufs["bci"][:, 0:T]:
                 e.tensor_tensor(ytmp[:, 0:T], x, b, ALU.mult))
            P.op("vector", lambda e, T=T, b=bufs["bcn"][:, 0:T]:
                 e.tensor_tensor(y_ext[:, 1:T + 1], ytmp[:, 0:T], b, ALU.add))
        P.op("vector", lambda e, T=T: e.tensor_tensor(
            dbuf[:, 0:T], y_ext[:, 1:T + 1], y_ext[:, 0:T], ALU.subtract))
        for (mu, hi, lo) in ((mur_s, bufs["xr_hi"], bufs["xr_lo"]),
                             (muk_s, bufs["xk_hi"], bufs["xk_lo"])):
            P.op("vector", lambda e, T=T, m=mu[:, c:c + 1]:
                 e.scalar_tensor_tensor(xf[:, 0:T], dbuf[:, 0:T], m,
                                        y_ext[:, 0:T], ALU.mult, ALU.add))
            P.op("vector", lambda e, T=T, o=hi[:, c * T:(c + 1) * T]:
                 e.tensor_copy(o, xf[:, 0:T]))
            P.op("vector", lambda e, T=T, o=lo[:, c * T:(c + 1) * T],
                        h=hi[:, c * T:(c + 1) * T]:
                 e.tensor_tensor(o, xf[:, 0:T], h, ALU.subtract), sem=s["dve"])
        P.op("vector", lambda e, c=c, T=T: e.tensor_copy(
            bufs["ycar"][:, c:c + 1], y_ext[:, T:T + 1]), sem=s["dve"])

    def emit_mix(bufs, x_src_fn, tokw, carry_init_fn, pre_normalized):
        for c in range(NCH):
            emit_mix_chunk(bufs, c, x_src_fn(c), tokw, carry_init_fn,
                           pre_normalized)

    with ExitStack() as ctx:
        with ExitStack() as yctx:
            xprow_s = sb(yctx, "xprow_s", [1, E], F32)
            yprow_s = sb(yctx, "yprow_s", [1, E], F32)
            pscal_s = sb(yctx, "pscal_s", [1, 8], F32)
            ypcol_tok = emit_yprev(xprow_s, yprow_s, pscal_s)
        wtmp = sb(ctx, "p0_wtmp", [128, E], F32)
        lotmp = sb(ctx, "p0_lotmp", [128, E], BF16)
        wr_hi_s = sb(ctx, "wr_hi_s", [128, NCH * E], BF16)
        wk_hi_s = sb(ctx, "wk_hi_s", [128, NCH * E], BF16)
        wv_hi_s = sb(ctx, "wv_hi_s", [128, NCH * E], BF16)
        wp_tok = None
        for (wd, hs, lsp, hsp) in ((wrT, wr_hi_s, wr_lo, wr_hi_sp),
                                   (wkT, wk_hi_s, wk_lo, wk_hi_sp),
                                   (wvT, wv_hi_s, wv_lo, wv_hi_sp)):
            wp_tok = emit_weight_prep(wd.ap(), E, hs, lsp, hsp.ap(), wtmp,
                                      lotmp, prev_tok=wp_tok)
        phase_barrier()

        xs_b = [sb(ctx, f"a1_xs{i}", [128, NCH * TOKA], F32) for i in range(2)]
        xtm_b = [sb(ctx, f"a1_xtm{i}", [128, NSUBA * 1024], F32) for i in range(2)]
        xsq = sb(ctx, "a1_xsq", [128, 1024], F32)
        pt_sb = sb(ctx, "a1_ptsb", [128, NSUBA * 1024], F32)
        st1 = mk_stats(ctx, "a1_st1", NSUBA)
        st2 = mk_stats(ctx, "a1_st2", NSUBA)
        bci1_b = [sb(ctx, f"a1_bci{i}", [128, TOKA], F32) for i in range(2)]
        bcn1_b = [sb(ctx, f"a1_bcn{i}", [128, TOKA], F32) for i in range(2)]
        bci2 = sb(ctx, "a1_bci2", [128, TOKA], F32)
        bcn2 = sb(ctx, "a1_bcn2", [128, TOKA], F32)
        y_ext = sb(ctx, "a1_yext", [128, TOKA + 1], F32)
        ytmp = sb(ctx, "a1_ytmp", [128, TOKA], F32)
        dbuf = sb(ctx, "a1_dbuf", [128, TOKA], F32)
        xf = sb(ctx, "a1_xf", [128, TOKA], F32)
        kxf = sb(ctx, "a1_kxf", [128, TOKA], F32)
        ycar = sb(ctx, "a1_ycar", [128, NCH], F32)
        xr_hi_b = [sb(ctx, f"a1_xrh{i}", [128, NCH * TOKA], BF16) for i in range(2)]
        xr_lo_b = [sb(ctx, f"a1_xrl{i}", [128, NCH * TOKA], BF16) for i in range(2)]
        xk_hi_b = [sb(ctx, f"a1_xkh{i}", [128, NCH * TOKA], BF16) for i in range(2)]
        xk_lo_b = [sb(ctx, f"a1_xkl{i}", [128, NCH * TOKA], BF16) for i in range(2)]
        ka_hi = sb(ctx, "a1_ka_hi", [128, NCH * TOKA], BF16)
        ka_lo = sb(ctx, "a1_ka_lo", [128, NCH * TOKA], BF16)
        r_sig = sb(ctx, "a1_rsig", [128, NCH * TOKA], F32)
        kf = sb(ctx, "a1_kf", [128, TOKA], F32)
        of = sb(ctx, "a1_of", [128, 2 * TOKA], F32)
        r1t_b = [sb(ctx, f"a1_r1{i}", [128, NCH * TOKA], F32) for i in range(2)]
        wlo_ring = sb(ctx, "a1_wlo", [128, NSLOTA * NCH * 128], BF16)
        x2h = sb(ctx, "a1_x2h", [128, 2 * TOKA], BF16)
        x2l = sb(ctx, "a1_x2l", [128, 2 * TOKA], BF16)
        pb = [ps(ctx, "a1_pb0", [128, TOKA], F32), ps(ctx, "a1_pb1", [128, TOKA], F32)]
        bci_ps = ps(ctx, "a1_bci_ps", [128, TOKA], F32)
        bcn_ps = ps(ctx, "a1_bcn_ps", [128, TOKA], F32)
        pt_ps = ps(ctx, "a1_pt_ps", [128, 1024], F32)

        wlo_sps = {"wr": wr_lo, "wk": wk_lo, "wv": wv_lo}
        ring_state = {"i": 0, "loaded": {}}
        x2_free = [None, None]
        r1_sp_tok = [None, None]
        t_xs = [None, None]
        t_xtm = [None, None]
        ffn_pe_mark = [0, 0]
        mix_mark = [{"dve": 0, "act": 0}, {"dve": 0, "act": 0}]

        def lo_load(proj, o):
            slot = ring_state["i"] % NSLOTA
            ring_state["i"] += 1
            base = slot * NCH * 128
            src = _ap3(wlo_sps[proj], o * NCH * 128 * 128,
                       [[NCH * 128, 128], [128, NCH], [1, 128]])
            P.wait("sync", s["pe"], val=ring_state["loaded"].get(slot, 0))
            t = P.dmas("sync", [(wlo_ring[:, base:base + NCH * 128], src)])
            P.wait_tok("tensor", t)
            ring_state["loaded"][slot] = s["pe"].n + 1
            return wlo_ring[:, base:base + NCH * 128]

        wdict = {"wr_hi": wr_hi_s, "wk_hi": wk_hi_s, "wv_hi": wv_hi_s,
                 "lo_load": lo_load}
        bank_free_dve = [0, 0]
        bank_free_act = [0, 0]
        bufs_u = []
        for i in range(2):
            bufs_u.append({
                "xr_hi": xr_hi_b[i], "xr_lo": xr_lo_b[i],
                "xk_hi": xk_hi_b[i], "xk_lo": xk_lo_b[i],
                "ka_hi": ka_hi, "ka_lo": ka_lo, "r_sig": r_sig, "kf": kf,
                "kxf": kxf, "of": of, "pb": pb, "y_ext": y_ext, "ytmp": ytmp,
                "dbuf": dbuf, "xf": xf, "ycar": ycar,
                "bci": bci1_b[i], "bcn": bcn1_b[i],
                "bank_free_dve": bank_free_dve, "bank_free_act": bank_free_act})

        def emit_tile_loads(jj):
            u2 = jj % 2
            T0j = jj * TOKA
            P.wait("scalar", s["dve"])
            t_xtm[u2] = P.dmas("scalar", [
                (xtm_b[u2][:, :].rearrange("p (s f) -> p s f", s=NSUBA),
                 _ap3(x_tm, T0j * E, [[E, 128], [128 * E, NSUBA], [1, E]]))])
            t_xs[u2] = P.dmas("scalar", [
                (xs_b[u2][:, :].rearrange("p (c t) -> p c t", c=NCH),
                 _ap3(x_fm, T0j, [[S, 128], [128 * S, NCH], [1, TOKA]]))])

        def mk_carry(jj):
            def carry_init(c, jj=jj):
                if jj == 0:
                    P.op("vector", lambda e, c=c: e.tensor_copy(
                        y_ext[:, 0:1], ypcol_s[:, c:c + 1]))
                elif jj == OWN // TOKA:
                    P.op("vector", lambda e, c=c: e.tensor_tensor(
                        y_ext[:, 0:1], ycar[:, c:c + 1], m2048_s[:, 0:1],
                        ALU.mult))
                else:
                    P.op("vector", lambda e, c=c: e.tensor_copy(
                        y_ext[:, 0:1], ycar[:, c:c + 1]))
            return carry_init

        def prod_thunks(jj):
            u2 = jj % 2
            th = []
            for sub in range(NSUBA):
                def t_stat(sub=sub, u2=u2):
                    P.wait_tok("scalar", t_xtm[u2])
                    P.wait_tok("vector", t_xtm[u2])
                    stat_sub(st1, xtm_b[u2][:, sub * 1024:(sub + 1) * 1024],
                             xsq, sub)
                th.append(t_stat)
            th.append(lambda: ln_stats_chain(st1, NSUBA))

            def t_bc1(u2=u2):
                bc_tiles(st1, NSUBA, bci_ps, bcn_ps)
                P.wait("vector", s["pe"])
                P.op("vector", lambda e: e.tensor_copy(bci1_b[u2][:, :],
                                                       bci_ps[:, 0:TOKA]))
                P.op("vector", lambda e: e.tensor_copy(bcn1_b[u2][:, :],
                                                       bcn_ps[:, 0:TOKA]),
                     sem=s["dve"])
                P.drain("vector")
            th.append(t_bc1)
            ci = mk_carry(jj)
            for c in range(NCH):
                def t_mix(c=c, u2=u2, ci=ci):
                    P.wait("vector", s["pe"], val=ffn_pe_mark[u2])
                    P.wait_tok("vector", t_xs[u2])
                    emit_mix_chunk(bufs_u[u2], c,
                                   xs_b[u2][:, c * TOKA:(c + 1) * TOKA],
                                   TOKA, ci, False)
                    if c == NCH - 1:
                        mix_mark[u2]["dve"] = s["dve"].n
                        mix_mark[u2]["act"] = s["act"].n
                th.append(t_mix)
            return th

        def ln2_stat_thunks(jj):
            th = []
            for sub in range(NSUBA):
                def t_l(sub=sub):
                    P.wait("vector", s["act"])
                    stat_sub(st2, pt_sb[:, sub * 1024:(sub + 1) * 1024],
                             xsq, sub)
                th.append(t_l)
            th.append(lambda: ln_stats_chain(st2, NSUBA))

            def t_bc2():
                bc_tiles(st2, NSUBA, bci_ps, bcn_ps)
                P.wait("vector", s["pe"])
                P.op("vector", lambda e: e.tensor_copy(bci2[:, :],
                                                       bci_ps[:, 0:TOKA]))
                P.op("vector", lambda e: e.tensor_copy(bcn2[:, :],
                                                       bcn_ps[:, 0:TOKA]),
                     sem=s["dve"])
                P.drain("vector")
            th.append(t_bc2)
            return th

        def ln2_split_thunks(jj):
            u = jj % 2
            T0j = jj * TOKA
            th = []
            for c in range(NCH):
                def t_sp(c=c, u=u, T0j=T0j):
                    slot = c % 2
                    sl = slice(slot * TOKA, (slot + 1) * TOKA)
                    P.op("vector", lambda e, c=c, u=u: e.tensor_tensor(
                        ytmp[:, 0:TOKA], r1t_b[u][:, c * TOKA:(c + 1) * TOKA],
                        bci2[:, :], ALU.mult))
                    P.op("vector", lambda e: e.tensor_tensor(
                        xf[:, 0:TOKA], ytmp[:, 0:TOKA], bcn2[:, :], ALU.add))
                    P.wait_tok("vector", x2_free[slot])
                    P.op("vector", lambda e, sl=sl: e.tensor_copy(
                        x2h[:, sl], xf[:, 0:TOKA]))
                    P.op("vector", lambda e, sl=sl: e.tensor_tensor(
                        x2l[:, sl], xf[:, 0:TOKA], x2h[:, sl], ALU.subtract),
                        sem=s["dve"])
                    P.wait("gpsimd", s["dve"])
                    x2_free[slot] = P.dmas("gpsimd", [
                        (x2h_sp.ap()[c * 128:(c + 1) * 128, T0j:T0j + TOKA],
                         x2h[:, sl]),
                        (x2l_sp.ap()[c * 128:(c + 1) * 128, T0j:T0j + TOKA],
                         x2l[:, sl])])
                th.append(t_sp)
            return th

        def emit_ln2_transposes(jj):
            u = jj % 2
            for sub in range(NSUBA):
                P.wait("tensor", s["dve"])
                P.wait("tensor", s["act"])
                for c in range(NCH):
                    P.op("tensor", lambda e, u=u,
                                o=pt_ps[:, c * 128:(c + 1) * 128],
                                i=r1t_b[u][:, c * TOKA + sub * 128:
                                            c * TOKA + sub * 128 + 128]:
                         e.transpose(o, i, id_f32_s[:, :]),
                         sem=(s["pe"] if c == NCH - 1 else None))
                P.wait("scalar", s["pe"])
                P.wait("scalar", s["dve"])
                P.op("scalar", lambda e, sub=sub: e.activation(
                    pt_sb[:, sub * 1024:(sub + 1) * 1024], pt_ps[:, 0:1024],
                    AF.Identity, bias=0.0, scale=1.0), sem=s["act"])

        NTA_run = NTA if phases.get("A1") in (None, 0) else min(phases["A1"], NTA)
        if "A1" in phases and NTA_run:
            emit_tile_loads(0)
            P.wait_tok("vector", ypcol_tok)
            for t in prod_thunks(0):
                t()
            for j in range(NTA_run):
                u = j % 2
                if j + 1 < NTA_run:
                    emit_tile_loads(j + 1)
                pt = prod_thunks(j + 1) if j + 1 < NTA_run else None
                lt = ln2_stat_thunks(j - 1) if j >= 1 else None
                sp = ln2_split_thunks(j - 1) if j >= 1 else None
                side = []
                if lt:
                    side += lt[:2]
                if pt:
                    side += pt[:2]
                if lt:
                    side.append(lt[2])
                if pt:
                    side.append(pt[2])
                if lt:
                    side.append(lt[3])
                if pt:
                    side.append(pt[3])
                for c in range(NCH):
                    if sp:
                        side.append(sp[c])
                    if pt:
                        side.append(pt[4 + c])
                P.wait("tensor", s["dve"], val=mix_mark[u]["dve"])
                P.wait("tensor", s["act"], val=mix_mark[u]["act"])
                P.wait_tok("vector", r1_sp_tok[u])
                emit_ffn(bufs_u[u], wdict, TOKA,
                         (lambda o, u=u: xs_b[u][:, o * TOKA:(o + 1) * TOKA]),
                         None, None, side=side,
                         out_ap_fn=(lambda o, u=u:
                                    r1t_b[u][:, o * TOKA:(o + 1) * TOKA]))
                ffn_pe_mark[u] = s["pe"].n
                emit_ln2_transposes(j)
                P.wait("gpsimd", s["dve"])
                r1_sp_tok[u] = P.dmas("gpsimd", [
                    (_ap3(r1_sp, j * TOKA, [[S, 128], [128 * S, NCH], [1, TOKA]]),
                     r1t_b[u][:, :].rearrange("p (c t) -> p c t", c=NCH))])
            for t in ln2_stat_thunks(NTA_run - 1) + ln2_split_thunks(NTA_run - 1):
                t()
        phase_barrier()
        P.flush()


    if "A2" in phases:
      with ExitStack() as ctx:
        wq_hi_s = sb(ctx, "wq_hi_s", [128, NCH * M3], BF16)
        with ExitStack() as wctx:
            wtmp = sb(wctx, "p1_wtmp", [128, M3], F32)
            lotmp = sb(wctx, "p1_lotmp", [128, M3], BF16)
            emit_weight_prep(wqkv.ap(), M3, wq_hi_s, wq_lo, None, wtmp, lotmp)
        phase_barrier()

        x2ht_b = [sb(ctx, f"a2_x2h{i}", [128, NCH * TOK], BF16) for i in range(2)]
        x2lt_b = [sb(ctx, f"a2_x2l{i}", [128, NCH * TOK], BF16) for i in range(2)]
        qf = sb(ctx, "a2_qf", [128, TOK], F32)
        sth = sb(ctx, "a2_sth", [128, TOK], BF16)
        stl = sb(ctx, "a2_stl", [128, TOK], BF16)
        vst = sb(ctx, "a2_vst", [128, 2 * 128], BF16)
        wlo_ring = sb(ctx, "a2_wlo", [128, 2 * NCH * 128], BF16)
        pb = [ps(ctx, "a2_pb0", [128, TOK], F32), ps(ctx, "a2_pb1", [128, TOK], F32)]
        ptb = [ps(ctx, "a2_pt0", [128, 128], BF16), ps(ctx, "a2_pt1", [128, 128], BF16)]

        ring_state = {"i": 0, "loaded": {}}

        def qkv_lo_load(o):
            slot = ring_state["i"] % 2
            ring_state["i"] += 1
            base = slot * NCH * 128
            src = _ap3(wq_lo, o * NCH * 128 * 128,
                       [[NCH * 128, 128], [128, NCH], [1, 128]])
            P.wait("sync", s["pe"], val=ring_state["loaded"].get(slot, 0))
            t = P.dmas("sync", [(wlo_ring[:, base:base + NCH * 128], src)])
            P.wait_tok("tensor", t)
            ring_state["loaded"][slot] = s["pe"].n + 1
            return wlo_ring[:, base:base + NCH * 128]

        bank_free = {"dve": [0, 0], "act": [0, 0]}
        ptk_free = [0, 0]
        vst_free = [None, None]
        qk_sp_tok = [None]

        tile_pe_free = [0, 0]
        tile_tok = [None, None]

        def a2_load(j):
            u = j % 2
            T0j = j * TOK
            P.wait("sync", s["pe"], val=tile_pe_free[u])
            return P.dmas("sync", [
                (x2ht_b[u][:, :].rearrange("p (c t) -> p c t", c=NCH),
                 _ap3(x2h_sp, T0j, [[S, 128], [128 * S, NCH], [1, TOK]])),
                (x2lt_b[u][:, :].rearrange("p (c t) -> p c t", c=NCH),
                 _ap3(x2l_sp, T0j, [[S, 128], [128 * S, NCH], [1, TOK]]))])

        tile_tok[0] = a2_load(0)
        for j in range(NT):
            T0 = j * TOK
            u = j % 2
            if j + 1 < NT:
                tile_tok[(j + 1) % 2] = a2_load(j + 1)
            x2ht, x2lt = x2ht_b[u], x2lt_b[u]
            P.wait_tok("tensor", tile_tok[u])
            for o in range(24):
                is_q = o < 8
                is_v = o >= 16
                if is_q:
                    if T0 < OWN:
                        cs, cw = 0, TOK
                    elif T0 + TOK == S:
                        cs, cw = XQ0 - T0, S - XQ0
                    else:
                        continue
                else:
                    cs, cw = 0, TOK
                lo_ap = qkv_lo_load(o)
                bank = pb[o % 2]
                P.wait("tensor", s["dve"], val=bank_free["dve"][o % 2])
                P.wait("tensor", s["act"], val=bank_free["act"][o % 2])
                mms = ([(wq_hi_s[:, k * M3 + o * 128: k * M3 + o * 128 + 128],
                         x2ht[:, k * TOK + cs: k * TOK + cs + cw]) for k in range(NCH)] +
                       [(wq_hi_s[:, k * M3 + o * 128: k * M3 + o * 128 + 128],
                         x2lt[:, k * TOK + cs: k * TOK + cs + cw]) for k in range(NCH)] +
                       [(lo_ap[:, k * 128:(k + 1) * 128],
                         x2ht[:, k * TOK + cs: k * TOK + cs + cw]) for k in range(NCH)])
                n = len(mms)
                for i, (wap, xap) in enumerate(mms):
                    P.op("tensor",
                         lambda e, o_=bank[:, 0:cw], w_=wap, x_=xap,
                                st=(i == 0), sp2=(i == n - 1):
                         e.matmul(o_, w_, x_, start=st, stop=sp2),
                         sem=(s["pe"] if i == n - 1 else None))
                P.wait("vector", s["pe"])
                P.wait_tok("vector", qk_sp_tok[0])
                if is_q:
                    P.op("vector", lambda e, o_=qf[:, 0:cw], i=bank[:, 0:cw],
                                bb=bqkv_s[:, o:o + 1]:
                         e.tensor_scalar(o_, i, bb, 0.125, ALU.add, ALU.mult),
                         sem=s["dve"])
                else:
                    P.op("vector", lambda e, o_=qf[:, 0:cw], i=bank[:, 0:cw],
                                bb=bqkv_s[:, o:o + 1]:
                         e.tensor_scalar(o_, i, bb, None, ALU.add), sem=s["dve"])
                bank_free["dve"][o % 2] = s["dve"].n
                P.wait("scalar", s["dve"])
                P.wait("scalar", s["pe"])
                P.wait_tok("scalar", qk_sp_tok[0])
                P.op("scalar", lambda e, o_=sth[:, 0:cw], i=qf[:, 0:cw]:
                     e.activation(o_, i, AF.Identity, bias=0.0, scale=1.0), sem=s["act"])
                bank_free["act"][o % 2] = s["act"].n
                P.wait("vector", s["act"])
                P.wait("vector", s["pe"])
                P.op("vector", lambda e, o_=stl[:, 0:cw], i=qf[:, 0:cw], h=sth[:, 0:cw]:
                     e.tensor_tensor(o_, i, h, ALU.subtract), sem=s["dve"])
                if not is_v:
                    sp_h = qh_sp if is_q else kh_sp
                    sp_l = ql_sp if is_q else kl_sp
                    row = (o if is_q else o - 8) * 128
                    P.wait("gpsimd", s["act"])
                    P.wait("gpsimd", s["dve"])
                    qk_sp_tok[0] = P.dmas("gpsimd", [
                        (sp_h.ap()[row:row + 128, T0 + cs:T0 + cs + cw], sth[:, 0:cw]),
                        (sp_l.ap()[row:row + 128, T0 + cs:T0 + cs + cw], stl[:, 0:cw])])
                else:
                    hA = (o - 16) * 2
                    P.wait("tensor", s["act"])
                    P.wait("tensor", s["dve"])
                    for sub in range(NSUB):
                        for hl, stt in ((0, sth), (1, stl)):
                            pslot = (sub * 2 + hl) % 2
                            ptk = ptb[pslot]
                            P.wait("tensor", s["dve"], val=ptk_free[pslot])
                            P.op("tensor", lambda e, o_=ptk[:, :],
                                        i=stt[:, sub * 128:(sub + 1) * 128]:
                                 e.transpose(o_, i, id_bf_s[:, :]), sem=s["pe"])
                            P.wait("vector", s["pe"])
                            P.wait_tok("vector", vst_free[pslot])
                            vsl = vst[:, pslot * 128:(pslot + 1) * 128]
                            P.op("vector", lambda e, o_=vsl, i=ptk[:, :]:
                                 e.tensor_copy(o_, i), sem=s["dve"])
                            ptk_free[pslot] = s["dve"].n
                            P.wait("gpsimd", s["dve"])
                            spv = vth_sp if hl == 0 else vtl_sp
                            dst = _ap3(spv, hA * S * HD + (T0 + sub * 128) * HD,
                                       [[HD, 128], [S * HD, 2], [1, HD]])
                            vst_free[pslot] = P.dmas("gpsimd", [
                                (dst, vsl.rearrange("p (h d) -> p h d", h=2))])
            tile_pe_free[u] = s["pe"].n
        phase_barrier()
        P.flush()

    QGS = [(0, 512), (512, 512), (1024, 512), (1536, 512), (2048, XQW)]
    if "B" in phases:
      with ExitStack() as ctx:
        kp_hi = [sb(ctx, f"b_kph{i}", [128, S], BF16) for i in range(2)]
        kp_lo = [sb(ctx, f"b_kpl{i}", [128, S], BF16) for i in range(2)]
        qp_hi = [sb(ctx, f"b_qph{i}", [128, QTOT], BF16) for i in range(2)]
        qp_lo = [sb(ctx, f"b_qpl{i}", [128, QTOT], BF16) for i in range(2)]
        vA_hi = [sb(ctx, f"b_vah{i}", [128, NKC * HD], BF16) for i in range(2)]
        vA_lo = [sb(ctx, f"b_val{i}", [128, NKC * HD], BF16) for i in range(2)]
        vB_hi = [sb(ctx, f"b_vbh{i}", [128, NKC * HD], BF16) for i in range(2)]
        vB_lo = [sb(ctx, f"b_vbl{i}", [128, NKC * HD], BF16) for i in range(2)]
        paf = [sb(ctx, f"b_paf{i}", [128, 512], F32) for i in range(2)]
        pbf = [sb(ctx, f"b_pbf{i}", [128, 512], F32) for i in range(2)]
        pah = [sb(ctx, f"b_pah{i}", [128, 512], BF16) for i in range(2)]
        pal = [sb(ctx, f"b_pal{i}", [128, 512], BF16) for i in range(2)]
        pbh = [sb(ctx, f"b_pbh{i}", [128, 512], BF16) for i in range(2)]
        pbl = [sb(ctx, f"b_pbl{i}", [128, 512], BF16) for i in range(2)]
        dsb = sb(ctx, "b_dsb", [128, 512], F32)
        binv = sb(ctx, "b_binv", [128, 512], F32)
        o_f = sb(ctx, "b_of", [128, 512], F32)
        o_hi = sb(ctx, "b_ohi", [128, 512], BF16)
        o_lo = sb(ctx, "b_olo", [128, 512], BF16)
        st_ps = [(ps(ctx, "b_stA0", [128, 512], F32), ps(ctx, "b_stB0", [128, 512], F32)),
                 (ps(ctx, "b_stA1", [128, 512], F32), ps(ctx, "b_stB1", [128, 512], F32))]
        pv = ps(ctx, "b_pv", [128, 512], F32)
        dps = ps(ctx, "b_dps", [128, 512], F32)
        ddp = ps(ctx, "b_ddp", [128, 512], F32)
        o_sp_tok = [None]
        P.op("vector", lambda e: e.memset(dsb[:, :], 0.0))
        for t in (st_ps[0][0], st_ps[0][1], st_ps[1][0], st_ps[1][1], pv, dps, ddp):
            P.op("vector", lambda e, t=t: e.memset(t[:, :], 0.0))
        P.op("vector", lambda e: e.engine_nop(), sem=s["dve"])

        pair_free_pe = [0, 0]
        pair_tok = [None, None]
        g_exp = [0, 0]
        g_lo = [0, 0]
        g_castA = [0, 0]
        g_pv = [0, 0]

        def emit_pair_loads(pc):
            u = pc % 2
            grp = []
            for dst, spv in ((kp_hi[u], kh_sp), (kp_lo[u], kl_sp)):
                grp.append((dst[:, :], spv.ap()[pc * 128:(pc + 1) * 128, :]))
            for dst, spv in ((qp_hi[u], qh_sp), (qp_lo[u], ql_sp)):
                grp.append((dst[:, 0:OWN], spv.ap()[pc * 128:(pc + 1) * 128, 0:OWN]))
                grp.append((dst[:, OWN:QTOT], spv.ap()[pc * 128:(pc + 1) * 128, XQ0:S]))
            for dst, spv, h in ((vA_hi[u], vth_sp, 2 * pc), (vA_lo[u], vtl_sp, 2 * pc),
                                (vB_hi[u], vth_sp, 2 * pc + 1), (vB_lo[u], vtl_sp, 2 * pc + 1)):
                grp.append((dst[:, :].rearrange("p (c d) -> p c d", c=NKC),
                            _ap3(spv, h * S * HD, [[HD, 128], [128 * HD, NKC], [1, HD]])))
            P.wait("sync", s["pe"], val=pair_free_pe[u])
            return P.dmas("sync", grp)

        pair_tok[0] = emit_pair_loads(0)
        for pc in range(8):
            u = pc % 2
            if pc + 1 < 8:
                pair_tok[(pc + 1) % 2] = emit_pair_loads(pc + 1)
            P.wait_tok("tensor", pair_tok[u])
            kph, kpl = kp_hi[u], kp_lo[u]
            qph, qpl = qp_hi[u], qp_lo[u]
            vah, val_, vbh, vbl = vA_hi[u], vA_lo[u], vB_hi[u], vB_lo[u]

            for (g0, gw) in QGS:
                lo_done = {}
                castA_done = {}

                def emit_pv(cc, g0=g0, gw=gw, vah=vah, val_=val_, vbh=vbh, vbl=vbl):
                    v = cc % 2
                    P.wait("tensor", s["dve"], val=lo_done[cc])
                    P.wait("tensor", s["act"], val=castA_done[cc])
                    pvs = [(vah, pah[v]), (val_, pah[v]), (vah, pal[v])]
                    pvsB = [(vbh, pbh[v]), (vbl, pbh[v]), (vbh, pbl[v])]
                    st_c, sp_c = (cc == 0), (cc == NKC - 1)
                    for i in range(3):
                        va_, pa_ = pvs[i]
                        vb_, pb_ = pvsB[i]
                        st_i = st_c and (i == 0)
                        sp_i = sp_c and (i == 2)
                        P.op("tensor", lambda e, w_=va_[:, cc * HD:(cc + 1) * HD],
                                    x_=pa_[:, 0:gw], a=st_i, b_=sp_i:
                             e.matmul(pv[0:64, 0:gw], w_, x_, start=a, stop=b_,
                                      tile_position=(0, 0), skip_group_check=True))
                        P.op("tensor", lambda e, w_=vb_[:, cc * HD:(cc + 1) * HD],
                                    x_=pb_[:, 0:gw], a=st_i, b_=sp_i:
                             e.matmul(pv[64:128, 0:gw], w_, x_, start=a, stop=b_,
                                      tile_position=(0, 64), skip_group_check=True))
                    dmov = [(pah[v], 0), (pal[v], 32), (pbh[v], 64), (pbl[v], 96)]
                    for i, (pm, col) in enumerate(dmov):
                        P.op("tensor", lambda e, o_=dps[col:col + 1, 0:gw],
                                    x_=pm[:, 0:gw], a=st_c, b_=sp_c, col=col:
                             e.matmul(o_, ones_bf_s[:, :], x_, start=a, stop=b_,
                                      tile_position=(0, col), skip_group_check=True),
                             sem=(s["pe"] if i == len(dmov) - 1 else None))
                    g_pv[v] = s["pe"].n

                for c in range(NKC):
                    v = c % 2
                    stA, stB = st_ps[v]
                    P.wait("tensor", s["act"], val=g_exp[v])
                    qk = [(kph, qph), (kph, qpl), (kpl, qph)]
                    for i, (kk, qq) in enumerate(qk):
                        st_, sp2_ = (i == 0), (i == len(qk) - 1)
                        P.op("tensor", lambda e, o_=stA[:, 0:gw],
                                    w_=kk[0:64, c * 128:(c + 1) * 128],
                                    x_=qq[0:64, g0:g0 + gw], a=st_, b_=sp2_:
                             e.matmul(o_, w_, x_, start=a, stop=b_,
                                      tile_position=(0, 0)))
                        P.op("tensor", lambda e, o_=stB[:, 0:gw],
                                    w_=kk[64:128, c * 128:(c + 1) * 128],
                                    x_=qq[64:128, g0:g0 + gw], a=st_, b_=sp2_:
                             e.matmul(o_, w_, x_, start=a, stop=b_,
                                      tile_position=(64, 0)),
                             sem=(s["pe"] if sp2_ else None))
                    qk_done_c = s["pe"].n
                    if c >= 1:
                        emit_pv(c - 1)
                    P.wait("scalar", s["pe"], val=qk_done_c)
                    P.wait("scalar", s["dve"], val=g_lo[v])
                    P.op("scalar", lambda e, v=v, gw=gw, sA=stA: e.activation(
                        paf[v][:, 0:gw], sA[:, 0:gw], AF.Exp, bias=0.0, scale=1.0))
                    P.op("scalar", lambda e, v=v, gw=gw, sB=stB: e.activation(
                        pbf[v][:, 0:gw], sB[:, 0:gw], AF.Exp, bias=0.0, scale=1.0),
                        sem=s["act"])
                    g_exp[v] = s["act"].n
                    exp_done_c = s["act"].n
                    P.wait("scalar", s["pe"], val=g_pv[v])
                    P.op("scalar", lambda e, v=v, gw=gw: e.activation(
                        pah[v][:, 0:gw], paf[v][:, 0:gw], AF.Identity,
                        bias=0.0, scale=1.0), sem=s["act"])
                    castA_done[c] = s["act"].n
                    P.wait("vector", s["act"], val=exp_done_c)
                    P.wait("vector", s["pe"], val=g_pv[v])
                    P.op("vector", lambda e, v=v, gw=gw: e.tensor_copy(
                        pbh[v][:, 0:gw], pbf[v][:, 0:gw]))
                    P.wait("vector", s["act"], val=castA_done[c])
                    P.op("vector", lambda e, v=v, gw=gw: e.tensor_tensor(
                        pal[v][:, 0:gw], paf[v][:, 0:gw], pah[v][:, 0:gw], ALU.subtract))
                    P.op("vector", lambda e, v=v, gw=gw: e.tensor_tensor(
                        pbl[v][:, 0:gw], pbf[v][:, 0:gw], pbh[v][:, 0:gw], ALU.subtract),
                        sem=s["dve"])
                    lo_done[c] = s["dve"].n
                    g_lo[v] = s["dve"].n
                emit_pv(NKC - 1)

                P.wait("vector", s["pe"])
                for drow in (0, 32, 64, 96):
                    P.op("vector", lambda e, gw=gw, dr=drow: e.tensor_copy(
                        dsb[dr:dr + 1, 0:gw], dps[dr:dr + 1, 0:gw]),
                        sem=(s["dve"] if drow == 96 else None))
                P.wait("tensor", s["dve"])
                P.op("tensor", lambda e, gw=gw: e.matmul(
                    ddp[:, 0:gw], dmask_s[:, :], dsb[:, 0:gw], start=True, stop=True),
                    sem=s["pe"])
                P.wait("vector", s["pe"])
                P.op("vector", lambda e, gw=gw: e.reciprocal(binv[:, 0:gw], ddp[:, 0:gw]),
                     sem=s["dve"])
                P.op("vector", lambda e, gw=gw: e.tensor_tensor(
                    o_f[:, 0:gw], pv[:, 0:gw], binv[:, 0:gw], ALU.mult), sem=s["dve"])
                P.wait("scalar", s["dve"])
                P.wait_tok("scalar", o_sp_tok[0])
                P.op("scalar", lambda e, gw=gw: e.activation(
                    o_hi[:, 0:gw], o_f[:, 0:gw], AF.Identity, bias=0.0, scale=1.0),
                    sem=s["act"])
                P.wait("vector", s["act"])
                P.wait_tok("vector", o_sp_tok[0])
                P.op("vector", lambda e, gw=gw: e.tensor_tensor(
                    o_lo[:, 0:gw], o_f[:, 0:gw], o_hi[:, 0:gw], ALU.subtract),
                    sem=s["dve"])
                P.wait("gpsimd", s["dve"])
                P.wait("gpsimd", s["act"])
                o_sp_tok[0] = P.dmas("gpsimd", [
                    (oh_sp.ap()[pc * 128:(pc + 1) * 128, g0:g0 + gw], o_hi[:, 0:gw]),
                    (ol_sp.ap()[pc * 128:(pc + 1) * 128, g0:g0 + gw], o_lo[:, 0:gw])])
            pair_free_pe[u] = s["pe"].n
        phase_barrier()
        P.flush()

    if "C" in phases:
      with ExitStack() as ctx:
        wo_hi_s = sb(ctx, "wo_hi_s", [128, NCH * E], BF16)
        wr_hi_s2 = sb(ctx, "wr_hi_s2", [128, NCH * E], BF16)
        wk_hi_s2 = sb(ctx, "wk_hi_s2", [128, NCH * E], BF16)
        wv_hi_s2 = sb(ctx, "wv_hi_s2", [128, NCH * E], BF16)
        with ExitStack() as wctx:
            wtmp = sb(wctx, "p2_wtmp", [128, E], F32)
            lotmp = sb(wctx, "p2_lotmp", [128, E], BF16)
            emit_weight_prep(wout.ap(), E, wo_hi_s, wo_lo, None, wtmp, lotmp)
        P.dmas("sync", [
            (hs[:, :].rearrange("p (k m) -> p k m", k=NCH),
             _ap3(hsp, 0, [[E, 128], [128 * E, NCH], [1, E]]))
            for hs, hsp in ((wr_hi_s2, wr_hi_sp), (wk_hi_s2, wk_hi_sp),
                            (wv_hi_s2, wv_hi_sp))])
        phase_barrier()

        oht = sb(ctx, "c_oh", [128, NCH * TOK], BF16)
        olt = sb(ctx, "c_ol", [128, NCH * TOK], BF16)
        r1ct = sb(ctx, "c_r1", [128, NCH * TOK], F32)
        r2t = sb(ctx, "c_r2", [128, NCH * TOK], F32)
        xsq = sb(ctx, "c_xsq", [128, 1024], F32)
        st3 = mk_stats(ctx, "c_st3", NSUB)
        bci = sb(ctx, "c_bci", [128, TOK], F32)
        bcn = sb(ctx, "c_bcn", [128, TOK], F32)
        y_ext = sb(ctx, "c_yext", [128, TOK + 1], F32)
        ytmp = sb(ctx, "c_ytmp", [128, TOK], F32)
        dbuf = sb(ctx, "c_dbuf", [128, TOK], F32)
        xf = sb(ctx, "c_xf", [128, TOK], F32)
        ycar = sb(ctx, "c_ycar", [128, NCH], F32)
        x3car = sb(ctx, "c_x3car", [128, NCH], F32)
        xr_hi = sb(ctx, "c_xr_hi", [128, NCH * TOK], BF16)
        xr_lo = sb(ctx, "c_xr_lo", [128, NCH * TOK], BF16)
        xk_hi = sb(ctx, "c_xk_hi", [128, NCH * TOK], BF16)
        xk_lo = sb(ctx, "c_xk_lo", [128, NCH * TOK], BF16)
        ka_hi = sb(ctx, "c_ka_hi", [128, NCH * TOK], BF16)
        ka_lo = sb(ctx, "c_ka_lo", [128, NCH * TOK], BF16)
        r_sig = sb(ctx, "c_rsig", [128, NCH * TOK], F32)
        kf = sb(ctx, "c_kf", [128, TOK], F32)
        of = sb(ctx, "c_of", [128, 2 * TOK], F32)
        wlo_ring = sb(ctx, "c_wlo", [128, 2 * NCH * 128], BF16)
        wolo_ring = wlo_ring
        pb = [ps(ctx, "c_pb0", [128, TOK], F32), ps(ctx, "c_pb1", [128, TOK], F32)]
        bci_ps = ps(ctx, "c_bci_ps", [128, TOK], F32)
        bcn_ps = ps(ctx, "c_bcn_ps", [128, TOK], F32)
        pt_ps = ps(ctx, "c_pt_ps", [128, 1024], F32)
        for t in (pb[0], pb[1], bci_ps, bcn_ps, pt_ps):
            P.op("vector", lambda e, t=t: e.memset(t[:, :], 0.0))
        P.op("vector", lambda e: e.memset(ycar[:, :], 0.0))
        P.op("vector", lambda e: e.memset(x3car[:, :], 0.0))
        P.op("vector", lambda e: e.engine_nop(), sem=s["dve"])

        wlo_sps = {"wr": wr_lo, "wk": wk_lo, "wv": wv_lo}
        ring_state = {"i": 0, "loaded": {}}
        wo_ring = {"i": 0, "loaded": {}}
        out_toks = [None, None]

        def lo_load(proj, o):
            slot = ring_state["i"] % 2
            ring_state["i"] += 1
            base = slot * NCH * 128
            src = _ap3(wlo_sps[proj], o * NCH * 128 * 128,
                       [[NCH * 128, 128], [128, NCH], [1, 128]])
            P.wait("sync", s["pe"], val=ring_state["loaded"].get(slot, 0))
            t = P.dmas("sync", [(wlo_ring[:, base:base + NCH * 128], src)])
            P.wait_tok("tensor", t)
            ring_state["loaded"][slot] = s["pe"].n + 1
            return wlo_ring[:, base:base + NCH * 128]

        def wo_lo_load(o):
            slot = ring_state["i"] % 2
            ring_state["i"] += 1
            base = slot * NCH * 128
            src = _ap3(wo_lo, o * NCH * 128 * 128,
                       [[NCH * 128, 128], [128, NCH], [1, 128]])
            P.wait("sync", s["pe"], val=ring_state["loaded"].get(slot, 0))
            t = P.dmas("sync", [(wolo_ring[:, base:base + NCH * 128], src)])
            P.wait_tok("tensor", t)
            ring_state["loaded"][slot] = s["pe"].n + 1
            return wolo_ring[:, base:base + NCH * 128]

        wdict = {"wr_hi": wr_hi_s2, "wk_hi": wk_hi_s2, "wv_hi": wv_hi_s2,
                 "lo_load": lo_load}
        bufs = {"xr_hi": xr_hi, "xr_lo": xr_lo, "xk_hi": xk_hi, "xk_lo": xk_lo,
                "ka_hi": ka_hi, "ka_lo": ka_lo, "r_sig": r_sig, "kf": kf,
                "of": of, "pb": pb, "y_ext": y_ext, "ytmp": ytmp, "dbuf": dbuf,
                "xf": xf, "ycar": ycar, "bci": bci, "bcn": bcn,
                "bank_free_dve": [0, 0], "bank_free_act": [0, 0],
                "of_free": [0, 0]}

        def tile_wout_ln3(oc0, tokw, r1_off):
            T = tokw
            nsub = T // 128
            P.wait("sync", s["dve"])
            P.wait("sync", s["pe"])
            t_c = P.dmas("sync", [
                (oht[:, 0:NCH * T].rearrange("p (c t) -> p c t", c=NCH),
                 _ap3(oh_sp, oc0, [[QTOT, 128], [128 * QTOT, NCH], [1, T]])),
                (olt[:, 0:NCH * T].rearrange("p (c t) -> p c t", c=NCH),
                 _ap3(ol_sp, oc0, [[QTOT, 128], [128 * QTOT, NCH], [1, T]])),
                (r1ct[:, 0:NCH * T].rearrange("p (c t) -> p c t", c=NCH),
                 _ap3(r1_sp, r1_off, [[S, 128], [128 * S, NCH], [1, T]]))])
            P.wait_tok("tensor", t_c)
            P.wait_tok("vector", t_c)
            for o in range(NCH):
                lo_ap = wo_lo_load(o)
                bank = pb[o % 2]
                P.wait("tensor", s["dve"], val=bufs["bank_free_dve"][o % 2])
                mms = ([(wo_hi_s[:, k * E + o * 128: k * E + o * 128 + 128],
                         oht[:, k * T:(k + 1) * T]) for k in range(NCH)] +
                       [(wo_hi_s[:, k * E + o * 128: k * E + o * 128 + 128],
                         olt[:, k * T:(k + 1) * T]) for k in range(NCH)] +
                       [(lo_ap[:, k * 128:(k + 1) * 128],
                         oht[:, k * T:(k + 1) * T]) for k in range(NCH)])
                n = len(mms)
                for i, (wap, xap) in enumerate(mms):
                    P.op("tensor",
                         lambda e, o_=bank[:, 0:T], w_=wap, x_=xap,
                                st=(i == 0), sp2=(i == n - 1):
                         e.matmul(o_, w_, x_, start=st, stop=sp2),
                         sem=(s["pe"] if i == n - 1 else None))
                P.wait("vector", s["pe"])
                P.op("vector", lambda e, o=o, T=T, b_=bank: e.scalar_tensor_tensor(
                    r2t[:, o * T:(o + 1) * T], b_[:, 0:T], bout_s[:, o:o + 1],
                    r1ct[:, o * T:(o + 1) * T], ALU.add, ALU.add), sem=s["dve"])
                bufs["bank_free_dve"][o % 2] = s["dve"].n
            for sub in range(nsub):
                P.wait("tensor", s["dve"])
                P.wait("tensor", s["act"])
                for c in range(NCH):
                    P.op("tensor", lambda e, o=pt_ps[:, c * 128:(c + 1) * 128],
                                i=r2t[:, c * T + sub * 128: c * T + sub * 128 + 128]:
                         e.transpose(o, i, id_f32_s[:, :]),
                         sem=(s["pe"] if c == NCH - 1 else None))
                P.wait("scalar", s["pe"])
                P.wait("scalar", s["dve"])
                P.op("scalar", lambda e: e.activation(
                    xsq[:, :], pt_ps[:, 0:1024], AF.Identity, bias=0.0, scale=1.0),
                    sem=s["act"])
                P.wait("vector", s["act"])
                P.op("vector", lambda e, o=st3["s1"][:, sub:sub + 1]:
                     e.reduce_sum(o, xsq[:, :], axis=AXX), sem=s["dve"])
                P.wait("scalar", s["dve"])
                P.op("scalar", lambda e: e.activation(
                    xsq[:, :], pt_ps[:, 0:1024], AF.Square, bias=0.0, scale=1.0),
                    sem=s["act"])
                P.wait("vector", s["act"])
                P.op("vector", lambda e, o=st3["s2"][:, sub:sub + 1]:
                     e.reduce_sum(o, xsq[:, :], axis=AXX), sem=s["dve"])
            ln_stats_chain(st3, nsub)
            bc_tiles(st3, nsub, bci_ps, bcn_ps)
            P.wait("vector", s["pe"])
            P.op("vector", lambda e, T=T: e.tensor_copy(bci[:, 0:T], bci_ps[:, 0:T]))
            P.op("vector", lambda e, T=T: e.tensor_copy(bcn[:, 0:T], bcn_ps[:, 0:T]),
                 sem=s["dve"])

        c_tiles = phases.get("C")
        if c_tiles != 0 and "NOXT" not in phases:
            tile_wout_ln3(OWN, XQW, XQ0)
        lc = XQW - 1
        for c in range(NCH if (c_tiles != 0 and "NOXT" not in phases) else 0):
            P.op("vector", lambda e, c=c: e.tensor_tensor(
                ytmp[:, 0:1], r2t[:, c * XQW + lc: c * XQW + lc + 1],
                bci[:, lc:lc + 1], ALU.mult))
            P.drain("vector")
            P.op("vector", lambda e, c=c: e.tensor_tensor(
                ytmp[:, 1:2], ytmp[:, 0:1], bcn[:, lc:lc + 1], ALU.add))
            P.drain("vector")
            P.op("vector", lambda e, c=c: e.tensor_tensor(
                x3car[:, c:c + 1], ytmp[:, 1:2], m0x_s[:, 0:1], ALU.mult),
                sem=s["dve"])

        NTC = OWN // TOK if c_tiles is None else min(c_tiles, OWN // TOK)
        for j in range(NTC):
            T0 = j * TOK
            tile_wout_ln3(T0, TOK, T0)
            if dbg and j == 0:
                P.wait("gpsimd", s["dve"])
                P.dmas("gpsimd", [(
                    _ap3(dbg_r2, 0, [[TOK, 128], [128 * TOK, NCH], [1, TOK]]),
                    r2t[:, :].rearrange("p (c t) -> p c t", c=NCH))])

            def carry_init(c, j=j):
                src = x3car if (j == 0 and "NOXT" not in phases) else ycar
                P.op("vector", lambda e, c=c, sr=src: e.tensor_copy(
                    y_ext[:, 0:1], sr[:, c:c + 1]))

            emit_mix(bufs, lambda c: r2t[:, c * TOK:(c + 1) * TOK], TOK,
                     carry_init, pre_normalized=False)
            P.wait("tensor", s["dve"])

            def out_store(o, src_ap, T0=T0):
                P.wait("gpsimd", s["dve"])
                out_toks[o % 2] = P.dmas("gpsimd", [
                    (out_fm.ap()[o * 128:(o + 1) * 128, T0:T0 + TOK], src_ap)])

            emit_ffn(bufs, wdict, TOK,
                     lambda o: r2t[:, o * TOK:(o + 1) * TOK], out_store, out_toks)

        P.wait_all_dma("gpsimd")
        P.wait_all_dma("sync")
        P.flush()

    P.flush()
    return P



_QKV_PERM = np.concatenate([
    np.concatenate([np.arange(h * 192 + j * 64, h * 192 + j * 64 + 64)
                    for h in range(H)])
    for j in range(3)])


def _col8(v):
    return np.ascontiguousarray(np.asarray(v, np.float32).reshape(NCH, 128).T)


def _core_inputs(shared, x, b, p):
    q0g = p * OWN
    xb = np.asarray(x[b], np.float32)
    xr = np.roll(xb, -q0g, axis=0)
    xprev = xb[q0g - 1:q0g] if p == 1 else np.zeros((1, E), np.float32)
    d = {
        "x_fm": np.ascontiguousarray(xr.T),
        "x_tm": np.ascontiguousarray(xr),
        "xprev": np.ascontiguousarray(xprev),
        "m2048": np.full((128, 1), 1.0 - p, np.float32),
        "m0x": np.full((128, 1), float(p), np.float32),
    }
    d.update(shared)
    return d


def kernel(L0_attn_x, L0_attn_Wqkv, L0_attn_bqkv, L0_attn_Wout, L0_attn_bout,
           L0_ffn_W_r, L0_ffn_W_k, L0_ffn_W_v, L0_ffn_mu_r, L0_ffn_mu_k):
    x = np.asarray(L0_attn_x, np.float32)
    shared = {
        "wqkv": np.ascontiguousarray(np.asarray(L0_attn_Wqkv, np.float32)[:, _QKV_PERM]),
        "bqkv": np.ascontiguousarray(
            np.asarray(L0_attn_bqkv, np.float32)[_QKV_PERM].reshape(24, 128).T),
        "wout": np.ascontiguousarray(np.asarray(L0_attn_Wout, np.float32)),
        "bout": _col8(L0_attn_bout),
        "wrT": np.ascontiguousarray(np.asarray(L0_ffn_W_r, np.float32).T),
        "wkT": np.ascontiguousarray(np.asarray(L0_ffn_W_k, np.float32).T),
        "wvT": np.ascontiguousarray(np.asarray(L0_ffn_W_v, np.float32).T),
        "mu_r": _col8(L0_ffn_mu_r),
        "mu_k": _col8(L0_ffn_mu_k),
    }

    nc = bass.Bass("TRN2", target_bir_lowering=False, detect_race_conditions=False)
    build_program(nc)

    in_maps = [_core_inputs(shared, x, c // 2, c % 2) for c in range(8)]
    res = run_bass_kernel_spmd(nc, in_maps, core_ids=list(range(8)))
    out = np.empty((B, S, E), np.float32)
    for c in range(8):
        b, p = c // 2, c % 2
        out[b, p * OWN:(p + 1) * OWN, :] = np.asarray(res.results[c]["out_fm"]).T
    return out


# version 06
# speedup vs baseline: 1.1189x; 1.0168x over previous
import numpy as np
import ml_dtypes
import concourse.bass as bass
import concourse.mybir as mybir
from concourse.bass_utils import run_bass_kernel_spmd

F32 = mybir.dt.float32
BF16 = mybir.dt.bfloat16
AF = mybir.ActivationFunctionType
ALU = mybir.AluOpType
AXX = mybir.AxisListType.X

B, S, E, H, HD = 4, 4096, 1024, 16, 64
NCH = E // 128
TOK = 512
TOKA = 256
NTA = S // TOKA
NSUBA = TOKA // 128
NSLOTA = 4
NT = S // TOK
NSUB = TOK // 128
EPS = 1e-5
OWN = 2048
XQ0, XQW = 3968, 128
QTOT = OWN + XQW
NKC = S // 128
M3 = 3 * E

BF16NP = ml_dtypes.bfloat16


class _Sem:
    def __init__(self, nc, name):
        self.h = nc.alloc_semaphore(name=name)
        self.n = 0


class PB:
    ENGS = ("sync", "gpsimd", "scalar", "vector", "tensor")

    def __init__(self, nc):
        self.nc = nc
        self.streams = {k: [] for k in self.ENGS}
        self._waited = {k: {} for k in self.ENGS}
        self._pools = {}

    def sem(self, name):
        return _Sem(self.nc, name)

    def _pool(self, eng):
        if eng not in self._pools:
            self._pools[eng] = {"sems": [self.sem(f"{eng}_dq{i}") for i in range(12)],
                                "i": 0}
        return self._pools[eng]

    def dmas(self, eng, pairs, slow=False):
        pool = self._pool(eng)
        sem = pool["sems"][pool["i"] % len(pool["sems"])]
        pool["i"] += 1
        if sem.n > 0:
            self.wait(eng, sem)
        for (dst, srcap) in pairs:
            self.op(eng, (lambda e, d=dst, s_=srcap, sl=slow:
                          e.dma_start(d, s_, allow_slow_non_contiguous=sl)),
                    sem=sem, k=16)
        return (sem, sem.n)

    def drain(self, eng):
        self.streams[eng].append(lambda e: e.drain())

    def wait_tok(self, eng, tok):
        if tok is not None:
            self.wait(eng, tok[0], tok[1])

    def wait_all_dma(self, eng_waiter):
        for qeng in list(self._pools):
            for sem in self._pools[qeng]["sems"]:
                self.wait(eng_waiter, sem)

    def wait_pool(self, eng_waiter, qeng):
        if qeng in self._pools:
            for sem in self._pools[qeng]["sems"]:
                self.wait(eng_waiter, sem)

    def op(self, eng, fn, sem=None, k=1):
        h = None
        if sem is not None:
            sem.n += k
            h = sem.h
        def run(e, fn=fn, h=h, k=k):
            inst = fn(e)
            if h is not None:
                inst.then_inc(h, k)
        self.streams[eng].append(run)

    def wait(self, eng, sem, val=None):
        v = sem.n if val is None else val
        if v <= 0:
            return
        if v <= self._waited[eng].get(id(sem), -1):
            return
        self._waited[eng][id(sem)] = v
        self.streams[eng].append(lambda e, h=sem.h, v=v: e.wait_ge(h, v))

    def emit(self, blk):
        for name in self.ENGS:
            ops = self.streams[name]
            if not ops:
                continue
            def body(e, ops=ops):
                for f in ops:
                    f(e)
            getattr(blk, name)(body)
        self.streams = {k: [] for k in self.ENGS}

    def flush(self):
        if any(self.streams.values()):
            with self.nc.Block() as blk:
                self.emit(blk)


def _ap3(handle, offset, dims):
    return bass.AP(handle, offset, [list(d) for d in dims])


def build_program(nc, dbg=False, phases=("A1", "A2", "B", "C")):
    phases = {p.split(":")[0]: (int(p.split(":")[1]) if ":" in p else None)
              for p in phases}
    spill_kind = "ExternalOutput" if dbg else "Internal"

    d_in = lambda n, shp, dt=F32: nc.dram_tensor(n, shp, dt, kind="ExternalInput")
    x_fm = d_in("x_fm", [E, S])
    x_tm = d_in("x_tm", [S, E])
    xprev = d_in("xprev", [1, E])
    m2048 = d_in("m2048", [128, 1])
    m0x = d_in("m0x", [128, 1])
    mu_r = d_in("mu_r", [128, NCH])
    mu_k = d_in("mu_k", [128, NCH])
    wrT = d_in("wrT", [E, E])
    wkT = d_in("wkT", [E, E])
    wvT = d_in("wvT", [E, E])
    wqkv = d_in("wqkv", [E, M3])
    bqkv = d_in("bqkv", [128, 24])
    wout = d_in("wout", [E, E])
    bout = d_in("bout", [128, NCH])
    out_fm = nc.dram_tensor("out_fm", [E, OWN], F32, kind="ExternalOutput")

    mk_sp = lambda n, shp, dt: nc.dram_tensor(n, shp, dt, kind=spill_kind)
    r1_sp = mk_sp("r1_sp", [E, S], F32)
    x2h_sp = mk_sp("x2h_sp", [E, S], BF16)
    x2l_sp = mk_sp("x2l_sp", [E, S], BF16)
    qh_sp = mk_sp("qh_sp", [E, S], BF16)
    ql_sp = mk_sp("ql_sp", [E, S], BF16)
    kh_sp = mk_sp("kh_sp", [E, S], BF16)
    kl_sp = mk_sp("kl_sp", [E, S], BF16)
    vth_sp = mk_sp("vth_sp", [H, S, HD], BF16)
    vtl_sp = mk_sp("vtl_sp", [H, S, HD], BF16)
    oh_sp = mk_sp("oh_sp", [E, QTOT], BF16)
    ol_sp = mk_sp("ol_sp", [E, QTOT], BF16)
    yprow_sp = mk_sp("yprow_sp", [1, E], F32)
    dbg_bci = mk_sp("dbg_bci", [128, TOK], F32)
    dbg_bcn = mk_sp("dbg_bcn", [128, TOK], F32)
    dbg_st = mk_sp("dbg_st", [128, 16 * NSUB], F32)
    dbg_xtm = mk_sp("dbg_xtm", [128, NSUB * 1024], F32)
    dbg_y = mk_sp("dbg_y", [128, TOK + 1], F32)
    dbg_r2 = mk_sp("dbg_r2", [E, TOK], F32)
    wr_lo = mk_sp("wr_lo", [E, E], BF16)
    wk_lo = mk_sp("wk_lo", [E, E], BF16)
    wv_lo = mk_sp("wv_lo", [E, E], BF16)
    wr_hi_sp = mk_sp("wr_hi_sp", [E, E], BF16)
    wk_hi_sp = mk_sp("wk_hi_sp", [E, E], BF16)
    wv_hi_sp = mk_sp("wv_hi_sp", [E, E], BF16)
    wq_lo = mk_sp("wq_lo", [E, M3], BF16)
    wo_lo = mk_sp("wo_lo", [E, E], BF16)

    id_f32 = nc.inline_tensor(np.eye(128, dtype=np.float32), name="id_f32")
    id_bf = nc.inline_tensor(np.eye(128, dtype=BF16NP), name="id_bf")
    ones_bf = nc.inline_tensor(np.ones((128, 1), dtype=BF16NP), name="ones_bf")
    dmask_np = np.zeros((128, 128), dtype=np.float32)
    dmask_np[0, 0:64] = dmask_np[32, 0:64] = 1.0
    dmask_np[64, 64:128] = dmask_np[96, 64:128] = 1.0
    dmask = nc.inline_tensor(dmask_np, name="dmask")

    P = PB(nc)
    s = {k: P.sem(k) for k in ("ld", "sp", "dve", "act", "pe", "bar", "out")}

    from contextlib import ExitStack

    def sb(ctx, name, shape, dt):
        return ctx.enter_context(nc.sbuf_tensor(name, shape, dt))

    def ps(ctx, name, shape, dt=F32):
        return ctx.enter_context(nc.psum_tensor(name, shape, dt))

    def phase_barrier():
        for eng in PB.ENGS:
            P.wait_all_dma(eng)
            for k in ("dve", "act", "pe"):
                P.wait(eng, s[k])

    id_f32_s = nc.alloc_sbuf_tensor("id_f32_s", [128, 128], F32)
    id_bf_s = nc.alloc_sbuf_tensor("id_bf_s", [128, 128], BF16)
    ones_bf_s = nc.alloc_sbuf_tensor("ones_bf_s", [128, 1], BF16)
    dmask_s = nc.alloc_sbuf_tensor("dmask_s", [128, 128], F32)
    mur_s = nc.alloc_sbuf_tensor("mur_s", [128, NCH], F32)
    muk_s = nc.alloc_sbuf_tensor("muk_s", [128, NCH], F32)
    bqkv_s = nc.alloc_sbuf_tensor("bqkv_s", [128, 24], F32)
    bout_s = nc.alloc_sbuf_tensor("bout_s", [128, NCH], F32)
    m2048_s = nc.alloc_sbuf_tensor("m2048_s", [128, 1], F32)
    m0x_s = nc.alloc_sbuf_tensor("m0x_s", [128, 1], F32)
    ypcol_s = nc.alloc_sbuf_tensor("ypcol_s", [128, NCH], F32)

    const_tok = P.dmas("sync", [
        (dst[:, :], src.ap()[:, :]) for dst, src in
        ((id_f32_s, id_f32), (id_bf_s, id_bf), (ones_bf_s, ones_bf),
         (dmask_s, dmask), (mur_s, mu_r), (muk_s, mu_k), (bqkv_s, bqkv),
         (bout_s, bout), (m2048_s, m2048), (m0x_s, m0x))])
    for eng in ("tensor", "scalar", "vector", "gpsimd"):
        P.wait_tok(eng, const_tok)

    def emit_yprev(xprow_s, yprow_s, pscal_s):
        t = P.dmas("sync", [(xprow_s[:, :], xprev.ap()[:, :])])
        P.wait_tok("scalar", t)
        P.wait_tok("vector", t)
        P.op("scalar", lambda e: e.activation(yprow_s[:, :], xprow_s[:, :],
                                              AF.Square, bias=0.0, scale=1.0), sem=s["act"])
        P.wait("vector", s["act"])
        P.op("vector", lambda e: e.reduce_sum(pscal_s[:, 0:1], xprow_s[:, :], axis=AXX))
        P.op("vector", lambda e: e.reduce_sum(pscal_s[:, 1:2], yprow_s[:, :], axis=AXX))
        P.drain("vector")
        P.op("vector", lambda e: e.tensor_scalar_mul(pscal_s[:, 2:3], pscal_s[:, 0:1], 1.0 / E))
        P.op("vector", lambda e: e.tensor_scalar_mul(pscal_s[:, 3:4], pscal_s[:, 1:2], 1.0 / E))
        P.drain("vector")
        P.op("vector", lambda e: e.tensor_mul(pscal_s[:, 4:5], pscal_s[:, 2:3], pscal_s[:, 2:3]))
        P.drain("vector")
        P.op("vector", lambda e: e.tensor_sub(pscal_s[:, 5:6], pscal_s[:, 3:4], pscal_s[:, 4:5]))
        P.drain("vector")
        P.op("vector", lambda e: e.tensor_scalar_add(pscal_s[:, 5:6], pscal_s[:, 5:6], EPS),
             sem=s["dve"])
        P.wait("scalar", s["dve"])
        P.op("scalar", lambda e: e.activation(pscal_s[:, 6:7], pscal_s[:, 5:6],
                                              AF.Sqrt, bias=0.0, scale=1.0), sem=s["act"])
        P.wait("vector", s["act"])
        P.op("vector", lambda e: e.reciprocal(pscal_s[:, 6:7], pscal_s[:, 6:7]))
        P.op("vector", lambda e: e.tensor_scalar_mul(pscal_s[:, 7:8], pscal_s[:, 2:3], -1.0))
        P.drain("vector")
        P.op("vector", lambda e: e.tensor_scalar(
            yprow_s[:, :], xprow_s[:, :], pscal_s[:, 7:8], pscal_s[:, 6:7],
            ALU.add, ALU.mult), sem=s["dve"])
        P.wait("gpsimd", s["dve"])
        t2 = P.dmas("gpsimd", [(yprow_sp.ap()[:, :], yprow_s[:, :])])
        P.wait_tok("sync", t2)
        return P.dmas("sync", [(ypcol_s[:, :],
                                _ap3(yprow_sp, 0, [[1, 128], [128, NCH]]))],
                      slow=True)



    def emit_weight_prep(w_dram_ap, m, hi_sb, lo_sp_h, hi_sp_h, tmp, lotmp,
                         prev_tok=None, tok_sink=None, side=None):
        ncho = m // 128
        sp_tok = prev_tok
        for k in range(NCH):
            P.wait("sync", s["dve"])
            t = P.dmas("sync", [(tmp[:, 0:m], w_dram_ap[k * 128:(k + 1) * 128, :])])
            P.wait_tok("scalar", t)
            hi_slice = hi_sb[:, k * m:(k + 1) * m]
            P.op("scalar", lambda e, o=hi_slice, i=tmp[:, 0:m]:
                 e.activation(o, i, AF.Identity, bias=0.0, scale=1.0), sem=s["act"])
            P.wait("vector", s["act"])
            P.wait_tok("vector", sp_tok)
            P.wait_tok("vector", t)
            P.op("vector", lambda e, o=lotmp[:, 0:m], i=tmp[:, 0:m], h=hi_slice:
                 e.tensor_tensor(o, i, h, ALU.subtract), sem=s["dve"])
            P.wait("gpsimd", s["dve"])
            grp = [(_ap3(lo_sp_h, k * 128,
                         [[NCH * 128, 128], [128 * NCH * 128, ncho], [1, 128]]),
                    lotmp[:, 0:m].rearrange("p (o c) -> p o c", o=ncho))]
            if hi_sp_h is not None:
                P.wait("gpsimd", s["act"])
                grp.append((hi_sp_h[k * 128:(k + 1) * 128, :], hi_slice))
            sp_tok = P.dmas("gpsimd", grp)
            if tok_sink is not None:
                tok_sink.append(sp_tok)
            if side:
                side.pop(0)()
        return sp_tok

    def mk_stats(ctx, tag, nsub):
        names = ("s1", "s2", "mean", "ex2", "m2", "var", "sd", "inv", "nm")
        return {n: sb(ctx, f"{tag}_{n}", [128, nsub], F32) for n in names}

    def ln_stats_chain(st, nsub):
        P.drain("vector")
        P.op("vector", lambda e: e.tensor_scalar_mul(st["mean"][:, 0:nsub], st["s1"][:, 0:nsub], 1.0 / E))
        P.op("vector", lambda e: e.tensor_scalar_mul(st["ex2"][:, 0:nsub], st["s2"][:, 0:nsub], 1.0 / E))
        P.drain("vector")
        P.op("vector", lambda e: e.tensor_mul(st["m2"][:, 0:nsub], st["mean"][:, 0:nsub], st["mean"][:, 0:nsub]))
        P.drain("vector")
        P.op("vector", lambda e: e.tensor_sub(st["var"][:, 0:nsub], st["ex2"][:, 0:nsub], st["m2"][:, 0:nsub]))
        P.drain("vector")
        P.op("vector", lambda e: e.tensor_scalar_add(st["var"][:, 0:nsub], st["var"][:, 0:nsub], EPS),
             sem=s["dve"])
        P.wait("scalar", s["dve"])
        P.op("scalar", lambda e: e.activation(st["sd"][:, 0:nsub], st["var"][:, 0:nsub],
                                              AF.Sqrt, bias=0.0, scale=1.0), sem=s["act"])
        P.wait("vector", s["act"])
        P.op("vector", lambda e: e.reciprocal(st["inv"][:, 0:nsub], st["sd"][:, 0:nsub]))
        P.drain("vector")
        P.op("vector", lambda e: e.tensor_mul(st["nm"][:, 0:nsub], st["mean"][:, 0:nsub], st["inv"][:, 0:nsub]))
        P.drain("vector")
        P.op("vector", lambda e: e.tensor_scalar_mul(st["nm"][:, 0:nsub], st["nm"][:, 0:nsub], -1.0),
             sem=s["dve"])

    def stat_sub(st, src, xsq, sub):
        P.op("vector", lambda e, o=xsq[:, :], i=src:
             e.tensor_tensor(o, i, i, ALU.mult))
        P.op("vector", lambda e, o=st["s1"][:, sub:sub + 1], i=src:
             e.reduce_sum(o, i, axis=AXX))
        P.op("vector", lambda e, o=st["s2"][:, sub:sub + 1], i=xsq[:, :]:
             e.reduce_sum(o, i, axis=AXX), sem=s["dve"])

    def emit_stats_from_tm(st, src_list, xsq, nsub):
        for sub in range(nsub):
            stat_sub(st, src_list[sub], xsq, sub)
        ln_stats_chain(st, nsub)

    def bc_tiles(st, nsub, bci_ps, bcn_ps, dve_val=None):
        P.wait("tensor", s["dve"], val=dve_val)
        for sub in range(nsub):
            iap = st["inv"][:, sub:sub + 1].broadcast_to([128, 128])
            nap = st["nm"][:, sub:sub + 1].broadcast_to([128, 128])
            P.op("tensor", lambda e, o=bci_ps[:, sub * 128:(sub + 1) * 128], w=iap:
                 e.matmul(o, w, id_f32_s[:, :], start=True, stop=True))
            P.op("tensor", lambda e, o=bcn_ps[:, sub * 128:(sub + 1) * 128], w=nap:
                 e.matmul(o, w, id_f32_s[:, :], start=True, stop=True),
                 sem=(s["pe"] if sub == nsub - 1 else None))

    def emit_ffn(bufs, w, tokw, resid_ap_fn, out_fn, out_toks,
                 side=None, out_ap_fn=None):
        T = tokw
        xr_hi, xr_lo = bufs["xr_hi"], bufs["xr_lo"]
        xk_hi, xk_lo = bufs["xk_hi"], bufs["xk_lo"]
        ka_hi, ka_lo = bufs["ka_hi"], bufs["ka_lo"]
        r_sig, kf, of, pb = bufs["r_sig"], bufs["kf"], bufs["of"], bufs["pb"]
        side = list(side) if side else []
        kmark = {"dve": 0, "act": 0}

        def sweep(proj, src_hi, src_lo, evac):
            whi = w[proj + "_hi"]
            for o in range(NCH):
                lo_ap = w["lo_load"](proj, o)
                bank = pb[o % 2]
                P.wait("tensor", s["dve"], val=bufs["bank_free_dve"][o % 2])
                P.wait("tensor", s["act"], val=bufs["bank_free_act"][o % 2])
                mms = ([(whi[:, k * E + o * 128: k * E + o * 128 + 128],
                         src_hi[:, k * T:(k + 1) * T]) for k in range(NCH)] +
                       [(whi[:, k * E + o * 128: k * E + o * 128 + 128],
                         src_lo[:, k * T:(k + 1) * T]) for k in range(NCH)] +
                       [(lo_ap[:, k * 128:(k + 1) * 128],
                         src_hi[:, k * T:(k + 1) * T]) for k in range(NCH)])
                n = len(mms)
                for i, (wap, xap) in enumerate(mms):
                    P.op("tensor",
                         lambda e, o_=bank[:, 0:T], w_=wap, x_=xap,
                                st=(i == 0), sp2=(i == n - 1):
                         e.matmul(o_, w_, x_, start=st, stop=sp2),
                         sem=(s["pe"] if i == n - 1 else None))
                evac(o, bank)
                if side:
                    side.pop(0)()

        def evac_r(o, bank):
            P.wait("scalar", s["pe"])
            P.op("scalar", lambda e, o_=r_sig[:, o * T:(o + 1) * T], i=bank[:, 0:T]:
                 e.activation(o_, i, AF.Sigmoid, bias=0.0, scale=1.0), sem=s["act"])
            bufs["bank_free_act"][o % 2] = s["act"].n

        def evac_k(o, bank):
            P.wait("vector", s["pe"])
            P.op("vector", lambda e, o_=kf[:, 0:T], i=bank[:, 0:T]:
                 e.tensor_scalar_max(o_, i, 0.0), sem=s["dve"])
            bufs["bank_free_dve"][o % 2] = s["dve"].n
            xfb = bufs.get("kxf", bufs["xf"])
            P.op("vector", lambda e, o_=xfb[:, 0:T], i=kf[:, 0:T]:
                 e.tensor_tensor(o_, i, i, ALU.mult))
            P.op("vector", lambda e, o_=ka_hi[:, o * T:(o + 1) * T], i=xfb[:, 0:T]:
                 e.tensor_copy(o_, i))
            P.op("vector", lambda e, o_=ka_lo[:, o * T:(o + 1) * T], i=xfb[:, 0:T],
                        h=ka_hi[:, o * T:(o + 1) * T]:
                 e.tensor_tensor(o_, i, h, ALU.subtract), sem=s["dve"])
            kmark["dve"] = s["dve"].n
            kmark["act"] = s["act"].n

        def evac_v(o, bank):
            P.wait("vector", s["pe"])
            P.wait("vector", s["act"])
            slot = o % 2
            if out_ap_fn is None:
                of_ap = of[:, slot * T:(slot + 1) * T]
                if out_toks is not None:
                    P.wait_tok("vector", out_toks[slot])
            else:
                of_ap = out_ap_fn(o)
            P.op("vector", lambda e, o_=kf[:, 0:T], i=bank[:, 0:T],
                        r=r_sig[:, o * T:(o + 1) * T]:
                 e.tensor_tensor(o_, i, r, ALU.mult), sem=s["dve"])
            bufs["bank_free_dve"][o % 2] = s["dve"].n
            P.op("vector", lambda e, o_=of_ap, i=kf[:, 0:T], r=resid_ap_fn(o):
                 e.scalar_tensor_tensor(o_, i, 0.5, r, ALU.mult, ALU.add),
                 sem=s["dve"])
            if out_fn is not None:
                out_fn(o, of_ap)

        sweep("wr", xr_hi, xr_lo, evac_r)
        sweep("wk", xk_hi, xk_lo, evac_k)
        P.wait("tensor", s["dve"], val=kmark["dve"])
        P.wait("tensor", s["act"], val=kmark["act"])
        sweep("wv", ka_hi, ka_lo, evac_v)
        while side:
            side.pop(0)()

    def emit_mix_chunk(bufs, c, x_c, T, carry_init_fn, pre_normalized):
        y_ext, ytmp, dbuf, xf = bufs["y_ext"], bufs["ytmp"], bufs["dbuf"], bufs["xf"]
        carry_init_fn(c)
        if pre_normalized:
            P.op("vector", lambda e, x=x_c, T=T: e.tensor_copy(y_ext[:, 1:T + 1], x))
        else:
            P.op("vector", lambda e, x=x_c, T=T, b=bufs["bci"][:, 0:T]:
                 e.tensor_tensor(ytmp[:, 0:T], x, b, ALU.mult))
            P.op("vector", lambda e, T=T, b=bufs["bcn"][:, 0:T]:
                 e.tensor_tensor(y_ext[:, 1:T + 1], ytmp[:, 0:T], b, ALU.add))
        P.op("vector", lambda e, T=T: e.tensor_tensor(
            dbuf[:, 0:T], y_ext[:, 1:T + 1], y_ext[:, 0:T], ALU.subtract))
        for (mu, hi, lo) in ((mur_s, bufs["xr_hi"], bufs["xr_lo"]),
                             (muk_s, bufs["xk_hi"], bufs["xk_lo"])):
            P.op("vector", lambda e, T=T, m=mu[:, c:c + 1]:
                 e.scalar_tensor_tensor(xf[:, 0:T], dbuf[:, 0:T], m,
                                        y_ext[:, 0:T], ALU.mult, ALU.add))
            P.op("vector", lambda e, T=T, o=hi[:, c * T:(c + 1) * T]:
                 e.tensor_copy(o, xf[:, 0:T]))
            P.op("vector", lambda e, T=T, o=lo[:, c * T:(c + 1) * T],
                        h=hi[:, c * T:(c + 1) * T]:
                 e.tensor_tensor(o, xf[:, 0:T], h, ALU.subtract), sem=s["dve"])
        P.op("vector", lambda e, c=c, T=T: e.tensor_copy(
            bufs["ycar"][:, c:c + 1], y_ext[:, T:T + 1]), sem=s["dve"])

    def emit_mix(bufs, x_src_fn, tokw, carry_init_fn, pre_normalized):
        for c in range(NCH):
            emit_mix_chunk(bufs, c, x_src_fn(c), tokw, carry_init_fn,
                           pre_normalized)

    with ExitStack() as ctx:
        with ExitStack() as yctx:
            xprow_s = sb(yctx, "xprow_s", [1, E], F32)
            yprow_s = sb(yctx, "yprow_s", [1, E], F32)
            pscal_s = sb(yctx, "pscal_s", [1, 8], F32)
            ypcol_tok = emit_yprev(xprow_s, yprow_s, pscal_s)
        wtmp = sb(ctx, "p0_wtmp", [128, E], F32)
        lotmp = sb(ctx, "p0_lotmp", [128, E], BF16)
        wr_hi_s = sb(ctx, "wr_hi_s", [128, NCH * E], BF16)
        wk_hi_s = sb(ctx, "wk_hi_s", [128, NCH * E], BF16)
        wv_hi_s = sb(ctx, "wv_hi_s", [128, NCH * E], BF16)
        prep_toks = {"wr": [], "wk": [], "wv": []}

        xs_b = [sb(ctx, f"a1_xs{i}", [128, NCH * TOKA], F32) for i in range(2)]
        xtm_b = [sb(ctx, f"a1_xtm{i}", [128, NSUBA * 1024], F32) for i in range(2)]
        xsq = sb(ctx, "a1_xsq", [128, 1024], F32)
        pt_sb = sb(ctx, "a1_ptsb", [128, NSUBA * 1024], F32)
        st1 = mk_stats(ctx, "a1_st1", NSUBA)
        st2 = mk_stats(ctx, "a1_st2", NSUBA)
        bci1_b = [sb(ctx, f"a1_bci{i}", [128, TOKA], F32) for i in range(2)]
        bcn1_b = [sb(ctx, f"a1_bcn{i}", [128, TOKA], F32) for i in range(2)]
        bci2 = sb(ctx, "a1_bci2", [128, TOKA], F32)
        bcn2 = sb(ctx, "a1_bcn2", [128, TOKA], F32)
        y_ext = sb(ctx, "a1_yext", [128, TOKA + 1], F32)
        ytmp = sb(ctx, "a1_ytmp", [128, TOKA], F32)
        dbuf = sb(ctx, "a1_dbuf", [128, TOKA], F32)
        xf = sb(ctx, "a1_xf", [128, TOKA], F32)
        kxf = sb(ctx, "a1_kxf", [128, TOKA], F32)
        ycar = sb(ctx, "a1_ycar", [128, NCH], F32)
        xr_hi_b = [sb(ctx, f"a1_xrh{i}", [128, NCH * TOKA], BF16) for i in range(2)]
        xr_lo_b = [sb(ctx, f"a1_xrl{i}", [128, NCH * TOKA], BF16) for i in range(2)]
        xk_hi_b = [sb(ctx, f"a1_xkh{i}", [128, NCH * TOKA], BF16) for i in range(2)]
        xk_lo_b = [sb(ctx, f"a1_xkl{i}", [128, NCH * TOKA], BF16) for i in range(2)]
        ka_hi = sb(ctx, "a1_ka_hi", [128, NCH * TOKA], BF16)
        ka_lo = sb(ctx, "a1_ka_lo", [128, NCH * TOKA], BF16)
        r_sig = sb(ctx, "a1_rsig", [128, NCH * TOKA], F32)
        kf = sb(ctx, "a1_kf", [128, TOKA], F32)
        of = sb(ctx, "a1_of", [128, 2 * TOKA], F32)
        r1t_b = [sb(ctx, f"a1_r1{i}", [128, NCH * TOKA], F32) for i in range(2)]
        wlo_ring = sb(ctx, "a1_wlo", [128, NSLOTA * NCH * 128], BF16)
        x2h = sb(ctx, "a1_x2h", [128, 2 * TOKA], BF16)
        x2l = sb(ctx, "a1_x2l", [128, 2 * TOKA], BF16)
        pb = [ps(ctx, "a1_pb0", [128, TOKA], F32), ps(ctx, "a1_pb1", [128, TOKA], F32)]
        bci_ps = ps(ctx, "a1_bci_ps", [128, TOKA], F32)
        bcn_ps = ps(ctx, "a1_bcn_ps", [128, TOKA], F32)
        pt_ps = ps(ctx, "a1_pt_ps", [128, 1024], F32)

        wlo_sps = {"wr": wr_lo, "wk": wk_lo, "wv": wv_lo}
        ring_state = {"i": 0, "loaded": {}}
        x2_free = [None, None]
        r1_sp_tok = [None, None]
        t_xs = [None, None]
        t_xtm = [None, None]
        ffn_pe_mark = [0, 0]
        ffn_dve_mark = [0, 0]
        mix_mark = [{"dve": 0, "act": 0}, {"dve": 0, "act": 0}]

        def lo_load(proj, o):
            slot = ring_state["i"] % NSLOTA
            ring_state["i"] += 1
            base = slot * NCH * 128
            src = _ap3(wlo_sps[proj], o * NCH * 128 * 128,
                       [[NCH * 128, 128], [128, NCH], [1, 128]])
            for pt_ in prep_toks[proj]:
                P.wait_tok("sync", pt_)
            P.wait("sync", s["pe"], val=ring_state["loaded"].get(slot, 0))
            t = P.dmas("sync", [(wlo_ring[:, base:base + NCH * 128], src)])
            P.wait_tok("tensor", t)
            ring_state["loaded"][slot] = s["pe"].n + 1
            return wlo_ring[:, base:base + NCH * 128]

        wdict = {"wr_hi": wr_hi_s, "wk_hi": wk_hi_s, "wv_hi": wv_hi_s,
                 "lo_load": lo_load}
        bank_free_dve = [0, 0]
        bank_free_act = [0, 0]
        bufs_u = []
        for i in range(2):
            bufs_u.append({
                "xr_hi": xr_hi_b[i], "xr_lo": xr_lo_b[i],
                "xk_hi": xk_hi_b[i], "xk_lo": xk_lo_b[i],
                "ka_hi": ka_hi, "ka_lo": ka_lo, "r_sig": r_sig, "kf": kf,
                "kxf": kxf, "of": of, "pb": pb, "y_ext": y_ext, "ytmp": ytmp,
                "dbuf": dbuf, "xf": xf, "ycar": ycar,
                "bci": bci1_b[i], "bcn": bcn1_b[i],
                "bank_free_dve": bank_free_dve, "bank_free_act": bank_free_act})

        def emit_tile_loads(jj):
            u2 = jj % 2
            T0j = jj * TOKA
            P.wait("scalar", s["dve"], val=ffn_dve_mark[u2])
            t_xtm[u2] = P.dmas("scalar", [
                (xtm_b[u2][:, :].rearrange("p (s f) -> p s f", s=NSUBA),
                 _ap3(x_tm, T0j * E, [[E, 128], [128 * E, NSUBA], [1, E]]))])
            t_xs[u2] = P.dmas("scalar", [
                (xs_b[u2][:, :].rearrange("p (c t) -> p c t", c=NCH),
                 _ap3(x_fm, T0j, [[S, 128], [128 * S, NCH], [1, TOKA]]))])

        def mk_carry(jj):
            def carry_init(c, jj=jj):
                if jj == 0:
                    P.op("vector", lambda e, c=c: e.tensor_copy(
                        y_ext[:, 0:1], ypcol_s[:, c:c + 1]))
                elif jj == OWN // TOKA:
                    P.op("vector", lambda e, c=c: e.tensor_tensor(
                        y_ext[:, 0:1], ycar[:, c:c + 1], m2048_s[:, 0:1],
                        ALU.mult))
                else:
                    P.op("vector", lambda e, c=c: e.tensor_copy(
                        y_ext[:, 0:1], ycar[:, c:c + 1]))
            return carry_init

        def prod_thunks(jj):
            u2 = jj % 2
            th = []
            for sub in range(NSUBA):
                def t_stat(sub=sub, u2=u2):
                    P.wait_tok("scalar", t_xtm[u2])
                    P.wait_tok("vector", t_xtm[u2])
                    stat_sub(st1, xtm_b[u2][:, sub * 1024:(sub + 1) * 1024],
                             xsq, sub)
                th.append(t_stat)
            th.append(lambda: ln_stats_chain(st1, NSUBA))

            def t_bc1(u2=u2):
                bc_tiles(st1, NSUBA, bci_ps, bcn_ps)
                P.wait("vector", s["pe"])
                P.op("vector", lambda e: e.tensor_copy(bci1_b[u2][:, :],
                                                       bci_ps[:, 0:TOKA]))
                P.op("vector", lambda e: e.tensor_copy(bcn1_b[u2][:, :],
                                                       bcn_ps[:, 0:TOKA]),
                     sem=s["dve"])
                P.drain("vector")
            th.append(t_bc1)
            ci = mk_carry(jj)
            for c in range(NCH):
                def t_mix(c=c, u2=u2, ci=ci):
                    P.wait("vector", s["pe"], val=ffn_pe_mark[u2])
                    P.wait_tok("vector", t_xs[u2])
                    emit_mix_chunk(bufs_u[u2], c,
                                   xs_b[u2][:, c * TOKA:(c + 1) * TOKA],
                                   TOKA, ci, False)
                    if c == NCH - 1:
                        mix_mark[u2]["dve"] = s["dve"].n
                        mix_mark[u2]["act"] = s["act"].n
                th.append(t_mix)
            return th

        def ln2_stat_thunks(jj):
            th = []
            for sub in range(NSUBA):
                def t_l(sub=sub):
                    P.wait("vector", s["act"])
                    stat_sub(st2, pt_sb[:, sub * 1024:(sub + 1) * 1024],
                             xsq, sub)
                th.append(t_l)
            th.append(lambda: ln_stats_chain(st2, NSUBA))

            def t_bc2():
                bc_tiles(st2, NSUBA, bci_ps, bcn_ps)
                P.wait("vector", s["pe"])
                P.op("vector", lambda e: e.tensor_copy(bci2[:, :],
                                                       bci_ps[:, 0:TOKA]))
                P.op("vector", lambda e: e.tensor_copy(bcn2[:, :],
                                                       bcn_ps[:, 0:TOKA]),
                     sem=s["dve"])
                P.drain("vector")
            th.append(t_bc2)
            return th

        def ln2_split_thunks(jj):
            u = jj % 2
            T0j = jj * TOKA
            th = []
            for c in range(NCH):
                def t_sp(c=c, u=u, T0j=T0j):
                    slot = c % 2
                    sl = slice(slot * TOKA, (slot + 1) * TOKA)
                    P.op("vector", lambda e, c=c, u=u: e.tensor_tensor(
                        ytmp[:, 0:TOKA], r1t_b[u][:, c * TOKA:(c + 1) * TOKA],
                        bci2[:, :], ALU.mult))
                    P.op("vector", lambda e: e.tensor_tensor(
                        xf[:, 0:TOKA], ytmp[:, 0:TOKA], bcn2[:, :], ALU.add))
                    P.wait_tok("vector", x2_free[slot])
                    P.op("vector", lambda e, sl=sl: e.tensor_copy(
                        x2h[:, sl], xf[:, 0:TOKA]))
                    P.op("vector", lambda e, sl=sl: e.tensor_tensor(
                        x2l[:, sl], xf[:, 0:TOKA], x2h[:, sl], ALU.subtract),
                        sem=s["dve"])
                    P.wait("gpsimd", s["dve"])
                    x2_free[slot] = P.dmas("gpsimd", [
                        (x2h_sp.ap()[c * 128:(c + 1) * 128, T0j:T0j + TOKA],
                         x2h[:, sl]),
                        (x2l_sp.ap()[c * 128:(c + 1) * 128, T0j:T0j + TOKA],
                         x2l[:, sl])])
                th.append(t_sp)
            return th

        def emit_ln2_transposes(jj):
            u = jj % 2
            for sub in range(NSUBA):
                P.wait("tensor", s["dve"])
                P.wait("tensor", s["act"])
                for c in range(NCH):
                    P.op("tensor", lambda e, u=u,
                                o=pt_ps[:, c * 128:(c + 1) * 128],
                                i=r1t_b[u][:, c * TOKA + sub * 128:
                                            c * TOKA + sub * 128 + 128]:
                         e.transpose(o, i, id_f32_s[:, :]),
                         sem=(s["pe"] if c == NCH - 1 else None))
                P.wait("scalar", s["pe"])
                P.wait("scalar", s["dve"])
                P.op("scalar", lambda e, sub=sub: e.activation(
                    pt_sb[:, sub * 1024:(sub + 1) * 1024], pt_ps[:, 0:1024],
                    AF.Identity, bias=0.0, scale=1.0), sem=s["act"])

        NTA_run = NTA if phases.get("A1") in (None, 0) else min(phases["A1"], NTA)
        side0 = []
        if "A1" in phases and NTA_run:
            emit_tile_loads(0)
            P.wait_tok("vector", ypcol_tok)
            side0 = prod_thunks(0)
        wp_tok = None
        for (pj, wd, hs, lsp, hsp) in (("wr", wrT, wr_hi_s, wr_lo, wr_hi_sp),
                                       ("wk", wkT, wk_hi_s, wk_lo, wk_hi_sp),
                                       ("wv", wvT, wv_hi_s, wv_lo, wv_hi_sp)):
            wp_tok = emit_weight_prep(wd.ap(), E, hs, lsp, hsp.ap(), wtmp,
                                      lotmp, prev_tok=wp_tok,
                                      tok_sink=prep_toks[pj], side=side0)
        while side0:
            side0.pop(0)()
        if "A1" in phases and NTA_run:
            P.wait("tensor", s["act"])
            for j in range(NTA_run):
                u = j % 2
                if j + 1 < NTA_run:
                    emit_tile_loads(j + 1)
                pt = prod_thunks(j + 1) if j + 1 < NTA_run else None
                lt = ln2_stat_thunks(j - 1) if j >= 1 else None
                sp = ln2_split_thunks(j - 1) if j >= 1 else None
                side = []
                if lt:
                    side += lt[:2]
                if pt:
                    side += pt[:2]
                if lt:
                    side.append(lt[2])
                if pt:
                    side.append(pt[2])
                if lt:
                    side.append(lt[3])
                if pt:
                    side.append(pt[3])
                for c in range(NCH):
                    if sp:
                        side.append(sp[c])
                    if pt:
                        side.append(pt[4 + c])
                P.wait("tensor", s["dve"], val=mix_mark[u]["dve"])
                P.wait("tensor", s["act"], val=mix_mark[u]["act"])
                P.wait_tok("vector", r1_sp_tok[u])
                emit_ffn(bufs_u[u], wdict, TOKA,
                         (lambda o, u=u: xs_b[u][:, o * TOKA:(o + 1) * TOKA]),
                         None, None, side=side,
                         out_ap_fn=(lambda o, u=u:
                                    r1t_b[u][:, o * TOKA:(o + 1) * TOKA]))
                ffn_pe_mark[u] = s["pe"].n
                ffn_dve_mark[u] = s["dve"].n
                emit_ln2_transposes(j)
                P.wait("gpsimd", s["dve"])
                r1_sp_tok[u] = P.dmas("gpsimd", [
                    (_ap3(r1_sp, j * TOKA, [[S, 128], [128 * S, NCH], [1, TOKA]]),
                     r1t_b[u][:, :].rearrange("p (c t) -> p c t", c=NCH))])
            for t in ln2_stat_thunks(NTA_run - 1) + ln2_split_thunks(NTA_run - 1):
                t()
        phase_barrier()
        P.flush()


    if "A2" in phases:
      with ExitStack() as ctx:
        wq_hi_s = sb(ctx, "wq_hi_s", [128, NCH * M3], BF16)
        with ExitStack() as wctx:
            wtmp = sb(wctx, "p1_wtmp", [128, M3], F32)
            lotmp = sb(wctx, "p1_lotmp", [128, M3], BF16)
            emit_weight_prep(wqkv.ap(), M3, wq_hi_s, wq_lo, None, wtmp, lotmp)
        phase_barrier()

        x2ht_b = [sb(ctx, f"a2_x2h{i}", [128, NCH * TOK], BF16) for i in range(2)]
        x2lt_b = [sb(ctx, f"a2_x2l{i}", [128, NCH * TOK], BF16) for i in range(2)]
        qf = sb(ctx, "a2_qf", [128, TOK], F32)
        sth = sb(ctx, "a2_sth", [128, TOK], BF16)
        stl = sb(ctx, "a2_stl", [128, TOK], BF16)
        vst = sb(ctx, "a2_vst", [128, 2 * 128], BF16)
        wlo_ring = sb(ctx, "a2_wlo", [128, 2 * NCH * 128], BF16)
        pb = [ps(ctx, "a2_pb0", [128, TOK], F32), ps(ctx, "a2_pb1", [128, TOK], F32)]
        ptb = [ps(ctx, "a2_pt0", [128, 128], BF16), ps(ctx, "a2_pt1", [128, 128], BF16)]

        ring_state = {"i": 0, "loaded": {}}

        def qkv_lo_load(o):
            slot = ring_state["i"] % 2
            ring_state["i"] += 1
            base = slot * NCH * 128
            src = _ap3(wq_lo, o * NCH * 128 * 128,
                       [[NCH * 128, 128], [128, NCH], [1, 128]])
            P.wait("sync", s["pe"], val=ring_state["loaded"].get(slot, 0))
            t = P.dmas("sync", [(wlo_ring[:, base:base + NCH * 128], src)])
            P.wait_tok("tensor", t)
            ring_state["loaded"][slot] = s["pe"].n + 1
            return wlo_ring[:, base:base + NCH * 128]

        bank_free = {"dve": [0, 0], "act": [0, 0]}
        ptk_free = [0, 0]
        vst_free = [None, None]
        qk_sp_tok = [None]

        tile_pe_free = [0, 0]
        tile_tok = [None, None]

        def a2_load(j):
            u = j % 2
            T0j = j * TOK
            P.wait("sync", s["pe"], val=tile_pe_free[u])
            return P.dmas("sync", [
                (x2ht_b[u][:, :].rearrange("p (c t) -> p c t", c=NCH),
                 _ap3(x2h_sp, T0j, [[S, 128], [128 * S, NCH], [1, TOK]])),
                (x2lt_b[u][:, :].rearrange("p (c t) -> p c t", c=NCH),
                 _ap3(x2l_sp, T0j, [[S, 128], [128 * S, NCH], [1, TOK]]))])

        tile_tok[0] = a2_load(0)
        for j in range(NT):
            T0 = j * TOK
            u = j % 2
            if j + 1 < NT:
                tile_tok[(j + 1) % 2] = a2_load(j + 1)
            x2ht, x2lt = x2ht_b[u], x2lt_b[u]
            P.wait_tok("tensor", tile_tok[u])
            for o in range(24):
                is_q = o < 8
                is_v = o >= 16
                if is_q:
                    if T0 < OWN:
                        cs, cw = 0, TOK
                    elif T0 + TOK == S:
                        cs, cw = XQ0 - T0, S - XQ0
                    else:
                        continue
                else:
                    cs, cw = 0, TOK
                lo_ap = qkv_lo_load(o)
                bank = pb[o % 2]
                P.wait("tensor", s["dve"], val=bank_free["dve"][o % 2])
                P.wait("tensor", s["act"], val=bank_free["act"][o % 2])
                mms = ([(wq_hi_s[:, k * M3 + o * 128: k * M3 + o * 128 + 128],
                         x2ht[:, k * TOK + cs: k * TOK + cs + cw]) for k in range(NCH)] +
                       [(wq_hi_s[:, k * M3 + o * 128: k * M3 + o * 128 + 128],
                         x2lt[:, k * TOK + cs: k * TOK + cs + cw]) for k in range(NCH)] +
                       [(lo_ap[:, k * 128:(k + 1) * 128],
                         x2ht[:, k * TOK + cs: k * TOK + cs + cw]) for k in range(NCH)])
                n = len(mms)
                for i, (wap, xap) in enumerate(mms):
                    P.op("tensor",
                         lambda e, o_=bank[:, 0:cw], w_=wap, x_=xap,
                                st=(i == 0), sp2=(i == n - 1):
                         e.matmul(o_, w_, x_, start=st, stop=sp2),
                         sem=(s["pe"] if i == n - 1 else None))
                P.wait("vector", s["pe"])
                P.wait_tok("vector", qk_sp_tok[0])
                if is_q:
                    P.op("vector", lambda e, o_=qf[:, 0:cw], i=bank[:, 0:cw],
                                bb=bqkv_s[:, o:o + 1]:
                         e.tensor_scalar(o_, i, bb, 0.125, ALU.add, ALU.mult),
                         sem=s["dve"])
                else:
                    P.op("vector", lambda e, o_=qf[:, 0:cw], i=bank[:, 0:cw],
                                bb=bqkv_s[:, o:o + 1]:
                         e.tensor_scalar(o_, i, bb, None, ALU.add), sem=s["dve"])
                bank_free["dve"][o % 2] = s["dve"].n
                P.op("vector", lambda e, o_=sth[:, 0:cw], i=qf[:, 0:cw]:
                     e.tensor_copy(o_, i))
                P.op("vector", lambda e, o_=stl[:, 0:cw], i=qf[:, 0:cw], h=sth[:, 0:cw]:
                     e.tensor_tensor(o_, i, h, ALU.subtract), sem=s["dve"])
                if not is_v:
                    sp_h = qh_sp if is_q else kh_sp
                    sp_l = ql_sp if is_q else kl_sp
                    row = (o if is_q else o - 8) * 128
                    P.wait("gpsimd", s["dve"])
                    qk_sp_tok[0] = P.dmas("gpsimd", [
                        (sp_h.ap()[row:row + 128, T0 + cs:T0 + cs + cw], sth[:, 0:cw]),
                        (sp_l.ap()[row:row + 128, T0 + cs:T0 + cs + cw], stl[:, 0:cw])])
                else:
                    hA = (o - 16) * 2
                    P.wait("tensor", s["act"])
                    P.wait("tensor", s["dve"])
                    for sub in range(NSUB):
                        for hl, stt in ((0, sth), (1, stl)):
                            pslot = (sub * 2 + hl) % 2
                            ptk = ptb[pslot]
                            P.wait("tensor", s["dve"], val=ptk_free[pslot])
                            P.op("tensor", lambda e, o_=ptk[:, :],
                                        i=stt[:, sub * 128:(sub + 1) * 128]:
                                 e.transpose(o_, i, id_bf_s[:, :]), sem=s["pe"])
                            P.wait("vector", s["pe"])
                            P.wait_tok("vector", vst_free[pslot])
                            vsl = vst[:, pslot * 128:(pslot + 1) * 128]
                            P.op("vector", lambda e, o_=vsl, i=ptk[:, :]:
                                 e.tensor_copy(o_, i), sem=s["dve"])
                            ptk_free[pslot] = s["dve"].n
                            P.wait("gpsimd", s["dve"])
                            spv = vth_sp if hl == 0 else vtl_sp
                            dst = _ap3(spv, hA * S * HD + (T0 + sub * 128) * HD,
                                       [[HD, 128], [S * HD, 2], [1, HD]])
                            vst_free[pslot] = P.dmas("gpsimd", [
                                (dst, vsl.rearrange("p (h d) -> p h d", h=2))])
            tile_pe_free[u] = s["pe"].n
        phase_barrier()
        P.flush()

    QGS = [(0, 512), (512, 512), (1024, 512), (1536, 512), (2048, XQW)]
    if "B" in phases:
      with ExitStack() as ctx:
        kp_hi = [sb(ctx, f"b_kph{i}", [128, S], BF16) for i in range(2)]
        kp_lo = [sb(ctx, f"b_kpl{i}", [128, S], BF16) for i in range(2)]
        qp_hi = [sb(ctx, f"b_qph{i}", [128, QTOT], BF16) for i in range(2)]
        qp_lo = [sb(ctx, f"b_qpl{i}", [128, QTOT], BF16) for i in range(2)]
        vA_hi = [sb(ctx, f"b_vah{i}", [128, NKC * HD], BF16) for i in range(2)]
        vA_lo = [sb(ctx, f"b_val{i}", [128, NKC * HD], BF16) for i in range(2)]
        vB_hi = [sb(ctx, f"b_vbh{i}", [128, NKC * HD], BF16) for i in range(2)]
        vB_lo = [sb(ctx, f"b_vbl{i}", [128, NKC * HD], BF16) for i in range(2)]
        paf = [sb(ctx, f"b_paf{i}", [128, 512], F32) for i in range(2)]
        pbf = [sb(ctx, f"b_pbf{i}", [128, 512], F32) for i in range(2)]
        pah = [sb(ctx, f"b_pah{i}", [128, 512], BF16) for i in range(2)]
        pal = [sb(ctx, f"b_pal{i}", [128, 512], BF16) for i in range(2)]
        pbh = [sb(ctx, f"b_pbh{i}", [128, 512], BF16) for i in range(2)]
        pbl = [sb(ctx, f"b_pbl{i}", [128, 512], BF16) for i in range(2)]
        dsb = sb(ctx, "b_dsb", [128, 512], F32)
        binv = sb(ctx, "b_binv", [128, 512], F32)
        o_f = sb(ctx, "b_of", [128, 512], F32)
        o_hi = sb(ctx, "b_ohi", [128, 512], BF16)
        o_lo = sb(ctx, "b_olo", [128, 512], BF16)
        st_ps = [(ps(ctx, "b_stA0", [128, 512], F32), ps(ctx, "b_stB0", [128, 512], F32)),
                 (ps(ctx, "b_stA1", [128, 512], F32), ps(ctx, "b_stB1", [128, 512], F32))]
        pv = ps(ctx, "b_pv", [128, 512], F32)
        dps = ps(ctx, "b_dps", [128, 512], F32)
        ddp = ps(ctx, "b_ddp", [128, 512], F32)
        o_sp_tok = [None]
        P.op("vector", lambda e: e.memset(dsb[:, :], 0.0))
        for t in (st_ps[0][0], st_ps[0][1], st_ps[1][0], st_ps[1][1], pv, dps, ddp):
            P.op("vector", lambda e, t=t: e.memset(t[:, :], 0.0))
        P.op("vector", lambda e: e.engine_nop(), sem=s["dve"])

        pair_free_pe = [0, 0]
        pair_tok = [None, None]
        g_exp = [0, 0]
        g_lo = [0, 0]
        g_castA = [0, 0]
        g_pv = [0, 0]

        def emit_pair_loads(pc):
            u = pc % 2
            grp = []
            for dst, spv in ((kp_hi[u], kh_sp), (kp_lo[u], kl_sp)):
                grp.append((dst[:, :], spv.ap()[pc * 128:(pc + 1) * 128, :]))
            for dst, spv in ((qp_hi[u], qh_sp), (qp_lo[u], ql_sp)):
                grp.append((dst[:, 0:OWN], spv.ap()[pc * 128:(pc + 1) * 128, 0:OWN]))
                grp.append((dst[:, OWN:QTOT], spv.ap()[pc * 128:(pc + 1) * 128, XQ0:S]))
            for dst, spv, h in ((vA_hi[u], vth_sp, 2 * pc), (vA_lo[u], vtl_sp, 2 * pc),
                                (vB_hi[u], vth_sp, 2 * pc + 1), (vB_lo[u], vtl_sp, 2 * pc + 1)):
                grp.append((dst[:, :].rearrange("p (c d) -> p c d", c=NKC),
                            _ap3(spv, h * S * HD, [[HD, 128], [128 * HD, NKC], [1, HD]])))
            P.wait("sync", s["pe"], val=pair_free_pe[u])
            return P.dmas("sync", grp)

        pair_tok[0] = emit_pair_loads(0)
        for pc in range(8):
            u = pc % 2
            if pc + 1 < 8:
                pair_tok[(pc + 1) % 2] = emit_pair_loads(pc + 1)
            P.wait_tok("tensor", pair_tok[u])
            kph, kpl = kp_hi[u], kp_lo[u]
            qph, qpl = qp_hi[u], qp_lo[u]
            vah, val_, vbh, vbl = vA_hi[u], vA_lo[u], vB_hi[u], vB_lo[u]

            for (g0, gw) in QGS:
                lo_done = {}
                castA_done = {}

                def emit_pv(cc, g0=g0, gw=gw, vah=vah, val_=val_, vbh=vbh, vbl=vbl):
                    v = cc % 2
                    P.wait("tensor", s["dve"], val=lo_done[cc])
                    P.wait("tensor", s["act"], val=castA_done[cc])
                    pvs = [(vah, pah[v]), (val_, pah[v]), (vah, pal[v])]
                    pvsB = [(vbh, pbh[v]), (vbl, pbh[v]), (vbh, pbl[v])]
                    st_c, sp_c = (cc == 0), (cc == NKC - 1)
                    for i in range(3):
                        va_, pa_ = pvs[i]
                        vb_, pb_ = pvsB[i]
                        st_i = st_c and (i == 0)
                        sp_i = sp_c and (i == 2)
                        P.op("tensor", lambda e, w_=va_[:, cc * HD:(cc + 1) * HD],
                                    x_=pa_[:, 0:gw], a=st_i, b_=sp_i:
                             e.matmul(pv[0:64, 0:gw], w_, x_, start=a, stop=b_,
                                      tile_position=(0, 0), skip_group_check=True))
                        P.op("tensor", lambda e, w_=vb_[:, cc * HD:(cc + 1) * HD],
                                    x_=pb_[:, 0:gw], a=st_i, b_=sp_i:
                             e.matmul(pv[64:128, 0:gw], w_, x_, start=a, stop=b_,
                                      tile_position=(0, 64), skip_group_check=True))
                    dmov = [(pah[v], 0), (pal[v], 32), (pbh[v], 64), (pbl[v], 96)]
                    for i, (pm, col) in enumerate(dmov):
                        P.op("tensor", lambda e, o_=dps[col:col + 1, 0:gw],
                                    x_=pm[:, 0:gw], a=st_c, b_=sp_c, col=col:
                             e.matmul(o_, ones_bf_s[:, :], x_, start=a, stop=b_,
                                      tile_position=(0, col), skip_group_check=True),
                             sem=(s["pe"] if i == len(dmov) - 1 else None))
                    g_pv[v] = s["pe"].n

                for c in range(NKC):
                    v = c % 2
                    stA, stB = st_ps[v]
                    P.wait("tensor", s["act"], val=g_exp[v])
                    qk = [(kph, qph), (kph, qpl), (kpl, qph)]
                    for i, (kk, qq) in enumerate(qk):
                        st_, sp2_ = (i == 0), (i == len(qk) - 1)
                        P.op("tensor", lambda e, o_=stA[:, 0:gw],
                                    w_=kk[0:64, c * 128:(c + 1) * 128],
                                    x_=qq[0:64, g0:g0 + gw], a=st_, b_=sp2_:
                             e.matmul(o_, w_, x_, start=a, stop=b_,
                                      tile_position=(0, 0)))
                        P.op("tensor", lambda e, o_=stB[:, 0:gw],
                                    w_=kk[64:128, c * 128:(c + 1) * 128],
                                    x_=qq[64:128, g0:g0 + gw], a=st_, b_=sp2_:
                             e.matmul(o_, w_, x_, start=a, stop=b_,
                                      tile_position=(64, 0)),
                             sem=(s["pe"] if sp2_ else None))
                    qk_done_c = s["pe"].n
                    if c >= 1:
                        emit_pv(c - 1)
                    P.wait("scalar", s["pe"], val=qk_done_c)
                    P.wait("scalar", s["dve"], val=g_lo[v])
                    P.op("scalar", lambda e, v=v, gw=gw, sA=stA: e.activation(
                        paf[v][:, 0:gw], sA[:, 0:gw], AF.Exp, bias=0.0, scale=1.0))
                    P.op("scalar", lambda e, v=v, gw=gw, sB=stB: e.activation(
                        pbf[v][:, 0:gw], sB[:, 0:gw], AF.Exp, bias=0.0, scale=1.0),
                        sem=s["act"])
                    g_exp[v] = s["act"].n
                    exp_done_c = s["act"].n
                    P.wait("scalar", s["pe"], val=g_pv[v])
                    P.op("scalar", lambda e, v=v, gw=gw: e.activation(
                        pah[v][:, 0:gw], paf[v][:, 0:gw], AF.Identity,
                        bias=0.0, scale=1.0), sem=s["act"])
                    castA_done[c] = s["act"].n
                    P.wait("vector", s["act"], val=exp_done_c)
                    P.wait("vector", s["pe"], val=g_pv[v])
                    P.op("vector", lambda e, v=v, gw=gw: e.tensor_copy(
                        pbh[v][:, 0:gw], pbf[v][:, 0:gw]))
                    P.wait("vector", s["act"], val=castA_done[c])
                    P.op("vector", lambda e, v=v, gw=gw: e.tensor_tensor(
                        pal[v][:, 0:gw], paf[v][:, 0:gw], pah[v][:, 0:gw], ALU.subtract))
                    P.op("vector", lambda e, v=v, gw=gw: e.tensor_tensor(
                        pbl[v][:, 0:gw], pbf[v][:, 0:gw], pbh[v][:, 0:gw], ALU.subtract),
                        sem=s["dve"])
                    lo_done[c] = s["dve"].n
                    g_lo[v] = s["dve"].n
                emit_pv(NKC - 1)

                P.wait("vector", s["pe"])
                for drow in (0, 32, 64, 96):
                    P.op("vector", lambda e, gw=gw, dr=drow: e.tensor_copy(
                        dsb[dr:dr + 1, 0:gw], dps[dr:dr + 1, 0:gw]),
                        sem=(s["dve"] if drow == 96 else None))
                P.wait("tensor", s["dve"])
                P.op("tensor", lambda e, gw=gw: e.matmul(
                    ddp[:, 0:gw], dmask_s[:, :], dsb[:, 0:gw], start=True, stop=True),
                    sem=s["pe"])
                P.wait("vector", s["pe"])
                P.op("vector", lambda e, gw=gw: e.reciprocal(binv[:, 0:gw], ddp[:, 0:gw]),
                     sem=s["dve"])
                P.op("vector", lambda e, gw=gw: e.tensor_tensor(
                    o_f[:, 0:gw], pv[:, 0:gw], binv[:, 0:gw], ALU.mult), sem=s["dve"])
                P.wait("scalar", s["dve"])
                P.wait_tok("scalar", o_sp_tok[0])
                P.op("scalar", lambda e, gw=gw: e.activation(
                    o_hi[:, 0:gw], o_f[:, 0:gw], AF.Identity, bias=0.0, scale=1.0),
                    sem=s["act"])
                P.wait("vector", s["act"])
                P.wait_tok("vector", o_sp_tok[0])
                P.op("vector", lambda e, gw=gw: e.tensor_tensor(
                    o_lo[:, 0:gw], o_f[:, 0:gw], o_hi[:, 0:gw], ALU.subtract),
                    sem=s["dve"])
                P.wait("gpsimd", s["dve"])
                P.wait("gpsimd", s["act"])
                o_sp_tok[0] = P.dmas("gpsimd", [
                    (oh_sp.ap()[pc * 128:(pc + 1) * 128, g0:g0 + gw], o_hi[:, 0:gw]),
                    (ol_sp.ap()[pc * 128:(pc + 1) * 128, g0:g0 + gw], o_lo[:, 0:gw])])
            pair_free_pe[u] = s["pe"].n
        phase_barrier()
        P.flush()

    if "C" in phases:
      with ExitStack() as ctx:
        wo_hi_s = sb(ctx, "wo_hi_s", [128, NCH * E], BF16)
        wr_hi_s2 = sb(ctx, "wr_hi_s2", [128, NCH * E], BF16)
        wk_hi_s2 = sb(ctx, "wk_hi_s2", [128, NCH * E], BF16)
        wv_hi_s2 = sb(ctx, "wv_hi_s2", [128, NCH * E], BF16)
        with ExitStack() as wctx:
            wtmp = sb(wctx, "p2_wtmp", [128, E], F32)
            lotmp = sb(wctx, "p2_lotmp", [128, E], BF16)
            emit_weight_prep(wout.ap(), E, wo_hi_s, wo_lo, None, wtmp, lotmp)
        P.dmas("sync", [
            (hs[:, :].rearrange("p (k m) -> p k m", k=NCH),
             _ap3(hsp, 0, [[E, 128], [128 * E, NCH], [1, E]]))
            for hs, hsp in ((wr_hi_s2, wr_hi_sp), (wk_hi_s2, wk_hi_sp),
                            (wv_hi_s2, wv_hi_sp))])
        phase_barrier()

        oht = sb(ctx, "c_oh", [128, NCH * TOK], BF16)
        olt = sb(ctx, "c_ol", [128, NCH * TOK], BF16)
        r1ct = sb(ctx, "c_r1", [128, NCH * TOK], F32)
        r2t = sb(ctx, "c_r2", [128, NCH * TOK], F32)
        xsq = sb(ctx, "c_xsq", [128, 1024], F32)
        st3 = mk_stats(ctx, "c_st3", NSUB)
        bci = sb(ctx, "c_bci", [128, TOK], F32)
        bcn = sb(ctx, "c_bcn", [128, TOK], F32)
        y_ext = sb(ctx, "c_yext", [128, TOK + 1], F32)
        ytmp = sb(ctx, "c_ytmp", [128, TOK], F32)
        dbuf = sb(ctx, "c_dbuf", [128, TOK], F32)
        xf = sb(ctx, "c_xf", [128, TOK], F32)
        ycar = sb(ctx, "c_ycar", [128, NCH], F32)
        x3car = sb(ctx, "c_x3car", [128, NCH], F32)
        xr_hi = sb(ctx, "c_xr_hi", [128, NCH * TOK], BF16)
        xr_lo = sb(ctx, "c_xr_lo", [128, NCH * TOK], BF16)
        xk_hi = sb(ctx, "c_xk_hi", [128, NCH * TOK], BF16)
        xk_lo = sb(ctx, "c_xk_lo", [128, NCH * TOK], BF16)
        ka_hi = sb(ctx, "c_ka_hi", [128, NCH * TOK], BF16)
        ka_lo = sb(ctx, "c_ka_lo", [128, NCH * TOK], BF16)
        r_sig = sb(ctx, "c_rsig", [128, NCH * TOK], F32)
        kf = sb(ctx, "c_kf", [128, TOK], F32)
        of = sb(ctx, "c_of", [128, 2 * TOK], F32)
        wlo_ring = sb(ctx, "c_wlo", [128, 2 * NCH * 128], BF16)
        wolo_ring = wlo_ring
        pb = [ps(ctx, "c_pb0", [128, TOK], F32), ps(ctx, "c_pb1", [128, TOK], F32)]
        bci_ps = ps(ctx, "c_bci_ps", [128, TOK], F32)
        bcn_ps = ps(ctx, "c_bcn_ps", [128, TOK], F32)
        pt_ps = ps(ctx, "c_pt_ps", [128, 1024], F32)
        for t in (pb[0], pb[1], bci_ps, bcn_ps, pt_ps):
            P.op("vector", lambda e, t=t: e.memset(t[:, :], 0.0))
        P.op("vector", lambda e: e.memset(ycar[:, :], 0.0))
        P.op("vector", lambda e: e.memset(x3car[:, :], 0.0))
        P.op("vector", lambda e: e.engine_nop(), sem=s["dve"])

        wlo_sps = {"wr": wr_lo, "wk": wk_lo, "wv": wv_lo}
        ring_state = {"i": 0, "loaded": {}}
        wo_ring = {"i": 0, "loaded": {}}
        out_toks = [None, None]

        def lo_load(proj, o):
            slot = ring_state["i"] % 2
            ring_state["i"] += 1
            base = slot * NCH * 128
            src = _ap3(wlo_sps[proj], o * NCH * 128 * 128,
                       [[NCH * 128, 128], [128, NCH], [1, 128]])
            P.wait("sync", s["pe"], val=ring_state["loaded"].get(slot, 0))
            t = P.dmas("sync", [(wlo_ring[:, base:base + NCH * 128], src)])
            P.wait_tok("tensor", t)
            ring_state["loaded"][slot] = s["pe"].n + 1
            return wlo_ring[:, base:base + NCH * 128]

        def wo_lo_load(o):
            slot = ring_state["i"] % 2
            ring_state["i"] += 1
            base = slot * NCH * 128
            src = _ap3(wo_lo, o * NCH * 128 * 128,
                       [[NCH * 128, 128], [128, NCH], [1, 128]])
            P.wait("sync", s["pe"], val=ring_state["loaded"].get(slot, 0))
            t = P.dmas("sync", [(wolo_ring[:, base:base + NCH * 128], src)])
            P.wait_tok("tensor", t)
            ring_state["loaded"][slot] = s["pe"].n + 1
            return wolo_ring[:, base:base + NCH * 128]

        wdict = {"wr_hi": wr_hi_s2, "wk_hi": wk_hi_s2, "wv_hi": wv_hi_s2,
                 "lo_load": lo_load}
        bufs = {"xr_hi": xr_hi, "xr_lo": xr_lo, "xk_hi": xk_hi, "xk_lo": xk_lo,
                "ka_hi": ka_hi, "ka_lo": ka_lo, "r_sig": r_sig, "kf": kf,
                "of": of, "pb": pb, "y_ext": y_ext, "ytmp": ytmp, "dbuf": dbuf,
                "xf": xf, "ycar": ycar, "bci": bci, "bcn": bcn,
                "bank_free_dve": [0, 0], "bank_free_act": [0, 0],
                "of_free": [0, 0]}

        def tile_wout_ln3(oc0, tokw, r1_off):
            T = tokw
            nsub = T // 128
            P.wait("sync", s["dve"])
            P.wait("sync", s["pe"])
            t_c = P.dmas("sync", [
                (oht[:, 0:NCH * T].rearrange("p (c t) -> p c t", c=NCH),
                 _ap3(oh_sp, oc0, [[QTOT, 128], [128 * QTOT, NCH], [1, T]])),
                (olt[:, 0:NCH * T].rearrange("p (c t) -> p c t", c=NCH),
                 _ap3(ol_sp, oc0, [[QTOT, 128], [128 * QTOT, NCH], [1, T]])),
                (r1ct[:, 0:NCH * T].rearrange("p (c t) -> p c t", c=NCH),
                 _ap3(r1_sp, r1_off, [[S, 128], [128 * S, NCH], [1, T]]))])
            P.wait_tok("tensor", t_c)
            P.wait_tok("vector", t_c)
            for o in range(NCH):
                lo_ap = wo_lo_load(o)
                bank = pb[o % 2]
                P.wait("tensor", s["dve"], val=bufs["bank_free_dve"][o % 2])
                mms = ([(wo_hi_s[:, k * E + o * 128: k * E + o * 128 + 128],
                         oht[:, k * T:(k + 1) * T]) for k in range(NCH)] +
                       [(wo_hi_s[:, k * E + o * 128: k * E + o * 128 + 128],
                         olt[:, k * T:(k + 1) * T]) for k in range(NCH)] +
                       [(lo_ap[:, k * 128:(k + 1) * 128],
                         oht[:, k * T:(k + 1) * T]) for k in range(NCH)])
                n = len(mms)
                for i, (wap, xap) in enumerate(mms):
                    P.op("tensor",
                         lambda e, o_=bank[:, 0:T], w_=wap, x_=xap,
                                st=(i == 0), sp2=(i == n - 1):
                         e.matmul(o_, w_, x_, start=st, stop=sp2),
                         sem=(s["pe"] if i == n - 1 else None))
                P.wait("vector", s["pe"])
                P.op("vector", lambda e, o=o, T=T, b_=bank: e.scalar_tensor_tensor(
                    r2t[:, o * T:(o + 1) * T], b_[:, 0:T], bout_s[:, o:o + 1],
                    r1ct[:, o * T:(o + 1) * T], ALU.add, ALU.add), sem=s["dve"])
                bufs["bank_free_dve"][o % 2] = s["dve"].n
            for sub in range(nsub):
                P.wait("tensor", s["dve"])
                P.wait("tensor", s["act"])
                for c in range(NCH):
                    P.op("tensor", lambda e, o=pt_ps[:, c * 128:(c + 1) * 128],
                                i=r2t[:, c * T + sub * 128: c * T + sub * 128 + 128]:
                         e.transpose(o, i, id_f32_s[:, :]),
                         sem=(s["pe"] if c == NCH - 1 else None))
                P.wait("scalar", s["pe"])
                P.wait("scalar", s["dve"])
                P.op("scalar", lambda e: e.activation(
                    xsq[:, :], pt_ps[:, 0:1024], AF.Identity, bias=0.0, scale=1.0),
                    sem=s["act"])
                P.wait("vector", s["act"])
                P.op("vector", lambda e, o=st3["s1"][:, sub:sub + 1]:
                     e.reduce_sum(o, xsq[:, :], axis=AXX), sem=s["dve"])
                P.wait("scalar", s["dve"])
                P.op("scalar", lambda e: e.activation(
                    xsq[:, :], pt_ps[:, 0:1024], AF.Square, bias=0.0, scale=1.0),
                    sem=s["act"])
                P.wait("vector", s["act"])
                P.op("vector", lambda e, o=st3["s2"][:, sub:sub + 1]:
                     e.reduce_sum(o, xsq[:, :], axis=AXX), sem=s["dve"])
            ln_stats_chain(st3, nsub)
            bc_tiles(st3, nsub, bci_ps, bcn_ps)
            P.wait("vector", s["pe"])
            P.op("vector", lambda e, T=T: e.tensor_copy(bci[:, 0:T], bci_ps[:, 0:T]))
            P.op("vector", lambda e, T=T: e.tensor_copy(bcn[:, 0:T], bcn_ps[:, 0:T]),
                 sem=s["dve"])

        c_tiles = phases.get("C")
        if c_tiles != 0 and "NOXT" not in phases:
            tile_wout_ln3(OWN, XQW, XQ0)
        lc = XQW - 1
        for c in range(NCH if (c_tiles != 0 and "NOXT" not in phases) else 0):
            P.op("vector", lambda e, c=c: e.tensor_tensor(
                ytmp[:, 0:1], r2t[:, c * XQW + lc: c * XQW + lc + 1],
                bci[:, lc:lc + 1], ALU.mult))
            P.drain("vector")
            P.op("vector", lambda e, c=c: e.tensor_tensor(
                ytmp[:, 1:2], ytmp[:, 0:1], bcn[:, lc:lc + 1], ALU.add))
            P.drain("vector")
            P.op("vector", lambda e, c=c: e.tensor_tensor(
                x3car[:, c:c + 1], ytmp[:, 1:2], m0x_s[:, 0:1], ALU.mult),
                sem=s["dve"])

        NTC = OWN // TOK if c_tiles is None else min(c_tiles, OWN // TOK)
        for j in range(NTC):
            T0 = j * TOK
            tile_wout_ln3(T0, TOK, T0)
            if dbg and j == 0:
                P.wait("gpsimd", s["dve"])
                P.dmas("gpsimd", [(
                    _ap3(dbg_r2, 0, [[TOK, 128], [128 * TOK, NCH], [1, TOK]]),
                    r2t[:, :].rearrange("p (c t) -> p c t", c=NCH))])

            def carry_init(c, j=j):
                src = x3car if (j == 0 and "NOXT" not in phases) else ycar
                P.op("vector", lambda e, c=c, sr=src: e.tensor_copy(
                    y_ext[:, 0:1], sr[:, c:c + 1]))

            emit_mix(bufs, lambda c: r2t[:, c * TOK:(c + 1) * TOK], TOK,
                     carry_init, pre_normalized=False)
            P.wait("tensor", s["dve"])

            def out_store(o, src_ap, T0=T0):
                P.wait("gpsimd", s["dve"])
                out_toks[o % 2] = P.dmas("gpsimd", [
                    (out_fm.ap()[o * 128:(o + 1) * 128, T0:T0 + TOK], src_ap)])

            emit_ffn(bufs, wdict, TOK,
                     lambda o: r2t[:, o * TOK:(o + 1) * TOK], out_store, out_toks)

        P.wait_all_dma("gpsimd")
        P.wait_all_dma("sync")
        P.flush()

    P.flush()
    return P



_QKV_PERM = np.concatenate([
    np.concatenate([np.arange(h * 192 + j * 64, h * 192 + j * 64 + 64)
                    for h in range(H)])
    for j in range(3)])


def _col8(v):
    return np.ascontiguousarray(np.asarray(v, np.float32).reshape(NCH, 128).T)


def _core_inputs(shared, x, b, p):
    q0g = p * OWN
    xb = np.asarray(x[b], np.float32)
    xr = np.roll(xb, -q0g, axis=0)
    xprev = xb[q0g - 1:q0g] if p == 1 else np.zeros((1, E), np.float32)
    d = {
        "x_fm": np.ascontiguousarray(xr.T),
        "x_tm": np.ascontiguousarray(xr),
        "xprev": np.ascontiguousarray(xprev),
        "m2048": np.full((128, 1), 1.0 - p, np.float32),
        "m0x": np.full((128, 1), float(p), np.float32),
    }
    d.update(shared)
    return d


def kernel(L0_attn_x, L0_attn_Wqkv, L0_attn_bqkv, L0_attn_Wout, L0_attn_bout,
           L0_ffn_W_r, L0_ffn_W_k, L0_ffn_W_v, L0_ffn_mu_r, L0_ffn_mu_k):
    x = np.asarray(L0_attn_x, np.float32)
    shared = {
        "wqkv": np.ascontiguousarray(np.asarray(L0_attn_Wqkv, np.float32)[:, _QKV_PERM]),
        "bqkv": np.ascontiguousarray(
            np.asarray(L0_attn_bqkv, np.float32)[_QKV_PERM].reshape(24, 128).T),
        "wout": np.ascontiguousarray(np.asarray(L0_attn_Wout, np.float32)),
        "bout": _col8(L0_attn_bout),
        "wrT": np.ascontiguousarray(np.asarray(L0_ffn_W_r, np.float32).T),
        "wkT": np.ascontiguousarray(np.asarray(L0_ffn_W_k, np.float32).T),
        "wvT": np.ascontiguousarray(np.asarray(L0_ffn_W_v, np.float32).T),
        "mu_r": _col8(L0_ffn_mu_r),
        "mu_k": _col8(L0_ffn_mu_k),
    }

    nc = bass.Bass("TRN2", target_bir_lowering=False, detect_race_conditions=False)
    build_program(nc)

    in_maps = [_core_inputs(shared, x, c // 2, c % 2) for c in range(8)]
    res = run_bass_kernel_spmd(nc, in_maps, core_ids=list(range(8)))
    out = np.empty((B, S, E), np.float32)
    for c in range(8):
        b, p = c // 2, c % 2
        out[b, p * OWN:(p + 1) * OWN, :] = np.asarray(res.results[c]["out_fm"]).T
    return out


# version 08
# speedup vs baseline: 1.1941x; 1.0672x over previous
import numpy as np
import ml_dtypes
import concourse.bass as bass
import concourse.mybir as mybir
from concourse.bass_utils import run_bass_kernel_spmd

F32 = mybir.dt.float32
BF16 = mybir.dt.bfloat16
AF = mybir.ActivationFunctionType
ALU = mybir.AluOpType
AXX = mybir.AxisListType.X

B, S, E, H, HD = 4, 4096, 1024, 16, 64
NCH = E // 128
TOK = 512
TOKA = 256
NTA = S // TOKA
NSUBA = TOKA // 128
NSLOTA = 4
NT = S // TOK
NSUB = TOK // 128
EPS = 1e-5
OWN = 2048
XQ0, XQW = 3968, 128
QTOT = OWN + XQW
NKC = S // 128
M3 = 3 * E

BF16NP = ml_dtypes.bfloat16


class _Sem:
    def __init__(self, nc, name):
        self.h = nc.alloc_semaphore(name=name)
        self.n = 0


class PB:
    ENGS = ("sync", "gpsimd", "scalar", "vector", "tensor")

    def __init__(self, nc):
        self.nc = nc
        self.streams = {k: [] for k in self.ENGS}
        self._waited = {k: {} for k in self.ENGS}
        self._pools = {}

    def sem(self, name):
        return _Sem(self.nc, name)

    def _pool(self, eng):
        if eng not in self._pools:
            self._pools[eng] = {"sems": [self.sem(f"{eng}_dq{i}") for i in range(12)],
                                "i": 0}
        return self._pools[eng]

    def dmas(self, eng, pairs, slow=False):
        pool = self._pool(eng)
        sem = pool["sems"][pool["i"] % len(pool["sems"])]
        pool["i"] += 1
        if sem.n > 0:
            self.wait(eng, sem)
        for (dst, srcap) in pairs:
            self.op(eng, (lambda e, d=dst, s_=srcap, sl=slow:
                          e.dma_start(d, s_, allow_slow_non_contiguous=sl)),
                    sem=sem, k=16)
        return (sem, sem.n)

    def drain(self, eng):
        self.streams[eng].append(lambda e: e.drain())

    def wait_tok(self, eng, tok):
        if tok is not None:
            self.wait(eng, tok[0], tok[1])

    def wait_all_dma(self, eng_waiter):
        for qeng in list(self._pools):
            for sem in self._pools[qeng]["sems"]:
                self.wait(eng_waiter, sem)

    def wait_pool(self, eng_waiter, qeng):
        if qeng in self._pools:
            for sem in self._pools[qeng]["sems"]:
                self.wait(eng_waiter, sem)

    def op(self, eng, fn, sem=None, k=1):
        h = None
        if sem is not None:
            sem.n += k
            h = sem.h
        def run(e, fn=fn, h=h, k=k):
            inst = fn(e)
            if h is not None:
                inst.then_inc(h, k)
        self.streams[eng].append(run)

    def wait(self, eng, sem, val=None):
        v = sem.n if val is None else val
        if v <= 0:
            return
        if v <= self._waited[eng].get(id(sem), -1):
            return
        self._waited[eng][id(sem)] = v
        self.streams[eng].append(lambda e, h=sem.h, v=v: e.wait_ge(h, v))

    def emit(self, blk):
        for name in self.ENGS:
            ops = self.streams[name]
            if not ops:
                continue
            def body(e, ops=ops):
                for f in ops:
                    f(e)
            getattr(blk, name)(body)
        self.streams = {k: [] for k in self.ENGS}

    def flush(self):
        if any(self.streams.values()):
            with self.nc.Block() as blk:
                self.emit(blk)


def _ap3(handle, offset, dims):
    return bass.AP(handle, offset, [list(d) for d in dims])


def build_program(nc, dbg=False, phases=("A1", "A2", "B", "C")):
    phases = {p.split(":")[0]: (int(p.split(":")[1]) if ":" in p else None)
              for p in phases}
    spill_kind = "ExternalOutput" if dbg else "Internal"

    d_in = lambda n, shp, dt=F32: nc.dram_tensor(n, shp, dt, kind="ExternalInput")
    x_fm = d_in("x_fm", [E, S])
    x_tm = d_in("x_tm", [S, E])
    xprev = d_in("xprev", [1, E])
    m2048 = d_in("m2048", [128, 1])
    m0x = d_in("m0x", [128, 1])
    mu_r = d_in("mu_r", [128, NCH])
    mu_k = d_in("mu_k", [128, NCH])
    wrT = d_in("wrT", [E, E])
    wkT = d_in("wkT", [E, E])
    wvT = d_in("wvT", [E, E])
    wqkv = d_in("wqkv", [E, M3])
    bqkv = d_in("bqkv", [128, 24])
    wout = d_in("wout", [E, E])
    bout = d_in("bout", [128, NCH])
    out_fm = nc.dram_tensor("out_fm", [E, OWN], F32, kind="ExternalOutput")

    mk_sp = lambda n, shp, dt: nc.dram_tensor(n, shp, dt, kind=spill_kind)
    r1_sp = mk_sp("r1_sp", [E, S], F32)
    x2h_sp = mk_sp("x2h_sp", [E, S], BF16)
    x2l_sp = mk_sp("x2l_sp", [E, S], BF16)
    qh_sp = mk_sp("qh_sp", [E, S], BF16)
    ql_sp = mk_sp("ql_sp", [E, S], BF16)
    kh_sp = mk_sp("kh_sp", [E, S], BF16)
    kl_sp = mk_sp("kl_sp", [E, S], BF16)
    vth_sp = mk_sp("vth_sp", [H, S, HD], BF16)
    vtl_sp = mk_sp("vtl_sp", [H, S, HD], BF16)
    oh_sp = mk_sp("oh_sp", [E, QTOT], BF16)
    ol_sp = mk_sp("ol_sp", [E, QTOT], BF16)
    yprow_sp = mk_sp("yprow_sp", [1, E], F32)
    dbg_bci = mk_sp("dbg_bci", [128, TOK], F32)
    dbg_bcn = mk_sp("dbg_bcn", [128, TOK], F32)
    dbg_st = mk_sp("dbg_st", [128, 16 * NSUB], F32)
    dbg_xtm = mk_sp("dbg_xtm", [128, NSUB * 1024], F32)
    dbg_y = mk_sp("dbg_y", [128, TOK + 1], F32)
    dbg_r2 = mk_sp("dbg_r2", [E, TOK], F32)
    wr_lo = mk_sp("wr_lo", [E, E], BF16)
    wk_lo = mk_sp("wk_lo", [E, E], BF16)
    wv_lo = mk_sp("wv_lo", [E, E], BF16)
    wr_hi_sp = mk_sp("wr_hi_sp", [E, E], BF16)
    wk_hi_sp = mk_sp("wk_hi_sp", [E, E], BF16)
    wv_hi_sp = mk_sp("wv_hi_sp", [E, E], BF16)
    wq_lo = mk_sp("wq_lo", [E, M3], BF16)
    wo_lo = mk_sp("wo_lo", [E, E], BF16)

    id_f32 = nc.inline_tensor(np.eye(128, dtype=np.float32), name="id_f32")
    id_bf = nc.inline_tensor(np.eye(128, dtype=BF16NP), name="id_bf")
    ones_bf = nc.inline_tensor(np.ones((128, 1), dtype=BF16NP), name="ones_bf")
    dmask_np = np.zeros((128, 128), dtype=np.float32)
    dmask_np[0, 0:64] = dmask_np[32, 0:64] = 1.0
    dmask_np[64, 64:128] = dmask_np[96, 64:128] = 1.0
    dmask = nc.inline_tensor(dmask_np, name="dmask")

    P = PB(nc)
    s = {k: P.sem(k) for k in ("ld", "sp", "dve", "act", "pe", "bar", "out")}

    from contextlib import ExitStack

    def sb(ctx, name, shape, dt):
        return ctx.enter_context(nc.sbuf_tensor(name, shape, dt))

    def ps(ctx, name, shape, dt=F32):
        return ctx.enter_context(nc.psum_tensor(name, shape, dt))

    def phase_barrier():
        for eng in PB.ENGS:
            P.wait_all_dma(eng)
            for k in ("dve", "act", "pe"):
                P.wait(eng, s[k])

    id_f32_s = nc.alloc_sbuf_tensor("id_f32_s", [128, 128], F32)
    id_bf_s = nc.alloc_sbuf_tensor("id_bf_s", [128, 128], BF16)
    ones_bf_s = nc.alloc_sbuf_tensor("ones_bf_s", [128, 1], BF16)
    dmask_s = nc.alloc_sbuf_tensor("dmask_s", [128, 128], F32)
    mur_s = nc.alloc_sbuf_tensor("mur_s", [128, NCH], F32)
    muk_s = nc.alloc_sbuf_tensor("muk_s", [128, NCH], F32)
    bqkv_s = nc.alloc_sbuf_tensor("bqkv_s", [128, 24], F32)
    bout_s = nc.alloc_sbuf_tensor("bout_s", [128, NCH], F32)
    m2048_s = nc.alloc_sbuf_tensor("m2048_s", [128, 1], F32)
    m0x_s = nc.alloc_sbuf_tensor("m0x_s", [128, 1], F32)
    ypcol_s = nc.alloc_sbuf_tensor("ypcol_s", [128, NCH], F32)

    const_tok = P.dmas("sync", [
        (dst[:, :], src.ap()[:, :]) for dst, src in
        ((id_f32_s, id_f32), (id_bf_s, id_bf), (ones_bf_s, ones_bf),
         (dmask_s, dmask), (mur_s, mu_r), (muk_s, mu_k), (bqkv_s, bqkv),
         (bout_s, bout), (m2048_s, m2048), (m0x_s, m0x))])
    for eng in ("tensor", "scalar", "vector", "gpsimd"):
        P.wait_tok(eng, const_tok)

    def emit_yprev(xprow_s, yprow_s, pscal_s):
        t = P.dmas("sync", [(xprow_s[:, :], xprev.ap()[:, :])])
        P.wait_tok("scalar", t)
        P.wait_tok("vector", t)
        P.op("scalar", lambda e: e.activation(yprow_s[:, :], xprow_s[:, :],
                                              AF.Square, bias=0.0, scale=1.0), sem=s["act"])
        P.wait("vector", s["act"])
        P.op("vector", lambda e: e.reduce_sum(pscal_s[:, 0:1], xprow_s[:, :], axis=AXX))
        P.op("vector", lambda e: e.reduce_sum(pscal_s[:, 1:2], yprow_s[:, :], axis=AXX))
        P.drain("vector")
        P.op("vector", lambda e: e.tensor_scalar_mul(pscal_s[:, 2:3], pscal_s[:, 0:1], 1.0 / E))
        P.op("vector", lambda e: e.tensor_scalar_mul(pscal_s[:, 3:4], pscal_s[:, 1:2], 1.0 / E))
        P.drain("vector")
        P.op("vector", lambda e: e.tensor_mul(pscal_s[:, 4:5], pscal_s[:, 2:3], pscal_s[:, 2:3]))
        P.drain("vector")
        P.op("vector", lambda e: e.tensor_sub(pscal_s[:, 5:6], pscal_s[:, 3:4], pscal_s[:, 4:5]))
        P.drain("vector")
        P.op("vector", lambda e: e.tensor_scalar_add(pscal_s[:, 5:6], pscal_s[:, 5:6], EPS),
             sem=s["dve"])
        P.wait("scalar", s["dve"])
        P.op("scalar", lambda e: e.activation(pscal_s[:, 6:7], pscal_s[:, 5:6],
                                              AF.Sqrt, bias=0.0, scale=1.0), sem=s["act"])
        P.wait("vector", s["act"])
        P.op("vector", lambda e: e.reciprocal(pscal_s[:, 6:7], pscal_s[:, 6:7]))
        P.op("vector", lambda e: e.tensor_scalar_mul(pscal_s[:, 7:8], pscal_s[:, 2:3], -1.0))
        P.drain("vector")
        P.op("vector", lambda e: e.tensor_scalar(
            yprow_s[:, :], xprow_s[:, :], pscal_s[:, 7:8], pscal_s[:, 6:7],
            ALU.add, ALU.mult), sem=s["dve"])
        P.wait("gpsimd", s["dve"])
        t2 = P.dmas("gpsimd", [(yprow_sp.ap()[:, :], yprow_s[:, :])])
        P.wait_tok("sync", t2)
        return P.dmas("sync", [(ypcol_s[:, :],
                                _ap3(yprow_sp, 0, [[1, 128], [128, NCH]]))],
                      slow=True)



    def emit_weight_prep(w_dram_ap, m, hi_sb, lo_sp_h, hi_sp_h, tmp, lotmp,
                         prev_tok=None, tok_sink=None, side=None):
        ncho = m // 128
        sp_tok = prev_tok
        for k in range(NCH):
            P.wait("sync", s["dve"])
            t = P.dmas("sync", [(tmp[:, 0:m], w_dram_ap[k * 128:(k + 1) * 128, :])])
            P.wait_tok("scalar", t)
            hi_slice = hi_sb[:, k * m:(k + 1) * m]
            P.op("scalar", lambda e, o=hi_slice, i=tmp[:, 0:m]:
                 e.activation(o, i, AF.Identity, bias=0.0, scale=1.0), sem=s["act"])
            P.wait("vector", s["act"])
            P.wait_tok("vector", sp_tok)
            P.wait_tok("vector", t)
            P.op("vector", lambda e, o=lotmp[:, 0:m], i=tmp[:, 0:m], h=hi_slice:
                 e.tensor_tensor(o, i, h, ALU.subtract), sem=s["dve"])
            P.wait("gpsimd", s["dve"])
            grp = [(_ap3(lo_sp_h, k * 128,
                         [[NCH * 128, 128], [128 * NCH * 128, ncho], [1, 128]]),
                    lotmp[:, 0:m].rearrange("p (o c) -> p o c", o=ncho))]
            if hi_sp_h is not None:
                P.wait("gpsimd", s["act"])
                grp.append((hi_sp_h[k * 128:(k + 1) * 128, :], hi_slice))
            sp_tok = P.dmas("gpsimd", grp)
            if tok_sink is not None:
                tok_sink.append(sp_tok)
            if side:
                side.pop(0)()
        return sp_tok

    def mk_stats(ctx, tag, nsub):
        names = ("s1", "s2", "mean", "ex2", "m2", "var", "sd", "inv", "nm")
        return {n: sb(ctx, f"{tag}_{n}", [128, nsub], F32) for n in names}

    def ln_stats_chain(st, nsub):
        P.drain("vector")
        P.op("vector", lambda e: e.tensor_scalar_mul(st["mean"][:, 0:nsub], st["s1"][:, 0:nsub], 1.0 / E))
        P.op("vector", lambda e: e.tensor_scalar_mul(st["ex2"][:, 0:nsub], st["s2"][:, 0:nsub], 1.0 / E))
        P.drain("vector")
        P.op("vector", lambda e: e.tensor_mul(st["m2"][:, 0:nsub], st["mean"][:, 0:nsub], st["mean"][:, 0:nsub]))
        P.drain("vector")
        P.op("vector", lambda e: e.tensor_sub(st["var"][:, 0:nsub], st["ex2"][:, 0:nsub], st["m2"][:, 0:nsub]))
        P.drain("vector")
        P.op("vector", lambda e: e.tensor_scalar_add(st["var"][:, 0:nsub], st["var"][:, 0:nsub], EPS),
             sem=s["dve"])
        P.wait("scalar", s["dve"])
        P.op("scalar", lambda e: e.activation(st["sd"][:, 0:nsub], st["var"][:, 0:nsub],
                                              AF.Sqrt, bias=0.0, scale=1.0), sem=s["act"])
        P.wait("vector", s["act"])
        P.op("vector", lambda e: e.reciprocal(st["inv"][:, 0:nsub], st["sd"][:, 0:nsub]))
        P.drain("vector")
        P.op("vector", lambda e: e.tensor_mul(st["nm"][:, 0:nsub], st["mean"][:, 0:nsub], st["inv"][:, 0:nsub]))
        P.drain("vector")
        P.op("vector", lambda e: e.tensor_scalar_mul(st["nm"][:, 0:nsub], st["nm"][:, 0:nsub], -1.0),
             sem=s["dve"])

    def stat_sub(st, src, xsq, sub):
        P.op("vector", lambda e, o=xsq[:, :], i=src:
             e.tensor_tensor(o, i, i, ALU.mult))
        P.op("vector", lambda e, o=st["s1"][:, sub:sub + 1], i=src:
             e.reduce_sum(o, i, axis=AXX))
        P.op("vector", lambda e, o=st["s2"][:, sub:sub + 1], i=xsq[:, :]:
             e.reduce_sum(o, i, axis=AXX), sem=s["dve"])

    def emit_stats_from_tm(st, src_list, xsq, nsub):
        for sub in range(nsub):
            stat_sub(st, src_list[sub], xsq, sub)
        ln_stats_chain(st, nsub)

    def bc_tiles(st, nsub, bci_ps, bcn_ps, dve_val=None):
        P.wait("tensor", s["dve"], val=dve_val)
        for sub in range(nsub):
            iap = st["inv"][:, sub:sub + 1].broadcast_to([128, 128])
            nap = st["nm"][:, sub:sub + 1].broadcast_to([128, 128])
            P.op("tensor", lambda e, o=bci_ps[:, sub * 128:(sub + 1) * 128], w=iap:
                 e.matmul(o, w, id_f32_s[:, :], start=True, stop=True))
            P.op("tensor", lambda e, o=bcn_ps[:, sub * 128:(sub + 1) * 128], w=nap:
                 e.matmul(o, w, id_f32_s[:, :], start=True, stop=True),
                 sem=(s["pe"] if sub == nsub - 1 else None))

    def emit_ffn(bufs, w, tokw, resid_ap_fn, out_fn, out_toks,
                 side=None, out_ap_fn=None):
        T = tokw
        xr_hi, xr_lo = bufs["xr_hi"], bufs["xr_lo"]
        xk_hi, xk_lo = bufs["xk_hi"], bufs["xk_lo"]
        ka_hi, ka_lo = bufs["ka_hi"], bufs["ka_lo"]
        r_sig, kf, of, pb = bufs["r_sig"], bufs["kf"], bufs["of"], bufs["pb"]
        side = list(side) if side else []
        kmark = {"dve": 0, "act": 0}

        def sweep(proj, src_hi, src_lo, evac):
            whi = w[proj + "_hi"]
            for o in range(NCH):
                lo_ap = w["lo_load"](proj, o)
                bank = pb[o % 2]
                P.wait("tensor", s["dve"], val=bufs["bank_free_dve"][o % 2])
                P.wait("tensor", s["act"], val=bufs["bank_free_act"][o % 2])
                mms = ([(whi[:, k * E + o * 128: k * E + o * 128 + 128],
                         src_hi[:, k * T:(k + 1) * T]) for k in range(NCH)] +
                       [(whi[:, k * E + o * 128: k * E + o * 128 + 128],
                         src_lo[:, k * T:(k + 1) * T]) for k in range(NCH)] +
                       [(lo_ap[:, k * 128:(k + 1) * 128],
                         src_hi[:, k * T:(k + 1) * T]) for k in range(NCH)])
                n = len(mms)
                for i, (wap, xap) in enumerate(mms):
                    P.op("tensor",
                         lambda e, o_=bank[:, 0:T], w_=wap, x_=xap,
                                st=(i == 0), sp2=(i == n - 1):
                         e.matmul(o_, w_, x_, start=st, stop=sp2),
                         sem=(s["pe"] if i == n - 1 else None))
                evac(o, bank)
                if side:
                    side.pop(0)()

        def evac_r(o, bank):
            P.wait("scalar", s["pe"])
            P.op("scalar", lambda e, o_=r_sig[:, o * T:(o + 1) * T], i=bank[:, 0:T]:
                 e.activation(o_, i, AF.Sigmoid, bias=0.0, scale=1.0), sem=s["act"])
            bufs["bank_free_act"][o % 2] = s["act"].n

        def evac_k(o, bank):
            P.wait("vector", s["pe"])
            P.op("vector", lambda e, o_=kf[:, 0:T], i=bank[:, 0:T]:
                 e.tensor_scalar_max(o_, i, 0.0), sem=s["dve"])
            bufs["bank_free_dve"][o % 2] = s["dve"].n
            xfb = bufs.get("kxf", bufs["xf"])
            P.op("vector", lambda e, o_=xfb[:, 0:T], i=kf[:, 0:T]:
                 e.tensor_tensor(o_, i, i, ALU.mult))
            P.op("vector", lambda e, o_=ka_hi[:, o * T:(o + 1) * T], i=xfb[:, 0:T]:
                 e.tensor_copy(o_, i))
            P.op("vector", lambda e, o_=ka_lo[:, o * T:(o + 1) * T], i=xfb[:, 0:T],
                        h=ka_hi[:, o * T:(o + 1) * T]:
                 e.tensor_tensor(o_, i, h, ALU.subtract), sem=s["dve"])
            kmark["dve"] = s["dve"].n
            kmark["act"] = s["act"].n

        def evac_v(o, bank):
            P.wait("vector", s["pe"])
            P.wait("vector", s["act"])
            slot = o % 2
            if out_ap_fn is None:
                of_ap = of[:, slot * T:(slot + 1) * T]
                if out_toks is not None:
                    P.wait_tok("vector", out_toks[slot])
            else:
                of_ap = out_ap_fn(o)
            P.op("vector", lambda e, o_=kf[:, 0:T], i=bank[:, 0:T],
                        r=r_sig[:, o * T:(o + 1) * T]:
                 e.tensor_tensor(o_, i, r, ALU.mult), sem=s["dve"])
            bufs["bank_free_dve"][o % 2] = s["dve"].n
            P.op("vector", lambda e, o_=of_ap, i=kf[:, 0:T], r=resid_ap_fn(o):
                 e.scalar_tensor_tensor(o_, i, 0.5, r, ALU.mult, ALU.add),
                 sem=s["dve"])
            if out_fn is not None:
                out_fn(o, of_ap)

        sweep("wr", xr_hi, xr_lo, evac_r)
        sweep("wk", xk_hi, xk_lo, evac_k)
        P.wait("tensor", s["dve"], val=kmark["dve"])
        P.wait("tensor", s["act"], val=kmark["act"])
        sweep("wv", ka_hi, ka_lo, evac_v)
        while side:
            side.pop(0)()

    def emit_mix_chunk(bufs, c, x_c, T, carry_init_fn, pre_normalized):
        y_ext, ytmp, dbuf, xf = bufs["y_ext"], bufs["ytmp"], bufs["dbuf"], bufs["xf"]
        carry_init_fn(c)
        if pre_normalized:
            P.op("vector", lambda e, x=x_c, T=T: e.tensor_copy(y_ext[:, 1:T + 1], x))
        else:
            P.op("vector", lambda e, x=x_c, T=T, b=bufs["bci"][:, 0:T]:
                 e.tensor_tensor(ytmp[:, 0:T], x, b, ALU.mult))
            P.op("vector", lambda e, T=T, b=bufs["bcn"][:, 0:T]:
                 e.tensor_tensor(y_ext[:, 1:T + 1], ytmp[:, 0:T], b, ALU.add))
        P.op("vector", lambda e, T=T: e.tensor_tensor(
            dbuf[:, 0:T], y_ext[:, 1:T + 1], y_ext[:, 0:T], ALU.subtract))
        for (mu, hi, lo) in ((mur_s, bufs["xr_hi"], bufs["xr_lo"]),
                             (muk_s, bufs["xk_hi"], bufs["xk_lo"])):
            P.op("vector", lambda e, T=T, m=mu[:, c:c + 1]:
                 e.scalar_tensor_tensor(xf[:, 0:T], dbuf[:, 0:T], m,
                                        y_ext[:, 0:T], ALU.mult, ALU.add))
            P.op("vector", lambda e, T=T, o=hi[:, c * T:(c + 1) * T]:
                 e.tensor_copy(o, xf[:, 0:T]))
            P.op("vector", lambda e, T=T, o=lo[:, c * T:(c + 1) * T],
                        h=hi[:, c * T:(c + 1) * T]:
                 e.tensor_tensor(o, xf[:, 0:T], h, ALU.subtract), sem=s["dve"])
        P.op("vector", lambda e, c=c, T=T: e.tensor_copy(
            bufs["ycar"][:, c:c + 1], y_ext[:, T:T + 1]), sem=s["dve"])

    def emit_mix(bufs, x_src_fn, tokw, carry_init_fn, pre_normalized):
        for c in range(NCH):
            emit_mix_chunk(bufs, c, x_src_fn(c), tokw, carry_init_fn,
                           pre_normalized)

    with ExitStack() as ctx:
        with ExitStack() as yctx:
            xprow_s = sb(yctx, "xprow_s", [1, E], F32)
            yprow_s = sb(yctx, "yprow_s", [1, E], F32)
            pscal_s = sb(yctx, "pscal_s", [1, 8], F32)
            ypcol_tok = emit_yprev(xprow_s, yprow_s, pscal_s)
        wtmp = sb(ctx, "p0_wtmp", [128, E], F32)
        lotmp = sb(ctx, "p0_lotmp", [128, E], BF16)
        wr_hi_s = sb(ctx, "wr_hi_s", [128, NCH * E], BF16)
        wk_hi_s = sb(ctx, "wk_hi_s", [128, NCH * E], BF16)
        wv_hi_s = sb(ctx, "wv_hi_s", [128, NCH * E], BF16)
        prep_toks = {"wr": [], "wk": [], "wv": []}

        xs_b = [sb(ctx, f"a1_xs{i}", [128, NCH * TOKA], F32) for i in range(2)]
        xtm_b = [sb(ctx, f"a1_xtm{i}", [128, NSUBA * 1024], F32) for i in range(2)]
        xsq = sb(ctx, "a1_xsq", [128, 1024], F32)
        pt_sb = sb(ctx, "a1_ptsb", [128, NSUBA * 1024], F32)
        st1 = mk_stats(ctx, "a1_st1", NSUBA)
        st2 = mk_stats(ctx, "a1_st2", NSUBA)
        bci1_b = [sb(ctx, f"a1_bci{i}", [128, TOKA], F32) for i in range(2)]
        bcn1_b = [sb(ctx, f"a1_bcn{i}", [128, TOKA], F32) for i in range(2)]
        bci2 = sb(ctx, "a1_bci2", [128, TOKA], F32)
        bcn2 = sb(ctx, "a1_bcn2", [128, TOKA], F32)
        y_ext = sb(ctx, "a1_yext", [128, TOKA + 1], F32)
        ytmp = sb(ctx, "a1_ytmp", [128, TOKA], F32)
        dbuf = sb(ctx, "a1_dbuf", [128, TOKA], F32)
        xf = sb(ctx, "a1_xf", [128, TOKA], F32)
        kxf = sb(ctx, "a1_kxf", [128, TOKA], F32)
        ycar = sb(ctx, "a1_ycar", [128, NCH], F32)
        xr_hi_b = [sb(ctx, f"a1_xrh{i}", [128, NCH * TOKA], BF16) for i in range(2)]
        xr_lo_b = [sb(ctx, f"a1_xrl{i}", [128, NCH * TOKA], BF16) for i in range(2)]
        xk_hi_b = [sb(ctx, f"a1_xkh{i}", [128, NCH * TOKA], BF16) for i in range(2)]
        xk_lo_b = [sb(ctx, f"a1_xkl{i}", [128, NCH * TOKA], BF16) for i in range(2)]
        ka_hi = sb(ctx, "a1_ka_hi", [128, NCH * TOKA], BF16)
        ka_lo = sb(ctx, "a1_ka_lo", [128, NCH * TOKA], BF16)
        r_sig = sb(ctx, "a1_rsig", [128, NCH * TOKA], F32)
        kf = sb(ctx, "a1_kf", [128, TOKA], F32)
        of = sb(ctx, "a1_of", [128, 2 * TOKA], F32)
        r1t_b = [sb(ctx, f"a1_r1{i}", [128, NCH * TOKA], F32) for i in range(2)]
        wlo_ring = sb(ctx, "a1_wlo", [128, NSLOTA * NCH * 128], BF16)
        x2h = sb(ctx, "a1_x2h", [128, 2 * TOKA], BF16)
        x2l = sb(ctx, "a1_x2l", [128, 2 * TOKA], BF16)
        pb = [ps(ctx, "a1_pb0", [128, TOKA], F32), ps(ctx, "a1_pb1", [128, TOKA], F32)]
        bci_ps = ps(ctx, "a1_bci_ps", [128, TOKA], F32)
        bcn_ps = ps(ctx, "a1_bcn_ps", [128, TOKA], F32)
        pt_ps = ps(ctx, "a1_pt_ps", [128, 1024], F32)

        wlo_sps = {"wr": wr_lo, "wk": wk_lo, "wv": wv_lo}
        ring_state = {"i": 0, "loaded": {}}
        x2_free = [None, None]
        r1_sp_tok = [None, None]
        t_xs = [None, None]
        t_xtm = [None, None]
        ffn_pe_mark = [0, 0]
        ffn_dve_mark = [0, 0]
        mix_mark = [{"dve": 0, "act": 0}, {"dve": 0, "act": 0}]

        def lo_load(proj, o):
            slot = ring_state["i"] % NSLOTA
            ring_state["i"] += 1
            base = slot * NCH * 128
            src = _ap3(wlo_sps[proj], o * NCH * 128 * 128,
                       [[NCH * 128, 128], [128, NCH], [1, 128]])
            for pt_ in prep_toks[proj]:
                P.wait_tok("sync", pt_)
            P.wait("sync", s["pe"], val=ring_state["loaded"].get(slot, 0))
            t = P.dmas("sync", [(wlo_ring[:, base:base + NCH * 128], src)])
            P.wait_tok("tensor", t)
            ring_state["loaded"][slot] = s["pe"].n + 1
            return wlo_ring[:, base:base + NCH * 128]

        wdict = {"wr_hi": wr_hi_s, "wk_hi": wk_hi_s, "wv_hi": wv_hi_s,
                 "lo_load": lo_load}
        bank_free_dve = [0, 0]
        bank_free_act = [0, 0]
        bufs_u = []
        for i in range(2):
            bufs_u.append({
                "xr_hi": xr_hi_b[i], "xr_lo": xr_lo_b[i],
                "xk_hi": xk_hi_b[i], "xk_lo": xk_lo_b[i],
                "ka_hi": ka_hi, "ka_lo": ka_lo, "r_sig": r_sig, "kf": kf,
                "kxf": kxf, "of": of, "pb": pb, "y_ext": y_ext, "ytmp": ytmp,
                "dbuf": dbuf, "xf": xf, "ycar": ycar,
                "bci": bci1_b[i], "bcn": bcn1_b[i],
                "bank_free_dve": bank_free_dve, "bank_free_act": bank_free_act})

        def emit_tile_loads(jj):
            u2 = jj % 2
            T0j = jj * TOKA
            P.wait("scalar", s["dve"], val=ffn_dve_mark[u2])
            t_xtm[u2] = P.dmas("scalar", [
                (xtm_b[u2][:, :].rearrange("p (s f) -> p s f", s=NSUBA),
                 _ap3(x_tm, T0j * E, [[E, 128], [128 * E, NSUBA], [1, E]]))])
            t_xs[u2] = P.dmas("scalar", [
                (xs_b[u2][:, :].rearrange("p (c t) -> p c t", c=NCH),
                 _ap3(x_fm, T0j, [[S, 128], [128 * S, NCH], [1, TOKA]]))])

        def mk_carry(jj):
            def carry_init(c, jj=jj):
                if jj == 0:
                    P.op("vector", lambda e, c=c: e.tensor_copy(
                        y_ext[:, 0:1], ypcol_s[:, c:c + 1]))
                elif jj == OWN // TOKA:
                    P.op("vector", lambda e, c=c: e.tensor_tensor(
                        y_ext[:, 0:1], ycar[:, c:c + 1], m2048_s[:, 0:1],
                        ALU.mult))
                else:
                    P.op("vector", lambda e, c=c: e.tensor_copy(
                        y_ext[:, 0:1], ycar[:, c:c + 1]))
            return carry_init

        def prod_thunks(jj):
            u2 = jj % 2
            th = []
            for sub in range(NSUBA):
                def t_stat(sub=sub, u2=u2):
                    P.wait_tok("scalar", t_xtm[u2])
                    P.wait_tok("vector", t_xtm[u2])
                    stat_sub(st1, xtm_b[u2][:, sub * 1024:(sub + 1) * 1024],
                             xsq, sub)
                th.append(t_stat)
            th.append(lambda: ln_stats_chain(st1, NSUBA))

            def t_bc1(u2=u2):
                bc_tiles(st1, NSUBA, bci_ps, bcn_ps)
                P.wait("vector", s["pe"])
                P.op("vector", lambda e: e.tensor_copy(bci1_b[u2][:, :],
                                                       bci_ps[:, 0:TOKA]))
                P.op("vector", lambda e: e.tensor_copy(bcn1_b[u2][:, :],
                                                       bcn_ps[:, 0:TOKA]),
                     sem=s["dve"])
                P.drain("vector")
            th.append(t_bc1)
            ci = mk_carry(jj)
            for c in range(NCH):
                def t_mix(c=c, u2=u2, ci=ci):
                    P.wait("vector", s["pe"], val=ffn_pe_mark[u2])
                    P.wait_tok("vector", t_xs[u2])
                    emit_mix_chunk(bufs_u[u2], c,
                                   xs_b[u2][:, c * TOKA:(c + 1) * TOKA],
                                   TOKA, ci, False)
                    if c == NCH - 1:
                        mix_mark[u2]["dve"] = s["dve"].n
                        mix_mark[u2]["act"] = s["act"].n
                th.append(t_mix)
            return th

        def ln2_stat_thunks(jj):
            th = []
            for sub in range(NSUBA):
                def t_l(sub=sub):
                    P.wait("vector", s["act"])
                    stat_sub(st2, pt_sb[:, sub * 1024:(sub + 1) * 1024],
                             xsq, sub)
                th.append(t_l)
            th.append(lambda: ln_stats_chain(st2, NSUBA))

            def t_bc2():
                bc_tiles(st2, NSUBA, bci_ps, bcn_ps)
                P.wait("vector", s["pe"])
                P.op("vector", lambda e: e.tensor_copy(bci2[:, :],
                                                       bci_ps[:, 0:TOKA]))
                P.op("vector", lambda e: e.tensor_copy(bcn2[:, :],
                                                       bcn_ps[:, 0:TOKA]),
                     sem=s["dve"])
                P.drain("vector")
            th.append(t_bc2)
            return th

        def ln2_split_thunks(jj):
            u = jj % 2
            T0j = jj * TOKA
            th = []
            for c in range(NCH):
                def t_sp(c=c, u=u, T0j=T0j):
                    slot = c % 2
                    sl = slice(slot * TOKA, (slot + 1) * TOKA)
                    P.op("vector", lambda e, c=c, u=u: e.tensor_tensor(
                        ytmp[:, 0:TOKA], r1t_b[u][:, c * TOKA:(c + 1) * TOKA],
                        bci2[:, :], ALU.mult))
                    P.op("vector", lambda e: e.tensor_tensor(
                        xf[:, 0:TOKA], ytmp[:, 0:TOKA], bcn2[:, :], ALU.add))
                    P.wait_tok("vector", x2_free[slot])
                    P.op("vector", lambda e, sl=sl: e.tensor_copy(
                        x2h[:, sl], xf[:, 0:TOKA]))
                    P.op("vector", lambda e, sl=sl: e.tensor_tensor(
                        x2l[:, sl], xf[:, 0:TOKA], x2h[:, sl], ALU.subtract),
                        sem=s["dve"])
                    P.wait("gpsimd", s["dve"])
                    x2_free[slot] = P.dmas("gpsimd", [
                        (x2h_sp.ap()[c * 128:(c + 1) * 128, T0j:T0j + TOKA],
                         x2h[:, sl]),
                        (x2l_sp.ap()[c * 128:(c + 1) * 128, T0j:T0j + TOKA],
                         x2l[:, sl])])
                th.append(t_sp)
            return th

        def emit_ln2_transposes(jj):
            u = jj % 2
            for sub in range(NSUBA):
                P.wait("tensor", s["dve"])
                P.wait("tensor", s["act"])
                for c in range(NCH):
                    P.op("tensor", lambda e, u=u,
                                o=pt_ps[:, c * 128:(c + 1) * 128],
                                i=r1t_b[u][:, c * TOKA + sub * 128:
                                            c * TOKA + sub * 128 + 128]:
                         e.transpose(o, i, id_f32_s[:, :]),
                         sem=(s["pe"] if c == NCH - 1 else None))
                P.wait("scalar", s["pe"])
                P.wait("scalar", s["dve"])
                P.op("scalar", lambda e, sub=sub: e.activation(
                    pt_sb[:, sub * 1024:(sub + 1) * 1024], pt_ps[:, 0:1024],
                    AF.Identity, bias=0.0, scale=1.0), sem=s["act"])

        NTA_run = NTA if phases.get("A1") in (None, 0) else min(phases["A1"], NTA)
        side0 = []
        if "A1" in phases and NTA_run:
            emit_tile_loads(0)
            P.wait_tok("vector", ypcol_tok)
            side0 = prod_thunks(0)
        wp_tok = None
        for (pj, wd, hs, lsp, hsp) in (("wr", wrT, wr_hi_s, wr_lo, wr_hi_sp),
                                       ("wk", wkT, wk_hi_s, wk_lo, wk_hi_sp),
                                       ("wv", wvT, wv_hi_s, wv_lo, wv_hi_sp)):
            wp_tok = emit_weight_prep(wd.ap(), E, hs, lsp, hsp.ap(), wtmp,
                                      lotmp, prev_tok=wp_tok,
                                      tok_sink=prep_toks[pj], side=side0)
        while side0:
            side0.pop(0)()
        if "A1" in phases and NTA_run:
            P.wait("tensor", s["act"])
            for j in range(NTA_run):
                u = j % 2
                if j + 1 < NTA_run:
                    emit_tile_loads(j + 1)
                pt = prod_thunks(j + 1) if j + 1 < NTA_run else None
                lt = ln2_stat_thunks(j - 1) if j >= 1 else None
                sp = ln2_split_thunks(j - 1) if j >= 1 else None
                side = []
                if lt:
                    side += lt[:2]
                if pt:
                    side += pt[:2]
                if lt:
                    side.append(lt[2])
                if pt:
                    side.append(pt[2])
                if lt:
                    side.append(lt[3])
                if pt:
                    side.append(pt[3])
                for c in range(NCH):
                    if sp:
                        side.append(sp[c])
                    if pt:
                        side.append(pt[4 + c])
                P.wait("tensor", s["dve"], val=mix_mark[u]["dve"])
                P.wait("tensor", s["act"], val=mix_mark[u]["act"])
                P.wait_tok("vector", r1_sp_tok[u])
                emit_ffn(bufs_u[u], wdict, TOKA,
                         (lambda o, u=u: xs_b[u][:, o * TOKA:(o + 1) * TOKA]),
                         None, None, side=side,
                         out_ap_fn=(lambda o, u=u:
                                    r1t_b[u][:, o * TOKA:(o + 1) * TOKA]))
                ffn_pe_mark[u] = s["pe"].n
                ffn_dve_mark[u] = s["dve"].n
                emit_ln2_transposes(j)
                P.wait("gpsimd", s["dve"])
                r1_sp_tok[u] = P.dmas("gpsimd", [
                    (_ap3(r1_sp, j * TOKA, [[S, 128], [128 * S, NCH], [1, TOKA]]),
                     r1t_b[u][:, :].rearrange("p (c t) -> p c t", c=NCH))])
            for t in ln2_stat_thunks(NTA_run - 1) + ln2_split_thunks(NTA_run - 1):
                t()
        phase_barrier()
        P.flush()


    if "A2" in phases:
      with ExitStack() as ctx:
        wq_hi_s = sb(ctx, "wq_hi_s", [128, NCH * M3], BF16)
        with ExitStack() as wctx:
            wtmp = sb(wctx, "p1_wtmp", [128, M3], F32)
            lotmp = sb(wctx, "p1_lotmp", [128, M3], BF16)
            emit_weight_prep(wqkv.ap(), M3, wq_hi_s, wq_lo, None, wtmp, lotmp)
        phase_barrier()

        x2ht_b = [sb(ctx, f"a2_x2h{i}", [128, NCH * TOK], BF16) for i in range(2)]
        x2lt_b = [sb(ctx, f"a2_x2l{i}", [128, NCH * TOK], BF16) for i in range(2)]
        qf = sb(ctx, "a2_qf", [128, TOK], F32)
        sth = sb(ctx, "a2_sth", [128, TOK], BF16)
        stl = sb(ctx, "a2_stl", [128, TOK], BF16)
        vst = sb(ctx, "a2_vst", [128, 2 * TOK], BF16)
        wlo_ring = sb(ctx, "a2_wlo", [128, 2 * NCH * 128], BF16)
        pb = [ps(ctx, "a2_pb0", [128, TOK], F32), ps(ctx, "a2_pb1", [128, TOK], F32)]
        ptb = [ps(ctx, "a2_pt0", [128, 2 * TOK], BF16),
               ps(ctx, "a2_pt1", [128, 2 * TOK], BF16)]

        ring_state = {"i": 0, "loaded": {}}

        def qkv_lo_load(o):
            slot = ring_state["i"] % 2
            ring_state["i"] += 1
            base = slot * NCH * 128
            src = _ap3(wq_lo, o * NCH * 128 * 128,
                       [[NCH * 128, 128], [128, NCH], [1, 128]])
            P.wait("sync", s["pe"], val=ring_state["loaded"].get(slot, 0))
            t = P.dmas("sync", [(wlo_ring[:, base:base + NCH * 128], src)])
            P.wait_tok("tensor", t)
            ring_state["loaded"][slot] = s["pe"].n + 1
            return wlo_ring[:, base:base + NCH * 128]

        bank_free = {"dve": [0, 0], "act": [0, 0]}
        ptk_free = [0, 0]
        vst_free = [None, None]
        qk_sp_tok = [None]

        tile_pe_free = [0, 0]
        tile_tok = [None, None]

        def a2_load(j):
            u = j % 2
            T0j = j * TOK
            P.wait("sync", s["pe"], val=tile_pe_free[u])
            return P.dmas("sync", [
                (x2ht_b[u][:, :].rearrange("p (c t) -> p c t", c=NCH),
                 _ap3(x2h_sp, T0j, [[S, 128], [128 * S, NCH], [1, TOK]])),
                (x2lt_b[u][:, :].rearrange("p (c t) -> p c t", c=NCH),
                 _ap3(x2l_sp, T0j, [[S, 128], [128 * S, NCH], [1, TOK]]))])

        tile_tok[0] = a2_load(0)
        for j in range(NT):
            T0 = j * TOK
            u = j % 2
            if j + 1 < NT:
                tile_tok[(j + 1) % 2] = a2_load(j + 1)
            x2ht, x2lt = x2ht_b[u], x2lt_b[u]
            P.wait_tok("tensor", tile_tok[u])
            for o in range(24):
                is_q = o < 8
                is_v = o >= 16
                if is_q:
                    if T0 < OWN:
                        cs, cw = 0, TOK
                    elif T0 + TOK == S:
                        cs, cw = XQ0 - T0, S - XQ0
                    else:
                        continue
                else:
                    cs, cw = 0, TOK
                lo_ap = qkv_lo_load(o)
                bank = pb[o % 2]
                P.wait("tensor", s["dve"], val=bank_free["dve"][o % 2])
                P.wait("tensor", s["act"], val=bank_free["act"][o % 2])
                mms = ([(wq_hi_s[:, k * M3 + o * 128: k * M3 + o * 128 + 128],
                         x2ht[:, k * TOK + cs: k * TOK + cs + cw]) for k in range(NCH)] +
                       [(wq_hi_s[:, k * M3 + o * 128: k * M3 + o * 128 + 128],
                         x2lt[:, k * TOK + cs: k * TOK + cs + cw]) for k in range(NCH)] +
                       [(lo_ap[:, k * 128:(k + 1) * 128],
                         x2ht[:, k * TOK + cs: k * TOK + cs + cw]) for k in range(NCH)])
                n = len(mms)
                for i, (wap, xap) in enumerate(mms):
                    P.op("tensor",
                         lambda e, o_=bank[:, 0:cw], w_=wap, x_=xap,
                                st=(i == 0), sp2=(i == n - 1):
                         e.matmul(o_, w_, x_, start=st, stop=sp2),
                         sem=(s["pe"] if i == n - 1 else None))
                P.wait("vector", s["pe"])
                P.wait_tok("vector", qk_sp_tok[0])
                if is_q:
                    P.op("vector", lambda e, o_=qf[:, 0:cw], i=bank[:, 0:cw],
                                bb=bqkv_s[:, o:o + 1]:
                         e.tensor_scalar(o_, i, bb, 0.125, ALU.add, ALU.mult),
                         sem=s["dve"])
                else:
                    P.op("vector", lambda e, o_=qf[:, 0:cw], i=bank[:, 0:cw],
                                bb=bqkv_s[:, o:o + 1]:
                         e.tensor_scalar(o_, i, bb, None, ALU.add), sem=s["dve"])
                bank_free["dve"][o % 2] = s["dve"].n
                P.op("vector", lambda e, o_=sth[:, 0:cw], i=qf[:, 0:cw]:
                     e.tensor_copy(o_, i))
                P.op("vector", lambda e, o_=stl[:, 0:cw], i=qf[:, 0:cw], h=sth[:, 0:cw]:
                     e.tensor_tensor(o_, i, h, ALU.subtract), sem=s["dve"])
                if not is_v:
                    sp_h = qh_sp if is_q else kh_sp
                    sp_l = ql_sp if is_q else kl_sp
                    row = (o if is_q else o - 8) * 128
                    P.wait("gpsimd", s["dve"])
                    qk_sp_tok[0] = P.dmas("gpsimd", [
                        (sp_h.ap()[row:row + 128, T0 + cs:T0 + cs + cw], sth[:, 0:cw]),
                        (sp_l.ap()[row:row + 128, T0 + cs:T0 + cs + cw], stl[:, 0:cw])])
                else:
                    hA = (o - 16) * 2
                    P.wait("tensor", s["dve"])
                    for hl, stt in ((0, sth), (1, stl)):
                        ptk = ptb[hl]
                        P.wait("tensor", s["act"], val=ptk_free[hl])
                        for sub in range(NSUB):
                            P.op("tensor", lambda e,
                                        o_=ptk[:, sub * 128:(sub + 1) * 128],
                                        i=stt[:, sub * 128:(sub + 1) * 128]:
                                 e.transpose(o_, i, id_bf_s[:, :]),
                                 sem=(s["pe"] if sub == NSUB - 1 else None))
                        P.wait("scalar", s["pe"])
                        P.wait_tok("scalar", vst_free[hl])
                        vsl = vst[:, hl * TOK:(hl + 1) * TOK]
                        P.op("scalar", lambda e, o_=vsl, i=ptk[:, 0:TOK]:
                             e.activation(o_, i, AF.Identity, bias=0.0,
                                          scale=1.0), sem=s["act"])
                        ptk_free[hl] = s["act"].n
                        P.wait("gpsimd", s["act"])
                        spv = vth_sp if hl == 0 else vtl_sp
                        vst_free[hl] = P.dmas("gpsimd", [
                            (_ap3(spv, (hA + h) * S * HD + T0 * HD,
                                  [[HD, 128], [128 * HD, NSUB], [1, HD]]),
                             _ap3(vst, hl * TOK + h * HD,
                                  [[2 * TOK, 128], [128, NSUB], [1, HD]]))
                            for h in range(2)])
            tile_pe_free[u] = s["pe"].n
        phase_barrier()
        P.flush()

    QGS = [(0, 512), (512, 512), (1024, 512), (1536, 512), (2048, XQW)]
    if "B" in phases:
      with ExitStack() as ctx:
        kp_hi = [sb(ctx, f"b_kph{i}", [128, S], BF16) for i in range(2)]
        kp_lo = [sb(ctx, f"b_kpl{i}", [128, S], BF16) for i in range(2)]
        qp_hi = [sb(ctx, f"b_qph{i}", [128, QTOT], BF16) for i in range(2)]
        qp_lo = [sb(ctx, f"b_qpl{i}", [128, QTOT], BF16) for i in range(2)]
        vA_hi = [sb(ctx, f"b_vah{i}", [128, NKC * HD], BF16) for i in range(2)]
        vA_lo = [sb(ctx, f"b_val{i}", [128, NKC * HD], BF16) for i in range(2)]
        vB_hi = [sb(ctx, f"b_vbh{i}", [128, NKC * HD], BF16) for i in range(2)]
        vB_lo = [sb(ctx, f"b_vbl{i}", [128, NKC * HD], BF16) for i in range(2)]
        paf = [sb(ctx, f"b_paf{i}", [128, 512], F32) for i in range(2)]
        pbf = [sb(ctx, f"b_pbf{i}", [128, 512], F32) for i in range(2)]
        pah = [sb(ctx, f"b_pah{i}", [128, 512], BF16) for i in range(2)]
        pal = [sb(ctx, f"b_pal{i}", [128, 512], BF16) for i in range(2)]
        pbh = [sb(ctx, f"b_pbh{i}", [128, 512], BF16) for i in range(2)]
        pbl = [sb(ctx, f"b_pbl{i}", [128, 512], BF16) for i in range(2)]
        dsb = sb(ctx, "b_dsb", [128, 512], F32)
        binv = sb(ctx, "b_binv", [128, 512], F32)
        o_f = sb(ctx, "b_of", [128, 512], F32)
        o_hi = sb(ctx, "b_ohi", [128, 512], BF16)
        o_lo = sb(ctx, "b_olo", [128, 512], BF16)
        st_ps = [(ps(ctx, "b_stA0", [128, 512], F32), ps(ctx, "b_stB0", [128, 512], F32)),
                 (ps(ctx, "b_stA1", [128, 512], F32), ps(ctx, "b_stB1", [128, 512], F32))]
        pv = ps(ctx, "b_pv", [128, 512], F32)
        dps = ps(ctx, "b_dps", [128, 512], F32)
        ddp = ps(ctx, "b_ddp", [128, 512], F32)
        o_sp_tok = [None]
        P.op("vector", lambda e: e.memset(dsb[:, :], 0.0))
        for t in (st_ps[0][0], st_ps[0][1], st_ps[1][0], st_ps[1][1], pv, dps, ddp):
            P.op("vector", lambda e, t=t: e.memset(t[:, :], 0.0))
        P.op("vector", lambda e: e.engine_nop(), sem=s["dve"])

        pair_free_pe = [0, 0]
        pair_tok = [None, None]
        g_exp = [0, 0]
        g_lo = [0, 0]
        g_castA = [0, 0]
        g_pv = [0, 0]

        def emit_pair_loads(pc):
            u = pc % 2
            grp = []
            for dst, spv in ((kp_hi[u], kh_sp), (kp_lo[u], kl_sp)):
                grp.append((dst[:, :], spv.ap()[pc * 128:(pc + 1) * 128, :]))
            for dst, spv in ((qp_hi[u], qh_sp), (qp_lo[u], ql_sp)):
                grp.append((dst[:, 0:OWN], spv.ap()[pc * 128:(pc + 1) * 128, 0:OWN]))
                grp.append((dst[:, OWN:QTOT], spv.ap()[pc * 128:(pc + 1) * 128, XQ0:S]))
            for dst, spv, h in ((vA_hi[u], vth_sp, 2 * pc), (vA_lo[u], vtl_sp, 2 * pc),
                                (vB_hi[u], vth_sp, 2 * pc + 1), (vB_lo[u], vtl_sp, 2 * pc + 1)):
                grp.append((dst[:, :].rearrange("p (c d) -> p c d", c=NKC),
                            _ap3(spv, h * S * HD, [[HD, 128], [128 * HD, NKC], [1, HD]])))
            P.wait("sync", s["pe"], val=pair_free_pe[u])
            return P.dmas("sync", grp)

        pair_tok[0] = emit_pair_loads(0)
        for pc in range(8):
            u = pc % 2
            if pc + 1 < 8:
                pair_tok[(pc + 1) % 2] = emit_pair_loads(pc + 1)
            P.wait_tok("tensor", pair_tok[u])
            kph, kpl = kp_hi[u], kp_lo[u]
            qph, qpl = qp_hi[u], qp_lo[u]
            vah, val_, vbh, vbl = vA_hi[u], vA_lo[u], vB_hi[u], vB_lo[u]

            for (g0, gw) in QGS:
                lo_done = {}
                castA_done = {}

                def emit_pv(cc, g0=g0, gw=gw, vah=vah, val_=val_, vbh=vbh, vbl=vbl):
                    v = cc % 2
                    P.wait("tensor", s["dve"], val=lo_done[cc])
                    P.wait("tensor", s["act"], val=castA_done[cc])
                    pvs = [(vah, pah[v]), (val_, pah[v]), (vah, pal[v])]
                    pvsB = [(vbh, pbh[v]), (vbl, pbh[v]), (vbh, pbl[v])]
                    st_c, sp_c = (cc == 0), (cc == NKC - 1)
                    for i in range(3):
                        va_, pa_ = pvs[i]
                        vb_, pb_ = pvsB[i]
                        st_i = st_c and (i == 0)
                        sp_i = sp_c and (i == 2)
                        P.op("tensor", lambda e, w_=va_[:, cc * HD:(cc + 1) * HD],
                                    x_=pa_[:, 0:gw], a=st_i, b_=sp_i:
                             e.matmul(pv[0:64, 0:gw], w_, x_, start=a, stop=b_,
                                      tile_position=(0, 0), skip_group_check=True))
                        P.op("tensor", lambda e, w_=vb_[:, cc * HD:(cc + 1) * HD],
                                    x_=pb_[:, 0:gw], a=st_i, b_=sp_i:
                             e.matmul(pv[64:128, 0:gw], w_, x_, start=a, stop=b_,
                                      tile_position=(0, 64), skip_group_check=True))
                    dmov = [(pah[v], 0), (pal[v], 32), (pbh[v], 64), (pbl[v], 96)]
                    for i, (pm, col) in enumerate(dmov):
                        P.op("tensor", lambda e, o_=dps[col:col + 1, 0:gw],
                                    x_=pm[:, 0:gw], a=st_c, b_=sp_c, col=col:
                             e.matmul(o_, ones_bf_s[:, :], x_, start=a, stop=b_,
                                      tile_position=(0, col), skip_group_check=True),
                             sem=(s["pe"] if i == len(dmov) - 1 else None))
                    g_pv[v] = s["pe"].n

                for c in range(NKC):
                    v = c % 2
                    stA, stB = st_ps[v]
                    P.wait("tensor", s["act"], val=g_exp[v])
                    qk = [(kph, qph), (kph, qpl), (kpl, qph)]
                    for i, (kk, qq) in enumerate(qk):
                        st_, sp2_ = (i == 0), (i == len(qk) - 1)
                        P.op("tensor", lambda e, o_=stA[:, 0:gw],
                                    w_=kk[0:64, c * 128:(c + 1) * 128],
                                    x_=qq[0:64, g0:g0 + gw], a=st_, b_=sp2_:
                             e.matmul(o_, w_, x_, start=a, stop=b_,
                                      tile_position=(0, 0)))
                        P.op("tensor", lambda e, o_=stB[:, 0:gw],
                                    w_=kk[64:128, c * 128:(c + 1) * 128],
                                    x_=qq[64:128, g0:g0 + gw], a=st_, b_=sp2_:
                             e.matmul(o_, w_, x_, start=a, stop=b_,
                                      tile_position=(64, 0)),
                             sem=(s["pe"] if sp2_ else None))
                    qk_done_c = s["pe"].n
                    if c >= 1:
                        emit_pv(c - 1)
                    P.wait("scalar", s["pe"], val=qk_done_c)
                    P.wait("scalar", s["dve"], val=g_lo[v])
                    P.op("scalar", lambda e, v=v, gw=gw, sA=stA: e.activation(
                        paf[v][:, 0:gw], sA[:, 0:gw], AF.Exp, bias=0.0, scale=1.0))
                    P.op("scalar", lambda e, v=v, gw=gw, sB=stB: e.activation(
                        pbf[v][:, 0:gw], sB[:, 0:gw], AF.Exp, bias=0.0, scale=1.0),
                        sem=s["act"])
                    g_exp[v] = s["act"].n
                    exp_done_c = s["act"].n
                    P.wait("scalar", s["pe"], val=g_pv[v])
                    P.op("scalar", lambda e, v=v, gw=gw: e.activation(
                        pah[v][:, 0:gw], paf[v][:, 0:gw], AF.Identity,
                        bias=0.0, scale=1.0), sem=s["act"])
                    castA_done[c] = s["act"].n
                    P.wait("vector", s["act"], val=exp_done_c)
                    P.wait("vector", s["pe"], val=g_pv[v])
                    P.op("vector", lambda e, v=v, gw=gw: e.tensor_copy(
                        pbh[v][:, 0:gw], pbf[v][:, 0:gw]))
                    P.wait("vector", s["act"], val=castA_done[c])
                    P.op("vector", lambda e, v=v, gw=gw: e.tensor_tensor(
                        pal[v][:, 0:gw], paf[v][:, 0:gw], pah[v][:, 0:gw], ALU.subtract))
                    P.op("vector", lambda e, v=v, gw=gw: e.tensor_tensor(
                        pbl[v][:, 0:gw], pbf[v][:, 0:gw], pbh[v][:, 0:gw], ALU.subtract),
                        sem=s["dve"])
                    lo_done[c] = s["dve"].n
                    g_lo[v] = s["dve"].n
                emit_pv(NKC - 1)

                P.wait("vector", s["pe"])
                for drow in (0, 32, 64, 96):
                    P.op("vector", lambda e, gw=gw, dr=drow: e.tensor_copy(
                        dsb[dr:dr + 1, 0:gw], dps[dr:dr + 1, 0:gw]),
                        sem=(s["dve"] if drow == 96 else None))
                P.wait("tensor", s["dve"])
                P.op("tensor", lambda e, gw=gw: e.matmul(
                    ddp[:, 0:gw], dmask_s[:, :], dsb[:, 0:gw], start=True, stop=True),
                    sem=s["pe"])
                P.wait("vector", s["pe"])
                P.op("vector", lambda e, gw=gw: e.reciprocal(binv[:, 0:gw], ddp[:, 0:gw]),
                     sem=s["dve"])
                P.op("vector", lambda e, gw=gw: e.tensor_tensor(
                    o_f[:, 0:gw], pv[:, 0:gw], binv[:, 0:gw], ALU.mult), sem=s["dve"])
                P.wait("scalar", s["dve"])
                P.wait_tok("scalar", o_sp_tok[0])
                P.op("scalar", lambda e, gw=gw: e.activation(
                    o_hi[:, 0:gw], o_f[:, 0:gw], AF.Identity, bias=0.0, scale=1.0),
                    sem=s["act"])
                P.wait("vector", s["act"])
                P.wait_tok("vector", o_sp_tok[0])
                P.op("vector", lambda e, gw=gw: e.tensor_tensor(
                    o_lo[:, 0:gw], o_f[:, 0:gw], o_hi[:, 0:gw], ALU.subtract),
                    sem=s["dve"])
                P.wait("gpsimd", s["dve"])
                P.wait("gpsimd", s["act"])
                o_sp_tok[0] = P.dmas("gpsimd", [
                    (oh_sp.ap()[pc * 128:(pc + 1) * 128, g0:g0 + gw], o_hi[:, 0:gw]),
                    (ol_sp.ap()[pc * 128:(pc + 1) * 128, g0:g0 + gw], o_lo[:, 0:gw])])
            pair_free_pe[u] = s["pe"].n
        phase_barrier()
        P.flush()

    if "C" in phases:
      with ExitStack() as ctx:
        wo_hi_s = sb(ctx, "wo_hi_s", [128, NCH * E], BF16)
        wr_hi_s2 = sb(ctx, "wr_hi_s2", [128, NCH * E], BF16)
        wk_hi_s2 = sb(ctx, "wk_hi_s2", [128, NCH * E], BF16)
        wv_hi_s2 = sb(ctx, "wv_hi_s2", [128, NCH * E], BF16)
        with ExitStack() as wctx:
            wtmp = sb(wctx, "p2_wtmp", [128, E], F32)
            lotmp = sb(wctx, "p2_lotmp", [128, E], BF16)
            emit_weight_prep(wout.ap(), E, wo_hi_s, wo_lo, None, wtmp, lotmp)
        P.dmas("sync", [
            (hs[:, :].rearrange("p (k m) -> p k m", k=NCH),
             _ap3(hsp, 0, [[E, 128], [128 * E, NCH], [1, E]]))
            for hs, hsp in ((wr_hi_s2, wr_hi_sp), (wk_hi_s2, wk_hi_sp),
                            (wv_hi_s2, wv_hi_sp))])
        phase_barrier()

        oht = sb(ctx, "c_oh", [128, NCH * TOK], BF16)
        olt = sb(ctx, "c_ol", [128, NCH * TOK], BF16)
        r1ct = sb(ctx, "c_r1", [128, NCH * TOK], F32)
        r2t = sb(ctx, "c_r2", [128, NCH * TOK], F32)
        xsq = sb(ctx, "c_xsq", [128, 1024], F32)
        st3 = mk_stats(ctx, "c_st3", NSUB)
        bci = sb(ctx, "c_bci", [128, TOK], F32)
        bcn = sb(ctx, "c_bcn", [128, TOK], F32)
        y_ext = sb(ctx, "c_yext", [128, TOK + 1], F32)
        ytmp = sb(ctx, "c_ytmp", [128, TOK], F32)
        dbuf = sb(ctx, "c_dbuf", [128, TOK], F32)
        xf = sb(ctx, "c_xf", [128, TOK], F32)
        ycar = sb(ctx, "c_ycar", [128, NCH], F32)
        x3car = sb(ctx, "c_x3car", [128, NCH], F32)
        xr_hi = sb(ctx, "c_xr_hi", [128, NCH * TOK], BF16)
        xr_lo = sb(ctx, "c_xr_lo", [128, NCH * TOK], BF16)
        xk_hi = sb(ctx, "c_xk_hi", [128, NCH * TOK], BF16)
        xk_lo = sb(ctx, "c_xk_lo", [128, NCH * TOK], BF16)
        ka_hi = sb(ctx, "c_ka_hi", [128, NCH * TOK], BF16)
        ka_lo = sb(ctx, "c_ka_lo", [128, NCH * TOK], BF16)
        r_sig = sb(ctx, "c_rsig", [128, NCH * TOK], F32)
        kf = sb(ctx, "c_kf", [128, TOK], F32)
        of = sb(ctx, "c_of", [128, 2 * TOK], F32)
        wlo_ring = sb(ctx, "c_wlo", [128, 2 * NCH * 128], BF16)
        wolo_ring = wlo_ring
        pb = [ps(ctx, "c_pb0", [128, TOK], F32), ps(ctx, "c_pb1", [128, TOK], F32)]
        bci_ps = ps(ctx, "c_bci_ps", [128, TOK], F32)
        bcn_ps = ps(ctx, "c_bcn_ps", [128, TOK], F32)
        pt_ps = ps(ctx, "c_pt_ps", [128, 1024], F32)
        for t in (pb[0], pb[1], bci_ps, bcn_ps, pt_ps):
            P.op("vector", lambda e, t=t: e.memset(t[:, :], 0.0))
        P.op("vector", lambda e: e.memset(ycar[:, :], 0.0))
        P.op("vector", lambda e: e.memset(x3car[:, :], 0.0))
        P.op("vector", lambda e: e.engine_nop(), sem=s["dve"])

        wlo_sps = {"wr": wr_lo, "wk": wk_lo, "wv": wv_lo}
        ring_state = {"i": 0, "loaded": {}}
        wo_ring = {"i": 0, "loaded": {}}
        out_toks = [None, None]

        def lo_load(proj, o):
            slot = ring_state["i"] % 2
            ring_state["i"] += 1
            base = slot * NCH * 128
            src = _ap3(wlo_sps[proj], o * NCH * 128 * 128,
                       [[NCH * 128, 128], [128, NCH], [1, 128]])
            P.wait("sync", s["pe"], val=ring_state["loaded"].get(slot, 0))
            t = P.dmas("sync", [(wlo_ring[:, base:base + NCH * 128], src)])
            P.wait_tok("tensor", t)
            ring_state["loaded"][slot] = s["pe"].n + 1
            return wlo_ring[:, base:base + NCH * 128]

        def wo_lo_load(o):
            slot = ring_state["i"] % 2
            ring_state["i"] += 1
            base = slot * NCH * 128
            src = _ap3(wo_lo, o * NCH * 128 * 128,
                       [[NCH * 128, 128], [128, NCH], [1, 128]])
            P.wait("sync", s["pe"], val=ring_state["loaded"].get(slot, 0))
            t = P.dmas("sync", [(wolo_ring[:, base:base + NCH * 128], src)])
            P.wait_tok("tensor", t)
            ring_state["loaded"][slot] = s["pe"].n + 1
            return wolo_ring[:, base:base + NCH * 128]

        wdict = {"wr_hi": wr_hi_s2, "wk_hi": wk_hi_s2, "wv_hi": wv_hi_s2,
                 "lo_load": lo_load}
        bufs = {"xr_hi": xr_hi, "xr_lo": xr_lo, "xk_hi": xk_hi, "xk_lo": xk_lo,
                "ka_hi": ka_hi, "ka_lo": ka_lo, "r_sig": r_sig, "kf": kf,
                "of": of, "pb": pb, "y_ext": y_ext, "ytmp": ytmp, "dbuf": dbuf,
                "xf": xf, "ycar": ycar, "bci": bci, "bcn": bcn,
                "bank_free_dve": [0, 0], "bank_free_act": [0, 0],
                "of_free": [0, 0]}

        def tile_wout_ln3(oc0, tokw, r1_off):
            T = tokw
            nsub = T // 128
            P.wait("sync", s["dve"])
            P.wait("sync", s["pe"])
            t_c = P.dmas("sync", [
                (oht[:, 0:NCH * T].rearrange("p (c t) -> p c t", c=NCH),
                 _ap3(oh_sp, oc0, [[QTOT, 128], [128 * QTOT, NCH], [1, T]])),
                (olt[:, 0:NCH * T].rearrange("p (c t) -> p c t", c=NCH),
                 _ap3(ol_sp, oc0, [[QTOT, 128], [128 * QTOT, NCH], [1, T]])),
                (r1ct[:, 0:NCH * T].rearrange("p (c t) -> p c t", c=NCH),
                 _ap3(r1_sp, r1_off, [[S, 128], [128 * S, NCH], [1, T]]))])
            P.wait_tok("tensor", t_c)
            P.wait_tok("vector", t_c)
            for o in range(NCH):
                lo_ap = wo_lo_load(o)
                bank = pb[o % 2]
                P.wait("tensor", s["dve"], val=bufs["bank_free_dve"][o % 2])
                mms = ([(wo_hi_s[:, k * E + o * 128: k * E + o * 128 + 128],
                         oht[:, k * T:(k + 1) * T]) for k in range(NCH)] +
                       [(wo_hi_s[:, k * E + o * 128: k * E + o * 128 + 128],
                         olt[:, k * T:(k + 1) * T]) for k in range(NCH)] +
                       [(lo_ap[:, k * 128:(k + 1) * 128],
                         oht[:, k * T:(k + 1) * T]) for k in range(NCH)])
                n = len(mms)
                for i, (wap, xap) in enumerate(mms):
                    P.op("tensor",
                         lambda e, o_=bank[:, 0:T], w_=wap, x_=xap,
                                st=(i == 0), sp2=(i == n - 1):
                         e.matmul(o_, w_, x_, start=st, stop=sp2),
                         sem=(s["pe"] if i == n - 1 else None))
                P.wait("vector", s["pe"])
                P.op("vector", lambda e, o=o, T=T, b_=bank: e.scalar_tensor_tensor(
                    r2t[:, o * T:(o + 1) * T], b_[:, 0:T], bout_s[:, o:o + 1],
                    r1ct[:, o * T:(o + 1) * T], ALU.add, ALU.add), sem=s["dve"])
                bufs["bank_free_dve"][o % 2] = s["dve"].n
            for sub in range(nsub):
                P.wait("tensor", s["dve"])
                P.wait("tensor", s["act"])
                for c in range(NCH):
                    P.op("tensor", lambda e, o=pt_ps[:, c * 128:(c + 1) * 128],
                                i=r2t[:, c * T + sub * 128: c * T + sub * 128 + 128]:
                         e.transpose(o, i, id_f32_s[:, :]),
                         sem=(s["pe"] if c == NCH - 1 else None))
                P.wait("scalar", s["pe"])
                P.wait("scalar", s["dve"])
                P.op("scalar", lambda e: e.activation(
                    xsq[:, :], pt_ps[:, 0:1024], AF.Identity, bias=0.0, scale=1.0),
                    sem=s["act"])
                P.wait("vector", s["act"])
                P.op("vector", lambda e, o=st3["s1"][:, sub:sub + 1]:
                     e.reduce_sum(o, xsq[:, :], axis=AXX), sem=s["dve"])
                P.wait("scalar", s["dve"])
                P.op("scalar", lambda e: e.activation(
                    xsq[:, :], pt_ps[:, 0:1024], AF.Square, bias=0.0, scale=1.0),
                    sem=s["act"])
                P.wait("vector", s["act"])
                P.op("vector", lambda e, o=st3["s2"][:, sub:sub + 1]:
                     e.reduce_sum(o, xsq[:, :], axis=AXX), sem=s["dve"])
            ln_stats_chain(st3, nsub)
            bc_tiles(st3, nsub, bci_ps, bcn_ps)
            P.wait("vector", s["pe"])
            P.op("vector", lambda e, T=T: e.tensor_copy(bci[:, 0:T], bci_ps[:, 0:T]))
            P.op("vector", lambda e, T=T: e.tensor_copy(bcn[:, 0:T], bcn_ps[:, 0:T]),
                 sem=s["dve"])

        c_tiles = phases.get("C")
        if c_tiles != 0 and "NOXT" not in phases:
            tile_wout_ln3(OWN, XQW, XQ0)
        lc = XQW - 1
        for c in range(NCH if (c_tiles != 0 and "NOXT" not in phases) else 0):
            P.op("vector", lambda e, c=c: e.tensor_tensor(
                ytmp[:, 0:1], r2t[:, c * XQW + lc: c * XQW + lc + 1],
                bci[:, lc:lc + 1], ALU.mult))
            P.drain("vector")
            P.op("vector", lambda e, c=c: e.tensor_tensor(
                ytmp[:, 1:2], ytmp[:, 0:1], bcn[:, lc:lc + 1], ALU.add))
            P.drain("vector")
            P.op("vector", lambda e, c=c: e.tensor_tensor(
                x3car[:, c:c + 1], ytmp[:, 1:2], m0x_s[:, 0:1], ALU.mult),
                sem=s["dve"])

        NTC = OWN // TOK if c_tiles is None else min(c_tiles, OWN // TOK)
        for j in range(NTC):
            T0 = j * TOK
            tile_wout_ln3(T0, TOK, T0)
            if dbg and j == 0:
                P.wait("gpsimd", s["dve"])
                P.dmas("gpsimd", [(
                    _ap3(dbg_r2, 0, [[TOK, 128], [128 * TOK, NCH], [1, TOK]]),
                    r2t[:, :].rearrange("p (c t) -> p c t", c=NCH))])

            def carry_init(c, j=j):
                src = x3car if (j == 0 and "NOXT" not in phases) else ycar
                P.op("vector", lambda e, c=c, sr=src: e.tensor_copy(
                    y_ext[:, 0:1], sr[:, c:c + 1]))

            emit_mix(bufs, lambda c: r2t[:, c * TOK:(c + 1) * TOK], TOK,
                     carry_init, pre_normalized=False)
            P.wait("tensor", s["dve"])

            def out_store(o, src_ap, T0=T0):
                P.wait("gpsimd", s["dve"])
                out_toks[o % 2] = P.dmas("gpsimd", [
                    (out_fm.ap()[o * 128:(o + 1) * 128, T0:T0 + TOK], src_ap)])

            emit_ffn(bufs, wdict, TOK,
                     lambda o: r2t[:, o * TOK:(o + 1) * TOK], out_store, out_toks)

        P.wait_all_dma("gpsimd")
        P.wait_all_dma("sync")
        P.flush()

    P.flush()
    return P



_QKV_PERM = np.concatenate([
    np.concatenate([np.arange(h * 192 + j * 64, h * 192 + j * 64 + 64)
                    for h in range(H)])
    for j in range(3)])


def _col8(v):
    return np.ascontiguousarray(np.asarray(v, np.float32).reshape(NCH, 128).T)


def _core_inputs(shared, x, b, p):
    q0g = p * OWN
    xb = np.asarray(x[b], np.float32)
    xr = np.roll(xb, -q0g, axis=0)
    xprev = xb[q0g - 1:q0g] if p == 1 else np.zeros((1, E), np.float32)
    d = {
        "x_fm": np.ascontiguousarray(xr.T),
        "x_tm": np.ascontiguousarray(xr),
        "xprev": np.ascontiguousarray(xprev),
        "m2048": np.full((128, 1), 1.0 - p, np.float32),
        "m0x": np.full((128, 1), float(p), np.float32),
    }
    d.update(shared)
    return d


def kernel(L0_attn_x, L0_attn_Wqkv, L0_attn_bqkv, L0_attn_Wout, L0_attn_bout,
           L0_ffn_W_r, L0_ffn_W_k, L0_ffn_W_v, L0_ffn_mu_r, L0_ffn_mu_k):
    x = np.asarray(L0_attn_x, np.float32)
    shared = {
        "wqkv": np.ascontiguousarray(np.asarray(L0_attn_Wqkv, np.float32)[:, _QKV_PERM]),
        "bqkv": np.ascontiguousarray(
            np.asarray(L0_attn_bqkv, np.float32)[_QKV_PERM].reshape(24, 128).T),
        "wout": np.ascontiguousarray(np.asarray(L0_attn_Wout, np.float32)),
        "bout": _col8(L0_attn_bout),
        "wrT": np.ascontiguousarray(np.asarray(L0_ffn_W_r, np.float32).T),
        "wkT": np.ascontiguousarray(np.asarray(L0_ffn_W_k, np.float32).T),
        "wvT": np.ascontiguousarray(np.asarray(L0_ffn_W_v, np.float32).T),
        "mu_r": _col8(L0_ffn_mu_r),
        "mu_k": _col8(L0_ffn_mu_k),
    }

    nc = bass.Bass("TRN2", target_bir_lowering=False, detect_race_conditions=False)
    build_program(nc)

    in_maps = [_core_inputs(shared, x, c // 2, c % 2) for c in range(8)]
    res = run_bass_kernel_spmd(nc, in_maps, core_ids=list(range(8)))
    out = np.empty((B, S, E), np.float32)
    for c in range(8):
        b, p = c // 2, c % 2
        out[b, p * OWN:(p + 1) * OWN, :] = np.asarray(res.results[c]["out_fm"]).T
    return out


# version 09
# speedup vs baseline: 1.3788x; 1.1547x over previous
import numpy as np
import ml_dtypes
import concourse.bass as bass
import concourse.mybir as mybir
from concourse.bass_utils import run_bass_kernel_spmd

F32 = mybir.dt.float32
BF16 = mybir.dt.bfloat16
AF = mybir.ActivationFunctionType
ALU = mybir.AluOpType
AXX = mybir.AxisListType.X

B, S, E, H, HD = 4, 4096, 1024, 16, 64
NCH = E // 128
TOK = 512
TOKA = 256
NTA = S // TOKA
NSUBA = TOKA // 128
NSLOTA = 4
NT = S // TOK
NSUB = TOK // 128
EPS = 1e-5
OWN = 2048
XQ0, XQW = 3968, 128
QTOT = OWN + XQW
NKC = S // 128
M3 = 3 * E

BF16NP = ml_dtypes.bfloat16


class _Sem:
    def __init__(self, nc, name):
        self.h = nc.alloc_semaphore(name=name)
        self.n = 0


class PB:
    ENGS = ("sync", "gpsimd", "scalar", "vector", "tensor")

    def __init__(self, nc):
        self.nc = nc
        self.streams = {k: [] for k in self.ENGS}
        self._waited = {k: {} for k in self.ENGS}
        self._pools = {}

    def sem(self, name):
        return _Sem(self.nc, name)

    def _pool(self, eng):
        if eng not in self._pools:
            self._pools[eng] = {"sems": [self.sem(f"{eng}_dq{i}") for i in range(12)],
                                "i": 0}
        return self._pools[eng]

    def dmas(self, eng, pairs, slow=False):
        pool = self._pool(eng)
        sem = pool["sems"][pool["i"] % len(pool["sems"])]
        pool["i"] += 1
        if sem.n > 0:
            self.wait(eng, sem)
        for (dst, srcap) in pairs:
            self.op(eng, (lambda e, d=dst, s_=srcap, sl=slow:
                          e.dma_start(d, s_, allow_slow_non_contiguous=sl)),
                    sem=sem, k=16)
        return (sem, sem.n)

    def drain(self, eng):
        self.streams[eng].append(lambda e: e.drain())

    def wait_tok(self, eng, tok):
        if tok is not None:
            self.wait(eng, tok[0], tok[1])

    def wait_all_dma(self, eng_waiter):
        for qeng in list(self._pools):
            for sem in self._pools[qeng]["sems"]:
                self.wait(eng_waiter, sem)

    def wait_pool(self, eng_waiter, qeng):
        if qeng in self._pools:
            for sem in self._pools[qeng]["sems"]:
                self.wait(eng_waiter, sem)

    def op(self, eng, fn, sem=None, k=1):
        h = None
        if sem is not None:
            sem.n += k
            h = sem.h
        def run(e, fn=fn, h=h, k=k):
            inst = fn(e)
            if h is not None:
                inst.then_inc(h, k)
        self.streams[eng].append(run)

    def wait(self, eng, sem, val=None):
        v = sem.n if val is None else val
        if v <= 0:
            return
        if v <= self._waited[eng].get(id(sem), -1):
            return
        self._waited[eng][id(sem)] = v
        self.streams[eng].append(lambda e, h=sem.h, v=v: e.wait_ge(h, v))

    def emit(self, blk):
        for name in self.ENGS:
            ops = self.streams[name]
            if not ops:
                continue
            def body(e, ops=ops):
                for f in ops:
                    f(e)
            getattr(blk, name)(body)
        self.streams = {k: [] for k in self.ENGS}

    def flush(self):
        if any(self.streams.values()):
            with self.nc.Block() as blk:
                self.emit(blk)


def _ap3(handle, offset, dims):
    return bass.AP(handle, offset, [list(d) for d in dims])


def build_program(nc, dbg=False, phases=("A1", "A2", "B", "C")):
    phases = {p.split(":")[0]: (int(p.split(":")[1]) if ":" in p else None)
              for p in phases}
    spill_kind = "ExternalOutput" if dbg else "Internal"

    d_in = lambda n, shp, dt=F32: nc.dram_tensor(n, shp, dt, kind="ExternalInput")
    x_fm = d_in("x_fm", [E, S])
    x_tm = d_in("x_tm", [S, E])
    xprev = d_in("xprev", [1, E])
    m2048 = d_in("m2048", [128, 1])
    m0x = d_in("m0x", [128, 1])
    mu_r = d_in("mu_r", [128, NCH])
    mu_k = d_in("mu_k", [128, NCH])
    wrT = d_in("wrT", [E, E])
    wkT = d_in("wkT", [E, E])
    wvT = d_in("wvT", [E, E])
    wqkv = d_in("wqkv", [E, M3])
    bqkv = d_in("bqkv", [128, 24])
    wout = d_in("wout", [E, E])
    bout = d_in("bout", [128, NCH])
    out_fm = nc.dram_tensor("out_fm", [E, OWN], F32, kind="ExternalOutput")

    mk_sp = lambda n, shp, dt: nc.dram_tensor(n, shp, dt, kind=spill_kind)
    r1_sp = mk_sp("r1_sp", [E, S], F32)
    x2h_sp = mk_sp("x2h_sp", [E, S], BF16)
    x2l_sp = mk_sp("x2l_sp", [E, S], BF16)
    qh_sp = mk_sp("qh_sp", [E, S], BF16)
    ql_sp = mk_sp("ql_sp", [E, S], BF16)
    kh_sp = mk_sp("kh_sp", [E, S], BF16)
    kl_sp = mk_sp("kl_sp", [E, S], BF16)
    vth_sp = mk_sp("vth_sp", [H, S, HD], BF16)
    vtl_sp = mk_sp("vtl_sp", [H, S, HD], BF16)
    oh_sp = mk_sp("oh_sp", [E, QTOT], BF16)
    ol_sp = mk_sp("ol_sp", [E, QTOT], BF16)
    yprow_sp = mk_sp("yprow_sp", [1, E], F32)
    dbg_bci = mk_sp("dbg_bci", [128, TOK], F32)
    dbg_bcn = mk_sp("dbg_bcn", [128, TOK], F32)
    dbg_st = mk_sp("dbg_st", [128, 16 * NSUB], F32)
    dbg_xtm = mk_sp("dbg_xtm", [128, NSUB * 1024], F32)
    dbg_y = mk_sp("dbg_y", [128, TOK + 1], F32)
    dbg_r2 = mk_sp("dbg_r2", [E, TOK], F32)
    wr_lo = mk_sp("wr_lo", [E, E], BF16)
    wk_lo = mk_sp("wk_lo", [E, E], BF16)
    wv_lo = mk_sp("wv_lo", [E, E], BF16)
    wr_hi_sp = mk_sp("wr_hi_sp", [E, E], BF16)
    wk_hi_sp = mk_sp("wk_hi_sp", [E, E], BF16)
    wv_hi_sp = mk_sp("wv_hi_sp", [E, E], BF16)
    wq_lo = mk_sp("wq_lo", [E, M3], BF16)
    wo_lo = mk_sp("wo_lo", [E, E], BF16)

    id_f32 = nc.inline_tensor(np.eye(128, dtype=np.float32), name="id_f32")
    id_bf = nc.inline_tensor(np.eye(128, dtype=BF16NP), name="id_bf")
    ones_bf = nc.inline_tensor(np.ones((128, 1), dtype=BF16NP), name="ones_bf")
    dmask_np = np.zeros((128, 128), dtype=np.float32)
    dmask_np[0, 0:64] = dmask_np[32, 0:64] = 1.0
    dmask_np[64, 64:128] = dmask_np[96, 64:128] = 1.0
    dmask = nc.inline_tensor(dmask_np, name="dmask")

    P = PB(nc)
    s = {k: P.sem(k) for k in ("ld", "sp", "dve", "act", "pe", "bar", "out")}

    from contextlib import ExitStack

    def sb(ctx, name, shape, dt):
        return ctx.enter_context(nc.sbuf_tensor(name, shape, dt))

    def ps(ctx, name, shape, dt=F32):
        return ctx.enter_context(nc.psum_tensor(name, shape, dt))

    def phase_barrier():
        for eng in PB.ENGS:
            P.wait_all_dma(eng)
            for k in ("dve", "act", "pe"):
                P.wait(eng, s[k])

    id_f32_s = nc.alloc_sbuf_tensor("id_f32_s", [128, 128], F32)
    id_bf_s = nc.alloc_sbuf_tensor("id_bf_s", [128, 128], BF16)
    ones_bf_s = nc.alloc_sbuf_tensor("ones_bf_s", [128, 1], BF16)
    dmask_s = nc.alloc_sbuf_tensor("dmask_s", [128, 128], F32)
    mur_s = nc.alloc_sbuf_tensor("mur_s", [128, NCH], F32)
    muk_s = nc.alloc_sbuf_tensor("muk_s", [128, NCH], F32)
    bqkv_s = nc.alloc_sbuf_tensor("bqkv_s", [128, 24], F32)
    bout_s = nc.alloc_sbuf_tensor("bout_s", [128, NCH], F32)
    m2048_s = nc.alloc_sbuf_tensor("m2048_s", [128, 1], F32)
    m0x_s = nc.alloc_sbuf_tensor("m0x_s", [128, 1], F32)
    ypcol_s = nc.alloc_sbuf_tensor("ypcol_s", [128, NCH], F32)

    const_tok = P.dmas("sync", [
        (dst[:, :], src.ap()[:, :]) for dst, src in
        ((id_f32_s, id_f32), (id_bf_s, id_bf), (ones_bf_s, ones_bf),
         (dmask_s, dmask), (mur_s, mu_r), (muk_s, mu_k), (bqkv_s, bqkv),
         (bout_s, bout), (m2048_s, m2048), (m0x_s, m0x))])
    for eng in ("tensor", "scalar", "vector", "gpsimd"):
        P.wait_tok(eng, const_tok)

    def emit_yprev(xprow_s, yprow_s, pscal_s):
        t = P.dmas("sync", [(xprow_s[:, :], xprev.ap()[:, :])])
        P.wait_tok("scalar", t)
        P.wait_tok("vector", t)
        P.op("scalar", lambda e: e.activation(yprow_s[:, :], xprow_s[:, :],
                                              AF.Square, bias=0.0, scale=1.0), sem=s["act"])
        P.wait("vector", s["act"])
        P.op("vector", lambda e: e.reduce_sum(pscal_s[:, 0:1], xprow_s[:, :], axis=AXX))
        P.op("vector", lambda e: e.reduce_sum(pscal_s[:, 1:2], yprow_s[:, :], axis=AXX))
        P.drain("vector")
        P.op("vector", lambda e: e.tensor_scalar_mul(pscal_s[:, 2:3], pscal_s[:, 0:1], 1.0 / E))
        P.op("vector", lambda e: e.tensor_scalar_mul(pscal_s[:, 3:4], pscal_s[:, 1:2], 1.0 / E))
        P.drain("vector")
        P.op("vector", lambda e: e.tensor_mul(pscal_s[:, 4:5], pscal_s[:, 2:3], pscal_s[:, 2:3]))
        P.drain("vector")
        P.op("vector", lambda e: e.tensor_sub(pscal_s[:, 5:6], pscal_s[:, 3:4], pscal_s[:, 4:5]))
        P.drain("vector")
        P.op("vector", lambda e: e.tensor_scalar_add(pscal_s[:, 5:6], pscal_s[:, 5:6], EPS),
             sem=s["dve"])
        P.wait("scalar", s["dve"])
        P.op("scalar", lambda e: e.activation(pscal_s[:, 6:7], pscal_s[:, 5:6],
                                              AF.Sqrt, bias=0.0, scale=1.0), sem=s["act"])
        P.wait("vector", s["act"])
        P.op("vector", lambda e: e.reciprocal(pscal_s[:, 6:7], pscal_s[:, 6:7]))
        P.op("vector", lambda e: e.tensor_scalar_mul(pscal_s[:, 7:8], pscal_s[:, 2:3], -1.0))
        P.drain("vector")
        P.op("vector", lambda e: e.tensor_scalar(
            yprow_s[:, :], xprow_s[:, :], pscal_s[:, 7:8], pscal_s[:, 6:7],
            ALU.add, ALU.mult), sem=s["dve"])
        P.wait("gpsimd", s["dve"])
        t2 = P.dmas("gpsimd", [(yprow_sp.ap()[:, :], yprow_s[:, :])])
        P.wait_tok("sync", t2)
        return P.dmas("sync", [(ypcol_s[:, :],
                                _ap3(yprow_sp, 0, [[1, 128], [128, NCH]]))],
                      slow=True)



    def emit_weight_prep(w_dram_ap, m, hi_sb, lo_sp_h, hi_sp_h, tmp, lotmp,
                         prev_tok=None, tok_sink=None, side=None):
        ncho = m // 128
        sp_tok = prev_tok
        for k in range(NCH):
            P.wait("sync", s["dve"])
            t = P.dmas("sync", [(tmp[:, 0:m], w_dram_ap[k * 128:(k + 1) * 128, :])])
            P.wait_tok("scalar", t)
            hi_slice = hi_sb[:, k * m:(k + 1) * m]
            P.op("scalar", lambda e, o=hi_slice, i=tmp[:, 0:m]:
                 e.activation(o, i, AF.Identity, bias=0.0, scale=1.0), sem=s["act"])
            P.wait("vector", s["act"])
            P.wait_tok("vector", sp_tok)
            P.wait_tok("vector", t)
            P.op("vector", lambda e, o=lotmp[:, 0:m], i=tmp[:, 0:m], h=hi_slice:
                 e.tensor_tensor(o, i, h, ALU.subtract), sem=s["dve"])
            P.wait("gpsimd", s["dve"])
            grp = [(_ap3(lo_sp_h, k * 128,
                         [[NCH * 128, 128], [128 * NCH * 128, ncho], [1, 128]]),
                    lotmp[:, 0:m].rearrange("p (o c) -> p o c", o=ncho))]
            if hi_sp_h is not None:
                P.wait("gpsimd", s["act"])
                grp.append((hi_sp_h[k * 128:(k + 1) * 128, :], hi_slice))
            sp_tok = P.dmas("gpsimd", grp)
            if tok_sink is not None:
                tok_sink.append(sp_tok)
            if side:
                side.pop(0)()
        return sp_tok

    def mk_stats(ctx, tag, nsub):
        names = ("s1", "s2", "mean", "ex2", "m2", "var", "sd", "inv", "nm")
        return {n: sb(ctx, f"{tag}_{n}", [128, nsub], F32) for n in names}

    def ln_stats_chain(st, nsub):
        P.drain("vector")
        P.op("vector", lambda e: e.tensor_scalar_mul(st["mean"][:, 0:nsub], st["s1"][:, 0:nsub], 1.0 / E))
        P.op("vector", lambda e: e.tensor_scalar_mul(st["ex2"][:, 0:nsub], st["s2"][:, 0:nsub], 1.0 / E))
        P.drain("vector")
        P.op("vector", lambda e: e.tensor_mul(st["m2"][:, 0:nsub], st["mean"][:, 0:nsub], st["mean"][:, 0:nsub]))
        P.drain("vector")
        P.op("vector", lambda e: e.tensor_sub(st["var"][:, 0:nsub], st["ex2"][:, 0:nsub], st["m2"][:, 0:nsub]))
        P.drain("vector")
        P.op("vector", lambda e: e.tensor_scalar_add(st["var"][:, 0:nsub], st["var"][:, 0:nsub], EPS),
             sem=s["dve"])
        P.wait("scalar", s["dve"])
        P.op("scalar", lambda e: e.activation(st["sd"][:, 0:nsub], st["var"][:, 0:nsub],
                                              AF.Sqrt, bias=0.0, scale=1.0), sem=s["act"])
        P.wait("vector", s["act"])
        P.op("vector", lambda e: e.reciprocal(st["inv"][:, 0:nsub], st["sd"][:, 0:nsub]))
        P.drain("vector")
        P.op("vector", lambda e: e.tensor_mul(st["nm"][:, 0:nsub], st["mean"][:, 0:nsub], st["inv"][:, 0:nsub]))
        P.drain("vector")
        P.op("vector", lambda e: e.tensor_scalar_mul(st["nm"][:, 0:nsub], st["nm"][:, 0:nsub], -1.0),
             sem=s["dve"])

    def stat_sub(st, src, xsq, sub):
        P.op("vector", lambda e, o=xsq[:, :], i=src:
             e.tensor_tensor(o, i, i, ALU.mult))
        P.op("vector", lambda e, o=st["s1"][:, sub:sub + 1], i=src:
             e.reduce_sum(o, i, axis=AXX))
        P.op("vector", lambda e, o=st["s2"][:, sub:sub + 1], i=xsq[:, :]:
             e.reduce_sum(o, i, axis=AXX), sem=s["dve"])

    def emit_stats_from_tm(st, src_list, xsq, nsub):
        for sub in range(nsub):
            stat_sub(st, src_list[sub], xsq, sub)
        ln_stats_chain(st, nsub)

    def bc_tiles(st, nsub, bci_ps, bcn_ps, dve_val=None):
        P.wait("tensor", s["dve"], val=dve_val)
        for sub in range(nsub):
            iap = st["inv"][:, sub:sub + 1].broadcast_to([128, 128])
            nap = st["nm"][:, sub:sub + 1].broadcast_to([128, 128])
            P.op("tensor", lambda e, o=bci_ps[:, sub * 128:(sub + 1) * 128], w=iap:
                 e.matmul(o, w, id_f32_s[:, :], start=True, stop=True))
            P.op("tensor", lambda e, o=bcn_ps[:, sub * 128:(sub + 1) * 128], w=nap:
                 e.matmul(o, w, id_f32_s[:, :], start=True, stop=True),
                 sem=(s["pe"] if sub == nsub - 1 else None))

    def emit_ffn(bufs, w, tokw, resid_ap_fn, out_fn, out_toks,
                 side=None, out_ap_fn=None):
        T = tokw
        xr_hi, xr_lo = bufs["xr_hi"], bufs["xr_lo"]
        xk_hi, xk_lo = bufs["xk_hi"], bufs["xk_lo"]
        ka_hi, ka_lo = bufs["ka_hi"], bufs["ka_lo"]
        r_sig, kf, of, pb = bufs["r_sig"], bufs["kf"], bufs["of"], bufs["pb"]
        side = list(side) if side else []
        kmark = {"dve": 0, "act": 0}

        def sweep(proj, src_hi, src_lo, evac):
            whi = w[proj + "_hi"]
            for o in range(NCH):
                lo_ap = w["lo_load"](proj, o)
                bank = pb[o % 2]
                P.wait("tensor", s["dve"], val=bufs["bank_free_dve"][o % 2])
                P.wait("tensor", s["act"], val=bufs["bank_free_act"][o % 2])
                mms = ([(whi[:, k * E + o * 128: k * E + o * 128 + 128],
                         src_hi[:, k * T:(k + 1) * T]) for k in range(NCH)] +
                       [(whi[:, k * E + o * 128: k * E + o * 128 + 128],
                         src_lo[:, k * T:(k + 1) * T]) for k in range(NCH)] +
                       [(lo_ap[:, k * 128:(k + 1) * 128],
                         src_hi[:, k * T:(k + 1) * T]) for k in range(NCH)])
                n = len(mms)
                for i, (wap, xap) in enumerate(mms):
                    P.op("tensor",
                         lambda e, o_=bank[:, 0:T], w_=wap, x_=xap,
                                st=(i == 0), sp2=(i == n - 1):
                         e.matmul(o_, w_, x_, start=st, stop=sp2),
                         sem=(s["pe"] if i == n - 1 else None))
                evac(o, bank)
                if side:
                    side.pop(0)()

        def evac_r(o, bank):
            P.wait("scalar", s["pe"])
            P.op("scalar", lambda e, o_=r_sig[:, o * T:(o + 1) * T], i=bank[:, 0:T]:
                 e.activation(o_, i, AF.Sigmoid, bias=0.0, scale=1.0), sem=s["act"])
            bufs["bank_free_act"][o % 2] = s["act"].n

        def evac_k(o, bank):
            P.wait("vector", s["pe"])
            P.op("vector", lambda e, o_=kf[:, 0:T], i=bank[:, 0:T]:
                 e.tensor_scalar_max(o_, i, 0.0), sem=s["dve"])
            bufs["bank_free_dve"][o % 2] = s["dve"].n
            xfb = bufs.get("kxf", bufs["xf"])
            P.op("vector", lambda e, o_=xfb[:, 0:T], i=kf[:, 0:T]:
                 e.tensor_tensor(o_, i, i, ALU.mult))
            P.op("vector", lambda e, o_=ka_hi[:, o * T:(o + 1) * T], i=xfb[:, 0:T]:
                 e.tensor_copy(o_, i))
            P.op("vector", lambda e, o_=ka_lo[:, o * T:(o + 1) * T], i=xfb[:, 0:T],
                        h=ka_hi[:, o * T:(o + 1) * T]:
                 e.tensor_tensor(o_, i, h, ALU.subtract), sem=s["dve"])
            kmark["dve"] = s["dve"].n
            kmark["act"] = s["act"].n

        def evac_v(o, bank):
            P.wait("vector", s["pe"])
            P.wait("vector", s["act"])
            slot = o % 2
            if out_ap_fn is None:
                of_ap = of[:, slot * T:(slot + 1) * T]
                if out_toks is not None:
                    P.wait_tok("vector", out_toks[slot])
            else:
                of_ap = out_ap_fn(o)
            P.op("vector", lambda e, o_=kf[:, 0:T], i=bank[:, 0:T],
                        r=r_sig[:, o * T:(o + 1) * T]:
                 e.tensor_tensor(o_, i, r, ALU.mult), sem=s["dve"])
            bufs["bank_free_dve"][o % 2] = s["dve"].n
            P.op("vector", lambda e, o_=of_ap, i=kf[:, 0:T], r=resid_ap_fn(o):
                 e.scalar_tensor_tensor(o_, i, 0.5, r, ALU.mult, ALU.add),
                 sem=s["dve"])
            if out_fn is not None:
                out_fn(o, of_ap)

        sweep("wr", xr_hi, xr_lo, evac_r)
        sweep("wk", xk_hi, xk_lo, evac_k)
        P.wait("tensor", s["dve"], val=kmark["dve"])
        P.wait("tensor", s["act"], val=kmark["act"])
        sweep("wv", ka_hi, ka_lo, evac_v)
        while side:
            side.pop(0)()

    def emit_mix_chunk(bufs, c, x_c, T, carry_init_fn, pre_normalized):
        y_ext, ytmp, dbuf, xf = bufs["y_ext"], bufs["ytmp"], bufs["dbuf"], bufs["xf"]
        carry_init_fn(c)
        if pre_normalized:
            P.op("vector", lambda e, x=x_c, T=T: e.tensor_copy(y_ext[:, 1:T + 1], x))
        else:
            P.op("vector", lambda e, x=x_c, T=T, b=bufs["bci"][:, 0:T]:
                 e.tensor_tensor(ytmp[:, 0:T], x, b, ALU.mult))
            P.op("vector", lambda e, T=T, b=bufs["bcn"][:, 0:T]:
                 e.tensor_tensor(y_ext[:, 1:T + 1], ytmp[:, 0:T], b, ALU.add))
        P.op("vector", lambda e, T=T: e.tensor_tensor(
            dbuf[:, 0:T], y_ext[:, 1:T + 1], y_ext[:, 0:T], ALU.subtract))
        for (mu, hi, lo) in ((mur_s, bufs["xr_hi"], bufs["xr_lo"]),
                             (muk_s, bufs["xk_hi"], bufs["xk_lo"])):
            P.op("vector", lambda e, T=T, m=mu[:, c:c + 1]:
                 e.scalar_tensor_tensor(xf[:, 0:T], dbuf[:, 0:T], m,
                                        y_ext[:, 0:T], ALU.mult, ALU.add))
            P.op("vector", lambda e, T=T, o=hi[:, c * T:(c + 1) * T]:
                 e.tensor_copy(o, xf[:, 0:T]))
            P.op("vector", lambda e, T=T, o=lo[:, c * T:(c + 1) * T],
                        h=hi[:, c * T:(c + 1) * T]:
                 e.tensor_tensor(o, xf[:, 0:T], h, ALU.subtract), sem=s["dve"])
        P.op("vector", lambda e, c=c, T=T: e.tensor_copy(
            bufs["ycar"][:, c:c + 1], y_ext[:, T:T + 1]), sem=s["dve"])

    def emit_mix(bufs, x_src_fn, tokw, carry_init_fn, pre_normalized):
        for c in range(NCH):
            emit_mix_chunk(bufs, c, x_src_fn(c), tokw, carry_init_fn,
                           pre_normalized)

    with ExitStack() as ctx:
        with ExitStack() as yctx:
            xprow_s = sb(yctx, "xprow_s", [1, E], F32)
            yprow_s = sb(yctx, "yprow_s", [1, E], F32)
            pscal_s = sb(yctx, "pscal_s", [1, 8], F32)
            ypcol_tok = emit_yprev(xprow_s, yprow_s, pscal_s)
        wtmp = sb(ctx, "p0_wtmp", [128, E], F32)
        lotmp = sb(ctx, "p0_lotmp", [128, E], BF16)
        wr_hi_s = sb(ctx, "wr_hi_s", [128, NCH * E], BF16)
        wk_hi_s = sb(ctx, "wk_hi_s", [128, NCH * E], BF16)
        wv_hi_s = sb(ctx, "wv_hi_s", [128, NCH * E], BF16)
        prep_toks = {"wr": [], "wk": [], "wv": []}

        xs_b = [sb(ctx, f"a1_xs{i}", [128, NCH * TOKA], F32) for i in range(2)]
        xtm_b = [sb(ctx, f"a1_xtm{i}", [128, NSUBA * 1024], F32) for i in range(2)]
        xsq = sb(ctx, "a1_xsq", [128, 1024], F32)
        pt_sb = sb(ctx, "a1_ptsb", [128, NSUBA * 1024], F32)
        st1 = mk_stats(ctx, "a1_st1", NSUBA)
        st2 = mk_stats(ctx, "a1_st2", NSUBA)
        bci1_b = [sb(ctx, f"a1_bci{i}", [128, TOKA], F32) for i in range(2)]
        bcn1_b = [sb(ctx, f"a1_bcn{i}", [128, TOKA], F32) for i in range(2)]
        bci2 = sb(ctx, "a1_bci2", [128, TOKA], F32)
        bcn2 = sb(ctx, "a1_bcn2", [128, TOKA], F32)
        y_ext = sb(ctx, "a1_yext", [128, TOKA + 1], F32)
        ytmp = sb(ctx, "a1_ytmp", [128, TOKA], F32)
        dbuf = sb(ctx, "a1_dbuf", [128, TOKA], F32)
        xf = sb(ctx, "a1_xf", [128, TOKA], F32)
        kxf = sb(ctx, "a1_kxf", [128, TOKA], F32)
        ycar = sb(ctx, "a1_ycar", [128, NCH], F32)
        xr_hi_b = [sb(ctx, f"a1_xrh{i}", [128, NCH * TOKA], BF16) for i in range(2)]
        xr_lo_b = [sb(ctx, f"a1_xrl{i}", [128, NCH * TOKA], BF16) for i in range(2)]
        xk_hi_b = [sb(ctx, f"a1_xkh{i}", [128, NCH * TOKA], BF16) for i in range(2)]
        xk_lo_b = [sb(ctx, f"a1_xkl{i}", [128, NCH * TOKA], BF16) for i in range(2)]
        ka_hi = sb(ctx, "a1_ka_hi", [128, NCH * TOKA], BF16)
        ka_lo = sb(ctx, "a1_ka_lo", [128, NCH * TOKA], BF16)
        r_sig = sb(ctx, "a1_rsig", [128, NCH * TOKA], F32)
        kf = sb(ctx, "a1_kf", [128, TOKA], F32)
        of = sb(ctx, "a1_of", [128, 2 * TOKA], F32)
        r1t_b = [sb(ctx, f"a1_r1{i}", [128, NCH * TOKA], F32) for i in range(2)]
        wlo_ring = sb(ctx, "a1_wlo", [128, NSLOTA * NCH * 128], BF16)
        x2h = sb(ctx, "a1_x2h", [128, 2 * TOKA], BF16)
        x2l = sb(ctx, "a1_x2l", [128, 2 * TOKA], BF16)
        pb = [ps(ctx, "a1_pb0", [128, TOKA], F32), ps(ctx, "a1_pb1", [128, TOKA], F32)]
        bci_ps = ps(ctx, "a1_bci_ps", [128, TOKA], F32)
        bcn_ps = ps(ctx, "a1_bcn_ps", [128, TOKA], F32)
        pt_ps = ps(ctx, "a1_pt_ps", [128, 1024], F32)

        wlo_sps = {"wr": wr_lo, "wk": wk_lo, "wv": wv_lo}
        ring_state = {"i": 0, "loaded": {}}
        x2_free = [None, None]
        r1_sp_tok = [None, None]
        t_xs = [None, None]
        t_xtm = [None, None]
        ffn_pe_mark = [0, 0]
        ffn_dve_mark = [0, 0]
        mix_mark = [{"dve": 0, "act": 0}, {"dve": 0, "act": 0}]

        def lo_load(proj, o):
            slot = ring_state["i"] % NSLOTA
            ring_state["i"] += 1
            base = slot * NCH * 128
            src = _ap3(wlo_sps[proj], o * NCH * 128 * 128,
                       [[NCH * 128, 128], [128, NCH], [1, 128]])
            for pt_ in prep_toks[proj]:
                P.wait_tok("sync", pt_)
            P.wait("sync", s["pe"], val=ring_state["loaded"].get(slot, 0))
            t = P.dmas("sync", [(wlo_ring[:, base:base + NCH * 128], src)])
            P.wait_tok("tensor", t)
            ring_state["loaded"][slot] = s["pe"].n + 1
            return wlo_ring[:, base:base + NCH * 128]

        wdict = {"wr_hi": wr_hi_s, "wk_hi": wk_hi_s, "wv_hi": wv_hi_s,
                 "lo_load": lo_load}
        bank_free_dve = [0, 0]
        bank_free_act = [0, 0]
        bufs_u = []
        for i in range(2):
            bufs_u.append({
                "xr_hi": xr_hi_b[i], "xr_lo": xr_lo_b[i],
                "xk_hi": xk_hi_b[i], "xk_lo": xk_lo_b[i],
                "ka_hi": ka_hi, "ka_lo": ka_lo, "r_sig": r_sig, "kf": kf,
                "kxf": kxf, "of": of, "pb": pb, "y_ext": y_ext, "ytmp": ytmp,
                "dbuf": dbuf, "xf": xf, "ycar": ycar,
                "bci": bci1_b[i], "bcn": bcn1_b[i],
                "bank_free_dve": bank_free_dve, "bank_free_act": bank_free_act})

        def emit_tile_loads(jj):
            u2 = jj % 2
            T0j = jj * TOKA
            P.wait("scalar", s["dve"], val=ffn_dve_mark[u2])
            t_xtm[u2] = P.dmas("scalar", [
                (xtm_b[u2][:, :].rearrange("p (s f) -> p s f", s=NSUBA),
                 _ap3(x_tm, T0j * E, [[E, 128], [128 * E, NSUBA], [1, E]]))])
            t_xs[u2] = P.dmas("scalar", [
                (xs_b[u2][:, :].rearrange("p (c t) -> p c t", c=NCH),
                 _ap3(x_fm, T0j, [[S, 128], [128 * S, NCH], [1, TOKA]]))])

        def mk_carry(jj):
            def carry_init(c, jj=jj):
                if jj == 0:
                    P.op("vector", lambda e, c=c: e.tensor_copy(
                        y_ext[:, 0:1], ypcol_s[:, c:c + 1]))
                elif jj == OWN // TOKA:
                    P.op("vector", lambda e, c=c: e.tensor_tensor(
                        y_ext[:, 0:1], ycar[:, c:c + 1], m2048_s[:, 0:1],
                        ALU.mult))
                else:
                    P.op("vector", lambda e, c=c: e.tensor_copy(
                        y_ext[:, 0:1], ycar[:, c:c + 1]))
            return carry_init

        def prod_thunks(jj):
            u2 = jj % 2
            th = []
            for sub in range(NSUBA):
                def t_stat(sub=sub, u2=u2):
                    P.wait_tok("scalar", t_xtm[u2])
                    P.wait_tok("vector", t_xtm[u2])
                    stat_sub(st1, xtm_b[u2][:, sub * 1024:(sub + 1) * 1024],
                             xsq, sub)
                th.append(t_stat)
            th.append(lambda: ln_stats_chain(st1, NSUBA))

            def t_bc1(u2=u2):
                bc_tiles(st1, NSUBA, bci_ps, bcn_ps)
                P.wait("vector", s["pe"])
                P.op("vector", lambda e: e.tensor_copy(bci1_b[u2][:, :],
                                                       bci_ps[:, 0:TOKA]))
                P.op("vector", lambda e: e.tensor_copy(bcn1_b[u2][:, :],
                                                       bcn_ps[:, 0:TOKA]),
                     sem=s["dve"])
                P.drain("vector")
            th.append(t_bc1)
            ci = mk_carry(jj)
            for c in range(NCH):
                def t_mix(c=c, u2=u2, ci=ci):
                    P.wait("vector", s["pe"], val=ffn_pe_mark[u2])
                    P.wait_tok("vector", t_xs[u2])
                    emit_mix_chunk(bufs_u[u2], c,
                                   xs_b[u2][:, c * TOKA:(c + 1) * TOKA],
                                   TOKA, ci, False)
                    if c == NCH - 1:
                        mix_mark[u2]["dve"] = s["dve"].n
                        mix_mark[u2]["act"] = s["act"].n
                th.append(t_mix)
            return th

        def ln2_stat_thunks(jj):
            th = []
            for sub in range(NSUBA):
                def t_l(sub=sub):
                    P.wait("vector", s["act"])
                    stat_sub(st2, pt_sb[:, sub * 1024:(sub + 1) * 1024],
                             xsq, sub)
                th.append(t_l)
            th.append(lambda: ln_stats_chain(st2, NSUBA))

            def t_bc2():
                bc_tiles(st2, NSUBA, bci_ps, bcn_ps)
                P.wait("vector", s["pe"])
                P.op("vector", lambda e: e.tensor_copy(bci2[:, :],
                                                       bci_ps[:, 0:TOKA]))
                P.op("vector", lambda e: e.tensor_copy(bcn2[:, :],
                                                       bcn_ps[:, 0:TOKA]),
                     sem=s["dve"])
                P.drain("vector")
            th.append(t_bc2)
            return th

        def ln2_split_thunks(jj):
            u = jj % 2
            T0j = jj * TOKA
            th = []
            for c in range(NCH):
                def t_sp(c=c, u=u, T0j=T0j):
                    slot = c % 2
                    sl = slice(slot * TOKA, (slot + 1) * TOKA)
                    P.op("vector", lambda e, c=c, u=u: e.tensor_tensor(
                        ytmp[:, 0:TOKA], r1t_b[u][:, c * TOKA:(c + 1) * TOKA],
                        bci2[:, :], ALU.mult))
                    P.op("vector", lambda e: e.tensor_tensor(
                        xf[:, 0:TOKA], ytmp[:, 0:TOKA], bcn2[:, :], ALU.add))
                    P.wait_tok("vector", x2_free[slot])
                    P.op("vector", lambda e, sl=sl: e.tensor_copy(
                        x2h[:, sl], xf[:, 0:TOKA]))
                    P.op("vector", lambda e, sl=sl: e.tensor_tensor(
                        x2l[:, sl], xf[:, 0:TOKA], x2h[:, sl], ALU.subtract),
                        sem=s["dve"])
                    P.wait("gpsimd", s["dve"])
                    x2_free[slot] = P.dmas("gpsimd", [
                        (x2h_sp.ap()[c * 128:(c + 1) * 128, T0j:T0j + TOKA],
                         x2h[:, sl]),
                        (x2l_sp.ap()[c * 128:(c + 1) * 128, T0j:T0j + TOKA],
                         x2l[:, sl])])
                th.append(t_sp)
            return th

        def emit_ln2_transposes(jj):
            u = jj % 2
            for sub in range(NSUBA):
                P.wait("tensor", s["dve"])
                P.wait("tensor", s["act"])
                for c in range(NCH):
                    P.op("tensor", lambda e, u=u,
                                o=pt_ps[:, c * 128:(c + 1) * 128],
                                i=r1t_b[u][:, c * TOKA + sub * 128:
                                            c * TOKA + sub * 128 + 128]:
                         e.transpose(o, i, id_f32_s[:, :]),
                         sem=(s["pe"] if c == NCH - 1 else None))
                P.wait("scalar", s["pe"])
                P.wait("scalar", s["dve"])
                P.op("scalar", lambda e, sub=sub: e.activation(
                    pt_sb[:, sub * 1024:(sub + 1) * 1024], pt_ps[:, 0:1024],
                    AF.Identity, bias=0.0, scale=1.0), sem=s["act"])

        NTA_run = NTA if phases.get("A1") in (None, 0) else min(phases["A1"], NTA)
        side0 = []
        if "A1" in phases and NTA_run:
            emit_tile_loads(0)
            P.wait_tok("vector", ypcol_tok)
            side0 = prod_thunks(0)
        wp_tok = None
        for (pj, wd, hs, lsp, hsp) in (("wr", wrT, wr_hi_s, wr_lo, wr_hi_sp),
                                       ("wk", wkT, wk_hi_s, wk_lo, wk_hi_sp),
                                       ("wv", wvT, wv_hi_s, wv_lo, wv_hi_sp)):
            wp_tok = emit_weight_prep(wd.ap(), E, hs, lsp, hsp.ap(), wtmp,
                                      lotmp, prev_tok=wp_tok,
                                      tok_sink=prep_toks[pj], side=side0)
        while side0:
            side0.pop(0)()
        if "A1" in phases and NTA_run:
            P.wait("tensor", s["act"])
            for j in range(NTA_run):
                u = j % 2
                if j + 1 < NTA_run:
                    emit_tile_loads(j + 1)
                pt = prod_thunks(j + 1) if j + 1 < NTA_run else None
                lt = ln2_stat_thunks(j - 1) if j >= 1 else None
                sp = ln2_split_thunks(j - 1) if j >= 1 else None
                side = []
                if lt:
                    side += lt[:2]
                if pt:
                    side += pt[:2]
                if lt:
                    side.append(lt[2])
                if pt:
                    side.append(pt[2])
                if lt:
                    side.append(lt[3])
                if pt:
                    side.append(pt[3])
                for c in range(NCH):
                    if sp:
                        side.append(sp[c])
                    if pt:
                        side.append(pt[4 + c])
                P.wait("tensor", s["dve"], val=mix_mark[u]["dve"])
                P.wait("tensor", s["act"], val=mix_mark[u]["act"])
                P.wait_tok("vector", r1_sp_tok[u])
                emit_ffn(bufs_u[u], wdict, TOKA,
                         (lambda o, u=u: xs_b[u][:, o * TOKA:(o + 1) * TOKA]),
                         None, None, side=side,
                         out_ap_fn=(lambda o, u=u:
                                    r1t_b[u][:, o * TOKA:(o + 1) * TOKA]))
                ffn_pe_mark[u] = s["pe"].n
                ffn_dve_mark[u] = s["dve"].n
                emit_ln2_transposes(j)
                P.wait("gpsimd", s["dve"])
                r1_sp_tok[u] = P.dmas("gpsimd", [
                    (_ap3(r1_sp, j * TOKA, [[S, 128], [128 * S, NCH], [1, TOKA]]),
                     r1t_b[u][:, :].rearrange("p (c t) -> p c t", c=NCH))])
            for t in ln2_stat_thunks(NTA_run - 1) + ln2_split_thunks(NTA_run - 1):
                t()
        phase_barrier()
        P.flush()


    if "A2" in phases:
      with ExitStack() as ctx:
        wq_hi_s = sb(ctx, "wq_hi_s", [128, NCH * M3], BF16)
        with ExitStack() as wctx:
            wtmp = sb(wctx, "p1_wtmp", [128, M3], F32)
            lotmp = sb(wctx, "p1_lotmp", [128, M3], BF16)
            emit_weight_prep(wqkv.ap(), M3, wq_hi_s, wq_lo, None, wtmp, lotmp)
        phase_barrier()

        x2ht_b = [sb(ctx, f"a2_x2h{i}", [128, NCH * TOK], BF16) for i in range(2)]
        x2lt_b = [sb(ctx, f"a2_x2l{i}", [128, NCH * TOK], BF16) for i in range(2)]
        qf = sb(ctx, "a2_qf", [128, TOK], F32)
        sth = sb(ctx, "a2_sth", [128, TOK], BF16)
        stl = sb(ctx, "a2_stl", [128, TOK], BF16)
        vst = sb(ctx, "a2_vst", [128, 2 * TOK], BF16)
        wlo_ring = sb(ctx, "a2_wlo", [128, 2 * NCH * 128], BF16)
        pb = [ps(ctx, "a2_pb0", [128, TOK], F32), ps(ctx, "a2_pb1", [128, TOK], F32)]
        ptb = [ps(ctx, "a2_pt0", [128, 2 * TOK], BF16),
               ps(ctx, "a2_pt1", [128, 2 * TOK], BF16)]

        ring_state = {"i": 0, "loaded": {}}

        def qkv_lo_load(o):
            slot = ring_state["i"] % 2
            ring_state["i"] += 1
            base = slot * NCH * 128
            src = _ap3(wq_lo, o * NCH * 128 * 128,
                       [[NCH * 128, 128], [128, NCH], [1, 128]])
            P.wait("sync", s["pe"], val=ring_state["loaded"].get(slot, 0))
            t = P.dmas("sync", [(wlo_ring[:, base:base + NCH * 128], src)])
            P.wait_tok("tensor", t)
            ring_state["loaded"][slot] = s["pe"].n + 1
            return wlo_ring[:, base:base + NCH * 128]

        bank_free = {"dve": [0, 0], "act": [0, 0]}
        ptk_free = [0, 0]
        vst_free = [None, None]
        qk_sp_tok = [None]

        tile_pe_free = [0, 0]
        tile_tok = [None, None]

        def a2_load(j):
            u = j % 2
            T0j = j * TOK
            P.wait("sync", s["pe"], val=tile_pe_free[u])
            return P.dmas("sync", [
                (x2ht_b[u][:, :].rearrange("p (c t) -> p c t", c=NCH),
                 _ap3(x2h_sp, T0j, [[S, 128], [128 * S, NCH], [1, TOK]])),
                (x2lt_b[u][:, :].rearrange("p (c t) -> p c t", c=NCH),
                 _ap3(x2l_sp, T0j, [[S, 128], [128 * S, NCH], [1, TOK]]))])

        tile_tok[0] = a2_load(0)
        for j in range(NT):
            T0 = j * TOK
            u = j % 2
            if j + 1 < NT:
                tile_tok[(j + 1) % 2] = a2_load(j + 1)
            x2ht, x2lt = x2ht_b[u], x2lt_b[u]
            P.wait_tok("tensor", tile_tok[u])
            for o in range(24):
                is_q = o < 8
                is_v = o >= 16
                if is_q:
                    if T0 < OWN:
                        cs, cw = 0, TOK
                    elif T0 + TOK == S:
                        cs, cw = XQ0 - T0, S - XQ0
                    else:
                        continue
                else:
                    cs, cw = 0, TOK
                lo_ap = qkv_lo_load(o)
                bank = pb[o % 2]
                P.wait("tensor", s["dve"], val=bank_free["dve"][o % 2])
                P.wait("tensor", s["act"], val=bank_free["act"][o % 2])
                mms = ([(wq_hi_s[:, k * M3 + o * 128: k * M3 + o * 128 + 128],
                         x2ht[:, k * TOK + cs: k * TOK + cs + cw]) for k in range(NCH)] +
                       [(wq_hi_s[:, k * M3 + o * 128: k * M3 + o * 128 + 128],
                         x2lt[:, k * TOK + cs: k * TOK + cs + cw]) for k in range(NCH)] +
                       [(lo_ap[:, k * 128:(k + 1) * 128],
                         x2ht[:, k * TOK + cs: k * TOK + cs + cw]) for k in range(NCH)])
                n = len(mms)
                for i, (wap, xap) in enumerate(mms):
                    P.op("tensor",
                         lambda e, o_=bank[:, 0:cw], w_=wap, x_=xap,
                                st=(i == 0), sp2=(i == n - 1):
                         e.matmul(o_, w_, x_, start=st, stop=sp2),
                         sem=(s["pe"] if i == n - 1 else None))
                P.wait("vector", s["pe"])
                P.wait_tok("vector", qk_sp_tok[0])
                if is_q:
                    P.op("vector", lambda e, o_=qf[:, 0:cw], i=bank[:, 0:cw],
                                bb=bqkv_s[:, o:o + 1]:
                         e.tensor_scalar(o_, i, bb, 0.125, ALU.add, ALU.mult),
                         sem=s["dve"])
                else:
                    P.op("vector", lambda e, o_=qf[:, 0:cw], i=bank[:, 0:cw],
                                bb=bqkv_s[:, o:o + 1]:
                         e.tensor_scalar(o_, i, bb, None, ALU.add), sem=s["dve"])
                bank_free["dve"][o % 2] = s["dve"].n
                P.op("vector", lambda e, o_=sth[:, 0:cw], i=qf[:, 0:cw]:
                     e.tensor_copy(o_, i))
                P.op("vector", lambda e, o_=stl[:, 0:cw], i=qf[:, 0:cw], h=sth[:, 0:cw]:
                     e.tensor_tensor(o_, i, h, ALU.subtract), sem=s["dve"])
                if not is_v:
                    sp_h = qh_sp if is_q else kh_sp
                    sp_l = ql_sp if is_q else kl_sp
                    row = (o if is_q else o - 8) * 128
                    P.wait("gpsimd", s["dve"])
                    qk_sp_tok[0] = P.dmas("gpsimd", [
                        (sp_h.ap()[row:row + 128, T0 + cs:T0 + cs + cw], sth[:, 0:cw]),
                        (sp_l.ap()[row:row + 128, T0 + cs:T0 + cs + cw], stl[:, 0:cw])])
                else:
                    hA = (o - 16) * 2
                    P.wait("tensor", s["dve"])
                    for hl, stt in ((0, sth), (1, stl)):
                        ptk = ptb[hl]
                        P.wait("tensor", s["act"], val=ptk_free[hl])
                        for sub in range(NSUB):
                            P.op("tensor", lambda e,
                                        o_=ptk[:, sub * 128:(sub + 1) * 128],
                                        i=stt[:, sub * 128:(sub + 1) * 128]:
                                 e.transpose(o_, i, id_bf_s[:, :]),
                                 sem=(s["pe"] if sub == NSUB - 1 else None))
                        P.wait("scalar", s["pe"])
                        P.wait_tok("scalar", vst_free[hl])
                        vsl = vst[:, hl * TOK:(hl + 1) * TOK]
                        P.op("scalar", lambda e, o_=vsl, i=ptk[:, 0:TOK]:
                             e.activation(o_, i, AF.Identity, bias=0.0,
                                          scale=1.0), sem=s["act"])
                        ptk_free[hl] = s["act"].n
                        P.wait("gpsimd", s["act"])
                        spv = vth_sp if hl == 0 else vtl_sp
                        vst_free[hl] = P.dmas("gpsimd", [
                            (_ap3(spv, (hA + h) * S * HD + T0 * HD,
                                  [[HD, 128], [128 * HD, NSUB], [1, HD]]),
                             _ap3(vst, hl * TOK + h * HD,
                                  [[2 * TOK, 128], [128, NSUB], [1, HD]]))
                            for h in range(2)])
            tile_pe_free[u] = s["pe"].n
        phase_barrier()
        P.flush()

    QGS = [(0, 512), (512, 512), (1024, 512), (1536, 512), (2048, XQW)]
    if "B" in phases:
      with ExitStack() as ctx:
        kp_hi = [sb(ctx, f"b_kph{i}", [128, S], BF16) for i in range(2)]
        kp_lo = [sb(ctx, f"b_kpl{i}", [128, S], BF16) for i in range(2)]
        qp_hi = [sb(ctx, f"b_qph{i}", [128, QTOT], BF16) for i in range(2)]
        qp_lo = [sb(ctx, f"b_qpl{i}", [128, QTOT], BF16) for i in range(2)]
        vA_hi = [sb(ctx, f"b_vah{i}", [128, NKC * HD], BF16) for i in range(2)]
        vA_lo = [sb(ctx, f"b_val{i}", [128, NKC * HD], BF16) for i in range(2)]
        vB_hi = [sb(ctx, f"b_vbh{i}", [128, NKC * HD], BF16) for i in range(2)]
        vB_lo = [sb(ctx, f"b_vbl{i}", [128, NKC * HD], BF16) for i in range(2)]
        paf = [sb(ctx, f"b_paf{i}", [128, 512], F32) for i in range(4)]
        pbf = [sb(ctx, f"b_pbf{i}", [128, 512], F32) for i in range(4)]
        pah = [sb(ctx, f"b_pah{i}", [128, 512], BF16) for i in range(4)]
        pal = [sb(ctx, f"b_pal{i}", [128, 512], BF16) for i in range(4)]
        pbh = [sb(ctx, f"b_pbh{i}", [128, 512], BF16) for i in range(4)]
        pbl = [sb(ctx, f"b_pbl{i}", [128, 512], BF16) for i in range(4)]
        dsb = sb(ctx, "b_dsb", [128, 512], F32)
        binv = sb(ctx, "b_binv", [128, 512], F32)
        o_f = sb(ctx, "b_of", [128, 512], F32)
        o_hi = sb(ctx, "b_ohi", [128, 512], BF16)
        o_lo = sb(ctx, "b_olo", [128, 512], BF16)
        st_ps = [(ps(ctx, "b_stA0", [128, 512], F32), ps(ctx, "b_stB0", [128, 512], F32)),
                 (ps(ctx, "b_stA1", [128, 512], F32), ps(ctx, "b_stB1", [128, 512], F32))]
        pv = ps(ctx, "b_pv", [128, 512], F32)
        dps = ps(ctx, "b_dps", [128, 512], F32)
        ddp = ps(ctx, "b_ddp", [128, 512], F32)
        o_sp_tok = [None]
        P.op("vector", lambda e: e.memset(dsb[:, :], 0.0))
        for t in (st_ps[0][0], st_ps[0][1], st_ps[1][0], st_ps[1][1], pv, dps, ddp):
            P.op("vector", lambda e, t=t: e.memset(t[:, :], 0.0))
        P.op("vector", lambda e: e.engine_nop(), sem=s["dve"])

        pair_free_pe = [0, 0]
        pair_tok = [None, None]
        g_exp = [0, 0]
        g_lo = [0, 0, 0, 0]
        g_pv = [0, 0, 0, 0]

        def emit_pair_loads(pc):
            u = pc % 2
            grp = []
            for dst, spv in ((kp_hi[u], kh_sp), (kp_lo[u], kl_sp)):
                grp.append((dst[:, :], spv.ap()[pc * 128:(pc + 1) * 128, :]))
            for dst, spv in ((qp_hi[u], qh_sp), (qp_lo[u], ql_sp)):
                grp.append((dst[:, 0:OWN], spv.ap()[pc * 128:(pc + 1) * 128, 0:OWN]))
                grp.append((dst[:, OWN:QTOT], spv.ap()[pc * 128:(pc + 1) * 128, XQ0:S]))
            for dst, spv, h in ((vA_hi[u], vth_sp, 2 * pc), (vA_lo[u], vtl_sp, 2 * pc),
                                (vB_hi[u], vth_sp, 2 * pc + 1), (vB_lo[u], vtl_sp, 2 * pc + 1)):
                grp.append((dst[:, :].rearrange("p (c d) -> p c d", c=NKC),
                            _ap3(spv, h * S * HD, [[HD, 128], [128 * HD, NKC], [1, HD]])))
            P.wait("sync", s["pe"], val=pair_free_pe[u])
            return P.dmas("sync", grp)

        pair_tok[0] = emit_pair_loads(0)
        for pc in range(8):
            u = pc % 2
            if pc + 1 < 8:
                pair_tok[(pc + 1) % 2] = emit_pair_loads(pc + 1)
            P.wait_tok("tensor", pair_tok[u])
            kph, kpl = kp_hi[u], kp_lo[u]
            qph, qpl = qp_hi[u], qp_lo[u]
            vah, val_, vbh, vbl = vA_hi[u], vA_lo[u], vB_hi[u], vB_lo[u]

            for (g0, gw) in QGS:
                lo_done = {}
                castA_done = {}

                def emit_pv(cc, g0=g0, gw=gw, vah=vah, val_=val_, vbh=vbh, vbl=vbl):
                    v = cc % 4
                    P.wait("tensor", s["dve"], val=lo_done[cc])
                    P.wait("tensor", s["act"], val=castA_done[cc])
                    pvs = [(vah, pah[v]), (val_, pah[v]), (vah, pal[v])]
                    pvsB = [(vbh, pbh[v]), (vbl, pbh[v]), (vbh, pbl[v])]
                    st_c, sp_c = (cc == 0), (cc == NKC - 1)
                    for i in range(3):
                        va_, pa_ = pvs[i]
                        vb_, pb_ = pvsB[i]
                        st_i = st_c and (i == 0)
                        sp_i = sp_c and (i == 2)
                        P.op("tensor", lambda e, w_=va_[:, cc * HD:(cc + 1) * HD],
                                    x_=pa_[:, 0:gw], a=st_i, b_=sp_i:
                             e.matmul(pv[0:64, 0:gw], w_, x_, start=a, stop=b_,
                                      tile_position=(0, 0), skip_group_check=True))
                        P.op("tensor", lambda e, w_=vb_[:, cc * HD:(cc + 1) * HD],
                                    x_=pb_[:, 0:gw], a=st_i, b_=sp_i:
                             e.matmul(pv[64:128, 0:gw], w_, x_, start=a, stop=b_,
                                      tile_position=(0, 64), skip_group_check=True))
                    dmov = [(pah[v], 0), (pal[v], 32), (pbh[v], 64), (pbl[v], 96)]
                    for i, (pm, col) in enumerate(dmov):
                        P.op("tensor", lambda e, o_=dps[col:col + 1, 0:gw],
                                    x_=pm[:, 0:gw], a=st_c, b_=sp_c, col=col:
                             e.matmul(o_, ones_bf_s[:, :], x_, start=a, stop=b_,
                                      tile_position=(0, col), skip_group_check=True),
                             sem=(s["pe"] if i == len(dmov) - 1 else None))
                    g_pv[v] = s["pe"].n

                for c in range(NKC):
                    v = c % 4
                    v2 = c % 2
                    stA, stB = st_ps[v2]
                    P.wait("tensor", s["act"], val=g_exp[v2])
                    qk = [(kph, qph), (kph, qpl), (kpl, qph)]
                    for i, (kk, qq) in enumerate(qk):
                        st_, sp2_ = (i == 0), (i == len(qk) - 1)
                        P.op("tensor", lambda e, o_=stA[:, 0:gw],
                                    w_=kk[0:64, c * 128:(c + 1) * 128],
                                    x_=qq[0:64, g0:g0 + gw], a=st_, b_=sp2_:
                             e.matmul(o_, w_, x_, start=a, stop=b_,
                                      tile_position=(0, 0)))
                        P.op("tensor", lambda e, o_=stB[:, 0:gw],
                                    w_=kk[64:128, c * 128:(c + 1) * 128],
                                    x_=qq[64:128, g0:g0 + gw], a=st_, b_=sp2_:
                             e.matmul(o_, w_, x_, start=a, stop=b_,
                                      tile_position=(64, 0)),
                             sem=(s["pe"] if sp2_ else None))
                    qk_done_c = s["pe"].n
                    if c >= 2:
                        emit_pv(c - 2)
                    P.wait("scalar", s["pe"], val=qk_done_c)
                    P.wait("scalar", s["dve"], val=g_lo[v])
                    P.op("scalar", lambda e, v=v, gw=gw, sA=stA: e.activation(
                        paf[v][:, 0:gw], sA[:, 0:gw], AF.Exp, bias=0.0, scale=1.0))
                    P.op("scalar", lambda e, v=v, gw=gw, sB=stB: e.activation(
                        pbf[v][:, 0:gw], sB[:, 0:gw], AF.Exp, bias=0.0, scale=1.0),
                        sem=s["act"])
                    g_exp[v2] = s["act"].n
                    exp_done_c = s["act"].n
                    P.wait("scalar", s["pe"], val=g_pv[v])
                    P.op("scalar", lambda e, v=v, gw=gw: e.activation(
                        pah[v][:, 0:gw], paf[v][:, 0:gw], AF.Identity,
                        bias=0.0, scale=1.0), sem=s["act"])
                    castA_done[c] = s["act"].n
                    P.wait("vector", s["act"], val=exp_done_c)
                    P.wait("vector", s["pe"], val=g_pv[v])
                    P.op("vector", lambda e, v=v, gw=gw: e.tensor_copy(
                        pbh[v][:, 0:gw], pbf[v][:, 0:gw]))
                    P.wait("vector", s["act"], val=castA_done[c])
                    P.op("vector", lambda e, v=v, gw=gw: e.tensor_tensor(
                        pal[v][:, 0:gw], paf[v][:, 0:gw], pah[v][:, 0:gw], ALU.subtract))
                    P.op("vector", lambda e, v=v, gw=gw: e.tensor_tensor(
                        pbl[v][:, 0:gw], pbf[v][:, 0:gw], pbh[v][:, 0:gw], ALU.subtract),
                        sem=s["dve"])
                    lo_done[c] = s["dve"].n
                    g_lo[v] = s["dve"].n
                emit_pv(NKC - 2)
                emit_pv(NKC - 1)

                P.wait("vector", s["pe"])
                for drow in (0, 32, 64, 96):
                    P.op("vector", lambda e, gw=gw, dr=drow: e.tensor_copy(
                        dsb[dr:dr + 1, 0:gw], dps[dr:dr + 1, 0:gw]),
                        sem=(s["dve"] if drow == 96 else None))
                P.wait("tensor", s["dve"])
                P.op("tensor", lambda e, gw=gw: e.matmul(
                    ddp[:, 0:gw], dmask_s[:, :], dsb[:, 0:gw], start=True, stop=True),
                    sem=s["pe"])
                P.wait("vector", s["pe"])
                P.op("vector", lambda e, gw=gw: e.reciprocal(binv[:, 0:gw], ddp[:, 0:gw]),
                     sem=s["dve"])
                P.op("vector", lambda e, gw=gw: e.tensor_tensor(
                    o_f[:, 0:gw], pv[:, 0:gw], binv[:, 0:gw], ALU.mult), sem=s["dve"])
                P.wait("scalar", s["dve"])
                P.wait_tok("scalar", o_sp_tok[0])
                P.op("scalar", lambda e, gw=gw: e.activation(
                    o_hi[:, 0:gw], o_f[:, 0:gw], AF.Identity, bias=0.0, scale=1.0),
                    sem=s["act"])
                P.wait("vector", s["act"])
                P.wait_tok("vector", o_sp_tok[0])
                P.op("vector", lambda e, gw=gw: e.tensor_tensor(
                    o_lo[:, 0:gw], o_f[:, 0:gw], o_hi[:, 0:gw], ALU.subtract),
                    sem=s["dve"])
                P.wait("gpsimd", s["dve"])
                P.wait("gpsimd", s["act"])
                o_sp_tok[0] = P.dmas("gpsimd", [
                    (oh_sp.ap()[pc * 128:(pc + 1) * 128, g0:g0 + gw], o_hi[:, 0:gw]),
                    (ol_sp.ap()[pc * 128:(pc + 1) * 128, g0:g0 + gw], o_lo[:, 0:gw])])
            pair_free_pe[u] = s["pe"].n
        phase_barrier()
        P.flush()

    if "C" in phases:
      with ExitStack() as ctx:
        wo_hi_s = sb(ctx, "wo_hi_s", [128, NCH * E], BF16)
        wr_hi_s2 = sb(ctx, "wr_hi_s2", [128, NCH * E], BF16)
        wk_hi_s2 = sb(ctx, "wk_hi_s2", [128, NCH * E], BF16)
        wv_hi_s2 = sb(ctx, "wv_hi_s2", [128, NCH * E], BF16)
        with ExitStack() as wctx:
            wtmp = sb(wctx, "p2_wtmp", [128, E], F32)
            lotmp = sb(wctx, "p2_lotmp", [128, E], BF16)
            emit_weight_prep(wout.ap(), E, wo_hi_s, wo_lo, None, wtmp, lotmp)
        P.dmas("sync", [
            (hs[:, :].rearrange("p (k m) -> p k m", k=NCH),
             _ap3(hsp, 0, [[E, 128], [128 * E, NCH], [1, E]]))
            for hs, hsp in ((wr_hi_s2, wr_hi_sp), (wk_hi_s2, wk_hi_sp),
                            (wv_hi_s2, wv_hi_sp))])
        phase_barrier()

        oht = sb(ctx, "c_oh", [128, NCH * TOK], BF16)
        olt = sb(ctx, "c_ol", [128, NCH * TOK], BF16)
        r1ct = sb(ctx, "c_r1", [128, NCH * TOK], F32)
        r2t = sb(ctx, "c_r2", [128, NCH * TOK], F32)
        xsq = sb(ctx, "c_xsq", [128, 1024], F32)
        st3 = mk_stats(ctx, "c_st3", NSUB)
        bci = sb(ctx, "c_bci", [128, TOK], F32)
        bcn = sb(ctx, "c_bcn", [128, TOK], F32)
        y_ext = sb(ctx, "c_yext", [128, TOK + 1], F32)
        ytmp = sb(ctx, "c_ytmp", [128, TOK], F32)
        dbuf = sb(ctx, "c_dbuf", [128, TOK], F32)
        xf = sb(ctx, "c_xf", [128, TOK], F32)
        ycar = sb(ctx, "c_ycar", [128, NCH], F32)
        x3car = sb(ctx, "c_x3car", [128, NCH], F32)
        xr_hi = sb(ctx, "c_xr_hi", [128, NCH * TOK], BF16)
        xr_lo = sb(ctx, "c_xr_lo", [128, NCH * TOK], BF16)
        xk_hi = sb(ctx, "c_xk_hi", [128, NCH * TOK], BF16)
        xk_lo = sb(ctx, "c_xk_lo", [128, NCH * TOK], BF16)
        ka_hi = sb(ctx, "c_ka_hi", [128, NCH * TOK], BF16)
        ka_lo = sb(ctx, "c_ka_lo", [128, NCH * TOK], BF16)
        r_sig = sb(ctx, "c_rsig", [128, NCH * TOK], F32)
        kf = sb(ctx, "c_kf", [128, TOK], F32)
        of = sb(ctx, "c_of", [128, 2 * TOK], F32)
        wlo_ring = sb(ctx, "c_wlo", [128, 2 * NCH * 128], BF16)
        wolo_ring = wlo_ring
        pb = [ps(ctx, "c_pb0", [128, TOK], F32), ps(ctx, "c_pb1", [128, TOK], F32)]
        bci_ps = ps(ctx, "c_bci_ps", [128, TOK], F32)
        bcn_ps = ps(ctx, "c_bcn_ps", [128, TOK], F32)
        pt_ps = ps(ctx, "c_pt_ps", [128, 1024], F32)
        for t in (pb[0], pb[1], bci_ps, bcn_ps, pt_ps):
            P.op("vector", lambda e, t=t: e.memset(t[:, :], 0.0))
        P.op("vector", lambda e: e.memset(ycar[:, :], 0.0))
        P.op("vector", lambda e: e.memset(x3car[:, :], 0.0))
        P.op("vector", lambda e: e.engine_nop(), sem=s["dve"])

        wlo_sps = {"wr": wr_lo, "wk": wk_lo, "wv": wv_lo}
        ring_state = {"i": 0, "loaded": {}}
        wo_ring = {"i": 0, "loaded": {}}
        out_toks = [None, None]

        def lo_load(proj, o):
            slot = ring_state["i"] % 2
            ring_state["i"] += 1
            base = slot * NCH * 128
            src = _ap3(wlo_sps[proj], o * NCH * 128 * 128,
                       [[NCH * 128, 128], [128, NCH], [1, 128]])
            P.wait("sync", s["pe"], val=ring_state["loaded"].get(slot, 0))
            t = P.dmas("sync", [(wlo_ring[:, base:base + NCH * 128], src)])
            P.wait_tok("tensor", t)
            ring_state["loaded"][slot] = s["pe"].n + 1
            return wlo_ring[:, base:base + NCH * 128]

        def wo_lo_load(o):
            slot = ring_state["i"] % 2
            ring_state["i"] += 1
            base = slot * NCH * 128
            src = _ap3(wo_lo, o * NCH * 128 * 128,
                       [[NCH * 128, 128], [128, NCH], [1, 128]])
            P.wait("sync", s["pe"], val=ring_state["loaded"].get(slot, 0))
            t = P.dmas("sync", [(wolo_ring[:, base:base + NCH * 128], src)])
            P.wait_tok("tensor", t)
            ring_state["loaded"][slot] = s["pe"].n + 1
            return wolo_ring[:, base:base + NCH * 128]

        wdict = {"wr_hi": wr_hi_s2, "wk_hi": wk_hi_s2, "wv_hi": wv_hi_s2,
                 "lo_load": lo_load}
        bufs = {"xr_hi": xr_hi, "xr_lo": xr_lo, "xk_hi": xk_hi, "xk_lo": xk_lo,
                "ka_hi": ka_hi, "ka_lo": ka_lo, "r_sig": r_sig, "kf": kf,
                "of": of, "pb": pb, "y_ext": y_ext, "ytmp": ytmp, "dbuf": dbuf,
                "xf": xf, "ycar": ycar, "bci": bci, "bcn": bcn,
                "bank_free_dve": [0, 0], "bank_free_act": [0, 0],
                "of_free": [0, 0]}

        def tile_wout_ln3(oc0, tokw, r1_off):
            T = tokw
            nsub = T // 128
            P.wait("sync", s["dve"])
            P.wait("sync", s["pe"])
            t_c = P.dmas("sync", [
                (oht[:, 0:NCH * T].rearrange("p (c t) -> p c t", c=NCH),
                 _ap3(oh_sp, oc0, [[QTOT, 128], [128 * QTOT, NCH], [1, T]])),
                (olt[:, 0:NCH * T].rearrange("p (c t) -> p c t", c=NCH),
                 _ap3(ol_sp, oc0, [[QTOT, 128], [128 * QTOT, NCH], [1, T]])),
                (r1ct[:, 0:NCH * T].rearrange("p (c t) -> p c t", c=NCH),
                 _ap3(r1_sp, r1_off, [[S, 128], [128 * S, NCH], [1, T]]))])
            P.wait_tok("tensor", t_c)
            P.wait_tok("vector", t_c)
            for o in range(NCH):
                lo_ap = wo_lo_load(o)
                bank = pb[o % 2]
                P.wait("tensor", s["dve"], val=bufs["bank_free_dve"][o % 2])
                mms = ([(wo_hi_s[:, k * E + o * 128: k * E + o * 128 + 128],
                         oht[:, k * T:(k + 1) * T]) for k in range(NCH)] +
                       [(wo_hi_s[:, k * E + o * 128: k * E + o * 128 + 128],
                         olt[:, k * T:(k + 1) * T]) for k in range(NCH)] +
                       [(lo_ap[:, k * 128:(k + 1) * 128],
                         oht[:, k * T:(k + 1) * T]) for k in range(NCH)])
                n = len(mms)
                for i, (wap, xap) in enumerate(mms):
                    P.op("tensor",
                         lambda e, o_=bank[:, 0:T], w_=wap, x_=xap,
                                st=(i == 0), sp2=(i == n - 1):
                         e.matmul(o_, w_, x_, start=st, stop=sp2),
                         sem=(s["pe"] if i == n - 1 else None))
                P.wait("vector", s["pe"])
                P.op("vector", lambda e, o=o, T=T, b_=bank: e.scalar_tensor_tensor(
                    r2t[:, o * T:(o + 1) * T], b_[:, 0:T], bout_s[:, o:o + 1],
                    r1ct[:, o * T:(o + 1) * T], ALU.add, ALU.add), sem=s["dve"])
                bufs["bank_free_dve"][o % 2] = s["dve"].n
            for sub in range(nsub):
                P.wait("tensor", s["dve"])
                P.wait("tensor", s["act"])
                for c in range(NCH):
                    P.op("tensor", lambda e, o=pt_ps[:, c * 128:(c + 1) * 128],
                                i=r2t[:, c * T + sub * 128: c * T + sub * 128 + 128]:
                         e.transpose(o, i, id_f32_s[:, :]),
                         sem=(s["pe"] if c == NCH - 1 else None))
                P.wait("scalar", s["pe"])
                P.wait("scalar", s["dve"])
                P.op("scalar", lambda e: e.activation(
                    xsq[:, :], pt_ps[:, 0:1024], AF.Identity, bias=0.0, scale=1.0),
                    sem=s["act"])
                P.wait("vector", s["act"])
                P.op("vector", lambda e, o=st3["s1"][:, sub:sub + 1]:
                     e.reduce_sum(o, xsq[:, :], axis=AXX), sem=s["dve"])
                P.wait("scalar", s["dve"])
                P.op("scalar", lambda e: e.activation(
                    xsq[:, :], pt_ps[:, 0:1024], AF.Square, bias=0.0, scale=1.0),
                    sem=s["act"])
                P.wait("vector", s["act"])
                P.op("vector", lambda e, o=st3["s2"][:, sub:sub + 1]:
                     e.reduce_sum(o, xsq[:, :], axis=AXX), sem=s["dve"])
            ln_stats_chain(st3, nsub)
            bc_tiles(st3, nsub, bci_ps, bcn_ps)
            P.wait("vector", s["pe"])
            P.op("vector", lambda e, T=T: e.tensor_copy(bci[:, 0:T], bci_ps[:, 0:T]))
            P.op("vector", lambda e, T=T: e.tensor_copy(bcn[:, 0:T], bcn_ps[:, 0:T]),
                 sem=s["dve"])

        c_tiles = phases.get("C")
        if c_tiles != 0 and "NOXT" not in phases:
            tile_wout_ln3(OWN, XQW, XQ0)
        lc = XQW - 1
        for c in range(NCH if (c_tiles != 0 and "NOXT" not in phases) else 0):
            P.op("vector", lambda e, c=c: e.tensor_tensor(
                ytmp[:, 0:1], r2t[:, c * XQW + lc: c * XQW + lc + 1],
                bci[:, lc:lc + 1], ALU.mult))
            P.drain("vector")
            P.op("vector", lambda e, c=c: e.tensor_tensor(
                ytmp[:, 1:2], ytmp[:, 0:1], bcn[:, lc:lc + 1], ALU.add))
            P.drain("vector")
            P.op("vector", lambda e, c=c: e.tensor_tensor(
                x3car[:, c:c + 1], ytmp[:, 1:2], m0x_s[:, 0:1], ALU.mult),
                sem=s["dve"])

        NTC = OWN // TOK if c_tiles is None else min(c_tiles, OWN // TOK)
        for j in range(NTC):
            T0 = j * TOK
            tile_wout_ln3(T0, TOK, T0)
            if dbg and j == 0:
                P.wait("gpsimd", s["dve"])
                P.dmas("gpsimd", [(
                    _ap3(dbg_r2, 0, [[TOK, 128], [128 * TOK, NCH], [1, TOK]]),
                    r2t[:, :].rearrange("p (c t) -> p c t", c=NCH))])

            def carry_init(c, j=j):
                src = x3car if (j == 0 and "NOXT" not in phases) else ycar
                P.op("vector", lambda e, c=c, sr=src: e.tensor_copy(
                    y_ext[:, 0:1], sr[:, c:c + 1]))

            emit_mix(bufs, lambda c: r2t[:, c * TOK:(c + 1) * TOK], TOK,
                     carry_init, pre_normalized=False)
            P.wait("tensor", s["dve"])

            def out_store(o, src_ap, T0=T0):
                P.wait("gpsimd", s["dve"])
                out_toks[o % 2] = P.dmas("gpsimd", [
                    (out_fm.ap()[o * 128:(o + 1) * 128, T0:T0 + TOK], src_ap)])

            emit_ffn(bufs, wdict, TOK,
                     lambda o: r2t[:, o * TOK:(o + 1) * TOK], out_store, out_toks)

        P.wait_all_dma("gpsimd")
        P.wait_all_dma("sync")
        P.flush()

    P.flush()
    return P



_QKV_PERM = np.concatenate([
    np.concatenate([np.arange(h * 192 + j * 64, h * 192 + j * 64 + 64)
                    for h in range(H)])
    for j in range(3)])


def _col8(v):
    return np.ascontiguousarray(np.asarray(v, np.float32).reshape(NCH, 128).T)


def _core_inputs(shared, x, b, p):
    q0g = p * OWN
    xb = np.asarray(x[b], np.float32)
    xr = np.roll(xb, -q0g, axis=0)
    xprev = xb[q0g - 1:q0g] if p == 1 else np.zeros((1, E), np.float32)
    d = {
        "x_fm": np.ascontiguousarray(xr.T),
        "x_tm": np.ascontiguousarray(xr),
        "xprev": np.ascontiguousarray(xprev),
        "m2048": np.full((128, 1), 1.0 - p, np.float32),
        "m0x": np.full((128, 1), float(p), np.float32),
    }
    d.update(shared)
    return d


def kernel(L0_attn_x, L0_attn_Wqkv, L0_attn_bqkv, L0_attn_Wout, L0_attn_bout,
           L0_ffn_W_r, L0_ffn_W_k, L0_ffn_W_v, L0_ffn_mu_r, L0_ffn_mu_k):
    x = np.asarray(L0_attn_x, np.float32)
    shared = {
        "wqkv": np.ascontiguousarray(np.asarray(L0_attn_Wqkv, np.float32)[:, _QKV_PERM]),
        "bqkv": np.ascontiguousarray(
            np.asarray(L0_attn_bqkv, np.float32)[_QKV_PERM].reshape(24, 128).T),
        "wout": np.ascontiguousarray(np.asarray(L0_attn_Wout, np.float32)),
        "bout": _col8(L0_attn_bout),
        "wrT": np.ascontiguousarray(np.asarray(L0_ffn_W_r, np.float32).T),
        "wkT": np.ascontiguousarray(np.asarray(L0_ffn_W_k, np.float32).T),
        "wvT": np.ascontiguousarray(np.asarray(L0_ffn_W_v, np.float32).T),
        "mu_r": _col8(L0_ffn_mu_r),
        "mu_k": _col8(L0_ffn_mu_k),
    }

    nc = bass.Bass("TRN2", target_bir_lowering=False, detect_race_conditions=False)
    build_program(nc)

    in_maps = [_core_inputs(shared, x, c // 2, c % 2) for c in range(8)]
    res = run_bass_kernel_spmd(nc, in_maps, core_ids=list(range(8)))
    out = np.empty((B, S, E), np.float32)
    for c in range(8):
        b, p = c // 2, c % 2
        out[b, p * OWN:(p + 1) * OWN, :] = np.asarray(res.results[c]["out_fm"]).T
    return out
